# version 24
# speedup vs baseline: 1.0288x; 1.0086x over previous
import numpy as np
from contextlib import ExitStack

import concourse.bass as bass
import concourse.mybir as mybir
from concourse.bass_utils import run_bass_kernel_spmd

F32 = mybir.dt.float32
BF16 = mybir.dt.bfloat16
AF = mybir.ActivationFunctionType
ALU = mybir.AluOpType
AX = mybir.AxisListType

B, SEQ, DM = 4, 4096, 1024
T = 2048
DMAMBA, DSTATE, DTRANK = 2048, 16, 64
FF0 = 4096
H, DK, DV = 6, 128, 256
KD, VD = H * DK, H * DV
NE, HID = 8, 1408
MAGIC = 12582912.0

ENGINES = ("sync", "scalar", "vector", "tensor", "gpsimd")


class Prog:

    def __init__(self, nc):
        self.nc = nc
        self.es = ExitStack()
        self.pes = ExitStack()
        self.sems = {}
        self.cnt = {}
        self.cur = []
        self.wr = {}
        self.rd = {}
        self.waited = {e: {} for e in ENGINES}
        self.floor = {}
        self.pend = {}
        self.nops = 0
        self.consts = {}
        self.psum_keys = set()
        self._cpool = self.es.enter_context(nc.sbuf_tensor("constpool", [128, 8], F32))

    def constcol(self, val):
        if val not in self.consts:
            i = len(self.consts)
            assert i < 8
            key = f"__const{i}"
            self.op("vector", lambda v, i=i, val=val: v.memset(
                self._cpool[:, i:i + 1], val), wr=(key,))
            self.consts[val] = (self._cpool[:, i:i + 1], key)
        return self.consts[val]

    def _sem(self, name):
        if name not in self.sems:
            self.sems[name] = self.es.enter_context(self.nc.semaphore(name))
            self.cnt[name] = 0
        return self.sems[name]

    def sb(self, name, shape, dtype, scope="phase"):
        if isinstance(scope, ExitStack):
            es = scope
        else:
            es = self.pes if scope == "phase" else self.es
        return es.enter_context(self.nc.sbuf_tensor(name, list(shape), dtype))

    def ps(self, name, shape, dtype=F32, scope="phase"):
        if isinstance(scope, ExitStack):
            es = scope
        else:
            es = self.pes if scope == "phase" else self.es
        self.psum_keys.add(name)
        return es.enter_context(self.nc.psum_tensor(name, list(shape), dtype))

    def _emit_phase(self):
        if not self.cur:
            return
        per = {}
        for rec in self.cur:
            per.setdefault(rec[0], []).append(rec)
        sems = self.sems
        with self.nc.Block() as block:
            for eng, ops in per.items():
                def mk(ops=ops):
                    def body(engobj):
                        for (_, fn, wl, tok, sem) in ops:
                            for (s, v) in wl:
                                engobj.wait_ge(sems[s], v)
                            if fn is None:
                                continue
                            inst = fn(engobj)
                            if tok is not None:
                                amt = 16 if sem.startswith("dma_") else 1
                                inst.then_inc(sems[sem], amt)
                    return body
                getattr(block, eng)(mk())
        self.cur = []

    def barrier(self, keep=()):
        assert not self.pend, f"open psum groups at barrier: {list(self.pend)}"
        drains = {s: v for s, v in self.cnt.items()
                  if s.startswith("dma_") and v > self.floor.get(s, 0)}
        if drains:
            self._record("sync", None, drains, None, "sync")
        self._emit_phase()
        self.pes.close()
        self.pes = ExitStack()
        for e in ENGINES:
            self.waited[e] = {}
        self.floor = dict(self.cnt)
        saved = {k: self.wr[k] for k in keep if k in self.wr}
        for k, (s, v) in saved.items():
            self.floor[s] = min(self.floor.get(s, 0), v - 1)
        self.wr = dict(saved)
        self.rd = {}

    def _need(self, waits, tok):
        if tok is None:
            return
        s, v = tok
        if v <= self.floor.get(s, 0):
            return
        if waits.get(s, 0) < v:
            waits[s] = v

    def _record(self, eng, fn, waits, tok, sem):
        wl = []
        for s, v in waits.items():
            if self.waited[eng].get(s, 0) < v:
                wl.append((s, v))
                self.waited[eng][s] = v
        self.cur.append((eng, fn, wl, tok, sem))
        self.nops += 1

    def op(self, eng, fn, rd=(), wr=(), dma=False, stream=None):
        assert eng in ENGINES
        sem = ("dma_" + (stream or eng)) if dma else eng
        self._sem(sem)
        waits = {}
        if dma:
            self._need(waits, (sem, self.cnt[sem]))
        for k in rd:
            assert k not in self.pend, f"read of open psum group {k}"
            self._need(waits, self.wr.get(k))
            if k in self.psum_keys:
                for t in self.rd.get(k, ()):
                    self._need(waits, t)
        for k in wr:
            assert k not in self.pend, f"write to open psum group {k}"
            self._need(waits, self.wr.get(k))
            for t in self.rd.get(k, ()):
                self._need(waits, t)
        amt = 16 if dma else 1
        self.cnt[sem] += amt
        tok = (sem, self.cnt[sem])
        self._record(eng, fn, waits, tok, sem)
        for k in wr:
            self.wr[k] = tok
            self.rd[k] = []
        for k in rd:
            self.rd.setdefault(k, []).append(tok)
        return tok

    def mm_group(self, psum_key, members):
        n = len(members)
        assert n >= 1
        all_rd = set()
        first_waits = {}
        self._need(first_waits, self.wr.get(psum_key))
        for t in self.rd.get(psum_key, ()):
            self._need(first_waits, t)
        self._sem("tensor")
        for i, (fn, rdk) in enumerate(members):
            waits = dict(first_waits) if i == 0 else {}
            for k in rdk:
                assert k not in self.pend
                self._need(waits, self.wr.get(k))
                all_rd.add(k)
            last = i == n - 1
            if last:
                self.cnt["tensor"] += 1
                tok = ("tensor", self.cnt["tensor"])
            else:
                tok = None
            self._record("tensor", fn, waits, tok, "tensor")
        self.wr[psum_key] = tok
        self.rd[psum_key] = []
        for k in all_rd:
            self.rd.setdefault(k, []).append(tok)
        return tok

    def mm_acc(self, psum_key, fn, rd=(), start=False, stop=False):
        self._sem("tensor")
        waits = {}
        if start:
            assert psum_key not in self.pend
            self._need(waits, self.wr.get(psum_key))
            for t in self.rd.get(psum_key, ()):
                self._need(waits, t)
            self.pend[psum_key] = set()
        else:
            assert psum_key in self.pend
        for k in rd:
            self._need(waits, self.wr.get(k))
        self.pend[psum_key].update(rd)
        tok = None
        if stop:
            self.cnt["tensor"] += 1
            tok = ("tensor", self.cnt["tensor"])
        self._record("tensor", fn, waits, tok, "tensor")
        if stop:
            for k in self.pend[psum_key]:
                self.rd.setdefault(k, []).append(tok)
            del self.pend[psum_key]
            self.wr[psum_key] = tok
            self.rd[psum_key] = []
        return tok

    def realize(self):
        self.barrier()
        return self.nc


def cdiv(a, b):
    return (a + b - 1) // b


def ternarize(p, src_dram, n_row_tiles, ncols, dst_sb, dst_stride, tag, tmp,
              sgn=None):
    CW = min(ncols, int(tmp.shape[1]) // 2)
    nh = ncols // CW
    acc = p.sb(f"tacc{tag}", [128, 2], F32)
    red = p.sb(f"tred{tag}", [128, 2], F32)
    ones1 = p.sb(f"tones{tag}", [128, 128], F32)
    if sgn is None:
        sgn = p.sb(f"tsgn{tag}", [128, 2 * CW], F32)
    rsc_ps = p.ps(f"trs{tag}", [128, 1], F32)
    rsc = p.sb(f"trsc{tag}", [128, 1], F32)
    p.op("vector", lambda v: v.memset(ones1[:, :], 1.0), wr=(f"ones1{tag}",))
    p.op("vector", lambda v: v.memset(acc[:, :], 0.0), wr=(f"acc{tag}",))
    for idx in range(n_row_tiles * nh):
        i, hh, par = idx // nh, idx % nh, idx % 2
        p.op("gpsimd", lambda g, i=i, hh=hh, par=par: g.dma_start(
            out=tmp[:, par * CW:par * CW + CW],
            in_=src_dram[i * 128:(i + 1) * 128, hh * CW:(hh + 1) * CW]),
            wr=(f"ttmp{par}",), dma=True, stream=f"tz{tag}{par}")
        p.op("vector", lambda v, par=par: v.reduce_sum(
            red[:, par:par + 1], tmp[:, par * CW:par * CW + CW], AX.X,
            apply_absolute_value=True),
            rd=(f"ttmp{par}",), wr=(f"red{tag}{par}",))
        p.op("vector", lambda v, par=par: v.tensor_add(
            acc[:, 0:1], acc[:, 0:1], red[:, par:par + 1]),
            rd=(f"red{tag}{par}", f"acc{tag}"), wr=(f"acc{tag}",))
    p.mm_group(f"rsps{tag}", [
        (lambda t: t.matmul(rsc_ps[0:1, 0:1], ones1[:, 0:1], acc[:, 0:1],
                            start=True, stop=True), (f"acc{tag}", f"ones1{tag}"))])
    p.op("vector", lambda v: v.tensor_scalar(
        red[0:1, 0:1], rsc_ps[0:1, 0:1], 1.0 / (n_row_tiles * 128 * ncols), 1e-8,
        ALU.mult, ALU.max), rd=(f"rsps{tag}",), wr=(f"red{tag}0",))
    p.op("vector", lambda v: v.reciprocal(red[0:1, 0:1], red[0:1, 0:1]),
         rd=(f"red{tag}0",), wr=(f"red{tag}0",))
    p.mm_group(f"rsps{tag}", [
        (lambda t: t.matmul(rsc_ps[:, 0:1], ones1[0:1, :], red[0:1, 0:1],
                            start=True, stop=True), (f"red{tag}0", f"ones1{tag}"))])
    p.op("vector", lambda v: v.tensor_copy(rsc[:, :], rsc_ps[:, :]),
         rd=(f"rsps{tag}",), wr=(f"rsc{tag}",))
    for idx in range(n_row_tiles * nh):
        i, hh, par = idx // nh, idx % nh, idx % 2
        p.op("gpsimd", lambda g, i=i, hh=hh, par=par: g.dma_start(
            out=tmp[:, par * CW:par * CW + CW],
            in_=src_dram[i * 128:(i + 1) * 128, hh * CW:(hh + 1) * CW]),
            wr=(f"ttmp{par}",), dma=True, stream=f"tz{tag}{par}")
        p.op("vector", lambda v, par=par: v.tensor_scalar(
            tmp[:, par * CW:par * CW + CW], tmp[:, par * CW:par * CW + CW],
            rsc[:, 0:1], None, ALU.mult),
            rd=(f"ttmp{par}", f"rsc{tag}"), wr=(f"ttmp{par}",))
        p.op("vector", lambda v, par=par: v.tensor_scalar(
            sgn[:, par * CW:par * CW + CW], tmp[:, par * CW:par * CW + CW],
            0.5, None, ALU.is_ge),
            rd=(f"ttmp{par}",), wr=(f"sgn{par}",)),
        p.op("vector", lambda v, par=par: v.tensor_scalar(
            tmp[:, par * CW:par * CW + CW], tmp[:, par * CW:par * CW + CW],
            -0.5, None, ALU.is_le),
            rd=(f"ttmp{par}",), wr=(f"ttmp{par}",))
        p.op("vector", lambda v, i=i, hh=hh, par=par: v.tensor_sub(
            dst_sb[:, i * dst_stride + hh * CW:i * dst_stride + (hh + 1) * CW],
            sgn[:, par * CW:par * CW + CW], tmp[:, par * CW:par * CW + CW]),
            rd=(f"ttmp{par}", f"sgn{par}"), wr=(f"wt{tag}",))



def build_l0pre(nch=None, phase2=True, stage=99):
    nc = bass.Bass("TRN2", target_bir_lowering=False)
    p = Prog(nc)
    TH = T + 3
    CH = 512
    NCH = nch if nch is not None else (T // CH)

    xh = nc.dram_tensor("xh", [DM, TH], F32, kind="ExternalInput").ap()
    w_in = nc.dram_tensor("w_in", [DM, 2 * DMAMBA], F32, kind="ExternalInput").ap()
    cwT = nc.dram_tensor("cwT", [128, 64], F32, kind="ExternalInput").ap()
    cb = nc.dram_tensor("cb", [128, 16], F32, kind="ExternalInput").ap()
    w_x = nc.dram_tensor("w_x", [DMAMBA, 128], F32, kind="ExternalInput").ap()
    w_dt = nc.dram_tensor("w_dt", [DTRANK, DMAMBA], F32, kind="ExternalInput").ap()
    b_dt = nc.dram_tensor("b_dt", [128, 16], F32, kind="ExternalInput").ap()
    w1T = nc.dram_tensor("w1T", [DM, FF0], F32, kind="ExternalInput").ap()
    b1d = nc.dram_tensor("b1", [128, 32], F32, kind="ExternalInput").ap()
    w2T = nc.dram_tensor("w2T", [FF0, DM], F32, kind="ExternalInput").ap()
    b2d = nc.dram_tensor("b2", [128, 8], F32, kind="ExternalInput").ap()

    o_dt = nc.dram_tensor("o_dt", [DMAMBA, T], F32, kind="ExternalOutput").ap()
    o_xmc = nc.dram_tensor("o_xmc", [DMAMBA, T], F32, kind="ExternalOutput").ap()
    o_sz = nc.dram_tensor("o_sz", [DMAMBA, T], F32, kind="ExternalOutput").ap()
    o_bc = nc.dram_tensor("o_bc", [32, T], F32, kind="ExternalOutput").ap()
    o_part = nc.dram_tensor("o_part", [DM, T], F32, kind="ExternalOutput").ap()

    xn16 = p.sb("xn16", [128, 8 * TH], BF16, scope="prog")

    w_in_s = p.sb("w_in_s", [128, 8 * 2 * DMAMBA], BF16)
    cw_s = p.sb("cw_s", [128, 16 * 4], F32)
    cb_s = p.sb("cb_s", [128, 16], F32)
    wx_s = p.sb("wx_s", [128, 16 * 128], BF16)
    wdt_s = p.sb("wdt_s", [64, DMAMBA], BF16)
    bdt_s = p.sb("bdt_s", [128, 16], F32)
    ones_s = p.sb("ones_s", [128, 128], F32)

    for kt in range(8):
        p.op("gpsimd", lambda g, kt=kt: g.dma_start(
            out=w_in_s[:, kt * 2 * DMAMBA:(kt + 1) * 2 * DMAMBA],
            in_=w_in[kt * 128:(kt + 1) * 128, :]), wr=("w_in_s",), dma=True)
    for mt in range(16):
        p.op("gpsimd", lambda g, mt=mt: g.dma_start(
            out=wx_s[:, mt * 128:(mt + 1) * 128],
            in_=w_x[mt * 128:(mt + 1) * 128, :]), wr=("wx_s",), dma=True)
    p.op("gpsimd", lambda g: g.dma_start(out=wdt_s[:, :], in_=w_dt),
         wr=("wdt_s",), dma=True)
    p.op("sync", lambda s: s.dma_start(out=cw_s[:, :], in_=cwT), wr=("cw_s",), dma=True)
    p.op("sync", lambda s: s.dma_start(out=cb_s[:, :], in_=cb), wr=("cb_s",), dma=True)
    p.op("sync", lambda s: s.dma_start(out=bdt_s[:, :], in_=b_dt), wr=("bdt_s",), dma=True)
    p.op("vector", lambda v: v.memset(ones_s[:, :], 1.0), wr=("ones_s",))

    NT = CH + 3
    xc = p.sb("xc", [128, 8 * NT], F32)
    stat = p.ps("stat", [128, 512], F32)
    stat2 = p.ps("stat2", [128, 512], F32)
    x2t = p.sb("x2t", [128, NT], F32)
    mb = p.sb("mb", [128, NT], F32)
    rb = p.sb("rb", [128, NT], F32)
    rbx = p.sb("rbx", [128, NT], F32)
    xm_ps = p.ps("xm_ps", [128, 512], F32)
    xm_ps2 = p.ps("xm_ps2", [128, 512], F32)
    xmt = p.sb("xmt", [128, NT], F32)
    xmc16 = p.sb("xmc16", [128, 16 * CH], BF16)
    xmcf = p.sb("xmcf", [128, CH], F32)
    szf = p.sb("szf", [128, CH], F32)
    xdbl_ps = p.ps("xdbl_ps", [128, CH], F32)
    xdbl = p.sb("xdbl", [128, CH], BF16)
    xdblf = p.sb("xdblf", [128, CH], F32)
    dt_ps = p.ps("dt_ps", [128, 512], F32)
    dtt = p.sb("dtt", [128, CH], F32)
    dta = p.sb("dta", [128, CH], F32)

    def emit_chunk_phase1(ci):
        xoff = ci * CH
        if stage < 1:
            return
        for d in range(8):
            p.op("sync", lambda s, d=d, xoff=xoff: s.dma_start(
                out=xc[:, d * NT:d * NT + NT],
                in_=xh[d * 128:(d + 1) * 128, xoff:xoff + NT]),
                wr=(f"xc{d}",), dma=True)
        if stage < 2:
            return
        for sl in range(2):
            f0, f1 = sl * 512, min(NT, 512 + sl * 512)
            p.mm_group("stat", [
                (lambda t, d=d, f0=f0, f1=f1: t.matmul(
                    stat[:, f0:f1], ones_s[:, :], xc[:, d * NT + f0:d * NT + f1],
                    start=(d == 0), stop=(d == 7)), (f"xc{d}", "ones_s"))
                for d in range(8)])
        for d in range(8):
            p.op("scalar", lambda s, d=d: s.activation(
                x2t[:, :], xc[:, d * NT:d * NT + NT], AF.Square),
                rd=(f"xc{d}",), wr=("x2t",))
            for sl in range(2):
                f0, f1 = sl * 512, min(NT, 512 + sl * 512)
                p.mm_group("stat2", [
                    (lambda t, d=d, f0=f0, f1=f1: t.matmul(
                        stat2[:, f0:f1], ones_s[:, :], x2t[:, f0:f1],
                        start=(d == 0), stop=(d == 7)), ("x2t", "ones_s"))])
        if stage < 3:
            return
        p.op("scalar", lambda s: s.activation(mb[:, :], stat[:, 0:NT], AF.Copy,
                                              scale=1.0 / DM),
             rd=("stat",), wr=("mb",))
        p.op("scalar", lambda s: s.activation(rb[:, :], mb[:, :], AF.Square),
             rd=("mb",), wr=("rb",))
        p.op("vector", lambda v: v.scalar_tensor_tensor(
            rb[:, :], stat2[:, 0:NT], 1.0 / DM, rb[:, :], ALU.mult, ALU.subtract),
            rd=("stat2", "rb"), wr=("rb",))
        epsc, epsk = p.constcol(1e-5)
        p.op("scalar", lambda s, epsc=epsc: s.activation(
            rb[:, :], rb[:, :], AF.Sqrt, bias=epsc[:, 0:1]),
            rd=("rb", epsk), wr=("rb",))
        p.op("vector", lambda v: v.reciprocal(rb[:, :], rb[:, :]),
             rd=("rb",), wr=("rb",))
        for d in range(8):
            p.op("vector", lambda v, d=d: v.tensor_sub(
                xc[:, d * NT:d * NT + NT], xc[:, d * NT:d * NT + NT], mb[:, :]),
                rd=(f"xc{d}", "mb"), wr=(f"xc{d}",))
            p.op("vector", lambda v, d=d, xoff=xoff: v.tensor_mul(
                xn16[:, d * TH + xoff:d * TH + xoff + NT],
                xc[:, d * NT:d * NT + NT], rb[:, :]),
                rd=(f"xc{d}", "rb"), wr=("xn16",))

        if stage < 4:
            return
        for mt in range(16):
            for sl in range(2):
                f0, f1 = sl * 512, min(NT, 512 + sl * 512)
                psk = "xm_ps" if sl == 0 else "xm_ps2"
                psum = xm_ps if sl == 0 else xm_ps2
                p.mm_group(psk, [
                    (lambda t, kt=kt, f0=f0, f1=f1, mt=mt, psum=psum, xoff=xoff:
                     t.matmul(psum[:, 0:f1 - f0],
                              w_in_s[:, kt * 2 * DMAMBA + mt * 128:
                                     kt * 2 * DMAMBA + mt * 128 + 128],
                              xn16[:, kt * TH + xoff + f0:kt * TH + xoff + f1],
                              start=(kt == 0), stop=(kt == 7)),
                     ("w_in_s", "xn16")) for kt in range(8)])
                p.op("scalar", lambda s, f0=f0, f1=f1, psum=psum, psk=psk: s.copy(
                    xmt[:, f0:f1], psum[:, 0:f1 - f0]),
                    rd=(psk,), wr=("xmt",))
            if stage < 5:
                continue
            p.op("vector", lambda v, mt=mt: v.tensor_scalar(
                xmcf[:, :], xmt[:, 0:CH], cw_s[:, mt * 4:mt * 4 + 1], None, ALU.mult),
                rd=("xmt", "cw_s"), wr=("xmcf",))
            for k in range(1, 4):
                p.op("vector", lambda v, mt=mt, k=k: v.scalar_tensor_tensor(
                    xmcf[:, :], xmt[:, k:k + CH], cw_s[:, mt * 4 + k:mt * 4 + k + 1],
                    xmcf[:, :], ALU.mult, ALU.add),
                    rd=("xmt", "cw_s", "xmcf"), wr=("xmcf",))
            p.op("scalar", lambda s, mt=mt: s.activation(
                xmc16[:, mt * CH:(mt + 1) * CH], xmcf[:, :], AF.Silu,
                bias=cb_s[:, mt:mt + 1]),
                rd=("xmcf", "cb_s"), wr=("xmc16",))
            p.op("scalar", lambda s, mt=mt: s.activation(
                xmcf[:, :], xmcf[:, :], AF.Silu, bias=cb_s[:, mt:mt + 1]),
                rd=("xmcf", "cb_s"), wr=("xmcf",))
            p.op("sync", lambda s, mt=mt, ci=ci: s.dma_start(
                out=o_xmc[mt * 128:(mt + 1) * 128, ci * CH:(ci + 1) * CH],
                in_=xmcf[:, :]), rd=("xmcf",), dma=True)
            if stage < 6:
                continue
            p.mm_group("xm_ps", [
                (lambda t, kt=kt, mt=mt, xoff=xoff: t.matmul(
                    xm_ps[:, 0:CH],
                    w_in_s[:, kt * 2 * DMAMBA + (16 + mt) * 128:
                           kt * 2 * DMAMBA + (16 + mt) * 128 + 128],
                    xn16[:, kt * TH + xoff + 1:kt * TH + xoff + 1 + CH],
                    start=(kt == 0), stop=(kt == 7)),
                 ("w_in_s", "xn16")) for kt in range(8)])
            p.op("scalar", lambda s: s.activation(szf[:, :], xm_ps[:, 0:CH], AF.Silu),
                 rd=("xm_ps",), wr=("szf",))
            p.op("sync", lambda s, mt=mt, ci=ci: s.dma_start(
                out=o_sz[mt * 128:(mt + 1) * 128, ci * CH:(ci + 1) * CH],
                in_=szf[:, :]), rd=("szf",), dma=True)
        if stage < 7:
            return
        p.mm_group("xdbl_ps", [
            (lambda t, mt=mt: t.matmul(
                xdbl_ps[:, :], wx_s[:, mt * 128:(mt + 1) * 128],
                xmc16[:, mt * CH:(mt + 1) * CH],
                start=(mt == 0), stop=(mt == 15)),
             ("wx_s", "xmc16")) for mt in range(16)])
        if stage >= 7.06:
            p.op("vector", lambda v: v.tensor_copy(xdbl[:, :], xdbl_ps[:, :]),
                 rd=("xdbl_ps",), wr=("xdbl",))
        if stage == 7.07:
            p.op("scalar", lambda s: s.copy(xdblf[:, :], xmcf[:, :]),
                 rd=("xmcf",), wr=("xdblf",))
        if stage >= 7.08:
            p.op("scalar", lambda s: s.copy(xdblf[:, :], xdbl_ps[:, :]),
                 rd=("xdbl_ps",), wr=("xdblf",))
        if stage >= 7.2:
            p.op("sync", lambda s, ci=ci: s.dma_start(
                out=o_bc[0:16, ci * CH:(ci + 1) * CH], in_=xdblf[64:80, :]),
                rd=("xdblf",), dma=True)
        if stage >= 7.3:
            p.op("sync", lambda s, ci=ci: s.dma_start(
                out=o_bc[16:32, ci * CH:(ci + 1) * CH], in_=xdblf[96:112, :]),
                rd=("xdblf",), dma=True)
        if stage < 8:
            return
        for pt in range(16):
            p.mm_group("dt_ps", [
                (lambda t, pt=pt: t.matmul(
                    dt_ps[:, 0:CH], wdt_s[:, pt * 128:(pt + 1) * 128],
                    xdbl[0:64, :], start=True, stop=True),
                 ("wdt_s", "xdbl"))])
            p.op("scalar", lambda s, pt=pt: s.activation(
                dta[:, :], dt_ps[:, 0:CH], AF.Abs, bias=bdt_s[:, pt:pt + 1]),
                rd=("dt_ps", "bdt_s"), wr=("dta",))
            p.op("scalar", lambda s: s.activation(dta[:, :], dta[:, :],
                                                  AF.Exp, scale=-1.0),
                 rd=("dta",), wr=("dta",))
            onec, onek = p.constcol(1.0)
            p.op("scalar", lambda s, onec=onec: s.activation(
                dta[:, :], dta[:, :], AF.Ln, bias=onec[:, 0:1]),
                rd=("dta", onek), wr=("dta",))
            p.op("scalar", lambda s, pt=pt: s.activation(
                dtt[:, :], dt_ps[:, 0:CH], AF.Relu, bias=bdt_s[:, pt:pt + 1]),
                rd=("dt_ps", "bdt_s"), wr=("dtt",))
            p.op("vector", lambda v: v.tensor_add(dtt[:, :], dtt[:, :], dta[:, :]),
                 rd=("dta", "dtt"), wr=("dtt",))
            p.op("sync", lambda s, pt=pt, ci=ci: s.dma_start(
                out=o_dt[pt * 128:(pt + 1) * 128, ci * CH:(ci + 1) * CH],
                in_=dtt[:, :]), rd=("dtt",), dma=True)

    for ci in range(NCH):
        emit_chunk_phase1(ci)
    p.barrier()
    if not phase2:
        return p

    w1t = p.sb("w1t", [128, 8 * FF0], BF16)
    w2t = p.sb("w2t", [128, 32 * DM], BF16)
    ttmp = p.sb("ttmp", [128, 2048], F32)
    tsgn = p.sb("tsgn", [128, 2048], F32)
    ternarize(p, w1T, 8, FF0, w1t, FF0, "w1", ttmp, tsgn)
    ternarize(p, w2T, 32, DM, w2t, DM, "w2", ttmp, tsgn)
    b1_s = p.sb("b1_s", [128, 32], F32)
    b2_s = p.sb("b2_s", [128, 8], F32)
    p.op("sync", lambda s: s.dma_start(out=b1_s[:, :], in_=b1d), wr=("b1_s",), dma=True)
    p.op("sync", lambda s: s.dma_start(out=b2_s[:, :], in_=b2d), wr=("b2_s",), dma=True)

    C2 = 256
    h_ps = p.ps("h_ps", [128, C2], F32)
    h_ps2 = p.ps("h_ps2", [128, C2], F32)
    h16 = p.sb("h16", [128, 32 * C2], BF16)
    ou_ps = p.ps("ou_ps", [128, C2], F32)
    ou_ps2 = p.ps("ou_ps2", [128, C2], F32)
    xpc = p.sb("xpc", [128, C2], F32)
    prt = p.sb("prt", [128, C2], F32)

    for ci in range(T // C2):
        xoff = ci * C2 + 1
        for pt in range(32):
            psk, psum = ("h_ps", h_ps) if pt % 2 == 0 else ("h_ps2", h_ps2)
            p.mm_group(psk, [
                (lambda t, kt=kt, pt=pt, psum=psum, xoff=xoff: t.matmul(
                    psum[:, :], w1t[:, kt * FF0 + pt * 128:kt * FF0 + pt * 128 + 128],
                    xn16[:, kt * TH + xoff:kt * TH + xoff + C2],
                    start=(kt == 0), stop=(kt == 7)),
                 ("wtw1", "xn16")) for kt in range(8)])
            p.op("scalar", lambda s, pt=pt, psum=psum, psk=psk: s.activation(
                h16[:, pt * C2:(pt + 1) * C2], psum[:, :], AF.Gelu,
                bias=b1_s[:, pt:pt + 1]),
                rd=(psk, "b1_s"), wr=("h16",))
        for po in range(8):
            p.op("sync", lambda s, po=po, xoff=xoff: s.dma_start(
                out=xpc[:, :], in_=xh[po * 128:(po + 1) * 128, xoff:xoff + C2]),
                wr=("xpc",), dma=True)
            psk, psum = ("ou_ps", ou_ps) if po % 2 == 0 else ("ou_ps2", ou_ps2)
            p.mm_group(psk, [
                (lambda t, ft=ft, po=po, psum=psum: t.matmul(
                    psum[:, :], w2t[:, ft * DM + po * 128:ft * DM + po * 128 + 128],
                    h16[:, ft * C2:(ft + 1) * C2],
                    start=(ft == 0), stop=(ft == 31)),
                 ("wtw2", "h16")) for ft in range(32)])
            p.op("vector", lambda v, po=po, psum=psum, psk=psk: v.scalar_tensor_tensor(
                prt[:, :], psum[:, :], b2_s[:, po:po + 1], xpc[:, :],
                ALU.add, ALU.add), rd=(psk, "b2_s", "xpc"), wr=("prt",))
            p.op("sync", lambda s, po=po, ci=ci: s.dma_start(
                out=o_part[po * 128:(po + 1) * 128, ci * C2:(ci + 1) * C2],
                in_=prt[:, :]), rd=("prt",), dma=True)
    p.barrier()
    return p


def host_l0pre(inputs):
    x = np.asarray(inputs["L0_attn_x"], dtype=np.float32)
    wx = np.asarray(inputs["L0_attn_W_x"], dtype=np.float32)
    wxp = np.zeros((DMAMBA, 128), dtype=np.float32)
    wxp[:, 0:64] = wx[:, 0:64]
    wxp[:, 64:80] = wx[:, 64:80]
    wxp[:, 96:112] = wx[:, 80:96]
    shared = {
        "w_in": np.ascontiguousarray(np.asarray(inputs["L0_attn_W_in"], np.float32)),
        "cwT": np.ascontiguousarray(np.asarray(inputs["L0_attn_conv_w"], np.float32)[:, 0, :].T.reshape(16, 128, 4).transpose(1, 0, 2).reshape(128, 64)),
        "cb": np.ascontiguousarray(np.asarray(inputs["L0_attn_conv_b"], np.float32).reshape(16, 128).T),
        "w_x": wxp,
        "w_dt": np.ascontiguousarray(np.asarray(inputs["L0_attn_W_dt"], np.float32)),
        "b_dt": np.ascontiguousarray(np.asarray(inputs["L0_attn_b_dt"], np.float32).reshape(16, 128).T),
        "w1T": np.ascontiguousarray(np.asarray(inputs["L0_ffn_W1"], np.float32).T),
        "b1": np.ascontiguousarray(np.asarray(inputs["L0_ffn_b1"], np.float32).reshape(32, 128).T),
        "w2T": np.ascontiguousarray(np.asarray(inputs["L0_ffn_W2"], np.float32).T),
        "b2": np.ascontiguousarray(np.asarray(inputs["L0_ffn_b2"], np.float32).reshape(8, 128).T),
    }
    maps = []
    for c in range(8):
        b, h = c // 2, c % 2
        s0 = h * T
        xhm = np.zeros((T + 3, DM), dtype=np.float32)
        lo, hi = s0 - 1, s0 + T + 2
        clo, chi = max(lo, 0), min(hi, SEQ)
        xhm[clo - lo:chi - lo] = x[b, clo:chi]
        m = dict(shared)
        m["xh"] = np.ascontiguousarray(xhm.T)
        maps.append(m)
    return maps





def build_l0scan():
    nc = bass.Bass("TRN2", target_bir_lowering=False)
    p = Prog(nc)
    NTOK = B * SEQ
    L2 = 2048
    NH = SEQ // L2

    dt_d = nc.dram_tensor("dt", [256, NTOK], F32, kind="ExternalInput").ap()
    xmc_d = nc.dram_tensor("xmc", [256, NTOK], F32, kind="ExternalInput").ap()
    sz_d = nc.dram_tensor("sz", [256, NTOK], F32, kind="ExternalInput").ap()
    Bd = nc.dram_tensor("Bmat", [B * DSTATE, SEQ], F32, kind="ExternalInput").ap()
    Cd = nc.dram_tensor("Cmat", [B * DSTATE, SEQ], F32, kind="ExternalInput").ap()
    Alog_d = nc.dram_tensor("Alog", [256, DSTATE], F32, kind="ExternalInput").ap()
    Dd = nc.dram_tensor("Dvec", [128, 2], F32, kind="ExternalInput").ap()
    o_yg = nc.dram_tensor("o_yg", [256, NTOK], F32, kind="ExternalOutput").ap()

    A_s = p.sb("A_s", [128, 2 * DSTATE], F32, scope="prog")
    D_s = p.sb("D_s", [128, 2], F32, scope="prog")
    p.op("sync", lambda s: s.dma_start(
        out=A_s.ap().rearrange("p (t s) -> p t s", t=2),
        in_=Alog_d.rearrange("(t p) s -> p t s", p=128)), wr=("A_s",), dma=True)
    p.op("scalar", lambda s: s.activation(A_s[:, :], A_s[:, :], AF.Exp),
         rd=("A_s",), wr=("A_s",))
    p.op("vector", lambda v: v.tensor_scalar_mul(A_s[:, :], A_s[:, :], -1.0),
         rd=("A_s",), wr=("A_s",))
    p.op("sync", lambda s: s.dma_start(out=D_s[:, :], in_=Dd), wr=("D_s",), dma=True)

    dts = [p.sb(f"dts{t}", [128, L2], F32) for t in range(2)]
    xms = [p.sb(f"xms{t}", [128, L2], F32) for t in range(2)]
    szs = [p.sb(f"szs{t}", [128, L2], F32) for t in range(2)]
    ws = [p.sb(f"ws{t}", [128, L2], F32) for t in range(2)]
    ys = [p.sb(f"ys{t}", [128, L2], F32) for t in range(2)]
    carry = [p.sb(f"carry{t}", [128, DSTATE], F32) for t in range(2)]
    brep = p.sb("brep", [128, L2], F32)
    crep = p.sb("crep", [128, L2], F32)
    da = p.sb("da", [128, L2], F32)
    bs = p.sb("bs", [128, L2], F32)
    st = p.sb("st", [128, L2], F32)
    zt = p.sb("zt", [128, L2], F32)

    for b in range(B):
        for hf in range(NH):
            off = b * SEQ + hf * L2
            for t in range(2):
                p.op("sync", lambda s, t=t, off=off: s.dma_start(
                    out=dts[t][:, :], in_=dt_d[t * 128:(t + 1) * 128, off:off + L2]),
                    wr=(f"dts{t}",), dma=True, stream="in0")
                p.op("sync", lambda s, t=t, off=off: s.dma_start(
                    out=xms[t][:, :], in_=xmc_d[t * 128:(t + 1) * 128, off:off + L2]),
                    wr=(f"xms{t}",), dma=True, stream="in1")
                p.op("sync", lambda s, t=t, off=off: s.dma_start(
                    out=szs[t][:, :], in_=sz_d[t * 128:(t + 1) * 128, off:off + L2]),
                    wr=(f"szs{t}",), dma=True, stream="in2")
                p.op("vector", lambda v, t=t: v.tensor_mul(
                    ws[t][:, :], dts[t][:, :], xms[t][:, :]),
                    rd=(f"dts{t}", f"xms{t}"), wr=(f"ws{t}",))
                if hf == 0:
                    p.op("vector", lambda v, t=t: v.memset(carry[t][:, :], 0.0),
                         wr=(f"carry{t}",))
            for s in range(DSTATE):
                p.op("gpsimd", lambda g, b=b, s=s, hf=hf: g.dma_start(
                    out=brep[:, :],
                    in_=Bd[b * DSTATE + s:b * DSTATE + s + 1,
                           hf * L2:(hf + 1) * L2].partition_broadcast(128)),
                    wr=("brep",), dma=True, stream="br")
                p.op("gpsimd", lambda g, b=b, s=s, hf=hf: g.dma_start(
                    out=crep[:, :],
                    in_=Cd[b * DSTATE + s:b * DSTATE + s + 1,
                           hf * L2:(hf + 1) * L2].partition_broadcast(128)),
                    wr=("crep",), dma=True, stream="cr")
                for t in range(2):
                    p.op("scalar", lambda sc, t=t, s=s: sc.activation(
                        da[:, :], dts[t][:, :], AF.Exp,
                        scale=A_s[:, t * DSTATE + s:t * DSTATE + s + 1]),
                        rd=(f"dts{t}", "A_s"), wr=("da",))
                    p.op("vector", lambda v, t=t: v.tensor_mul(
                        bs[:, :], ws[t][:, :], brep[:, :]),
                        rd=(f"ws{t}", "brep"), wr=("bs",))
                    p.op("vector", lambda v, t=t, s=s: v.tensor_tensor_scan(
                        st[:, :], da[:, :], bs[:, :],
                        carry[t][:, s:s + 1], ALU.mult, ALU.add),
                        rd=("da", "bs", f"carry{t}"), wr=("st",))
                    p.op("vector", lambda v, t=t, s=s: v.tensor_copy(
                        carry[t][:, s:s + 1], st[:, L2 - 1:L2]),
                        rd=("st",), wr=(f"carry{t}",))
                    p.op("vector", lambda v: v.tensor_mul(
                        zt[:, :], st[:, :], crep[:, :]),
                        rd=("st", "crep"), wr=("zt",))
                    if s == 0:
                        p.op("vector", lambda v, t=t: v.tensor_copy(
                            ys[t][:, :], zt[:, :]), rd=("zt",), wr=(f"ys{t}",))
                    else:
                        p.op("vector", lambda v, t=t: v.tensor_add(
                            ys[t][:, :], ys[t][:, :], zt[:, :]),
                            rd=("zt", f"ys{t}"), wr=(f"ys{t}",))
            for t in range(2):
                p.op("vector", lambda v, t=t: v.scalar_tensor_tensor(
                    ys[t][:, :], xms[t][:, :], D_s[:, t:t + 1], ys[t][:, :],
                    ALU.mult, ALU.add),
                    rd=(f"xms{t}", "D_s", f"ys{t}"), wr=(f"ys{t}",))
                p.op("vector", lambda v, t=t: v.tensor_mul(
                    ys[t][:, :], ys[t][:, :], szs[t][:, :]),
                    rd=(f"szs{t}", f"ys{t}"), wr=(f"ys{t}",))
                p.op("sync", lambda s, t=t, off=off: s.dma_start(
                    out=o_yg[t * 128:(t + 1) * 128, off:off + L2], in_=ys[t][:, :]),
                    rd=(f"ys{t}",), dma=True, stream="out")
    p.barrier()
    return p


def host_l0scan(dt_full, xmc_full, sz_full, Bc_full, Cc_full, inputs):
    Alog = np.asarray(inputs["L0_attn_A_log"], np.float32)
    Dv = np.asarray(inputs["L0_attn_D"], np.float32)
    Bm = np.ascontiguousarray(Bc_full.transpose(0, 2, 1).reshape(B * DSTATE, SEQ))
    Cm = np.ascontiguousarray(Cc_full.transpose(0, 2, 1).reshape(B * DSTATE, SEQ))
    maps = []
    for c in range(8):
        ch = slice(c * 256, (c + 1) * 256)
        maps.append({
            "dt": np.ascontiguousarray(dt_full[ch].reshape(256, B * SEQ)),
            "xmc": np.ascontiguousarray(xmc_full[ch].reshape(256, B * SEQ)),
            "sz": np.ascontiguousarray(sz_full[ch].reshape(256, B * SEQ)),
            "Bmat": Bm, "Cmat": Cm,
            "Alog": np.ascontiguousarray(Alog[ch]),
            "Dvec": np.ascontiguousarray(Dv[ch].reshape(2, 128).T),
            })
    return maps



def build_mid():
    nc = bass.Bass("TRN2", target_bir_lowering=False)
    p = Prog(nc)
    TH = T + 3
    SLABS = [(0, 3), (3, 512), (515, 512), (1027, 512), (1539, 512)]
    QSC = float(DK) ** -0.5

    yg_d = nc.dram_tensor("yg", [DMAMBA, TH], F32, kind="ExternalInput").ap()
    part_d = nc.dram_tensor("part", [DM, TH], F32, kind="ExternalInput").ap()
    wout_d = nc.dram_tensor("wout", [DMAMBA, DM], F32, kind="ExternalInput").ap()
    wqkvg_d = nc.dram_tensor("wqkvg", [DM, 4608], F32, kind="ExternalInput").ap()
    wba_d = nc.dram_tensor("wba", [128, 8 * 16], F32, kind="ExternalInput").ap()
    gwT_d = nc.dram_tensor("gwT", [DM, 8], F32, kind="ExternalInput").ap()
    ebias_d = nc.dram_tensor("ebias", [1, 8], F32, kind="ExternalInput").ap()
    cwq_d = nc.dram_tensor("cwq", [128, 4 * 24], F32, kind="ExternalInput").ap()
    alog_d = nc.dram_tensor("alog", [6, 1], F32, kind="ExternalInput").ap()
    dtb_d = nc.dram_tensor("dtb", [6, 1], F32, kind="ExternalInput").ap()

    o_out1 = nc.dram_tensor("o_out1", [DM, T], F32, kind="ExternalOutput").ap()
    o_xn1 = nc.dram_tensor("o_xn1", [DM, T], BF16, kind="ExternalOutput").ap()
    o_q = nc.dram_tensor("o_q", [KD, T], F32, kind="ExternalOutput").ap()
    o_k = nc.dram_tensor("o_k", [KD, T], F32, kind="ExternalOutput").ap()
    o_v = nc.dram_tensor("o_v", [VD, T], F32, kind="ExternalOutput").ap()
    o_gs = nc.dram_tensor("o_gs", [VD, T], F32, kind="ExternalOutput").ap()
    o_beta = nc.dram_tensor("o_beta", [6, T], F32, kind="ExternalOutput").ap()
    o_g = nc.dram_tensor("o_g", [6, T], F32, kind="ExternalOutput").ap()
    o_mw = nc.dram_tensor("o_mw", [8, T], F32, kind="ExternalOutput").ap()

    xn16 = p.sb("xn16", [128, 8 * TH], BF16, scope="prog")
    ones_s = p.sb("ones_s", [128, 128], F32, scope="prog")
    p.op("vector", lambda v: v.memset(ones_s[:, :], 1.0), wr=("ones_s",))
    epsc, epsk = p.constcol(1e-5)
    eps12c, eps12k = p.constcol(1e-12)
    onec, onek = p.constcol(1.0)

    wout_s = p.sb("wout_s", [128, 16 * DM], BF16)
    gw_s = p.sb("gw_s", [128, 8 * 8], F32)
    eb128 = p.sb("eb128", [128, 8], F32)
    ones8 = p.sb("ones8", [128, 8], F32)
    ident = p.sb("ident", [128, 128], F32)
    iota_c = p.sb("iota_c", [128, 1], F32)
    for kt in range(16):
        p.op("gpsimd", lambda g, kt=kt: g.dma_start(
            out=wout_s[:, kt * DM:(kt + 1) * DM],
            in_=wout_d[kt * 128:(kt + 1) * 128, :]), wr=("wout_s",), dma=True)
    for kt in range(8):
        p.op("sync", lambda s, kt=kt: s.dma_start(
            out=gw_s[:, kt * 8:(kt + 1) * 8],
            in_=gwT_d[kt * 128:(kt + 1) * 128, :]), wr=("gw_s",), dma=True, stream="w2")
    p.op("sync", lambda s: s.dma_start(
        out=eb128[:, :], in_=ebias_d[0:1, :].partition_broadcast(128)),
        wr=("eb128",), dma=True, stream="w2")
    p.op("vector", lambda v: v.memset(ones8[:, :], 1.0), wr=("ones8",))
    p.op("gpsimd", lambda g: g.iota(ident[:, :], [[1, 128]], channel_multiplier=0,
                                    allow_small_or_imprecise_dtypes=True),
         wr=("ident",))
    p.op("gpsimd", lambda g: g.iota(iota_c[:, :], [[0, 1]], channel_multiplier=1,
                                    allow_small_or_imprecise_dtypes=True),
         wr=("iota_c",))
    p.op("vector", lambda v: v.tensor_scalar(
        ident[:, :], ident[:, :], iota_c[:, 0:1], None, ALU.is_equal),
        rd=("ident", "iota_c"), wr=("ident",))

    ygs = p.sb("ygs", [128, 16 * 512], BF16)
    prt = p.sb("prt", [128, 512], F32)
    out1 = p.sb("out1", [128, 8 * 512], F32)
    ygps = p.ps("ygps", [128, 512], F32)
    ygps2 = p.ps("ygps2", [128, 512], F32)
    stat = p.ps("stat", [128, 512], F32)
    stat2 = p.ps("stat2", [128, 512], F32)
    x2t = p.sb("x2t", [128, 512], F32)
    mb = p.sb("mb", [128, 512], F32)
    rb = p.sb("rb", [128, 512], F32)
    gps = p.ps("gps", [8, 512], F32)
    scr = p.sb("scr", [8, 512], F32)
    trps = p.ps("trps", [128, 8], F32)
    trps2 = p.ps("trps2", [8, 128], F32)
    tok8 = p.sb("tok8", [128, 8], F32)
    t8a = p.sb("t8a", [128, 8], F32)
    t8b = p.sb("t8b", [128, 8], F32)
    t8c = p.sb("t8c", [128, 8], F32)
    m1 = p.sb("m1", [128, 1], F32)
    mwti = p.sb("mwti", [8, 128], F32)

    for si, (f0, w) in enumerate(SLABS):
        for mt in range(16):
            p.op("gpsimd", lambda g, mt=mt, f0=f0, w=w: g.dma_start(
                out=ygs[:, mt * 512:mt * 512 + w],
                in_=yg_d[mt * 128:(mt + 1) * 128, f0:f0 + w]),
                wr=("ygs",), dma=True, stream="yg")
        for d in range(8):
            p.op("sync", lambda s, d=d, f0=f0, w=w: s.dma_start(
                out=prt[:, 0:w], in_=part_d[d * 128:(d + 1) * 128, f0:f0 + w]),
                wr=("prt",), dma=True, stream="in0")
            psk, psum = ("ygps", ygps) if d % 2 == 0 else ("ygps2", ygps2)
            p.mm_group(psk, [
                (lambda t, mt=mt, w=w, psum=psum: t.matmul(
                    psum[:, 0:w], wout_s[:, mt * DM + d * 128:mt * DM + d * 128 + 128],
                    ygs[:, mt * 512:mt * 512 + w],
                    start=(mt == 0), stop=(mt == 15)),
                 ("wout_s", "ygs")) for mt in range(16)])
            p.op("vector", lambda v, d=d, w=w, psum=psum, psk=psk: v.tensor_add(
                out1[:, d * 512:d * 512 + w], psum[:, 0:w], prt[:, 0:w]),
                rd=(psk, "prt"), wr=("out1",))
            if si > 0:
                p.op("sync", lambda s, d=d, f0=f0, w=w: s.dma_start(
                    out=o_out1[d * 128:(d + 1) * 128, f0 - 3:f0 - 3 + w],
                    in_=out1[:, d * 512:d * 512 + w]),
                    rd=("out1",), dma=True, stream="oo1")
        p.mm_group("stat", [
            (lambda t, d=d, w=w: t.matmul(
                stat[:, 0:w], ones_s[:, :], out1[:, d * 512:d * 512 + w],
                start=(d == 0), stop=(d == 7)), ("out1", "ones_s"))
            for d in range(8)])
        for d in range(8):
            p.op("scalar", lambda s, d=d, w=w: s.activation(
                x2t[:, 0:w], out1[:, d * 512:d * 512 + w], AF.Square),
                rd=("out1",), wr=("x2t",))
            p.mm_group("stat2", [
                (lambda t, d=d, w=w: t.matmul(
                    stat2[:, 0:w], ones_s[:, :], x2t[:, 0:w],
                    start=(d == 0), stop=(d == 7)), ("x2t", "ones_s"))])
        p.op("scalar", lambda s, w=w: s.activation(mb[:, 0:w], stat[:, 0:w],
                                                   AF.Copy, scale=1.0 / DM),
             rd=("stat",), wr=("mb",))
        p.op("scalar", lambda s, w=w: s.activation(rb[:, 0:w], mb[:, 0:w], AF.Square),
             rd=("mb",), wr=("rb",))
        p.op("vector", lambda v, w=w: v.scalar_tensor_tensor(
            rb[:, 0:w], stat2[:, 0:w], 1.0 / DM, rb[:, 0:w], ALU.mult, ALU.subtract),
            rd=("stat2", "rb"), wr=("rb",))
        p.op("scalar", lambda s, w=w: s.activation(
            rb[:, 0:w], rb[:, 0:w], AF.Sqrt, bias=epsc),
            rd=("rb", epsk), wr=("rb",))
        p.op("vector", lambda v, w=w: v.reciprocal(rb[:, 0:w], rb[:, 0:w]),
             rd=("rb",), wr=("rb",))
        for d in range(8):
            p.op("vector", lambda v, d=d, w=w: v.tensor_sub(
                out1[:, d * 512:d * 512 + w], out1[:, d * 512:d * 512 + w], mb[:, 0:w]),
                rd=("out1", "mb"), wr=("out1",))
            p.op("vector", lambda v, d=d, w=w: v.tensor_mul(
                out1[:, d * 512:d * 512 + w], out1[:, d * 512:d * 512 + w], rb[:, 0:w]),
                rd=("out1", "rb"), wr=("out1",))
            p.op("vector", lambda v, d=d, w=w, f0=f0: v.tensor_copy(
                xn16[:, d * TH + f0:d * TH + f0 + w], out1[:, d * 512:d * 512 + w]),
                rd=("out1",), wr=("xn16",))
            if si > 0:
                p.op("sync", lambda s, d=d, f0=f0, w=w: s.dma_start(
                    out=o_xn1[d * 128:(d + 1) * 128, f0 - 3:f0 - 3 + w],
                    in_=xn16[:, d * TH + f0:d * TH + f0 + w]),
                    rd=("xn16",), dma=True, stream="ox1")
        if si == 0:
            continue
        p.mm_group("gps", [
            (lambda t, d=d, w=w: t.matmul(
                gps[:, 0:w], gw_s[:, d * 8:(d + 1) * 8], out1[:, d * 512:d * 512 + w],
                start=(d == 0), stop=(d == 7)), ("out1", "gw_s"))
            for d in range(8)])
        p.op("scalar", lambda s, w=w: s.activation(scr[:, 0:w], gps[:, 0:w],
                                                   AF.Sigmoid),
             rd=("gps",), wr=("scr",))
        for blk in range(4):
            c0 = blk * 128
            p.mm_group("trps", [
                (lambda t, c0=c0: t.transpose(
                    trps[:, :], scr[:, c0:c0 + 128], ident[0:8, 0:8]),
                 ("scr", "ident"))])
            p.op("vector", lambda v: v.tensor_copy(tok8[:, :], trps[:, :]),
                 rd=("trps",), wr=("tok8",))
            p.op("vector", lambda v: v.tensor_add(t8a[:, :], tok8[:, :], eb128[:, :]),
                 rd=("tok8", "eb128"), wr=("t8a",))
            p.op("vector", lambda v: v.reduce_max(m1[:, :], t8a[:, :], AX.X),
                 rd=("t8a",), wr=("m1",))
            p.op("vector", lambda v: v.tensor_scalar(
                t8b[:, :], t8a[:, :], m1[:, 0:1], None, ALU.is_ge),
                rd=("t8a", "m1"), wr=("t8b",))
            p.op("vector", lambda v: v.tensor_tensor_scan(
                t8c[:, :], ones8[:, :], t8b[:, :], 0.0, ALU.mult, ALU.add),
                rd=("ones8", "t8b"), wr=("t8c",))
            p.op("vector", lambda v: v.tensor_scalar(
                t8c[:, :], t8c[:, :], 1.0, None, ALU.is_le),
                rd=("t8c",), wr=("t8c",))
            p.op("vector", lambda v: v.tensor_mul(t8b[:, :], t8b[:, :], t8c[:, :]),
                 rd=("t8b", "t8c"), wr=("t8b",))
            p.op("vector", lambda v: v.scalar_tensor_tensor(
                t8a[:, :], t8b[:, :], -1e30, t8a[:, :], ALU.mult, ALU.add),
                rd=("t8a", "t8b"), wr=("t8a",))
            p.op("vector", lambda v: v.reduce_max(m1[:, :], t8a[:, :], AX.X),
                 rd=("t8a",), wr=("m1",))
            p.op("vector", lambda v: v.tensor_scalar(
                t8a[:, :], t8a[:, :], m1[:, 0:1], None, ALU.is_ge),
                rd=("t8a", "m1"), wr=("t8a",))
            p.op("vector", lambda v: v.tensor_tensor_scan(
                t8c[:, :], ones8[:, :], t8a[:, :], 0.0, ALU.mult, ALU.add),
                rd=("ones8", "t8a"), wr=("t8c",))
            p.op("vector", lambda v: v.tensor_scalar(
                t8c[:, :], t8c[:, :], 1.0, None, ALU.is_le),
                rd=("t8c",), wr=("t8c",))
            p.op("vector", lambda v: v.tensor_mul(t8a[:, :], t8a[:, :], t8c[:, :]),
                 rd=("t8a", "t8c"), wr=("t8a",))
            p.op("vector", lambda v: v.tensor_add(t8b[:, :], t8b[:, :], t8a[:, :]),
                 rd=("t8a", "t8b"), wr=("t8b",))
            p.op("vector", lambda v: v.tensor_mul(t8b[:, :], t8b[:, :], tok8[:, :]),
                 rd=("t8b", "tok8"), wr=("t8b",))
            p.mm_group("trps2", [
                (lambda t: t.transpose(trps2[:, :], t8b[:, :], ident[:, :]),
                 ("t8b", "ident"))])
            p.op("vector", lambda v: v.tensor_copy(mwti[:, :], trps2[:, :]),
                 rd=("trps2",), wr=("mwti",))
            p.op("sync", lambda s, f0=f0, c0=c0: s.dma_start(
                out=o_mw[:, f0 - 3 + c0:f0 - 3 + c0 + 128], in_=mwti[:, :]),
                rd=("mwti",), dma=True, stream="omw")
    p.barrier()

    wq_s = p.sb("wq_s", [128, 8 * 4608], BF16)
    wba_s = p.sb("wba_s", [128, 8 * 16], BF16)
    cwq_s = p.sb("cwq_s", [128, 4 * 24], F32)
    negA = p.sb("negA", [6, 1], F32)
    dtb_s = p.sb("dtb_s", [6, 1], F32)
    for kt in range(8):
        p.op("gpsimd", lambda g, kt=kt: g.dma_start(
            out=wq_s[:, kt * 4608:(kt + 1) * 4608],
            in_=wqkvg_d[kt * 128:(kt + 1) * 128, :]), wr=("wq_s",), dma=True)
    p.op("gpsimd", lambda g: g.dma_start(out=wba_s[:, :], in_=wba_d),
         wr=("wba_s",), dma=True)
    p.op("sync", lambda s: s.dma_start(out=cwq_s[:, :], in_=cwq_d),
         wr=("cwq_s",), dma=True, stream="w2")
    p.op("sync", lambda s: s.dma_start(out=negA[:, :], in_=alog_d),
         wr=("negA",), dma=True, stream="w2")
    p.op("sync", lambda s: s.dma_start(out=dtb_s[:, :], in_=dtb_d),
         wr=("dtb_s",), dma=True, stream="w2")
    p.op("scalar", lambda s: s.activation(negA[:, :], negA[:, :], AF.Exp),
         rd=("negA",), wr=("negA",))
    p.op("vector", lambda v: v.tensor_scalar_mul(negA[:, :], negA[:, :], -1.0),
         rd=("negA",), wr=("negA",))

    xq = p.sb("xq", [128, TH], F32)
    xcs = p.sb("xcs", [128, T], F32)
    sqb = p.sb("sqb", [128, T], F32)
    rr = p.sb("rr", [128, T], F32)
    qp = p.ps("qp", [128, 512], F32)
    qp2 = p.ps("qp2", [128, 512], F32)
    ssps = p.ps("ssps", [128, 512], F32)
    bps = p.ps("bps", [6, 512], F32)
    aps = p.ps("aps", [6, 512], F32)
    bg6 = p.sb("bg6", [6, 512], F32)
    bg6b = p.sb("bg6b", [6, 512], F32)

    for pt in range(36):
        for si, (f0, w) in enumerate(SLABS):
            psk, psum = ("qp", qp) if si % 2 == 0 else ("qp2", qp2)
            p.mm_group(psk, [
                (lambda t, kt=kt, pt=pt, f0=f0, w=w, psum=psum: t.matmul(
                    psum[:, 0:w],
                    wq_s[:, kt * 4608 + pt * 128:kt * 4608 + pt * 128 + 128],
                    xn16[:, kt * TH + f0:kt * TH + f0 + w],
                    start=(kt == 0), stop=(kt == 7)),
                 ("wq_s", "xn16")) for kt in range(8)])
            if pt < 24:
                p.op("scalar", lambda s, f0=f0, w=w, psum=psum, psk=psk: s.copy(
                    xq[:, f0:f0 + w], psum[:, 0:w]), rd=(psk,), wr=("xq",))
            else:
                if si > 0:
                    p.op("scalar", lambda s, f0=f0, w=w, psum=psum, psk=psk, pt=pt:
                         s.activation(xcs[:, f0 - 3:f0 - 3 + w], psum[:, 0:w], AF.Silu),
                         rd=(psk,), wr=("xcs",))
        if pt >= 24:
            p.op("sync", lambda s, pt=pt: s.dma_start(
                out=o_gs[(pt - 24) * 128:(pt - 23) * 128, :], in_=xcs[:, :]),
                rd=("xcs",), dma=True, stream="ogs")
            continue
        p.op("vector", lambda v, pt=pt: v.tensor_scalar(
            xcs[:, :], xq[:, 0:T], cwq_s[:, pt * 4:pt * 4 + 1], None, ALU.mult),
            rd=("xq", "cwq_s"), wr=("xcs",))
        for k in range(1, 4):
            p.op("vector", lambda v, pt=pt, k=k: v.scalar_tensor_tensor(
                xcs[:, :], xq[:, k:k + T], cwq_s[:, pt * 4 + k:pt * 4 + k + 1],
                xcs[:, :], ALU.mult, ALU.add),
                rd=("xq", "cwq_s", "xcs"), wr=("xcs",))
        p.op("scalar", lambda s: s.activation(xcs[:, :], xcs[:, :], AF.Silu),
             rd=("xcs",), wr=("xcs",))
        if pt >= 12:
            p.op("sync", lambda s, pt=pt: s.dma_start(
                out=o_v[(pt - 12) * 128:(pt - 11) * 128, :], in_=xcs[:, :]),
                rd=("xcs",), dma=True, stream="ov")
            continue
        p.op("scalar", lambda s: s.activation(sqb[:, :], xcs[:, :], AF.Square),
             rd=("xcs",), wr=("sqb",))
        for sl in range(4):
            p.mm_group("ssps", [
                (lambda t, sl=sl: t.matmul(
                    ssps[:, :], ones16[:, :], sqb[:, sl * 512:(sl + 1) * 512],
                    start=True, stop=True), ("sqb", "ones16"))])
            p.op("scalar", lambda s, sl=sl: s.activation(
                rr[:, sl * 512:(sl + 1) * 512], ssps[:, :], AF.Sqrt, bias=eps12c),
                rd=("ssps", eps12k), wr=("rr",))
        p.op("vector", lambda v: v.reciprocal(rr[:, :], rr[:, :]),
             rd=("rr",), wr=("rr",))
        sc = QSC if pt < 6 else 1.0
        p.op("vector", lambda v, sc=sc: v.scalar_tensor_tensor(
            xcs[:, :], xcs[:, :], sc, rr[:, :], ALU.mult, ALU.mult),
            rd=("xcs", "rr"), wr=("xcs",))
        if pt < 6:
            p.op("sync", lambda s, pt=pt: s.dma_start(
                out=o_q[pt * 128:(pt + 1) * 128, :], in_=xcs[:, :]),
                rd=("xcs",), dma=True, stream="oq")
        else:
            p.op("sync", lambda s, pt=pt: s.dma_start(
                out=o_k[(pt - 6) * 128:(pt - 5) * 128, :], in_=xcs[:, :]),
                rd=("xcs",), dma=True, stream="ok")
    for si, (f0, w) in enumerate(SLABS):
        if si == 0:
            continue
        p.mm_group("bps", [
            (lambda t, kt=kt, f0=f0, w=w: t.matmul(
                bps[:, 0:w], wba_s[:, kt * 16:kt * 16 + 6],
                xn16[:, kt * TH + f0:kt * TH + f0 + w],
                start=(kt == 0), stop=(kt == 7)),
             ("wba_s", "xn16")) for kt in range(8)])
        p.op("scalar", lambda s, w=w: s.activation(bg6[:, 0:w], bps[0:6, 0:w],
                                                   AF.Sigmoid),
             rd=("bps",), wr=("bg6",))
        p.op("sync", lambda s, f0=f0, w=w: s.dma_start(
            out=o_beta[:, f0 - 3:f0 - 3 + w], in_=bg6[0:6, 0:w]),
            rd=("bg6",), dma=True, stream="obeta")
        p.mm_group("aps", [
            (lambda t, kt=kt, f0=f0, w=w: t.matmul(
                aps[:, 0:w], wba_s[:, kt * 16 + 8:kt * 16 + 14],
                xn16[:, kt * TH + f0:kt * TH + f0 + w],
                start=(kt == 0), stop=(kt == 7)),
             ("wba_s", "xn16")) for kt in range(8)])
        p.op("scalar", lambda s, w=w: s.activation(
            bg6[:, 0:w], aps[0:6, 0:w], AF.Abs, bias=dtb_s[0:6, 0:1]),
            rd=("aps", "dtb_s"), wr=("bg6",))
        p.op("scalar", lambda s, w=w: s.activation(bg6[:, 0:w], bg6[:, 0:w],
                                                   AF.Exp, scale=-1.0),
             rd=("bg6",), wr=("bg6",))
        p.op("scalar", lambda s, w=w: s.activation(
            bg6[:, 0:w], bg6[:, 0:w], AF.Ln, bias=onec[0:6, 0:1]),
            rd=("bg6", onek), wr=("bg6",))
        p.op("scalar", lambda s, w=w: s.activation(
            bg6b[:, 0:w], aps[0:6, 0:w], AF.Relu, bias=dtb_s[0:6, 0:1]),
            rd=("aps", "dtb_s"), wr=("bg6b",))
        p.op("vector", lambda v, w=w: v.tensor_add(bg6[:, 0:w], bg6[:, 0:w],
                                                   bg6b[:, 0:w]),
             rd=("bg6", "bg6b"), wr=("bg6",))
        p.op("vector", lambda v, w=w: v.tensor_scalar(
            bg6[:, 0:w], bg6[:, 0:w], negA[0:6, 0:1], None, ALU.mult),
            rd=("bg6", "negA"), wr=("bg6",))
        p.op("sync", lambda s, f0=f0, w=w: s.dma_start(
            out=o_g[:, f0 - 3:f0 - 3 + w], in_=bg6[0:6, 0:w]),
            rd=("bg6",), dma=True, stream="og")
    p.barrier()
    return p


def host_mid(inputs, yg_full, part_full):
    wq = np.asarray(inputs["L1_attn_Wq"], np.float32)
    wk = np.asarray(inputs["L1_attn_Wk"], np.float32)
    wv = np.asarray(inputs["L1_attn_Wv"], np.float32)
    wg = np.asarray(inputs["L1_attn_Wg"], np.float32)
    wqkvg = np.ascontiguousarray(np.concatenate([wq, wk, wv, wg], axis=1))
    wb = np.asarray(inputs["L1_attn_Wb"], np.float32)
    wa = np.asarray(inputs["L1_attn_Wa"], np.float32)
    wba = np.zeros((DM, 16), np.float32)
    wba[:, 0:6] = wb
    wba[:, 8:14] = wa
    wba = np.ascontiguousarray(wba.reshape(8, 128, 16).transpose(1, 0, 2).reshape(128, 128))
    cq = np.asarray(inputs["L1_attn_conv_q"], np.float32)
    ck = np.asarray(inputs["L1_attn_conv_k"], np.float32)
    cv = np.asarray(inputs["L1_attn_conv_v"], np.float32)
    cwq = np.concatenate([cq.reshape(6, 128, 4), ck.reshape(6, 128, 4),
                          cv.reshape(12, 128, 4)], axis=0)
    cwq = np.ascontiguousarray(cwq.transpose(1, 0, 2).reshape(128, 96))
    shared = {
        "wout": np.ascontiguousarray(np.asarray(inputs["L0_attn_W_out"], np.float32)),
        "wqkvg": wqkvg,
        "wba": wba,
        "gwT": np.ascontiguousarray(np.asarray(inputs["L1_ffn_gate_w"], np.float32).T),
        "ebias": np.asarray(inputs["L1_ffn_expert_bias"], np.float32).reshape(1, 8),
        "cwq": cwq,
        "alog": np.asarray(inputs["L1_attn_A_log"], np.float32).reshape(6, 1),
        "dtb": np.asarray(inputs["L1_attn_dt_bias"], np.float32).reshape(6, 1),
    }
    maps = []
    for c in range(8):
        b, hh = c // 2, c % 2
        s0 = hh * T
        yg = np.zeros((DMAMBA, T + 3), np.float32)
        pt = np.zeros((DM, T + 3), np.float32)
        lo = s0 - 3
        clo = max(lo, 0)
        yg[:, clo - lo:] = yg_full[:, b, clo:s0 + T]
        pt[:, clo - lo:] = part_full[:, b, clo:s0 + T]
        m = dict(shared)
        m["yg"] = yg
        m["part"] = pt
        maps.append(m)
    return maps



def build_l1delta():
    nc = bass.Bass("TRN2", target_bir_lowering=False)
    p = Prog(nc)
    NU, C = 3, 128
    NCH = SEQ // C

    qf_d = nc.dram_tensor("qf", [NU * DK, SEQ], F32, kind="ExternalInput").ap()
    kf_d = nc.dram_tensor("kf", [NU * DK, SEQ], F32, kind="ExternalInput").ap()
    kT_d = nc.dram_tensor("kT", [NU * SEQ, DK], F32, kind="ExternalInput").ap()
    vT_d = nc.dram_tensor("vT", [NU * SEQ, DV], F32, kind="ExternalInput").ap()
    beta_d = nc.dram_tensor("betar", [NU, SEQ], F32, kind="ExternalInput").ap()
    g_d = nc.dram_tensor("gr", [NU, SEQ], F32, kind="ExternalInput").ap()
    bdl_d = nc.dram_tensor("bdl", [128, 128], F32, kind="ExternalInput").ap()
    bdlT_d = nc.dram_tensor("bdlT", [128, 128], F32, kind="ExternalInput").ap()
    sbl_d = nc.dram_tensor("sbl", [128, 128], F32, kind="ExternalInput").ap()
    sblT_d = nc.dram_tensor("sblT", [128, 128], F32, kind="ExternalInput").ap()
    trui_d = nc.dram_tensor("trui", [128, 128], F32, kind="ExternalInput").ap()
    eye_d = nc.dram_tensor("eye", [128, 128], F32, kind="ExternalInput").ap()
    o_d = nc.dram_tensor("o_o", [NU * SEQ, DV], F32, kind="ExternalOutput").ap()

    ones_s = p.sb("ones_s", [128, 128], F32, scope="prog")
    one11 = p.sb("one11", [1, 1], F32, scope="prog")
    masks = {}
    for nm, d in [("bdl", bdl_d), ("bdlT", bdlT_d), ("sbl", sbl_d),
                  ("sblT", sblT_d), ("trui", trui_d), ("eye", eye_d)]:
        masks[nm] = p.sb(nm + "_s", [128, 128], F32, scope="prog")
        p.op("sync", lambda s, nm=nm, d=d: s.dma_start(out=masks[nm][:, :], in_=d),
             wr=(nm,), dma=True, stream="msk")
    p.op("vector", lambda v: v.memset(ones_s[:, :], 1.0), wr=("ones_s",))
    p.op("vector", lambda v: v.memset(one11[:, :], 1.0), wr=("one11",))

    brow = p.sb("brow", [1, SEQ], F32, scope="prog")
    grow = p.sb("grow", [1, SEQ], F32, scope="prog")
    gcum = p.sb("gcum", [1, SEQ], F32, scope="prog")
    rst = p.sb("rst", [1, SEQ], F32, scope="prog")
    p.op("vector", lambda v: v.memset(rst[:, :], 1.0), wr=("rst",))
    p.op("vector", lambda v: v.memset(
        bass.AP(rst, 0, [[SEQ, 1], [C, NCH]]), 0.0), rd=("rst",), wr=("rst",))

    S0 = p.sb("S0", [128, DV], F32, scope="prog")
    S016 = p.sb("S016", [128, DV], BF16, scope="prog")
    qc = p.sb("qc", [128, C], BF16)
    kc = p.sb("kc", [128, C], BF16)
    ktc = p.sb("ktc", [128, DK], F32)
    vtc = p.sb("vtc", [128, DV], F32)
    rc16 = p.sb("rc16", [128, DV], BF16)
    bcol = p.sb("bcol", [128, 1], F32)
    gcol = p.sb("gcol", [128, 1], F32)
    lamc = p.sb("lamc", [128, 1], F32)
    nbl = p.sb("nbl", [128, 1], F32)
    gCc = p.sb("gCc", [128, 1], F32)
    ecol = p.sb("ecol", [128, 1], F32)
    dec = p.sb("dec", [128, 128], F32)
    dec2 = p.sb("dec2", [128, 128], F32)
    dd = p.sb("dd", [128, 128], F32)
    ddt = p.sb("ddt", [128, 128], F32)
    aqt = p.sb("aqt", [128, 128], BF16)
    Dm = p.sb("Dm", [128, 128], BF16)
    DmT = p.sb("DmT", [128, 128], BF16)
    Nm = p.sb("Nm", [128, 128], BF16)
    NmT = p.sb("NmT", [128, 128], BF16)
    Wm = p.sb("Wm", [128, 128], BF16)
    WmT = p.sb("WmT", [128, 128], BF16)
    Zm = p.sb("Zm", [128, 128], BF16)
    ZmT = p.sb("ZmT", [128, 128], BF16)
    Z2 = p.sb("Z2", [128, 128], BF16)
    Z2T = p.sb("Z2T", [128, 128], BF16)
    Z4T = p.sb("Z4T", [128, 128], BF16)
    tmpA = p.sb("tmpA", [128, 128], BF16)
    tmpB = p.sb("tmpB", [128, 128], BF16)
    U = p.sb("U", [128, DV], BF16)
    Uf = p.sb("Uf", [128, DV], F32)
    ktil = p.sb("ktil", [128, DK], BF16)
    qlam = p.sb("qlam", [128, C], BF16)
    ot = p.sb("ot", [128, DV], F32)
    psA = p.ps("psA", [128, DV], F32)
    psB = p.ps("psB", [128, DV], F32)
    psC = p.ps("psC", [128, 128], F32)
    psD = p.ps("psD", [128, 128], F32)
    psE = p.ps("psE", [128, 2], F32)

    def mm1(psk, psum, stat, statk, mov, movk):
        p.mm_group(psk, [
            (lambda t, psum=psum, stat=stat, mov=mov: t.matmul(
                psum[:, :], stat, mov, start=True, stop=True), (statk, movk))])

    def evac16(dst, dstk, psum, psk, eng="vector"):
        if eng == "vector":
            p.op("vector", lambda v: v.tensor_copy(dst[:, :], psum[:, :]),
                 rd=(psk,), wr=(dstk,))
        else:
            p.op("scalar", lambda s: s.copy(dst[:, :], psum[:, :]),
                 rd=(psk,), wr=(dstk,))

    for u in range(NU):
        p.op("sync", lambda s, u=u: s.dma_start(
            out=brow[:, :], in_=beta_d[u:u + 1, :]), wr=("brow",), dma=True,
            stream="bg")
        p.op("sync", lambda s, u=u: s.dma_start(
            out=grow[:, :], in_=g_d[u:u + 1, :]), wr=("grow",), dma=True,
            stream="bg")
        p.op("vector", lambda v: v.tensor_tensor_scan(
            gcum[:, :], rst[:, :], grow[:, :], 0.0, ALU.mult, ALU.add),
            rd=("rst", "grow"), wr=("gcum",))
        p.op("vector", lambda v: v.memset(S0[:, :], 0.0), wr=("S0",))
        p.op("vector", lambda v: v.memset(S016[:, :], 0.0), wr=("S016",))
        for ci in range(NCH):
            c0 = ci * C
            p.op("gpsimd", lambda g, u=u, c0=c0: g.dma_start(
                out=qc[:, :], in_=qf_d[u * DK:(u + 1) * DK, c0:c0 + C]),
                wr=("qc",), dma=True, stream="ld0")
            p.op("gpsimd", lambda g, u=u, c0=c0: g.dma_start(
                out=kc[:, :], in_=kf_d[u * DK:(u + 1) * DK, c0:c0 + C]),
                wr=("kc",), dma=True, stream="ld1")
            p.op("sync", lambda s, u=u, c0=c0: s.dma_start(
                out=ktc[:, :], in_=kT_d[u * SEQ + c0:u * SEQ + c0 + C, :]),
                wr=("ktc",), dma=True, stream="ld2")
            p.op("sync", lambda s, u=u, c0=c0: s.dma_start(
                out=vtc[:, :], in_=vT_d[u * SEQ + c0:u * SEQ + c0 + C, :]),
                wr=("vtc",), dma=True, stream="ld3")
            p.mm_group("psE", [
                (lambda t, c0=c0: t.matmul(psE[:, 0:1], brow[0:1, c0:c0 + C],
                                           one11[:, :], start=True, stop=True),
                 ("brow", "one11")),
                (lambda t, c0=c0: t.matmul(psE[:, 1:2], gcum[0:1, c0:c0 + C],
                                           one11[:, :], start=True, stop=True),
                 ("gcum", "one11"))])
            p.op("vector", lambda v: v.tensor_copy(bcol[:, :], psE[:, 0:1]),
                 rd=("psE",), wr=("bcol",))
            p.op("vector", lambda v: v.tensor_copy(gcol[:, :], psE[:, 1:2]),
                 rd=("psE",), wr=("gcol",))
            p.mm_group("psE", [
                (lambda t, c0=c0: t.matmul(
                    psE[:, 0:1], ones_s[0:1, :],
                    gcum[0:1, c0 + C - 1:c0 + C], start=True, stop=True),
                 ("gcum", "ones_s"))])
            p.op("vector", lambda v: v.tensor_copy(gCc[:, :], psE[:, 0:1]),
                 rd=("psE",), wr=("gCc",))
            p.op("scalar", lambda s: s.activation(lamc[:, :], gcol[:, :], AF.Exp),
                 rd=("gcol",), wr=("lamc",))
            p.op("vector", lambda v: v.tensor_mul(nbl[:, :], bcol[:, :], lamc[:, :]),
                 rd=("bcol", "lamc"), wr=("nbl",))
            p.op("vector", lambda v: v.tensor_scalar_mul(nbl[:, :], nbl[:, :], -1.0),
                 rd=("nbl",), wr=("nbl",))
            p.op("scalar", lambda s: s.activation(
                ecol[:, :], gcol[:, :], AF.Exp, scale=-1.0, bias=gCc[:, 0:1]),
                rd=("gcol", "gCc"), wr=("ecol",))
            mm1("psC", psC, ones_s[0:1, :], "ones_s", gcum[0:1, c0:c0 + C], "gcum")
            p.op("vector", lambda v: v.tensor_scalar(
                dec[:, :], psC[:, :], gcol[:, 0:1], 0.0, ALU.subtract, ALU.min),
                rd=("psC", "gcol"), wr=("dec",))
            p.op("scalar", lambda s: s.activation(dec[:, :], dec[:, :], AF.Exp),
                 rd=("dec",), wr=("dec",))
            p.op("vector", lambda v: v.tensor_scalar(
                dec2[:, :], psC[:, :], gcol[:, 0:1], None, ALU.subtract),
                rd=("psC", "gcol"), wr=("dec2",))
            p.op("vector", lambda v: v.tensor_scalar(
                dec2[:, :], dec2[:, :], -1.0, 0.0, ALU.mult, ALU.min),
                rd=("dec2",), wr=("dec2",))
            p.op("scalar", lambda s: s.activation(dec2[:, :], dec2[:, :], AF.Exp),
                 rd=("dec2",), wr=("dec2",))
            mm1("psD", psD, ones_s[0:1, :], "ones_s", brow[0:1, c0:c0 + C], "brow")
            mm1("psC", psC, kc[:, :], "kc", kc[:, :], "kc")
            p.op("vector", lambda v: v.tensor_mul(dd[:, :], psC[:, :], dec2[:, :]),
                 rd=("psC", "dec2"), wr=("dd",))
            p.op("vector", lambda v: v.tensor_scalar(
                dd[:, :], dd[:, :], bcol[:, 0:1], None, ALU.mult),
                rd=("dd", "bcol"), wr=("dd",))
            p.op("vector", lambda v: v.tensor_mul(ddt[:, :], psC[:, :], dec[:, :]),
                 rd=("psC", "dec"), wr=("ddt",))
            p.op("vector", lambda v: v.tensor_mul(ddt[:, :], ddt[:, :], psD[:, :]),
                 rd=("ddt", "psD"), wr=("ddt",))
            for dst, dstk, srcn, mk in [(Dm, "Dm", "dd", "bdl"), (Nm, "Nm", "dd", "sbl"),
                                        (DmT, "DmT", "ddt", "bdlT"),
                                        (NmT, "NmT", "ddt", "sblT")]:
                srcb = dd if srcn == "dd" else ddt
                p.op("vector", lambda v, dst=dst, srcb=srcb, mk=mk: v.tensor_mul(
                    dst[:, :], srcb[:, :], masks[mk][:, :]),
                    rd=(srcn, mk), wr=(dstk,))
            mm1("psC", psC, kc[:, :], "kc", qc[:, :], "qc")
            p.op("vector", lambda v: v.tensor_mul(aqt[:, :], psC[:, :], dec[:, :]),
                 rd=("psC", "dec"), wr=("aqt",))
            p.op("vector", lambda v: v.tensor_mul(aqt[:, :], aqt[:, :],
                                                  masks["trui"][:, :]),
                 rd=("aqt", "trui"), wr=("aqt",))
            p.op("vector", lambda v: v.tensor_scalar(
                vtc[:, :], vtc[:, :], bcol[:, 0:1], None, ALU.mult),
                rd=("vtc", "bcol"), wr=("vtc",))
            mm1("psA", psA, kc[:, :], "kc", S016[:, :], "S016")
            p.op("vector", lambda v: v.scalar_tensor_tensor(
                Uf[:, :], psA[:, :], nbl[:, 0:1], vtc[:, :], ALU.mult, ALU.add),
                rd=("psA", "nbl", "vtc"), wr=("Uf",))
            p.op("vector", lambda v: v.tensor_copy(rc16[:, :], Uf[:, :]),
                 rd=("Uf",), wr=("rc16",))
            p.op("vector", lambda v: v.tensor_sub(Wm[:, :], masks["eye"][:, :],
                                                  Dm[:, :]),
                 rd=("eye", "Dm"), wr=("Wm",))
            p.op("vector", lambda v: v.tensor_sub(WmT[:, :], masks["eye"][:, :],
                                                  DmT[:, :]),
                 rd=("eye", "DmT"), wr=("WmT",))
            pk, pkT, pkk, pkTk = Dm, DmT, "Dm", "DmT"
            for lev in range(3):
                mm1("psC", psC, pkT[:, :], pkTk, pk[:, :], pkk)
                mm1("psD", psD, pk[:, :], pkk, pkT[:, :], pkTk)
                p.op("vector", lambda v: v.tensor_add(tmpA[:, :], psC[:, :],
                                                      masks["eye"][:, :]),
                     rd=("psC", "eye"), wr=("tmpA",))
                p.op("vector", lambda v: v.tensor_add(tmpB[:, :], psD[:, :],
                                                      masks["eye"][:, :]),
                     rd=("psD", "eye"), wr=("tmpB",))
                mm1("psC", psC, WmT[:, :], "WmT", tmpA[:, :], "tmpA")
                mm1("psD", psD, tmpA[:, :], "tmpA", WmT[:, :], "WmT")
                evac16(Wm, "Wm", psC, "psC")
                evac16(WmT, "WmT", psD, "psD")
                if lev < 2:
                    mm1("psC", psC, pkT[:, :], pkTk, pk[:, :], pkk)
                    mm1("psD", psD, pk[:, :], pkk, pkT[:, :], pkTk)
                    nk, nkT = (tmpA, tmpB) if False else (None, None)
                    if lev == 0:
                        evac16(Z2, "Z2", psC, "psC")
                        evac16(Z2T, "Z2T", psD, "psD")
                        pk, pkT, pkk, pkTk = Z2, Z2T, "Z2", "Z2T"
                    else:
                        evac16(Z4T, "Z4T", psC, "psC")
                        evac16(tmpA, "tmpA", psD, "psD")
                        pk, pkT, pkk, pkTk = Z4T, tmpA, "Z4T", "tmpA"
            mm1("psC", psC, WmT[:, :], "WmT", Nm[:, :], "Nm")
            mm1("psD", psD, Nm[:, :], "Nm", WmT[:, :], "WmT")
            evac16(Zm, "Zm", psC, "psC")
            evac16(ZmT, "ZmT", psD, "psD")
            mm1("psC", psC, ZmT[:, :], "ZmT", Zm[:, :], "Zm")
            mm1("psD", psD, Zm[:, :], "Zm", ZmT[:, :], "ZmT")
            evac16(Z2, "Z2", psC, "psC")
            evac16(Z2T, "Z2T", psD, "psD")
            mm1("psC", psC, Z2[:, :], "Z2", Z2T[:, :], "Z2T")
            evac16(Z4T, "Z4T", psC, "psC")
            mm1("psA", psA, WmT[:, :], "WmT", rc16[:, :], "rc16")
            p.op("vector", lambda v: v.tensor_copy(Uf[:, :], psA[:, :]),
                 rd=("psA",), wr=("Uf",))
            p.op("vector", lambda v: v.tensor_copy(U[:, :], Uf[:, :]),
                 rd=("Uf",), wr=("U",))
            for statb, statk, sign in [(Z4T, "Z4T", 1), (Z2T, "Z2T", 1),
                                       (ZmT, "ZmT", -1)]:
                mm1("psA", psA, statb[:, :], statk, U[:, :], "U")
                if sign > 0:
                    p.op("vector", lambda v: v.tensor_add(Uf[:, :], Uf[:, :],
                                                          psA[:, :]),
                         rd=("psA", "Uf"), wr=("Uf",))
                else:
                    p.op("vector", lambda v: v.tensor_sub(Uf[:, :], Uf[:, :],
                                                          psA[:, :]),
                         rd=("psA", "Uf"), wr=("Uf",))
                p.op("vector", lambda v: v.tensor_copy(U[:, :], Uf[:, :]),
                     rd=("Uf",), wr=("U",))
            mm1("psC", psC, ones_s[0:1, :], "ones_s", gcum[0:1, c0:c0 + C], "gcum")
            p.op("scalar", lambda s: s.activation(dec[:, :], psC[:, :], AF.Exp),
                 rd=("psC",), wr=("dec",))
            p.op("vector", lambda v: v.tensor_mul(qlam[:, :], qc[:, :], dec[:, :]),
                 rd=("qc", "dec"), wr=("qlam",))
            p.mm_group("psB", [
                (lambda t: t.matmul(psB[:, :], aqt[:, :], U[:, :],
                                    start=True, stop=False), ("aqt", "U")),
                (lambda t: t.matmul(psB[:, :], qlam[:, :], S016[:, :],
                                    start=False, stop=True), ("qlam", "S016"))])
            p.op("scalar", lambda s: s.copy(ot[:, :], psB[:, :]),
                 rd=("psB",), wr=("ot",))
            p.op("sync", lambda s, u=u, c0=c0: s.dma_start(
                out=o_d[u * SEQ + c0:u * SEQ + c0 + C, :], in_=ot[:, :]),
                rd=("ot",), dma=True, stream="oo")
            p.op("vector", lambda v: v.tensor_scalar(
                ktil[:, :], ktc[:, :], ecol[:, 0:1], None, ALU.mult),
                rd=("ktc", "ecol"), wr=("ktil",))
            mm1("psA", psA, ktil[:, :], "ktil", U[:, :], "U")
            p.op("scalar", lambda s: s.activation(lamc[:, :], gCc[:, :], AF.Exp),
                 rd=("gCc",), wr=("lamc",))
            p.op("vector", lambda v: v.scalar_tensor_tensor(
                S0[:, :], S0[:, :], lamc[:, 0:1], psA[:, :], ALU.mult, ALU.add),
                rd=("psA", "lamc", "S0"), wr=("S0",))
            p.op("vector", lambda v: v.tensor_copy(S016[:, :], S0[:, :]),
                 rd=("S0",), wr=("S016",))
    p.barrier()
    return p


def host_l1delta(q_all, k_all, v_all, beta_all, g_all):
    tril_s = np.tril(np.ones((128, 128), np.float32), -1)
    blockm = np.zeros((128, 128), np.float32)
    for r in range(8):
        blockm[r * 16:(r + 1) * 16, r * 16:(r + 1) * 16] = 1
    bdl = tril_s * blockm
    sbl = tril_s - bdl
    trui = np.triu(np.ones((128, 128), np.float32), 0)
    eye = np.eye(128, dtype=np.float32)
    maps = []
    for c in range(8):
        units = [(g // H, g % H) for g in range(3 * c, 3 * c + 3)]
        qf = np.concatenate([q_all[b, h * DK:(h + 1) * DK] for b, h in units], 0)
        kf = np.concatenate([k_all[b, h * DK:(h + 1) * DK] for b, h in units], 0)
        kT = np.concatenate([k_all[b, h * DK:(h + 1) * DK].T for b, h in units], 0)
        vT = np.concatenate([v_all[b, h * DV:(h + 1) * DV].T for b, h in units], 0)
        br = np.stack([beta_all[b, h] for b, h in units], 0)
        gr = np.stack([g_all[b, h] for b, h in units], 0)
        maps.append({
            "qf": np.ascontiguousarray(qf), "kf": np.ascontiguousarray(kf),
            "kT": np.ascontiguousarray(kT), "vT": np.ascontiguousarray(vT),
            "betar": np.ascontiguousarray(br), "gr": np.ascontiguousarray(gr),
            "bdl": bdl, "bdlT": np.ascontiguousarray(bdl.T),
            "sbl": sbl, "sblT": np.ascontiguousarray(sbl.T),
            "trui": trui, "eye": eye})
    return maps



def build_l1post():
    nc = bass.Bass("TRN2", target_bir_lowering=False)
    p = Prog(nc)

    o_dr = nc.dram_tensor("o_in", [VD, T], F32, kind="ExternalInput").ap()
    gs_dr = nc.dram_tensor("gs", [VD, T], F32, kind="ExternalInput").ap()
    out1_dr = nc.dram_tensor("out1", [DM, T], F32, kind="ExternalInput").ap()
    xn1_dr = nc.dram_tensor("xn1", [DM, T], BF16, kind="ExternalInput").ap()
    mw_dr = nc.dram_tensor("mw", [8, T], F32, kind="ExternalInput").ap()
    nw_dr = nc.dram_tensor("nw", [128, 2], F32, kind="ExternalInput").ap()
    wo_dr = nc.dram_tensor("wo", [VD, DM], F32, kind="ExternalInput").ap()
    wup_dr = nc.dram_tensor("wup", [9 * DM, HID], F32, kind="ExternalInput").ap()
    wdn_dr = nc.dram_tensor("wdn", [9 * HID, DM], F32, kind="ExternalInput").ap()
    o_fin = nc.dram_tensor("o_fin", [DM, T], F32, kind="ExternalOutput").ap()

    ones_s = p.sb("ones_s", [128, 128], F32, scope="prog")
    p.op("vector", lambda v: v.memset(ones_s[:, :], 1.0), wr=("ones_s",))
    res1 = p.sb("res1", [128, 8 * T], F32, scope="prog")
    xn1b = p.sb("xn1b", [128, 8 * T], BF16, scope="prog")
    for d in range(8):
        p.op("sync", lambda s, d=d: s.dma_start(
            out=xn1b[:, d * T:(d + 1) * T], in_=xn1_dr[d * 128:(d + 1) * 128, :]),
            wr=("xn1b",), dma=True, stream="xin")
        p.op("sync", lambda s, d=d: s.dma_start(
            out=res1[:, d * T:(d + 1) * T], in_=out1_dr[d * 128:(d + 1) * 128, :]),
            wr=("res1",), dma=True, stream="xin2")
    epsc, epsk = p.constcol(1e-5)

    wo_s = p.sb("wo_s", [128, 12 * DM], BF16)
    nw_s = p.sb("nw_s", [128, 2], F32)
    ov = p.sb("ov", [128, 2 * T], F32)
    gs16 = p.sb("gs16", [128, 2 * T], BF16)
    an16 = p.sb("an16", [128, 2 * T], BF16)
    sq = p.sb("sq", [128, T], F32)
    rrb = p.sb("rrb", [128, T], F32)
    ssps = p.ps("ssps", [128, 512], F32)
    aps = p.ps("aps", [128, 512], F32)
    aps2 = p.ps("aps2", [128, 512], F32)
    for kt in range(12):
        p.op("gpsimd", lambda g, kt=kt: g.dma_start(
            out=wo_s[:, kt * DM:(kt + 1) * DM],
            in_=wo_dr[kt * 128:(kt + 1) * 128, :]), wr=("wo_s",), dma=True)
    p.op("sync", lambda s: s.dma_start(out=nw_s[:, :], in_=nw_dr),
         wr=("nw_s",), dma=True, stream="xin")
    for h in range(H):
        for j in range(2):
            p.op("sync", lambda s, h=h, j=j: s.dma_start(
                out=ov[:, j * T:(j + 1) * T],
                in_=o_dr[(2 * h + j) * 128:(2 * h + j + 1) * 128, :]),
                wr=("ov",), dma=True, stream="oin")
            p.op("gpsimd", lambda g, h=h, j=j: g.dma_start(
                out=gs16[:, j * T:(j + 1) * T],
                in_=gs_dr[(2 * h + j) * 128:(2 * h + j + 1) * 128, :]),
                wr=("gs16",), dma=True, stream="gin")
        for sl in range(4):
            f0 = sl * 512
            for j in range(2):
                p.op("scalar", lambda s, j=j, f0=f0: s.activation(
                    sq[:, f0:f0 + 512], ov[:, j * T + f0:j * T + f0 + 512],
                    AF.Square), rd=("ov",), wr=("sq",))
                p.mm_group("ssps", [
                    (lambda t, f0=f0, j=j: t.matmul(
                        ssps[:, :], ones_s[:, :], sq[:, f0:f0 + 512],
                        start=(j == 0), stop=(j == 1)), ("sq", "ones_s"))])
            p.op("scalar", lambda s, f0=f0: s.activation(
                rrb[:, f0:f0 + 512], ssps[:, :], AF.Sqrt, bias=epsc,
                scale=1.0 / DV), rd=("ssps", epsk), wr=("rrb",))
        p.op("vector", lambda v: v.reciprocal(rrb[:, :], rrb[:, :]),
             rd=("rrb",), wr=("rrb",))
        for j in range(2):
            p.op("vector", lambda v, j=j: v.scalar_tensor_tensor(
                ov[:, j * T:(j + 1) * T], ov[:, j * T:(j + 1) * T],
                nw_s[:, j:j + 1], rrb[:, :], ALU.mult, ALU.mult),
                rd=("ov", "nw_s", "rrb"), wr=("ov",))
            p.op("vector", lambda v, j=j: v.tensor_mul(
                an16[:, j * T:(j + 1) * T], ov[:, j * T:(j + 1) * T],
                gs16[:, j * T:(j + 1) * T]),
                rd=("ov", "gs16"), wr=("an16",))
        for sl in range(4):
            f0 = sl * 512
            for d in range(8):
                psk, psum = ("aps", aps) if d % 2 == 0 else ("aps2", aps2)
                p.mm_group(psk, [
                    (lambda t, d=d, f0=f0, h=h, j=j, psum=psum: t.matmul(
                        psum[:, :],
                        wo_s[:, (2 * h + j) * DM + d * 128:(2 * h + j) * DM + d * 128 + 128],
                        an16[:, j * T + f0:j * T + f0 + 512],
                        start=(j == 0), stop=(j == 1)), ("wo_s", "an16"))
                    for j in range(2)])
                p.op("vector", lambda v, d=d, f0=f0, psum=psum, psk=psk: v.tensor_add(
                    res1[:, d * T + f0:d * T + f0 + 512],
                    res1[:, d * T + f0:d * T + f0 + 512], psum[:, :]),
                    rd=(psk, "res1"), wr=("res1",))
    p.barrier()

    CS = 256
    NSL = T // CS
    wup_s = [p.sb(f"wup{i}", [128, 8 * HID], BF16) for i in range(2)]
    wdn_s = [p.sb(f"wdn{i}", [128, 11 * DM], BF16) for i in range(2)]
    webs = p.sb("webs", [128, 256], F32)
    xe16 = p.sb("xe16", [128, 8 * CS], BF16)
    hps = p.ps("hps", [128, CS], F32)
    hps2 = p.ps("hps2", [128, CS], F32)
    dps = p.ps("dps", [128, CS], F32)
    dps2 = p.ps("dps2", [128, CS], F32)
    h16 = p.sb("h16", [128, 11 * CS], BF16)
    hrl = p.sb("hrl", [128, CS], BF16)

    def load_expert(e, buf):
        for kt in range(8):
            p.op("gpsimd", lambda g, e=e, kt=kt, buf=buf: g.dma_start(
                out=wup_s[buf][:, kt * HID:(kt + 1) * HID],
                in_=wup_dr[e * DM + kt * 128:e * DM + kt * 128 + 128, :]),
                wr=(f"wup{buf}",), dma=True, stream="wup")
        for kt in range(11):
            p.op("gpsimd", lambda g, e=e, kt=kt, buf=buf: g.dma_start(
                out=wdn_s[buf][:, kt * DM:(kt + 1) * DM],
                in_=wdn_dr[e * HID + kt * 128:e * HID + kt * 128 + 128, :]),
                wr=(f"wdn{buf}",), dma=True, stream="wdn")

    load_expert(0, 0)
    for e in range(9):
        buf = e % 2
        if e < 8:
            load_expert(e + 1, (e + 1) % 2)
        for sl in range(NSL):
            f0 = sl * CS
            if e < 8:
                p.op("sync", lambda s, e=e, f0=f0: s.dma_start(
                    out=webs[:, 0:CS],
                    in_=mw_dr[e:e + 1, f0:f0 + CS].partition_broadcast(128)),
                    wr=("webs",), dma=True, stream="web")
                for d in range(8):
                    p.op("vector", lambda v, d=d, f0=f0: v.tensor_mul(
                        xe16[:, d * CS:(d + 1) * CS],
                        xn1b[:, d * T + f0:d * T + f0 + CS], webs[:, 0:CS]),
                        rd=("xn1b", "webs"), wr=("xe16",))
            for pt in range(11):
                psk, psum = ("hps", hps) if pt % 2 == 0 else ("hps2", hps2)
                if e < 8:
                    p.mm_group(psk, [
                        (lambda t, kt=kt, pt=pt, psum=psum, buf=buf: t.matmul(
                            psum[:, :],
                            wup_s[buf][:, kt * HID + pt * 128:kt * HID + pt * 128 + 128],
                            xe16[:, kt * CS:(kt + 1) * CS],
                            start=(kt == 0), stop=(kt == 7)),
                         (f"wup{buf}", "xe16")) for kt in range(8)])
                else:
                    p.mm_group(psk, [
                        (lambda t, kt=kt, pt=pt, psum=psum, buf=buf, f0=f0: t.matmul(
                            psum[:, :],
                            wup_s[buf][:, kt * HID + pt * 128:kt * HID + pt * 128 + 128],
                            xn1b[:, kt * T + f0:kt * T + f0 + CS],
                            start=(kt == 0), stop=(kt == 7)),
                         (f"wup{buf}", "xn1b")) for kt in range(8)])
                p.op("vector", lambda v, pt=pt, psum=psum, psk=psk: v.tensor_scalar_max(
                    hrl[:, :], psum[:, :], 0.0), rd=(psk,), wr=("hrl",))
                p.op("scalar", lambda s, pt=pt: s.activation(
                    h16[:, pt * CS:(pt + 1) * CS], hrl[:, :], AF.Square),
                    rd=("hrl",), wr=("h16",))
            for d in range(8):
                psk, psum = ("dps", dps) if d % 2 == 0 else ("dps2", dps2)
                p.mm_group(psk, [
                    (lambda t, kt=kt, d=d, psum=psum, buf=buf: t.matmul(
                        psum[:, :],
                        wdn_s[buf][:, kt * DM + d * 128:kt * DM + d * 128 + 128],
                        h16[:, kt * CS:(kt + 1) * CS],
                        start=(kt == 0), stop=(kt == 10)),
                     (f"wdn{buf}", "h16")) for kt in range(11)])
                p.op("vector", lambda v, d=d, f0=f0, psum=psum, psk=psk: v.tensor_add(
                    res1[:, d * T + f0:d * T + f0 + CS],
                    res1[:, d * T + f0:d * T + f0 + CS], psum[:, :]),
                    rd=(psk, "res1"), wr=("res1",))
    for d in range(8):
        p.op("sync", lambda s, d=d: s.dma_start(
            out=o_fin[d * 128:(d + 1) * 128, :], in_=res1[:, d * T:(d + 1) * T]),
            rd=("res1",), dma=True, stream="fin")
    p.barrier()
    return p


def host_l1post(inputs, o_fm, gs_cores, out1_cores, xn1_cores, mw_cores):
    nw = np.asarray(inputs["L1_attn_norm_w"], np.float32)
    nw_s = np.ascontiguousarray(nw.reshape(2, 128).T)
    wup = np.asarray(inputs["L1_ffn_w_up"], np.float32)
    wdn = np.asarray(inputs["L1_ffn_w_down"], np.float32)
    sup = np.asarray(inputs["L1_ffn_shared_w_up"], np.float32)
    sdn = np.asarray(inputs["L1_ffn_shared_w_down"], np.float32)
    wupT = np.concatenate([wup.transpose(0, 2, 1).reshape(8 * DM, HID),
                           sup.T], axis=0)
    wdnT = np.concatenate([wdn.transpose(0, 2, 1).reshape(8 * HID, DM),
                           sdn.T], axis=0)
    wo = np.ascontiguousarray(np.asarray(inputs["L1_attn_Wo"], np.float32))
    maps = []
    for c in range(8):
        b, hh = c // 2, c % 2
        s0 = hh * T
        maps.append({
            "o_in": np.ascontiguousarray(o_fm[b, :, s0:s0 + T]),
            "gs": gs_cores[c], "out1": out1_cores[c], "xn1": xn1_cores[c],
            "mw": mw_cores[c], "nw": nw_s, "wo": wo,
            "wup": np.ascontiguousarray(wupT), "wdn": np.ascontiguousarray(wdnT)})
    return maps



RG_PAIRS = [[0, 1], [2, 3], [4, 5], [6, 7]]


def build_fused():
    nc = bass.Bass("TRN2", target_bir_lowering=False)
    p = Prog(nc)
    TH = T + 3
    QSC = float(DK) ** -0.5

    xh = nc.dram_tensor("xh", [DM, TH], F32, kind="ExternalInput").ap()
    w_in = nc.dram_tensor("w_in", [DM, 2 * DMAMBA], F32, kind="ExternalInput").ap()
    cwT = nc.dram_tensor("cwT", [128, 64], F32, kind="ExternalInput").ap()
    cb = nc.dram_tensor("cb", [128, 16], F32, kind="ExternalInput").ap()
    w_xa = nc.dram_tensor("w_xa", [DMAMBA, 128], F32, kind="ExternalInput").ap()
    w_xc = nc.dram_tensor("w_xc", [DMAMBA, 16], F32, kind="ExternalInput").ap()
    w_dt = nc.dram_tensor("w_dt", [DTRANK, DMAMBA], F32, kind="ExternalInput").ap()
    b_dt = nc.dram_tensor("b_dt", [128, 16], F32, kind="ExternalInput").ap()
    w1T = nc.dram_tensor("w1T", [DM, FF0], F32, kind="ExternalInput").ap()
    b1d = nc.dram_tensor("b1", [128, 32], F32, kind="ExternalInput").ap()
    w2T = nc.dram_tensor("w2T", [FF0, DM], F32, kind="ExternalInput").ap()
    b2d = nc.dram_tensor("b2", [128, 8], F32, kind="ExternalInput").ap()
    alog0 = nc.dram_tensor("alog0", [128, 256], F32, kind="ExternalInput").ap()
    dvec = nc.dram_tensor("dvec", [128, 16], F32, kind="ExternalInput").ap()
    wout_d = nc.dram_tensor("wout", [DMAMBA, DM], F32, kind="ExternalInput").ap()
    wqkvg_d = nc.dram_tensor("wqkvg", [DM, 4608], F32, kind="ExternalInput").ap()
    wba_d = nc.dram_tensor("wba", [128, 8 * 16], F32, kind="ExternalInput").ap()
    gwT_d = nc.dram_tensor("gwT", [DM, 8], F32, kind="ExternalInput").ap()
    ebias_d = nc.dram_tensor("ebias", [1, 8], F32, kind="ExternalInput").ap()
    cwq_d = nc.dram_tensor("cwq", [128, 4 * 24], F32, kind="ExternalInput").ap()
    alog1 = nc.dram_tensor("alog1", [6, 1], F32, kind="ExternalInput").ap()
    dtb_d = nc.dram_tensor("dtb", [6, 1], F32, kind="ExternalInput").ap()
    nw_dr = nc.dram_tensor("nw", [128, 2], F32, kind="ExternalInput").ap()
    wo_dr = nc.dram_tensor("wo", [VD, DM], F32, kind="ExternalInput").ap()
    wup_dr = nc.dram_tensor("wup", [9 * DM, HID], F32, kind="ExternalInput").ap()
    wdn_dr = nc.dram_tensor("wdn", [9 * HID, DM], F32, kind="ExternalInput").ap()
    bdl_d = nc.dram_tensor("bdl", [128, 128], F32, kind="ExternalInput").ap()
    bdlT_d = nc.dram_tensor("bdlT", [128, 128], F32, kind="ExternalInput").ap()
    sbl_d = nc.dram_tensor("sbl", [128, 128], F32, kind="ExternalInput").ap()
    sblT_d = nc.dram_tensor("sblT", [128, 128], F32, kind="ExternalInput").ap()
    trui_d = nc.dram_tensor("trui", [128, 128], F32, kind="ExternalInput").ap()
    eye_d = nc.dram_tensor("eye", [128, 128], F32, kind="ExternalInput").ap()
    lrow_d = nc.dram_tensor("lrowd", [128, 128], F32, kind="ExternalInput").ap()
    fc1_d = nc.dram_tensor("fc1", [128, 1], F32, kind="ExternalInput").ap()
    fc0_d = nc.dram_tensor("fc0", [128, 1], F32, kind="ExternalInput").ap()
    o_fin = nc.dram_tensor("o_fin", [DM, T], F32, kind="ExternalOutput").ap()

    d_dt = nc.dram_tensor("d_dt", [1024, T], BF16).ap()
    d_xmc = nc.dram_tensor("d_xmc", [1024, T], BF16).ap()
    d_sz = nc.dram_tensor("d_sz", [1024, T], BF16).ap()
    d_dtx = nc.dram_tensor("d_dtx", [1024, T], BF16).ap()
    d_xmcx = nc.dram_tensor("d_xmcx", [1024, T], BF16).ap()
    d_szx = nc.dram_tensor("d_szx", [1024, T], BF16).ap()
    exo_dt = nc.dram_tensor("exo_dt", [2048, T], BF16).ap()
    exo_xmc = nc.dram_tensor("exo_xmc", [2048, T], BF16).ap()
    exo_sz = nc.dram_tensor("exo_sz", [2048, T], BF16).ap()
    d_bcg = nc.dram_tensor("d_bcg", [32, 2 * T], BF16).ap()
    d_ygF = nc.dram_tensor("d_ygF", [1024, 2 * T], BF16).ap()
    d_ygoF = nc.dram_tensor("d_ygoF", [2048, 2 * T], BF16).ap()
    d_part = nc.dram_tensor("d_part", [DM, T], F32).ap()
    d_out1 = nc.dram_tensor("d_out1", [DM, T], F32).ap()
    d_xn1 = nc.dram_tensor("d_xn1", [DM, T], BF16).ap()
    d_gs = nc.dram_tensor("d_gs", [VD, T], BF16).ap()
    d_mw = nc.dram_tensor("d_mw", [8, T], F32).ap()
    d_qloc = nc.dram_tensor("d_qloc", [128, 3 * T], BF16).ap()
    d_kloc = nc.dram_tensor("d_kloc", [128, 3 * T], BF16).ap()
    d_vloc = nc.dram_tensor("d_vloc", [128, 6 * T], BF16).ap()
    d_qx = nc.dram_tensor("d_qx", [128, 3 * T], BF16).ap()
    d_kx = nc.dram_tensor("d_kx", [128, 3 * T], BF16).ap()
    d_vx = nc.dram_tensor("d_vx", [128, 6 * T], BF16).ap()
    d_bgx = nc.dram_tensor("d_bgx", [6, T], F32).ap()
    exo_q = nc.dram_tensor("exo_q", [256, 3 * T], BF16).ap()
    exo_k = nc.dram_tensor("exo_k", [256, 3 * T], BF16).ap()
    exo_v = nc.dram_tensor("exo_v", [256, 6 * T], BF16).ap()
    exo_bg = nc.dram_tensor("exo_bg", [12, T], F32).ap()
    d_oF = nc.dram_tensor("d_oF", [3 * SEQ, DV], BF16).ap()
    exo_o = nc.dram_tensor("exo_o", [6 * SEQ, DV], BF16).ap()
    d_bc = nc.dram_tensor("d_bc", [16, 2 * T], BF16).ap()
    cc1i = nc.dram_tensor("cc1i", [128, 24], F32).ap()
    cc1o = nc.dram_tensor("cc1o", [128, 24], F32).ap()

    epsc, epsk = p.constcol(1e-5)
    eps12c, eps12k = p.constcol(1e-12)
    onec, onek = p.constcol(1.0)
    ones_s = p.sb("ones_s", [128, 128], F32, scope="prog")
    one11 = p.sb("one11", [1, 1], F32, scope="prog")
    fc1 = p.sb("fc1s", [128, 1], F32, scope="prog")
    fc0 = p.sb("fc0s", [128, 1], F32, scope="prog")
    A_s = p.sb("A_s", [128, 256], F32, scope="prog")
    D_s = p.sb("D_s", [128, 16], F32, scope="prog")
    carry = p.sb("carry", [128, 128], F32, scope="prog")
    p.op("vector", lambda v: v.memset(ones_s[:, :], 1.0), wr=("ones_s",))
    p.op("vector", lambda v: v.memset(one11[:, :], 1.0), wr=("one11",))
    p.op("sync", lambda s: s.dma_start(out=fc1[:, :], in_=fc1_d), wr=("fc1",),
         dma=True, stream="c0")
    p.op("sync", lambda s: s.dma_start(out=fc0[:, :], in_=fc0_d), wr=("fc0",),
         dma=True, stream="c0")
    p.op("sync", lambda s: s.dma_start(out=A_s[:, :], in_=alog0), wr=("A_s",),
         dma=True, stream="c0")
    p.op("scalar", lambda s: s.activation(A_s[:, :], A_s[:, :], AF.Exp),
         rd=("A_s",), wr=("A_s",))
    p.op("vector", lambda v: v.tensor_scalar_mul(A_s[:, :], A_s[:, :], -1.0),
         rd=("A_s",), wr=("A_s",))
    p.op("sync", lambda s: s.dma_start(out=D_s[:, :], in_=dvec), wr=("D_s",),
         dma=True, stream="c0")

    eraBC = ExitStack()
    BCb = p.sb("BCb", [16, 2 * T], BF16, scope=eraBC)

    era1 = ExitStack()
    xn16 = p.sb("xn16", [128, 8 * TH], BF16, scope=era1)

    CH = 512
    w_in_s = p.sb("w_in_s", [128, 8 * 2 * DMAMBA], BF16)
    cw_s = p.sb("cw_s", [128, 64], F32)
    cb_s = p.sb("cb_s", [128, 16], F32)
    wx_s = p.sb("wx_s", [128, 16 * 128], BF16)
    wxc_s = p.sb("wxc_s", [128, 16 * 16], BF16)
    wdt_s = p.sb("wdt_s", [64, DMAMBA], BF16)
    bdt_s = p.sb("bdt_s", [128, 16], F32)
    for kt in range(8):
        p.op("gpsimd", lambda g, kt=kt: g.dma_start(
            out=w_in_s[:, kt * 2 * DMAMBA:(kt + 1) * 2 * DMAMBA],
            in_=w_in[kt * 128:(kt + 1) * 128, :]), wr=("w_in_s",), dma=True)
    for mt in range(16):
        p.op("gpsimd", lambda g, mt=mt: g.dma_start(
            out=wx_s[:, mt * 128:(mt + 1) * 128],
            in_=w_xa[mt * 128:(mt + 1) * 128, :]), wr=("wx_s",), dma=True,
            stream="w2")
        p.op("gpsimd", lambda g, mt=mt: g.dma_start(
            out=wxc_s[:, mt * 16:(mt + 1) * 16],
            in_=w_xc[mt * 128:(mt + 1) * 128, :]), wr=("wxc_s",), dma=True,
            stream="w2")
    p.op("gpsimd", lambda g: g.dma_start(out=wdt_s[:, :], in_=w_dt),
         wr=("wdt_s",), dma=True, stream="w2")
    p.op("sync", lambda s: s.dma_start(out=cw_s[:, :], in_=cwT), wr=("cw_s",),
         dma=True, stream="c0")
    p.op("sync", lambda s: s.dma_start(out=cb_s[:, :], in_=cb), wr=("cb_s",),
         dma=True, stream="c0")
    p.op("sync", lambda s: s.dma_start(out=bdt_s[:, :], in_=b_dt), wr=("bdt_s",),
         dma=True, stream="c0")

    NT = CH + 3
    xc = p.sb("xc", [128, 8 * NT], F32)
    stat = p.ps("stat", [128, 512], F32)
    stat2 = p.ps("stat2", [128, 512], F32)
    x2t = p.sb("x2t", [128, NT], F32)
    mb = p.sb("mb", [128, NT], F32)
    rb = p.sb("rb", [128, NT], F32)
    rbx = p.sb("rbx", [128, NT], F32)
    xm_ps = p.ps("xm_ps", [128, 512], F32)
    xm_ps2 = p.ps("xm_ps2", [128, 512], F32)
    xmt = p.sb("xmt", [128, NT], F32)
    xmc16 = p.sb("xmc16", [128, 16 * CH], BF16)
    xmcf = p.sb("xmcf", [128, CH], F32)
    sz16 = p.sb("sz16", [128, CH], BF16)
    xdbl_ps = p.ps("xdbl_ps", [128, CH], F32)
    xdblC_ps = p.ps("xdblC_ps", [16, CH], F32)
    xdbl = p.sb("xdbl", [128, CH], BF16)
    dt_ps = p.ps("dt_ps", [128, 512], F32)
    dtt = p.sb("dtt", [128, CH], F32)
    dt16 = p.sb("dt16", [128, CH], BF16)
    dta = p.sb("dta", [128, CH], F32)

    for ci in range(T // CH):
        xoff = ci * CH
        ck = "xc"
        for d in range(8):
            p.op("sync", lambda s, d=d, xoff=xoff: s.dma_start(
                out=xc[:, d * NT:d * NT + NT],
                in_=xh[d * 128:(d + 1) * 128, xoff:xoff + NT]),
                wr=("xc",), dma=True, stream="xin")
        for sl in range(2):
            f0, f1 = sl * 512, min(NT, 512 + sl * 512)
            p.mm_group("stat", [
                (lambda t, d=d, f0=f0, f1=f1: t.matmul(
                    stat[:, 0:f1 - f0], ones_s[:, :],
                    xc[:, d * NT + f0:d * NT + f1],
                    start=(d == 0), stop=(d == 7)), ("xc", "ones_s"))
                for d in range(8)])
            p.op("scalar", lambda s, f0=f0, f1=f1: s.activation(
                mb[:, f0:f1], stat[:, 0:f1 - f0], AF.Copy, scale=1.0 / DM),
                rd=("stat",), wr=("mb",))
        for sl in range(2):
            f0, f1 = sl * 512, min(NT, 512 + sl * 512)
            for d in range(8):
                p.op("scalar", lambda s, d=d, f0=f0, f1=f1: s.activation(
                    x2t[:, f0:f1], xc[:, d * NT + f0:d * NT + f1], AF.Square),
                    rd=("xc",), wr=("x2t",))
                p.mm_group("stat2", [
                    (lambda t, d=d, f0=f0, f1=f1: t.matmul(
                        stat2[:, 0:f1 - f0], ones_s[:, :], x2t[:, f0:f1],
                        start=(d == 0), stop=(d == 7)), ("x2t", "ones_s"))])
            p.op("scalar", lambda s, f0=f0, f1=f1: s.activation(
                rbx[:, f0:f1], stat2[:, 0:f1 - f0], AF.Copy, scale=1.0 / DM),
                rd=("stat2",), wr=("rbx",))
        p.op("scalar", lambda s: s.activation(rb[:, :], mb[:, :], AF.Square),
             rd=("mb",), wr=("rb",))
        p.op("vector", lambda v: v.tensor_sub(rb[:, :], rbx[:, :], rb[:, :]),
            rd=("rbx", "rb"), wr=("rb",))
        p.op("scalar", lambda s: s.activation(rb[:, :], rb[:, :], AF.Ln,
                                              bias=epsc),
             rd=("rb", epsk), wr=("rb",))
        p.op("scalar", lambda s: s.activation(rb[:, :], rb[:, :], AF.Exp,
                                              scale=-0.5),
             rd=("rb",), wr=("rb",))
        for d in range(8):
            p.op("vector", lambda v, d=d: v.tensor_sub(
                xc[:, d * NT:d * NT + NT], xc[:, d * NT:d * NT + NT], mb[:, :]),
                rd=("xc", "mb"), wr=("xc",))
            p.op("vector", lambda v, d=d, xoff=xoff: v.tensor_mul(
                xn16[:, d * TH + xoff:d * TH + xoff + NT],
                xc[:, d * NT:d * NT + NT], rb[:, :]),
                rd=("xc", "rb"), wr=("xn16",))
        for mt in range(16):
            for sl in range(2):
                f0, f1 = sl * 512, min(NT, 512 + sl * 512)
                psk, psum = ("xm_ps", xm_ps) if sl == 0 else ("xm_ps2", xm_ps2)
                p.mm_group(psk, [
                    (lambda t, kt=kt, f0=f0, f1=f1, mt=mt, psum=psum, xoff=xoff:
                     t.matmul(psum[:, 0:f1 - f0],
                              w_in_s[:, kt * 2 * DMAMBA + mt * 128:
                                     kt * 2 * DMAMBA + mt * 128 + 128],
                              xn16[:, kt * TH + xoff + f0:kt * TH + xoff + f1],
                              start=(kt == 0), stop=(kt == 7)),
                     ("w_in_s", "xn16")) for kt in range(8)])
                p.op("scalar", lambda s, f0=f0, f1=f1, psum=psum, psk=psk: s.copy(
                    xmt[:, f0:f1], psum[:, 0:f1 - f0]),
                    rd=(psk,), wr=("xmt",))
            p.op("vector", lambda v, mt=mt: v.tensor_scalar(
                xmcf[:, :], xmt[:, 0:CH], cw_s[:, mt * 4:mt * 4 + 1], None,
                ALU.mult), rd=("xmt", "cw_s"), wr=("xmcf",))
            for k in range(1, 4):
                p.op("vector", lambda v, mt=mt, k=k: v.scalar_tensor_tensor(
                    xmcf[:, :], xmt[:, k:k + CH], cw_s[:, mt * 4 + k:mt * 4 + k + 1],
                    xmcf[:, :], ALU.mult, ALU.add),
                    rd=("xmt", "cw_s", "xmcf"), wr=("xmcf",))
            p.op("scalar", lambda s, mt=mt: s.activation(
                xmc16[:, mt * CH:(mt + 1) * CH], xmcf[:, :], AF.Silu,
                bias=cb_s[:, mt:mt + 1]),
                rd=("xmcf", "cb_s"), wr=("xmc16",))
            p.op("sync", lambda s, mt=mt, ci=ci: s.dma_start(
                out=(d_xmc if mt < 8 else d_xmcx)[
                    (mt % 8) * 128:(mt % 8) * 128 + 128, ci * CH:(ci + 1) * CH],
                in_=xmc16[:, mt * CH:(mt + 1) * CH]),
                rd=("xmc16",), dma=True, stream="oxmc")
            p.mm_group("xm_ps", [
                (lambda t, kt=kt, mt=mt, xoff=xoff: t.matmul(
                    xm_ps[:, 0:CH],
                    w_in_s[:, kt * 2 * DMAMBA + (16 + mt) * 128:
                           kt * 2 * DMAMBA + (16 + mt) * 128 + 128],
                    xn16[:, kt * TH + xoff + 1:kt * TH + xoff + 1 + CH],
                    start=(kt == 0), stop=(kt == 7)),
                 ("w_in_s", "xn16")) for kt in range(8)])
            p.op("scalar", lambda s: s.activation(sz16[:, :], xm_ps[:, 0:CH],
                                                  AF.Silu),
                 rd=("xm_ps",), wr=("sz16",))
            p.op("sync", lambda s, mt=mt, ci=ci: s.dma_start(
                out=(d_sz if mt < 8 else d_szx)[
                    (mt % 8) * 128:(mt % 8) * 128 + 128, ci * CH:(ci + 1) * CH],
                in_=sz16[:, :]), rd=("sz16",), dma=True, stream="osz")
        p.mm_group("xdbl_ps", [
            (lambda t, mt=mt: t.matmul(
                xdbl_ps[:, :], wx_s[:, mt * 128:(mt + 1) * 128],
                xmc16[:, mt * CH:(mt + 1) * CH],
                start=(mt == 0), stop=(mt == 15)),
             ("wx_s", "xmc16")) for mt in range(16)])
        p.mm_group("xdblC_ps", [
            (lambda t, mt=mt: t.matmul(
                xdblC_ps[:, :], wxc_s[:, mt * 16:(mt + 1) * 16],
                xmc16[:, mt * CH:(mt + 1) * CH],
                start=(mt == 0), stop=(mt == 15)),
             ("wxc_s", "xmc16")) for mt in range(16)])
        p.op("vector", lambda v: v.tensor_copy(xdbl[:, :], xdbl_ps[:, :]),
             rd=("xdbl_ps",), wr=("xdbl",))
        p.op("vector", lambda v, ci=ci: v.tensor_copy(
            BCb[0:16, ci * CH:(ci + 1) * CH], xdbl_ps[64:80, :]),
            rd=("xdbl_ps",), wr=("BCb",))
        p.op("vector", lambda v, ci=ci: v.tensor_copy(
            BCb[0:16, T + ci * CH:T + (ci + 1) * CH], xdblC_ps[0:16, :]),
            rd=("xdblC_ps",), wr=("BCb",))
        for pt in range(16):
            p.mm_group("dt_ps", [
                (lambda t, pt=pt: t.matmul(
                    dt_ps[:, 0:CH], wdt_s[:, pt * 128:(pt + 1) * 128],
                    xdbl[0:64, :], start=True, stop=True),
                 ("wdt_s", "xdbl"))])
            p.op("vector", lambda v, pt=pt: v.tensor_scalar(
                dtt[:, :], dt_ps[:, 0:CH], bdt_s[:, pt:pt + 1], 0.0,
                ALU.add, ALU.max),
                rd=("dt_ps", "bdt_s"), wr=("dtt",))
            p.op("vector", lambda v: v.scalar_tensor_tensor(
                dta[:, :], dtt[:, :], -2.0, dt_ps[:, 0:CH], ALU.mult, ALU.add),
                rd=("dt_ps", "dtt"), wr=("dta",))
            p.op("scalar", lambda s, pt=pt: s.activation(
                dta[:, :], dta[:, :], AF.Exp, bias=bdt_s[:, pt:pt + 1]),
                rd=("dta", "bdt_s"), wr=("dta",))
            p.op("scalar", lambda s: s.activation(dta[:, :], dta[:, :],
                                                  AF.Ln, bias=onec),
                 rd=("dta", onek), wr=("dta",))
            p.op("vector", lambda v: v.tensor_add(dt16[:, :], dtt[:, :], dta[:, :]),
                 rd=("dta", "dtt"), wr=("dt16",))
            p.op("sync", lambda s, pt=pt, ci=ci: s.dma_start(
                out=(d_dt if pt < 8 else d_dtx)[
                    (pt % 8) * 128:(pt % 8) * 128 + 128, ci * CH:(ci + 1) * CH],
                in_=dt16[:, :]), rd=("dt16",), dma=True, stream="odt")
    p.barrier()

    p.op("sync", lambda s: s.dma_start(out=d_bc, in_=BCb[:, :]),
         rd=("BCb",), wr=("d_bc",), dma=True, stream="obc")
    p.op("gpsimd", lambda g: g.collective_compute(
        "AllGather", ALU.bypass, replica_groups=RG_PAIRS,
        ins=[d_bc.opt()], outs=[d_bcg.opt()]),
        rd=("d_bc",), wr=("d_bcg",))
    p.op("gpsimd", lambda g: g.collective_compute(
        "AllGather", ALU.bypass, replica_groups=RG_PAIRS,
        ins=[d_dtx.opt()], outs=[exo_dt.opt()]), wr=("exo_dt",))
    p.op("gpsimd", lambda g: g.collective_compute(
        "AllGather", ALU.bypass, replica_groups=RG_PAIRS,
        ins=[d_xmcx.opt()], outs=[exo_xmc.opt()]), wr=("exo_xmc",))
    p.op("gpsimd", lambda g: g.collective_compute(
        "AllGather", ALU.bypass, replica_groups=RG_PAIRS,
        ins=[d_szx.opt()], outs=[exo_sz.opt()]), wr=("exo_sz",))

    w1t = p.sb("w1t", [128, 8 * FF0], BF16)
    w2t = p.sb("w2t", [128, 32 * DM], BF16)
    ttmp = p.sb("ttmp", [128, 2048], F32)
    tsgn = p.sb("tsgn", [128, 2048], F32)
    ternarize(p, w1T, 8, FF0, w1t, FF0, "w1", ttmp, tsgn)
    ternarize(p, w2T, 32, DM, w2t, DM, "w2", ttmp, tsgn)
    b1_s = p.sb("b1_s", [128, 32], F32)
    b2_s = p.sb("b2_s", [128, 8], F32)
    p.op("sync", lambda s: s.dma_start(out=b1_s[:, :], in_=b1d), wr=("b1_s",),
         dma=True, stream="c0")
    p.op("sync", lambda s: s.dma_start(out=b2_s[:, :], in_=b2d), wr=("b2_s",),
         dma=True, stream="c0")
    C2 = 256
    h_ps = p.ps("h_ps", [128, C2], F32)
    h_ps2 = p.ps("h_ps2", [128, C2], F32)
    h16f = p.sb("h16f", [128, 32 * C2], BF16)
    ou_ps = p.ps("ou_ps", [128, C2], F32)
    ou_ps2 = p.ps("ou_ps2", [128, C2], F32)
    xpc = p.sb("xpc", [128, C2], F32)
    prt = p.sb("prt", [128, C2], F32)
    for ci in range(T // C2):
        xoff = ci * C2 + 1
        for pt in range(32):
            psk, psum = ("h_ps", h_ps) if pt % 2 == 0 else ("h_ps2", h_ps2)
            p.mm_group(psk, [
                (lambda t, kt=kt, pt=pt, psum=psum, xoff=xoff: t.matmul(
                    psum[:, :], w1t[:, kt * FF0 + pt * 128:kt * FF0 + pt * 128 + 128],
                    xn16[:, kt * TH + xoff:kt * TH + xoff + C2],
                    start=(kt == 0), stop=(kt == 7)),
                 ("wtw1", "xn16")) for kt in range(8)])
            p.op("scalar", lambda s, pt=pt, psum=psum, psk=psk: s.activation(
                h16f[:, pt * C2:(pt + 1) * C2], psum[:, :], AF.Gelu,
                bias=b1_s[:, pt:pt + 1]),
                rd=(psk, "b1_s"), wr=("h16f",))
        for po in range(8):
            p.op("sync", lambda s, po=po, xoff=xoff: s.dma_start(
                out=xpc[:, :], in_=xh[po * 128:(po + 1) * 128, xoff:xoff + C2]),
                wr=("xpc",), dma=True, stream="xin")
            psk, psum = ("ou_ps", ou_ps) if po % 2 == 0 else ("ou_ps2", ou_ps2)
            p.mm_group(psk, [
                (lambda t, ft=ft, po=po, psum=psum: t.matmul(
                    psum[:, :], w2t[:, ft * DM + po * 128:ft * DM + po * 128 + 128],
                    h16f[:, ft * C2:(ft + 1) * C2],
                    start=(ft == 0), stop=(ft == 31)),
                 ("wtw2", "h16f")) for ft in range(32)])
            p.op("vector", lambda v, po=po, psum=psum, psk=psk: v.scalar_tensor_tensor(
                prt[:, :], psum[:, :], b2_s[:, po:po + 1], xpc[:, :],
                ALU.add, ALU.add), rd=(psk, "b2_s", "xpc"), wr=("prt",))
            p.op("sync", lambda s, po=po, ci=ci: s.dma_start(
                out=d_part[po * 128:(po + 1) * 128, ci * C2:(ci + 1) * C2],
                in_=prt[:, :]), rd=("prt",), dma=True, stream="opart")
    p.barrier(keep=("d_bcg", "exo_dt", "exo_xmc", "exo_sz"))
    era1.close()

    era2 = ExitStack()
    LH = 2048
    brepP = [p.sb("brepA0", [128, 16 * LH], BF16, scope=era2)]
    crepP = [p.sb("crepA0", [128, 16 * LH], BF16, scope=era2)]
    stg = {}
    for k in ("dt", "xm", "sz"):
        b0 = p.sb(f"tL{k}0", [128, LH], BF16, scope=era2)
        b1 = p.sb(f"tE{k}0", [128, LH], BF16, scope=era2)
        for par in range(2):
            stg[(k, par, 0)] = b0
            stg[(k, par, 1)] = b1
    dtb = p.sb("dtb_e2", [128, LH], BF16, scope=era2)
    xmcb = p.sb("xmcb", [128, LH], BF16, scope=era2)
    szb = p.sb("szb", [128, LH], BF16, scope=era2)
    w16 = p.sb("w16", [128, LH], BF16, scope=era2)
    da_b = p.sb("da_b", [128, LH], BF16, scope=era2)
    bs_b = p.sb("bs_b", [128, LH], BF16, scope=era2)
    st_b = p.sb("st_b", [128, LH], BF16, scope=era2)
    yac = p.sb("yac", [128, LH], BF16, scope=era2)

    srcL = {"dt": d_dt, "xm": d_xmc, "sz": d_sz}
    srcE = {"dt": exo_dt, "xm": exo_xmc, "sz": exo_sz}
    exok = {"dt": "exo_dt", "xm": "exo_xmc", "sz": "exo_sz"}
    dstb = {"dt": (dtb, "dtb"), "xm": (xmcb, "xmcb"), "sz": (szb, "szb")}

    def stage_loads(mtl, lh):
        h, c0h, par = (lh * LH) // T, (lh * LH) % T, mtl % 2
        for k in ("dt", "xm", "sz"):
            bL, bE = stg[(k, par, 0)], stg[(k, par, 1)]
            p.op("sync", lambda s, k=k, mtl=mtl, c0h=c0h, bL=bL: s.dma_start(
                out=bL[:, :],
                in_=srcL[k][mtl * 128:(mtl + 1) * 128, c0h:c0h + LH]),
                wr=(f"tL{k}0",), dma=True, stream=f"iL{k}{par}")
            p.op("sync", lambda s, k=k, mtl=mtl, h=h, c0h=c0h, bE=bE:
                 s.dma_start(
                     out=bE[:, :],
                     in_=srcE[k][h * 1024 + mtl * 128:
                                 h * 1024 + (mtl + 1) * 128, c0h:c0h + LH]),
                 wr=(f"tE{k}0",), rd=(exok[k],), dma=True,
                 stream=f"iE{k}{par}")

    def blend(mtl, lh):
        h, par = (lh * LH) // T, mtl % 2
        cL, cLk = (fc0, "fc0") if h == 0 else (fc1, "fc1")
        cE, cEk = (fc1, "fc1") if h == 0 else (fc0, "fc0")
        for k in ("dt", "xm", "sz"):
            d, dk = dstb[k]
            bL, bE = stg[(k, par, 0)], stg[(k, par, 1)]
            p.op("vector", lambda v, d=d, bL=bL, cL=cL: v.tensor_scalar(
                d[:, :], bL[:, :], cL[:, 0:1], None, ALU.mult),
                rd=(f"tL{k}0", cLk), wr=(dk,))
            p.op("vector", lambda v, d=d, bE=bE, cE=cE: v.scalar_tensor_tensor(
                d[:, :], bE[:, :], cE[:, 0:1], d[:, :], ALU.mult, ALU.add),
                rd=(f"tE{k}0", cEk, dk), wr=(dk,))

    stage_loads(0, 0)
    NLH = SEQ // LH
    for lh in range(NLH):
        h, c0h, lp = (lh * LH) // T, (lh * LH) % T, 0
        brepA, crepA = brepP[lp], crepP[lp]
        bk, ck = f"brepA{lp}", f"crepA{lp}"
        for s in range(DSTATE):
            p.op("gpsimd", lambda g, s=s, h=h, c0h=c0h, brepA=brepA:
                 g.dma_start(
                     out=brepA[:, s * LH:(s + 1) * LH],
                     in_=d_bcg[h * 16 + s:h * 16 + s + 1,
                               c0h:c0h + LH].partition_broadcast(128)),
                 wr=(bk + str(s),), rd=("d_bcg",), dma=True,
                 stream=f"brep{s % 2}")
            p.op("gpsimd", lambda g, s=s, h=h, c0h=c0h, crepA=crepA:
                 g.dma_start(
                     out=crepA[:, s * LH:(s + 1) * LH],
                     in_=d_bcg[h * 16 + s:h * 16 + s + 1,
                               T + c0h:T + c0h + LH].partition_broadcast(128)),
                 wr=(ck + str(s),), rd=("d_bcg",), dma=True,
                 stream=f"crep{s % 2}")
        for mtl in range(8):
            blend(mtl, lh)
            nmtl, nlh = (mtl + 1, lh) if mtl < 7 else (0, lh + 1)
            if nlh < NLH:
                stage_loads(nmtl, nlh)
            p.op("vector", lambda v: v.tensor_mul(w16[:, :], dtb[:, :],
                                                  xmcb[:, :]),
                 rd=("dtb", "xmcb"), wr=("w16",))
            for s in range(DSTATE):
                col = mtl * 16 + s
                p.op("scalar", lambda sc, col=col: sc.activation(
                    da_b[:, :], dtb[:, :], AF.Exp,
                    scale=A_s[:, col:col + 1]),
                    rd=("dtb", "A_s"), wr=("da_b",))
                p.op("vector", lambda v, s=s, brepA=brepA: v.tensor_mul(
                    bs_b[:, :], w16[:, :], brepA[:, s * LH:(s + 1) * LH]),
                    rd=("w16", bk + str(s)), wr=("bs_b",))
                if lh == 0:
                    p.op("vector", lambda v: v.tensor_tensor_scan(
                        st_b[:, :], da_b[:, :], bs_b[:, :], 0.0,
                        ALU.mult, ALU.add),
                        rd=("da_b", "bs_b"), wr=("st_b",))
                else:
                    p.op("vector", lambda v, col=col: v.tensor_tensor_scan(
                        st_b[:, :], da_b[:, :], bs_b[:, :],
                        carry[:, col:col + 1], ALU.mult, ALU.add),
                        rd=("da_b", "bs_b", "carry"), wr=("st_b",))
                if lh < NLH - 1:
                    p.op("scalar", lambda sc, col=col: sc.copy(
                        carry[:, col:col + 1], st_b[:, LH - 1:LH]),
                        rd=("st_b",), wr=("carry",))
                if s == 0:
                    p.op("vector", lambda v, crepA=crepA: v.tensor_mul(
                        yac[:, :], st_b[:, :], crepA[:, 0:LH]),
                        rd=("st_b", ck + "0"), wr=("yac",))
                else:
                    p.op("vector", lambda v, s=s, crepA=crepA: v.tensor_mul(
                        bs_b[:, :], st_b[:, :],
                        crepA[:, s * LH:(s + 1) * LH]),
                        rd=("st_b", ck + str(s)), wr=("bs_b",))
                    p.op("vector", lambda v: v.tensor_add(
                        yac[:, :], yac[:, :], bs_b[:, :]),
                        rd=("bs_b", "yac"), wr=("yac",))
            p.op("vector", lambda v, mtl=mtl: v.scalar_tensor_tensor(
                yac[:, :], xmcb[:, :], D_s[:, mtl:mtl + 1], yac[:, :],
                ALU.mult, ALU.add),
                rd=("xmcb", "D_s", "yac"), wr=("yac",))
            p.op("vector", lambda v: v.tensor_mul(
                st_b[:, :], yac[:, :], szb[:, :]),
                rd=("yac", "szb"), wr=("st_b",))
            p.op("sync", lambda s_, mtl=mtl, lh=lh: s_.dma_start(
                out=d_ygF[mtl * 128:(mtl + 1) * 128, lh * LH:(lh + 1) * LH],
                in_=st_b[:, :]), rd=("st_b",), dma=True,
                stream=f"oyg{mtl % 2}")
    p.barrier()
    era2.close()
    eraBC.close()

    era3 = ExitStack()
    p.op("gpsimd", lambda g: g.collective_compute(
        "AllGather", ALU.bypass, replica_groups=RG_PAIRS,
        ins=[d_ygF.opt()], outs=[d_ygoF.opt()]), wr=("d_ygoF",))
    lrow = p.sb("lrow", [128, 128], F32, scope=era3)
    eyeS = p.sb("eyeS", [128, 128], F32, scope=era3)
    eye16 = p.sb("eye16", [128, 128], BF16, scope=era3)
    ones16 = p.sb("ones16", [128, 128], BF16, scope=era3)
    eraM2 = ExitStack()
    xnE = p.sb("xnE", [128, 8 * TH], BF16, scope=eraM2)
    grow6 = p.sb("grow6", [6, T], F32, scope=eraM2)
    brow6 = p.sb("brow6", [6, T], F32, scope=eraM2)
    gcum6 = p.sb("gcum6", [6, T], F32, scope=eraM2)
    rst6 = p.sb("rst6", [6, T], F32, scope=eraM2)
    p.op("vector", lambda v: v.memset(ones16[:, :], 1.0), wr=("ones16",))
    p.op("sync", lambda s: s.dma_start(out=lrow[:, :], in_=lrow_d),
         wr=("lrow",), dma=True, stream="c0")

    wout_s = p.sb("wout_s", [128, 16 * DM], BF16)
    gw_s = p.sb("gw_s", [128, 8 * 8], F32)
    eb128 = p.sb("eb128", [128, 8], F32)
    ones8 = p.sb("ones8", [128, 8], F32)
    ygs = p.sb("ygs", [128, 16 * 512], BF16)
    prtm = p.sb("prtm", [128, 512], F32)
    out1 = p.sb("out1", [128, 8 * 512], F32)
    ygps = p.ps("ygps", [128, 512], F32)
    ygps2 = p.ps("ygps2", [128, 512], F32)
    statm = p.ps("statm", [128, 512], F32)
    statm2 = p.ps("statm2", [128, 512], F32)
    x2m = p.sb("x2m", [128, 512], F32)
    mbm = p.sb("mbm", [128, 512], F32)
    rbm = p.sb("rbm", [128, 512], F32)
    gpsm = p.ps("gpsm", [8, 512], F32)
    scrm = p.sb("scrm", [8, 512], F32)
    trpsm = p.ps("trpsm", [128, 8], F32)
    trpsm2 = p.ps("trpsm2", [8, 128], F32)
    tok8 = p.sb("tok8", [128, 8], F32)
    t8a = p.sb("t8a", [128, 8], F32)
    t8b = p.sb("t8b", [128, 8], F32)
    t8c = p.sb("t8c", [128, 8], F32)
    m1c = p.sb("m1c", [128, 1], F32)
    mwti = p.sb("mwti", [8, 128], F32)
    hl3 = p.sb("hl3", [128, 24], F32)
    for kt in range(16):
        p.op("gpsimd", lambda g, kt=kt: g.dma_start(
            out=wout_s[:, kt * DM:(kt + 1) * DM],
            in_=wout_d[kt * 128:(kt + 1) * 128, :]), wr=("wout_s",), dma=True)
    for kt in range(8):
        p.op("sync", lambda s, kt=kt: s.dma_start(
            out=gw_s[:, kt * 8:(kt + 1) * 8],
            in_=gwT_d[kt * 128:(kt + 1) * 128, :]), wr=("gw_s",), dma=True,
            stream="c0")
    p.op("sync", lambda s: s.dma_start(
        out=eb128[:, :], in_=ebias_d[0:1, :].partition_broadcast(128)),
        wr=("eb128",), dma=True, stream="c0")
    p.op("sync", lambda s: s.dma_start(out=eyeS[:, :], in_=eye_d),
         wr=("eye",), dma=True, stream="c0")
    p.op("vector", lambda v: v.tensor_copy(eye16[:, :], eyeS[:, :]),
         rd=("eye",), wr=("eye16",))
    p.op("vector", lambda v: v.memset(ones8[:, :], 1.0), wr=("ones8",))

    ygtA = [p.sb(f"ygtA{i}", [128, 512], BF16) for i in range(2)]
    ygtB = [p.sb(f"ygtB{i}", [128, 512], BF16) for i in range(2)]
    for si in range(4):
        f0 = si * 512
        for mt in range(16):
            par = mt % 2
            if mt < 8:
                sA = d_ygF[mt * 128:(mt + 1) * 128, f0:f0 + 512]
                sB = d_ygF[mt * 128:(mt + 1) * 128, T + f0:T + f0 + 512]
                rdk = ()
            else:
                ml = mt - 8
                sA = d_ygoF[1024 + ml * 128:1024 + (ml + 1) * 128, f0:f0 + 512]
                sB = d_ygoF[ml * 128:(ml + 1) * 128, T + f0:T + f0 + 512]
                rdk = ("d_ygoF",)
            p.op("gpsimd", lambda g, sA=sA, par=par: g.dma_start(
                out=ygtA[par][:, :], in_=sA),
                wr=(f"ygtA{par}",), rd=rdk, dma=True, stream=f"yg{par}")
            p.op("gpsimd", lambda g, sB=sB, par=par: g.dma_start(
                out=ygtB[par][:, :], in_=sB),
                wr=(f"ygtB{par}",), rd=rdk, dma=True, stream=f"yh{par}")
            p.op("vector", lambda v, mt=mt, par=par: v.tensor_scalar(
                ygs[:, mt * 512:mt * 512 + 512], ygtA[par][:, :],
                fc0[:, 0:1], None, ALU.mult),
                rd=(f"ygtA{par}", "fc0"), wr=("ygs",))
            p.op("vector", lambda v, mt=mt, par=par: v.scalar_tensor_tensor(
                ygs[:, mt * 512:mt * 512 + 512], ygtB[par][:, :],
                fc1[:, 0:1], ygs[:, mt * 512:mt * 512 + 512],
                ALU.mult, ALU.add),
                rd=(f"ygtB{par}", "fc1", "ygs"), wr=("ygs",))
        for d in range(8):
            p.op("sync", lambda s, d=d, f0=f0: s.dma_start(
                out=prtm[:, :], in_=d_part[d * 128:(d + 1) * 128, f0:f0 + 512]),
                wr=("prtm",), dma=True, stream="ipart")
            psk, psum = ("ygps", ygps) if d % 2 == 0 else ("ygps2", ygps2)
            p.mm_group(psk, [
                (lambda t, mt=mt, d=d, psum=psum: t.matmul(
                    psum[:, :], wout_s[:, mt * DM + d * 128:mt * DM + d * 128 + 128],
                    ygs[:, mt * 512:(mt + 1) * 512],
                    start=(mt == 0), stop=(mt == 15)),
                 ("wout_s", "ygs")) for mt in range(16)])
            p.op("vector", lambda v, d=d, psum=psum, psk=psk: v.tensor_add(
                out1[:, d * 512:d * 512 + 512], psum[:, :], prtm[:, :]),
                rd=(psk, "prtm"), wr=("out1",))
            p.op("sync", lambda s, d=d, f0=f0: s.dma_start(
                out=d_out1[d * 128:(d + 1) * 128, f0:f0 + 512],
                in_=out1[:, d * 512:d * 512 + 512]),
                rd=("out1",), dma=True, stream="oo1")
            if si == 3:
                p.op("vector", lambda v, d=d: v.tensor_copy(
                    hl3[:, d * 3:(d + 1) * 3], out1[:, d * 512 + 509:d * 512 + 512]),
                    rd=("out1",), wr=("hl3",))
        p.mm_group("statm", [
            (lambda t, d=d: t.matmul(
                statm[:, :], ones_s[:, :], out1[:, d * 512:d * 512 + 512],
                start=(d == 0), stop=(d == 7)), ("out1", "ones_s"))
            for d in range(8)])
        for d in range(8):
            p.op("scalar", lambda s, d=d: s.activation(
                x2m[:, :], out1[:, d * 512:d * 512 + 512], AF.Square),
                rd=("out1",), wr=("x2m",))
            p.mm_group("statm2", [
                (lambda t, d=d: t.matmul(
                    statm2[:, :], ones_s[:, :], x2m[:, :],
                    start=(d == 0), stop=(d == 7)), ("x2m", "ones_s"))])
        p.op("scalar", lambda s: s.activation(mbm[:, :], statm[:, :], AF.Copy,
                                              scale=1.0 / DM),
             rd=("statm",), wr=("mbm",))
        p.op("scalar", lambda s: s.activation(rbm[:, :], mbm[:, :], AF.Square),
             rd=("mbm",), wr=("rbm",))
        p.op("vector", lambda v: v.scalar_tensor_tensor(
            rbm[:, :], statm2[:, :], 1.0 / DM, rbm[:, :], ALU.mult, ALU.subtract),
            rd=("statm2", "rbm"), wr=("rbm",))
        p.op("scalar", lambda s: s.activation(rbm[:, :], rbm[:, :], AF.Ln,
                                              bias=epsc),
             rd=("rbm", epsk), wr=("rbm",))
        p.op("scalar", lambda s: s.activation(rbm[:, :], rbm[:, :], AF.Exp,
                                              scale=-0.5),
             rd=("rbm",), wr=("rbm",))
        for d in range(8):
            p.op("vector", lambda v, d=d: v.tensor_sub(
                out1[:, d * 512:d * 512 + 512], out1[:, d * 512:d * 512 + 512],
                mbm[:, :]), rd=("out1", "mbm"), wr=("out1",))
            p.op("vector", lambda v, d=d: v.tensor_mul(
                out1[:, d * 512:d * 512 + 512], out1[:, d * 512:d * 512 + 512],
                rbm[:, :]), rd=("out1", "rbm"), wr=("out1",))
            p.op("vector", lambda v, d=d, f0=f0: v.tensor_copy(
                xnE[:, d * TH + 3 + f0:d * TH + 3 + f0 + 512],
                out1[:, d * 512:d * 512 + 512]),
                rd=("out1",), wr=("xnE",))
            p.op("sync", lambda s, d=d, f0=f0: s.dma_start(
                out=d_xn1[d * 128:(d + 1) * 128, f0:f0 + 512],
                in_=xnE[:, d * TH + 3 + f0:d * TH + 3 + f0 + 512]),
                rd=("xnE",), dma=True, stream="ox1")
        p.mm_group("gpsm", [
            (lambda t, d=d: t.matmul(
                gpsm[:, :], gw_s[:, d * 8:(d + 1) * 8],
                out1[:, d * 512:d * 512 + 512],
                start=(d == 0), stop=(d == 7)), ("out1", "gw_s"))
            for d in range(8)])
        p.op("scalar", lambda s: s.activation(scrm[:, :], gpsm[:, :], AF.Sigmoid),
             rd=("gpsm",), wr=("scrm",))
        for blk in range(4):
            c0 = blk * 128
            p.mm_group("trpsm", [
                (lambda t, c0=c0: t.transpose(
                    trpsm[:, :], scrm[:, c0:c0 + 128], eyeS[0:8, 0:8]),
                 ("scrm", "eye"))])
            p.op("vector", lambda v: v.tensor_copy(tok8[:, :], trpsm[:, :]),
                 rd=("trpsm",), wr=("tok8",))
            p.op("vector", lambda v: v.tensor_add(t8a[:, :], tok8[:, :],
                                                  eb128[:, :]),
                 rd=("tok8", "eb128"), wr=("t8a",))
            p.op("vector", lambda v: v.reduce_max(m1c[:, :], t8a[:, :], AX.X),
                 rd=("t8a",), wr=("m1c",))
            p.op("vector", lambda v: v.tensor_scalar(
                t8b[:, :], t8a[:, :], m1c[:, 0:1], None, ALU.is_ge),
                rd=("t8a", "m1c"), wr=("t8b",))
            p.op("vector", lambda v: v.tensor_tensor_scan(
                t8c[:, :], ones8[:, :], t8b[:, :], 0.0, ALU.mult, ALU.add),
                rd=("ones8", "t8b"), wr=("t8c",))
            p.op("vector", lambda v: v.tensor_scalar(
                t8c[:, :], t8c[:, :], 1.0, None, ALU.is_le),
                rd=("t8c",), wr=("t8c",))
            p.op("vector", lambda v: v.tensor_mul(t8b[:, :], t8b[:, :], t8c[:, :]),
                 rd=("t8b", "t8c"), wr=("t8b",))
            p.op("vector", lambda v: v.scalar_tensor_tensor(
                t8a[:, :], t8b[:, :], -1e30, t8a[:, :], ALU.mult, ALU.add),
                rd=("t8a", "t8b"), wr=("t8a",))
            p.op("vector", lambda v: v.reduce_max(m1c[:, :], t8a[:, :], AX.X),
                 rd=("t8a",), wr=("m1c",))
            p.op("vector", lambda v: v.tensor_scalar(
                t8a[:, :], t8a[:, :], m1c[:, 0:1], None, ALU.is_ge),
                rd=("t8a", "m1c"), wr=("t8a",))
            p.op("vector", lambda v: v.tensor_tensor_scan(
                t8c[:, :], ones8[:, :], t8a[:, :], 0.0, ALU.mult, ALU.add),
                rd=("ones8", "t8a"), wr=("t8c",))
            p.op("vector", lambda v: v.tensor_scalar(
                t8c[:, :], t8c[:, :], 1.0, None, ALU.is_le),
                rd=("t8c",), wr=("t8c",))
            p.op("vector", lambda v: v.tensor_mul(t8a[:, :], t8a[:, :], t8c[:, :]),
                 rd=("t8a", "t8c"), wr=("t8a",))
            p.op("vector", lambda v: v.tensor_add(t8b[:, :], t8b[:, :], t8a[:, :]),
                 rd=("t8a", "t8b"), wr=("t8b",))
            p.op("vector", lambda v: v.tensor_mul(t8b[:, :], t8b[:, :], tok8[:, :]),
                 rd=("t8b", "tok8"), wr=("t8b",))
            p.mm_group("trpsm2", [
                (lambda t: t.transpose(trpsm2[:, :], t8b[:, :], eyeS[:, :]),
                 ("t8b", "eye"))])
            p.op("vector", lambda v: v.tensor_copy(mwti[:, :], trpsm2[:, :]),
                 rd=("trpsm2",), wr=("mwti",))
            p.op("sync", lambda s, f0=f0, c0=c0: s.dma_start(
                out=d_mw[:, f0 + c0:f0 + c0 + 128], in_=mwti[:, :]),
                rd=("mwti",), dma=True, stream="omw")
    p.op("vector", lambda v: v.tensor_scalar(
        hl3[:, :], hl3[:, :], fc0[:, 0:1], None, ALU.mult),
        rd=("hl3", "fc0"), wr=("hl3",))
    p.op("sync", lambda s: s.dma_start(out=cc1i, in_=hl3[:, :]),
         rd=("hl3",), wr=("cc1i",), dma=True, stream="cc")
    p.op("gpsimd", lambda g: g.collective_compute(
        "AllReduce", ALU.add, replica_groups=RG_PAIRS,
        ins=[cc1i.opt()], outs=[cc1o.opt()]),
        rd=("cc1i",), wr=("cc1o",))
    p.op("sync", lambda s: s.dma_start(out=hl3[:, :], in_=cc1o),
         rd=("cc1o",), wr=("hl3",), dma=True, stream="cc")
    p.op("vector", lambda v: v.tensor_scalar(
        hl3[:, :], hl3[:, :], fc1[:, 0:1], None, ALU.mult),
        rd=("hl3", "fc1"), wr=("hl3",))
    p.mm_group("statm", [
        (lambda t, d=d: t.matmul(
            statm[:, 0:3], ones_s[:, :], hl3[:, d * 3:(d + 1) * 3],
            start=(d == 0), stop=(d == 7)), ("hl3", "ones_s"))
        for d in range(8)])
    for d in range(8):
        p.op("scalar", lambda s, d=d: s.activation(
            x2m[:, 0:3], hl3[:, d * 3:(d + 1) * 3], AF.Square),
            rd=("hl3",), wr=("x2m",))
        p.mm_group("statm2", [
            (lambda t, d=d: t.matmul(
                statm2[:, 0:3], ones_s[:, :], x2m[:, 0:3],
                start=(d == 0), stop=(d == 7)), ("x2m", "ones_s"))])
    p.op("scalar", lambda s: s.activation(mbm[:, 0:3], statm[:, 0:3], AF.Copy,
                                          scale=1.0 / DM),
         rd=("statm",), wr=("mbm",))
    p.op("scalar", lambda s: s.activation(rbm[:, 0:3], mbm[:, 0:3], AF.Square),
         rd=("mbm",), wr=("rbm",))
    p.op("vector", lambda v: v.scalar_tensor_tensor(
        rbm[:, 0:3], statm2[:, 0:3], 1.0 / DM, rbm[:, 0:3], ALU.mult,
        ALU.subtract), rd=("statm2", "rbm"), wr=("rbm",))
    p.op("scalar", lambda s: s.activation(rbm[:, 0:3], rbm[:, 0:3], AF.Ln,
                                          bias=epsc),
         rd=("rbm", epsk), wr=("rbm",))
    p.op("scalar", lambda s: s.activation(rbm[:, 0:3], rbm[:, 0:3], AF.Exp,
                                          scale=-0.5),
         rd=("rbm",), wr=("rbm",))
    for d in range(8):
        p.op("vector", lambda v, d=d: v.tensor_sub(
            hl3[:, d * 3:(d + 1) * 3], hl3[:, d * 3:(d + 1) * 3], mbm[:, 0:3]),
            rd=("hl3", "mbm"), wr=("hl3",))
        p.op("vector", lambda v, d=d: v.tensor_mul(
            xnE[:, d * TH:d * TH + 3], hl3[:, d * 3:(d + 1) * 3], rbm[:, 0:3]),
            rd=("hl3", "rbm"), wr=("xnE",))
    p.barrier()

    wqpt = [p.sb(f"wqpt{i}", [128, 8 * 128], BF16) for i in range(2)]
    wba_s16 = p.sb("wba_s16", [128, 8 * 16], BF16)
    p.op("gpsimd", lambda g: g.dma_start(out=wba_s16[:, :], in_=wba_d),
         wr=("wba_s16",), dma=True, stream="wq")
    cwq_s = p.sb("cwq_s", [128, 96], F32)
    negA = p.sb("negA_sb", [6, 1], F32)
    dtb_s = p.sb("dtb_sb", [6, 1], F32)
    xq = p.sb("xq", [128, TH], F32)
    xcs = p.sb("xcs", [128, T], F32)
    sqb = p.sb("sqb", [128, T], BF16)
    rr = p.sb("rr", [128, T], F32)
    qp = p.ps("qp", [128, 512], F32)
    qp2 = p.ps("qp2", [128, 512], F32)
    ssps = p.ps("ssps", [128, 512], F32)
    bps = p.ps("bps", [6, 512], F32)
    bg6 = p.sb("bg6", [6, 512], F32)
    bg6b = p.sb("bg6b", [6, 512], F32)
    SLB = [(0, 3), (3, 512), (515, 512), (1027, 512), (1539, 512)]
    p.op("sync", lambda s: s.dma_start(out=cwq_s[:, :], in_=cwq_d),
         wr=("cwq_s",), dma=True, stream="c0")
    p.op("sync", lambda s: s.dma_start(out=negA[:, :], in_=alog1),
         wr=("negA",), dma=True, stream="c0")
    p.op("sync", lambda s: s.dma_start(out=dtb_s[:, :], in_=dtb_d),
         wr=("dtb_s",), dma=True, stream="c0")
    p.op("scalar", lambda s: s.activation(negA[:, :], negA[:, :], AF.Exp),
         rd=("negA",), wr=("negA",))
    p.op("vector", lambda v: v.tensor_scalar_mul(negA[:, :], negA[:, :], -1.0),
         rd=("negA",), wr=("negA",))

    def load_wqpt(pt, buf):
        p.op("gpsimd", lambda g, pt=pt, buf=buf: g.dma_start(
            out=wqpt[buf].ap().rearrange("p (kt f) -> p kt f", kt=8),
            in_=wqkvg_d[:, pt * 128:(pt + 1) * 128].rearrange(
                "(kt p) f -> p kt f", p=128)),
            wr=(f"wqpt{buf}",), dma=True, stream="wq")

    load_wqpt(0, 0)
    for pt in range(36):
        buf = pt % 2
        if pt < 35:
            load_wqpt(pt + 1, (pt + 1) % 2)
        for si, (f0, w) in enumerate(SLB):
            psk, psum = ("qp", qp) if si % 2 == 0 else ("qp2", qp2)
            p.mm_group(psk, [
                (lambda t, kt=kt, f0=f0, w=w, psum=psum, buf=buf: t.matmul(
                    psum[:, 0:w], wqpt[buf][:, kt * 128:(kt + 1) * 128],
                    xnE[:, kt * TH + f0:kt * TH + f0 + w],
                    start=(kt == 0), stop=(kt == 7)),
                 (f"wqpt{buf}", "xnE")) for kt in range(8)])
            if pt < 24:
                p.op("scalar", lambda s, f0=f0, w=w, psum=psum, psk=psk: s.copy(
                    xq[:, f0:f0 + w], psum[:, 0:w]), rd=(psk,), wr=("xq",))
            else:
                if si > 0:
                    p.op("scalar", lambda s, f0=f0, w=w, psum=psum, psk=psk:
                         s.activation(sqb[:, f0 - 3:f0 - 3 + w], psum[:, 0:w],
                                      AF.Silu),
                         rd=(psk,), wr=("sqb",))
        if pt >= 24:
            p.op("sync", lambda s, pt=pt: s.dma_start(
                out=d_gs[(pt - 24) * 128:(pt - 23) * 128, :], in_=sqb[:, 0:T]),
                rd=("sqb",), dma=True, stream="ogs")
            continue
        p.op("vector", lambda v, pt=pt: v.tensor_scalar(
            xcs[:, :], xq[:, 0:T], cwq_s[:, pt * 4:pt * 4 + 1], None, ALU.mult),
            rd=("xq", "cwq_s"), wr=("xcs",))
        for k in range(1, 4):
            p.op("vector", lambda v, pt=pt, k=k: v.scalar_tensor_tensor(
                xcs[:, :], xq[:, k:k + T], cwq_s[:, pt * 4 + k:pt * 4 + k + 1],
                xcs[:, :], ALU.mult, ALU.add),
                rd=("xq", "cwq_s", "xcs"), wr=("xcs",))
        p.op("scalar", lambda s: s.activation(xcs[:, :], xcs[:, :], AF.Silu),
             rd=("xcs",), wr=("xcs",))
        if pt >= 12:
            p.op("vector", lambda v: v.tensor_copy(sqb[:, 0:T], xcs[:, :]),
                 rd=("xcs",), wr=("sqb",))
            tl = pt - 12
            vdst = d_vloc if tl < 6 else d_vx
            vk = "d_vloc" if tl < 6 else "d_vx"
            p.op("sync", lambda s, vdst=vdst, tl=tl: s.dma_start(
                out=vdst[:, (tl % 6) * T:(tl % 6) * T + T], in_=sqb[:, 0:T]),
                rd=("sqb",), wr=(vk,), dma=True, stream="okt")
            continue
        p.op("scalar", lambda s: s.activation(sqb[:, 0:T], xcs[:, :], AF.Square),
             rd=("xcs",), wr=("sqb",))
        for sl in range(4):
            p.mm_group("ssps", [
                (lambda t, sl=sl: t.matmul(
                    ssps[:, :], ones16[:, :], sqb[:, sl * 512:(sl + 1) * 512],
                    start=True, stop=True), ("sqb", "ones16"))])
            p.op("scalar", lambda s, sl=sl: s.activation(
                rr[:, sl * 512:(sl + 1) * 512], ssps[:, :], AF.Ln,
                bias=eps12c), rd=("ssps", eps12k), wr=("rr",))
        p.op("scalar", lambda s: s.activation(rr[:, :], rr[:, :], AF.Exp,
                                              scale=-0.5),
             rd=("rr",), wr=("rr",))
        sc = QSC if pt < 6 else 1.0
        hd = pt if pt < 6 else pt - 6
        p.op("vector", lambda v, sc=sc: v.scalar_tensor_tensor(
            sqb[:, 0:T], xcs[:, :], sc, rr[:, :],
            ALU.mult, ALU.mult),
            rd=("xcs", "rr"), wr=("sqb",))
        if pt < 6:
            qdst = d_qloc if hd < 3 else d_qx
            qk = "d_qloc" if hd < 3 else "d_qx"
        else:
            qdst = d_kloc if hd < 3 else d_kx
            qk = "d_kloc" if hd < 3 else "d_kx"
        p.op("sync", lambda s, qdst=qdst, hd=hd: s.dma_start(
            out=qdst[:, (hd % 3) * T:(hd % 3) * T + T], in_=sqb[:, 0:T]),
            rd=("sqb",), wr=(qk,), dma=True, stream="okt")
    for si, (f0, w) in enumerate(SLB):
        if si == 0:
            continue
        p.mm_group("bps", [
            (lambda t, kt=kt, f0=f0, w=w: t.matmul(
                bps[:, 0:w], wba_s16[:, kt * 16:kt * 16 + 6],
                xnE[:, kt * TH + f0:kt * TH + f0 + w],
                start=(kt == 0), stop=(kt == 7)),
             ("wba_s16", "xnE")) for kt in range(8)])
        p.op("scalar", lambda s, f0=f0, w=w: s.activation(
            brow6[:, f0 - 3:f0 - 3 + w], bps[0:6, 0:w], AF.Sigmoid),
            rd=("bps",), wr=("brow6",))
        p.mm_group("bps", [
            (lambda t, kt=kt, f0=f0, w=w: t.matmul(
                bps[:, 0:w], wba_s16[:, kt * 16 + 8:kt * 16 + 14],
                xnE[:, kt * TH + f0:kt * TH + f0 + w],
                start=(kt == 0), stop=(kt == 7)),
             ("wba_s16", "xnE")) for kt in range(8)])
        p.op("vector", lambda v, w=w: v.tensor_scalar(
            bg6b[:, 0:w], bps[0:6, 0:w], dtb_s[0:6, 0:1], 0.0,
            ALU.add, ALU.max),
            rd=("bps", "dtb_s"), wr=("bg6b",))
        p.op("vector", lambda v, w=w: v.scalar_tensor_tensor(
            bg6[:, 0:w], bg6b[:, 0:w], -2.0, bps[0:6, 0:w], ALU.mult, ALU.add),
            rd=("bps", "bg6b"), wr=("bg6",))
        p.op("scalar", lambda s, w=w: s.activation(
            bg6[:, 0:w], bg6[:, 0:w], AF.Exp, bias=dtb_s[0:6, 0:1]),
            rd=("bg6", "dtb_s"), wr=("bg6",))
        p.op("scalar", lambda s, w=w: s.activation(
            bg6[:, 0:w], bg6[:, 0:w], AF.Ln, bias=onec[0:6, 0:1]),
            rd=("bg6", onek), wr=("bg6",))
        p.op("vector", lambda v, w=w: v.tensor_add(bg6[:, 0:w], bg6[:, 0:w],
                                                   bg6b[:, 0:w]),
             rd=("bg6", "bg6b"), wr=("bg6",))
        p.op("vector", lambda v, f0=f0, w=w: v.tensor_scalar(
            grow6[:, f0 - 3:f0 - 3 + w], bg6[:, 0:w], negA[0:6, 0:1], None,
            ALU.mult), rd=("bg6", "negA"), wr=("grow6",))
    p.op("vector", lambda v: v.memset(rst6[:, :], 1.0), wr=("rst6",))
    p.op("vector", lambda v: v.memset(
        bass.AP(rst6, 0, [[T, 6], [128, 16]]), 0.0), rd=("rst6",), wr=("rst6",))
    p.op("vector", lambda v: v.tensor_tensor_scan(
        gcum6[:, :], rst6[:, :], grow6[:, :], 0.0, ALU.mult, ALU.add),
        rd=("rst6", "grow6"), wr=("gcum6",))
    p.op("sync", lambda s: s.dma_start(out=d_bgx[0:3, :], in_=gcum6[3:6, :]),
         rd=("gcum6",), wr=("d_bgx",), dma=True, stream="obg")
    p.op("sync", lambda s: s.dma_start(out=d_bgx[3:6, :], in_=brow6[3:6, :]),
         rd=("brow6",), wr=("d_bgx",), dma=True, stream="obg")
    p.op("gpsimd", lambda g: g.collective_compute(
        "AllGather", ALU.bypass, replica_groups=RG_PAIRS,
        ins=[d_bgx.opt()], outs=[exo_bg.opt()]),
        rd=("d_bgx",), wr=("exo_bg",))
    p.op("gpsimd", lambda g: g.collective_compute(
        "AllGather", ALU.bypass, replica_groups=RG_PAIRS,
        ins=[d_qx.opt()], outs=[exo_q.opt()]),
        rd=("d_qx",), wr=("exo_q",))
    p.op("gpsimd", lambda g: g.collective_compute(
        "AllGather", ALU.bypass, replica_groups=RG_PAIRS,
        ins=[d_kx.opt()], outs=[exo_k.opt()]),
        rd=("d_kx",), wr=("exo_k",))
    p.op("gpsimd", lambda g: g.collective_compute(
        "AllGather", ALU.bypass, replica_groups=RG_PAIRS,
        ins=[d_vx.opt()], outs=[exo_v.opt()]),
        rd=("d_vx",), wr=("exo_v",))
    p.barrier(keep=("exo_q", "exo_k", "exo_v", "exo_bg"))

    era_qkv = ExitStack()
    qF = p.sb("qF", [128, 3 * SEQ], BF16, scope=era_qkv)
    kF = p.sb("kF", [128, 3 * SEQ], BF16, scope=era_qkv)
    gcolB = p.sb("gcolB", [128, 96], F32, scope=era_qkv)
    bcolB = p.sb("bcolB", [128, 96], F32, scope=era_qkv)
    gcumF = p.sb("gcumF", [3, SEQ], F32)
    browF = p.sb("browF", [3, SEQ], F32)
    stq = [p.sb(f"stq{i}", [128, T], BF16) for i in range(2)]
    stk = [p.sb(f"stk{i}", [128, T], BF16) for i in range(2)]
    stgg = p.sb("stgg", [3, T], F32)
    stgb = p.sb("stgb", [3, T], F32)
    tp_ps3 = p.ps("tp_ps3", [128, 8], F32)
    for g in range(2):
        cL, cLk = (fc0, "fc0") if g == 0 else (fc1, "fc1")
        cE, cEk = (fc1, "fc1") if g == 0 else (fc0, "fc0")
        p.op("sync", lambda s, g=g: s.dma_start(
            out=stgg[:, :], in_=exo_bg[g * 6:g * 6 + 3, :]),
            wr=("stgg",), rd=("exo_bg",), dma=True, stream="ibg")
        p.op("sync", lambda s, g=g: s.dma_start(
            out=stgb[:, :], in_=exo_bg[g * 6 + 3:g * 6 + 6, :]),
            wr=("stgb",), rd=("exo_bg",), dma=True, stream="ibg")
        p.op("vector", lambda v, g=g, cL=cL: v.tensor_scalar(
            gcumF[0:3, g * T:(g + 1) * T], gcum6[0:3, :], cL[0:3, 0:1],
            None, ALU.mult), rd=("gcum6", cLk), wr=("gcumF",))
        p.op("vector", lambda v, g=g, cE=cE: v.scalar_tensor_tensor(
            gcumF[0:3, g * T:(g + 1) * T], stgg[0:3, :], cE[0:3, 0:1],
            gcumF[0:3, g * T:(g + 1) * T], ALU.mult, ALU.add),
            rd=("stgg", cEk, "gcumF"), wr=("gcumF",))
        p.op("vector", lambda v, g=g, cL=cL: v.tensor_scalar(
            browF[0:3, g * T:(g + 1) * T], brow6[0:3, :], cL[0:3, 0:1],
            None, ALU.mult), rd=("brow6", cLk), wr=("browF",))
        p.op("vector", lambda v, g=g, cE=cE: v.scalar_tensor_tensor(
            browF[0:3, g * T:(g + 1) * T], stgb[0:3, :], cE[0:3, 0:1],
            browF[0:3, g * T:(g + 1) * T], ALU.mult, ALU.add),
            rd=("stgb", cEk, "browF"), wr=("browF",))
        for i in range(3):
            par = i % 2
            p.op("sync", lambda s, g=g, i=i, par=par: s.dma_start(
                out=stq[par][:, :],
                in_=exo_q[g * 128:(g + 1) * 128, i * T:(i + 1) * T]),
                wr=(f"stq{par}",), rd=("exo_q",), dma=True, stream=f"iq{par}")
            p.op("sync", lambda s, g=g, i=i, par=par: s.dma_start(
                out=stk[par][:, :],
                in_=exo_k[g * 128:(g + 1) * 128, i * T:(i + 1) * T]),
                wr=(f"stk{par}",), rd=("exo_k",), dma=True, stream=f"ik{par}")
            p.op("gpsimd", lambda gg, g=g, i=i, par=par: gg.dma_start(
                out=qF[:, i * SEQ + g * T:i * SEQ + (g + 1) * T],
                in_=d_qloc[:, i * T:(i + 1) * T]),
                wr=("qF",), dma=True, stream=f"jq{par}")
            p.op("gpsimd", lambda gg, g=g, i=i, par=par: gg.dma_start(
                out=kF[:, i * SEQ + g * T:i * SEQ + (g + 1) * T],
                in_=d_kloc[:, i * T:(i + 1) * T]),
                wr=("kF",), dma=True, stream=f"jk{par}")
            p.op("vector", lambda v, g=g, i=i, cL=cL: v.tensor_scalar(
                qF[:, i * SEQ + g * T:i * SEQ + (g + 1) * T],
                qF[:, i * SEQ + g * T:i * SEQ + (g + 1) * T], cL[:, 0:1],
                None, ALU.mult), rd=("qF", cLk), wr=("qF",))
            p.op("vector", lambda v, g=g, i=i, par=par, cE=cE:
                 v.scalar_tensor_tensor(
                     qF[:, i * SEQ + g * T:i * SEQ + (g + 1) * T],
                     stq[par][:, :], cE[:, 0:1],
                     qF[:, i * SEQ + g * T:i * SEQ + (g + 1) * T],
                     ALU.mult, ALU.add),
                 rd=(f"stq{par}", cEk, "qF"), wr=("qF",))
            p.op("vector", lambda v, g=g, i=i, cL=cL: v.tensor_scalar(
                kF[:, i * SEQ + g * T:i * SEQ + (g + 1) * T],
                kF[:, i * SEQ + g * T:i * SEQ + (g + 1) * T], cL[:, 0:1],
                None, ALU.mult), rd=("kF", cLk), wr=("kF",))
            p.op("vector", lambda v, g=g, i=i, par=par, cE=cE:
                 v.scalar_tensor_tensor(
                     kF[:, i * SEQ + g * T:i * SEQ + (g + 1) * T],
                     stk[par][:, :], cE[:, 0:1],
                     kF[:, i * SEQ + g * T:i * SEQ + (g + 1) * T],
                     ALU.mult, ALU.add),
                 rd=(f"stk{par}", cEk, "kF"), wr=("kF",))
    for blk in range(32):
        p.mm_group("tp_ps3", [
            (lambda t, blk=blk: t.transpose(
                tp_ps3[:, 0:3], gcumF[0:3, blk * 128:(blk + 1) * 128],
                eyeS[0:3, 0:3]), ("gcumF", "eye"))])
        p.op("vector", lambda v, blk=blk: v.tensor_copy(
            gcolB[:, blk * 3:(blk + 1) * 3], tp_ps3[:, 0:3]),
            rd=("tp_ps3",), wr=("gcolB",))
        p.mm_group("tp_ps3", [
            (lambda t, blk=blk: t.transpose(
                tp_ps3[:, 0:3], browF[0:3, blk * 128:(blk + 1) * 128],
                eyeS[0:3, 0:3]), ("browF", "eye"))])
        p.op("vector", lambda v, blk=blk: v.tensor_copy(
            bcolB[:, blk * 3:(blk + 1) * 3], tp_ps3[:, 0:3]),
            rd=("tp_ps3",), wr=("bcolB",))
    p.barrier()

    masksF = {}
    for nm, dd_ in [("bdl", bdl_d), ("bdlT", bdlT_d), ("sbl", sbl_d),
                    ("sblT", sblT_d), ("trui", trui_d)]:
        masksF[nm] = p.sb("m3" + nm, [128, 128], F32)
        p.op("sync", lambda s, nm=nm, dd_=dd_: s.dma_start(
            out=masksF[nm][:, :], in_=dd_), wr=(nm,), dma=True, stream="msk")
    masksF["eye"] = eyeS
    vslP = [p.sb(f"vsl{i}", [128, DV], BF16) for i in range(2)]
    vseP = [p.sb(f"vse{i}", [128, DV], BF16) for i in range(2)]
    PSA = [p.ps(f"psA{j}", [128, DV], F32) for j in range(2)]
    PSC = [p.ps(f"psC{j}", [128, 128], F32) for j in range(2)]
    PSD = [p.ps(f"psD{j}", [128, 128], F32) for j in range(2)]
    psER = p.ps("psER", [128, 130], F32)
    psT = p.ps("psT", [128, DV], BF16)

    BS = []
    for _hi in range(3):
        _d = {}
        _d["S0"] = p.sb(f"S0_{_hi}", [128, DV], F32)
        _d["S016"] = p.sb(f"S016_{_hi}", [128, DV], BF16)
        _d["ktc"] = p.sb(f"ktc_{_hi}", [128, DK], BF16)
        _d["vtc"] = p.sb(f"vtc_{_hi}", [128, DV], BF16)
        _d["vfm"] = p.sb(f"vfm_{_hi}", [128, DV], BF16)
        _d["bcol"] = p.sb(f"bcol_{_hi}", [128, 1], F32)
        _d["lamc"] = p.sb(f"lamc_{_hi}", [128, 1], F32)
        _d["lamCc"] = p.sb(f"lamCc_{_hi}", [128, 1], F32)
        _d["nbl"] = p.sb(f"nbl_{_hi}", [128, 1], F32)
        _d["ecol"] = p.sb(f"ecol_{_hi}", [128, 1], F32)
        _d["negel"] = p.sb(f"negel_{_hi}", [128, 1], F32)
        _d["growr"] = p.sb(f"growr_{_hi}", [1, 128], F32)
        _d["browr"] = p.sb(f"browr_{_hi}", [1, 128], F32)
        _d["dec"] = p.sb(f"dec_{_hi}", [128, 128], F32)
        _d["dec2"] = p.sb(f"dec2_{_hi}", [128, 128], F32)
        _d["dd"] = p.sb(f"dd_{_hi}", [128, 128], F32)
        _d["ddt"] = p.sb(f"ddt_{_hi}", [128, 128], F32)
        _d["aqt"] = p.sb(f"aqt_{_hi}", [128, 128], BF16)
        _d["Dm"] = p.sb(f"Dm_{_hi}", [128, 128], BF16)
        _d["DmT"] = p.sb(f"DmT_{_hi}", [128, 128], BF16)
        _d["Nm"] = p.sb(f"Nm_{_hi}", [128, 128], BF16)
        _d["NmT"] = p.sb(f"NmT_{_hi}", [128, 128], BF16)
        _d["Wm"] = p.sb(f"Wm_{_hi}", [128, 128], BF16)
        _d["WmT"] = p.sb(f"WmT_{_hi}", [128, 128], BF16)
        _d["Zm"] = p.sb(f"Zm_{_hi}", [128, 128], BF16)
        _d["ZmT"] = p.sb(f"ZmT_{_hi}", [128, 128], BF16)
        _d["Z2"] = p.sb(f"Z2_{_hi}", [128, 128], BF16)
        _d["Z2T"] = p.sb(f"Z2T_{_hi}", [128, 128], BF16)
        _d["Z4T"] = p.sb(f"Z4T_{_hi}", [128, 128], BF16)
        _d["tmpA"] = p.sb(f"tmpA_{_hi}", [128, 128], BF16)
        _d["tmpB"] = p.sb(f"tmpB_{_hi}", [128, 128], BF16)
        _d["U"] = p.sb(f"U_{_hi}", [128, DV], BF16)
        _d["Uf"] = p.sb(f"Uf_{_hi}", [128, DV], F32)
        _d["rc16"] = p.sb(f"rc16_{_hi}", [128, DV], BF16)
        _d["ktil"] = p.sb(f"ktil_{_hi}", [128, DK], BF16)
        _d["qlam"] = p.sb(f"qlam_{_hi}", [128, 128], BF16)
        _d["ot"] = p.sb(f"ot_{_hi}", [128, DV], BF16)
        BS.append(_d)
    def mm1(psk, psum, stat, statk, mov, movk):
        p.mm_group(psk, [
            (lambda t, psum=psum, stat=stat, mov=mov: t.matmul(
                psum, stat, mov, start=True, stop=True), (statk, movk))])


    def emit_init(i):
        sx = str(i)
        bs = BS[i]
        S0 = bs["S0"]
        S016 = bs["S016"]
        p.op("vector", lambda v: v.memset(S0[:, :], 0.0), wr=(("S0"+sx),))
        p.op("vector", lambda v: v.tensor_copy(S016[:, :], S0[:, :]),
             rd=(("S0"+sx),), wr=(("S016"+sx),))

    def chain_gen(i):
        sx = str(i)
        bs = BS[i]
        j2 = i % 2
        psA = PSA[j2]
        psC = PSC[j2]
        psD = PSD[j2]
        kpsA = "ps" + "A" + str(j2)
        kpsC = "ps" + "C" + str(j2)
        kpsD = "ps" + "D" + str(j2)
        S0 = bs["S0"]
        S016 = bs["S016"]
        ktc = bs["ktc"]
        vtc = bs["vtc"]
        vfm = bs["vfm"]
        bcol = bs["bcol"]
        lamc = bs["lamc"]
        lamCc = bs["lamCc"]
        nbl = bs["nbl"]
        ecol = bs["ecol"]
        negel = bs["negel"]
        growr = bs["growr"]
        browr = bs["browr"]
        dec = bs["dec"]
        dec2 = bs["dec2"]
        dd = bs["dd"]
        ddt = bs["ddt"]
        aqt = bs["aqt"]
        Dm = bs["Dm"]
        DmT = bs["DmT"]
        Nm = bs["Nm"]
        NmT = bs["NmT"]
        Wm = bs["Wm"]
        WmT = bs["WmT"]
        Zm = bs["Zm"]
        ZmT = bs["ZmT"]
        Z2 = bs["Z2"]
        Z2T = bs["Z2T"]
        Z4T = bs["Z4T"]
        tmpA = bs["tmpA"]
        tmpB = bs["tmpB"]
        U = bs["U"]
        Uf = bs["Uf"]
        rc16 = bs["rc16"]
        ktil = bs["ktil"]
        qlam = bs["qlam"]
        ot = bs["ot"]
        for ci in range(SEQ // 128):
            c0 = ci * 128
            cix = ci * 3 + i
            g = ci // 16
            rel = (ci % 16) * 128
            par = (ci * 3 + i) % 2
            cL, cLk = (fc0, "fc0") if g == 0 else (fc1, "fc1")
            cE, cEk = (fc1, "fc1") if g == 0 else (fc0, "fc0")
            vsl, vse = vslP[par], vseP[par]
            for j in range(2):
                p.op("sync", lambda s, i=i, j=j, rel=rel, vsl=vsl: s.dma_start(
                    out=vsl[:, j * 128:(j + 1) * 128],
                    in_=d_vloc[:, (2 * i + j) * T + rel:
                               (2 * i + j) * T + rel + 128]),
                    wr=(f"vsl{par}",), dma=True, stream=f"ld2{par}")
                p.op("sync", lambda s, i=i, j=j, g=g, rel=rel, vse=vse:
                     s.dma_start(
                         out=vse[:, j * 128:(j + 1) * 128],
                         in_=exo_v[g * 128:(g + 1) * 128,
                                   (2 * i + j) * T + rel:
                                   (2 * i + j) * T + rel + 128]),
                     wr=(f"vse{par}",), rd=("exo_v",), dma=True,
                     stream=f"ld3{par}")
            p.op("vector", lambda v, vsl=vsl, cL=cL: v.tensor_scalar(
                vfm[:, :], vsl[:, :], cL[:, 0:1], None, ALU.mult),
                rd=(f"vsl{par}", cLk), wr=(("vfm"+sx),))
            p.op("vector", lambda v, vse=vse, cE=cE: v.scalar_tensor_tensor(
                vfm[:, :], vse[:, :], cE[:, 0:1], vfm[:, :],
                ALU.mult, ALU.add),
                rd=(f"vse{par}", cEk, ("vfm"+sx)), wr=(("vfm"+sx),))
            p.mm_group("psT", [
                (lambda t, j=j: t.transpose(
                    psT[:, j * 128:(j + 1) * 128],
                    vfm[:, j * 128:(j + 1) * 128], eye16[:, :]),
                 (("vfm"+sx), "eye16")) for j in range(2)])
            p.op("scalar", lambda s: s.copy(vtc[:, :], psT[:, :]),
                 rd=("psT",), wr=(("vtc"+sx),))
            yield
            p.mm_group("psT", [
                (lambda t, i=i, c0=c0: t.transpose(
                    psT[:, 0:128], kF[:, i * SEQ + c0:i * SEQ + c0 + 128],
                    eye16[:, :]), ("kF", "eye16"))])
            p.op("scalar", lambda s: s.copy(ktc[:, :], psT[:, 0:128]),
                 rd=("psT",), wr=(("ktc"+sx),))
            yield
            p.op("scalar", lambda _s, cix=cix: _s.copy(
                bcol[:, :], bcolB[:, cix:cix + 1]), rd=("bcolB",), wr=(("bcol"+sx),))
            p.op("scalar", lambda s, cix=cix: s.activation(
                lamc[:, :], gcolB[:, cix:cix + 1], AF.Exp),
                rd=("gcolB",), wr=(("lamc"+sx),))
            mm1("psER", psER[:, 128:129], lrow[:, :], "lrow", lamc[:, :], ("lamc"+sx))
            p.op("scalar", lambda _s: _s.copy(lamCc[:, :], psER[:, 128:129]),
                 rd=("psER",), wr=(("lamCc"+sx),))
            p.op("scalar", lambda s, cix=cix: s.activation(
                negel[:, :], gcolB[:, cix:cix + 1], AF.Exp, scale=-1.0),
                rd=("gcolB",), wr=(("negel"+sx),))
            p.op("vector", lambda v: v.tensor_mul(ecol[:, :], negel[:, :],
                                                  lamCc[:, :]),
                 rd=(("negel"+sx), ("lamCc"+sx)), wr=(("ecol"+sx),))
            p.op("vector", lambda v: v.tensor_mul(nbl[:, :], bcol[:, :],
                                                  lamc[:, :]),
                 rd=(("bcol"+sx), ("lamc"+sx)), wr=(("nbl"+sx),))
            p.op("vector", lambda v: v.tensor_scalar_mul(nbl[:, :], nbl[:, :],
                                                         -1.0),
                 rd=(("nbl"+sx),), wr=(("nbl"+sx),))
            mm1("psER", psER[0:1, 0:128], gcolB[:, cix:cix + 1], "gcolB",
                eyeS[:, :], "eye")
            p.op("scalar", lambda _s: _s.copy(growr[:, :], psER[0:1, 0:128]),
                 rd=("psER",), wr=(("growr"+sx),))
            mm1("psER", psER[0:1, 0:128], bcolB[:, cix:cix + 1], "bcolB",
                eyeS[:, :], "eye")
            p.op("scalar", lambda _s: _s.copy(browr[:, :], psER[0:1, 0:128]),
                 rd=("psER",), wr=(("browr"+sx),))
            mm1(kpsC, psC[:, :], ones_s[0:1, :], "ones_s",
                growr[:, :], ("growr"+sx))
            p.op("vector", lambda v, cix=cix: v.tensor_scalar(
                dec[:, :], psC[:, :], gcolB[:, cix:cix + 1], 0.0,
                ALU.subtract, ALU.min), rd=(kpsC, "gcolB"), wr=(("dec"+sx),))
            p.op("scalar", lambda s: s.activation(dec[:, :], dec[:, :], AF.Exp),
                 rd=(("dec"+sx),), wr=(("dec"+sx),))
            p.op("vector", lambda v, cix=cix: v.tensor_scalar(
                dec2[:, :], psC[:, :], gcolB[:, cix:cix + 1], None,
                ALU.subtract), rd=(kpsC, "gcolB"), wr=(("dec2"+sx),))
            p.op("vector", lambda v: v.tensor_scalar(
                dec2[:, :], dec2[:, :], -1.0, 0.0, ALU.mult, ALU.min),
                rd=(("dec2"+sx),), wr=(("dec2"+sx),))
            p.op("scalar", lambda s: s.activation(dec2[:, :], dec2[:, :],
                                                  AF.Exp),
                 rd=(("dec2"+sx),), wr=(("dec2"+sx),))
            mm1(kpsD, psD[:, :], ones_s[0:1, :], "ones_s",
                browr[:, :], ("browr"+sx))
            yield
            p.mm_group(kpsC, [
                (lambda t, i=i, c0=c0: t.matmul(
                    psC[:, :], kF[:, i * SEQ + c0:i * SEQ + c0 + 128],
                    kF[:, i * SEQ + c0:i * SEQ + c0 + 128],
                    start=True, stop=True), ("kF",))])
            p.op("vector", lambda v: v.tensor_mul(dd[:, :], psC[:, :],
                                                  dec2[:, :]),
                 rd=(kpsC, ("dec2"+sx)), wr=(("dd"+sx),))
            p.op("vector", lambda v: v.tensor_scalar(
                dd[:, :], dd[:, :], bcol[:, 0:1], None, ALU.mult),
                rd=(("dd"+sx), ("bcol"+sx)), wr=(("dd"+sx),))
            p.op("vector", lambda v: v.tensor_mul(ddt[:, :], psC[:, :],
                                                  dec[:, :]),
                 rd=(kpsC, ("dec"+sx)), wr=(("ddt"+sx),))
            p.op("vector", lambda v: v.tensor_mul(ddt[:, :], ddt[:, :],
                                                  psD[:, :]),
                 rd=(("ddt"+sx), kpsD), wr=(("ddt"+sx),))
            for dst, dstk, srcn, mk in [
                    (Dm, ("Dm"+sx), ("dd"+sx), "bdl"), (Nm, ("Nm"+sx), ("dd"+sx), "sbl"),
                    (DmT, ("DmT"+sx), ("ddt"+sx), "bdlT"), (NmT, ("NmT"+sx), ("ddt"+sx), "sblT")]:
                srcb = dd if srcn == ("dd"+sx) else ddt
                p.op("vector", lambda v, dst=dst, srcb=srcb, mk=mk: v.tensor_mul(
                    dst[:, :], srcb[:, :], masksF[mk][:, :]),
                    rd=(srcn, mk), wr=(dstk,))
            p.mm_group(kpsC, [
                (lambda t, i=i, c0=c0: t.matmul(
                    psC[:, :], kF[:, i * SEQ + c0:i * SEQ + c0 + 128],
                    qF[:, i * SEQ + c0:i * SEQ + c0 + 128],
                    start=True, stop=True), ("kF", "qF"))])
            p.op("vector", lambda v: v.tensor_mul(aqt[:, :], psC[:, :],
                                                  dec[:, :]),
                 rd=(kpsC, ("dec"+sx)), wr=(("aqt"+sx),))
            p.op("vector", lambda v: v.tensor_mul(
                aqt[:, :], aqt[:, :], masksF["trui"][:, :]),
                rd=(("aqt"+sx), "trui"), wr=(("aqt"+sx),))
            yield
            p.mm_group(kpsA, [
                (lambda t, i=i, c0=c0: t.matmul(
                    psA[:, :], kF[:, i * SEQ + c0:i * SEQ + c0 + 128],
                    S016[:, :], start=True, stop=True), ("kF", ("S016"+sx)))])
            p.op("vector", lambda v: v.tensor_scalar(
                Uf[:, :], vtc[:, :], bcol[:, 0:1], None, ALU.mult),
                rd=(("vtc"+sx), ("bcol"+sx)), wr=(("Uf"+sx),))
            p.op("vector", lambda v: v.scalar_tensor_tensor(
                Uf[:, :], psA[:, :], nbl[:, 0:1], Uf[:, :],
                ALU.mult, ALU.add),
                rd=(kpsA, ("nbl"+sx), ("Uf"+sx)), wr=(("Uf"+sx),))
            p.op("vector", lambda v: v.tensor_copy(rc16[:, :], Uf[:, :]),
                 rd=(("Uf"+sx),), wr=(("rc16"+sx),))
            yield
            p.op("vector", lambda v: v.tensor_sub(Wm[:, :],
                                                  masksF["eye"][:, :], Dm[:, :]),
                 rd=("eye", ("Dm"+sx)), wr=(("Wm"+sx),))
            p.op("vector", lambda v: v.tensor_sub(WmT[:, :],
                                                  masksF["eye"][:, :], DmT[:, :]),
                 rd=("eye", ("DmT"+sx)), wr=(("WmT"+sx),))
            pk, pkT, pkk, pkTk = Dm, DmT, ("Dm"+sx), ("DmT"+sx)
            for lev in range(3):
                yield
                mm1(kpsC, psC[:, :], pkT[:, :], pkTk, pk[:, :], pkk)
                mm1(kpsD, psD[:, :], pk[:, :], pkk, pkT[:, :], pkTk)
                p.op("vector", lambda v: v.tensor_add(
                    tmpA[:, :], psC[:, :], masksF["eye"][:, :]),
                    rd=(kpsC, "eye"), wr=(("tmpA"+sx),))
                p.op("vector", lambda v: v.tensor_add(
                    tmpB[:, :], psD[:, :], masksF["eye"][:, :]),
                    rd=(kpsD, "eye"), wr=(("tmpB"+sx),))
                if lev < 2:
                    p.op("scalar", lambda _s: _s.copy(Z2[:, :], psC[:, :]),
                         rd=(kpsC,), wr=(("Z2"+sx),)) if lev == 0 else \
                    p.op("scalar", lambda _s: _s.copy(Z4T[:, :], psC[:, :]),
                         rd=(kpsC,), wr=(("Z4T"+sx),))
                    p.op("vector", lambda v: v.tensor_copy(Z2T[:, :], psD[:, :]),
                         rd=(kpsD,), wr=(("Z2T"+sx),)) if lev == 0 else \
                    p.op("vector", lambda v: v.tensor_copy(ZmT[:, :], psD[:, :]),
                         rd=(kpsD,), wr=(("ZmT"+sx),))
                mm1(kpsC, psC[:, :], WmT[:, :], ("WmT"+sx), tmpA[:, :], ("tmpA"+sx))
                mm1(kpsD, psD[:, :], tmpA[:, :], ("tmpA"+sx), WmT[:, :], ("WmT"+sx))
                p.op("scalar", lambda _s: _s.copy(Wm[:, :], psC[:, :]),
                     rd=(kpsC,), wr=(("Wm"+sx),))
                p.op("vector", lambda v: v.tensor_copy(WmT[:, :], psD[:, :]),
                     rd=(kpsD,), wr=(("WmT"+sx),))
                if lev == 0:
                    pk, pkT, pkk, pkTk = Z2, Z2T, ("Z2"+sx), ("Z2T"+sx)
                elif lev == 1:
                    pk, pkT, pkk, pkTk = Z4T, ZmT, ("Z4T"+sx), ("ZmT"+sx)
            yield
            mm1(kpsC, psC[:, :], WmT[:, :], ("WmT"+sx), Nm[:, :], ("Nm"+sx))
            mm1(kpsD, psD[:, :], Nm[:, :], ("Nm"+sx), WmT[:, :], ("WmT"+sx))
            p.op("scalar", lambda _s: _s.copy(Zm[:, :], psC[:, :]),
                 rd=(kpsC,), wr=(("Zm"+sx),))
            p.op("vector", lambda v: v.tensor_copy(ZmT[:, :], psD[:, :]),
                 rd=(kpsD,), wr=(("ZmT"+sx),))
            mm1(kpsC, psC[:, :], ZmT[:, :], ("ZmT"+sx), Zm[:, :], ("Zm"+sx))
            mm1(kpsD, psD[:, :], Zm[:, :], ("Zm"+sx), ZmT[:, :], ("ZmT"+sx))
            p.op("scalar", lambda _s: _s.copy(Z2[:, :], psC[:, :]),
                 rd=(kpsC,), wr=(("Z2"+sx),))
            p.op("vector", lambda v: v.tensor_copy(Z2T[:, :], psD[:, :]),
                 rd=(kpsD,), wr=(("Z2T"+sx),))
            mm1(kpsC, psC[:, :], Z2[:, :], ("Z2"+sx), Z2T[:, :], ("Z2T"+sx))
            p.op("scalar", lambda _s: _s.copy(Z4T[:, :], psC[:, :]),
                 rd=(kpsC,), wr=(("Z4T"+sx),))
            yield
            mm1(kpsA, psA[:, :], WmT[:, :], ("WmT"+sx), rc16[:, :], ("rc16"+sx))
            p.op("scalar", lambda _s: _s.copy(Uf[:, :], psA[:, :]),
                 rd=(kpsA,), wr=(("Uf"+sx),))
            p.op("vector", lambda v: v.tensor_copy(U[:, :], Uf[:, :]),
                 rd=(("Uf"+sx),), wr=(("U"+sx),))
            for statb, statk, sign in [(Z4T, ("Z4T"+sx), 1), (Z2T, ("Z2T"+sx), 1),
                                       (ZmT, ("ZmT"+sx), -1)]:
                mm1(kpsA, psA[:, :], statb[:, :], statk, U[:, :], ("U"+sx))
                if sign > 0:
                    p.op("vector", lambda v: v.tensor_add(
                        Uf[:, :], Uf[:, :], psA[:, :]),
                        rd=(kpsA, ("Uf"+sx)), wr=(("Uf"+sx),))
                else:
                    p.op("vector", lambda v: v.tensor_sub(
                        Uf[:, :], Uf[:, :], psA[:, :]),
                        rd=(kpsA, ("Uf"+sx)), wr=(("Uf"+sx),))
                p.op("vector", lambda v: v.tensor_copy(U[:, :], Uf[:, :]),
                     rd=(("Uf"+sx),), wr=(("U"+sx),))
            yield
            mm1(kpsC, psC[:, :], ones_s[0:1, :], "ones_s",
                growr[:, :], ("growr"+sx))
            p.op("scalar", lambda s: s.activation(dec[:, :], psC[:, :],
                                                  AF.Exp),
                 rd=(kpsC,), wr=(("dec"+sx),))
            p.op("vector", lambda v, i=i, c0=c0: v.tensor_mul(
                qlam[:, :], qF[:, i * SEQ + c0:i * SEQ + c0 + 128], dec[:, :]),
                rd=("qF", ("dec"+sx)), wr=(("qlam"+sx),))
            p.mm_group(kpsA, [
                (lambda t: t.matmul(psA[:, :], aqt[:, :], U[:, :],
                                    start=True, stop=False),
                 (("aqt"+sx), ("U"+sx))),
                (lambda t: t.matmul(psA[:, :], qlam[:, :], S016[:, :],
                                    start=False, stop=True),
                 (("qlam"+sx), ("S016"+sx)))])
            p.op("scalar", lambda s: s.copy(ot[:, :], psA[:, :]),
                 rd=(kpsA,), wr=(("ot"+sx),))
            p.op("sync", lambda s, i=i, c0=c0: s.dma_start(
                out=d_oF[i * SEQ + c0:i * SEQ + c0 + 128, :], in_=ot[:, :]),
                rd=(("ot"+sx),), dma=True, stream="oo")
            yield
            p.op("vector", lambda v: v.tensor_scalar(
                ktil[:, :], ktc[:, :], ecol[:, 0:1], None, ALU.mult),
                rd=(("ktc"+sx), ("ecol"+sx)), wr=(("ktil"+sx),))
            mm1(kpsA, psA[:, :], ktil[:, :], ("ktil"+sx), U[:, :], ("U"+sx))
            p.op("vector", lambda v: v.scalar_tensor_tensor(
                S0[:, :], S0[:, :], lamCc[:, 0:1], psA[:, :],
                ALU.mult, ALU.add),
                rd=(kpsA, ("lamCc"+sx), ("S0"+sx)), wr=(("S0"+sx),))
            p.op("scalar", lambda _s: _s.copy(S016[:, :], S0[:, :]),
                 rd=(("S0"+sx),), wr=(("S016"+sx),))

    for _hi in range(3):
        emit_init(_hi)
    _gens = [chain_gen(_hi) for _hi in range(3)]
    _alive = list(_gens)
    while _alive:
        for _g in list(_alive):
            try:
                next(_g)
            except StopIteration:
                _alive.remove(_g)
    p.barrier()
    era_qkv.close()
    eraM2.close()
    era3.close()

    era4 = ExitStack()
    res1 = p.sb("res1", [128, 8 * T], F32, scope=era4)
    xn1b = p.sb("xn1b", [128, 8 * T], BF16, scope=era4)
    for d in range(8):
        p.op("sync", lambda s, d=d: s.dma_start(
            out=res1[:, d * T:(d + 1) * T], in_=d_out1[d * 128:(d + 1) * 128, :]),
            wr=("res1",), dma=True, stream="xin2")
        p.op("sync", lambda s, d=d: s.dma_start(
            out=xn1b[:, d * T:(d + 1) * T], in_=d_xn1[d * 128:(d + 1) * 128, :]),
            wr=("xn1b",), dma=True, stream="xin")
    wo_s = p.sb("wo_s", [128, 12 * DM], BF16)
    nw_s = p.sb("nw_s", [128, 2], F32)
    ov16 = p.sb("ov16", [128, 2 * T], BF16)
    gs16b = p.sb("gs16b", [128, 2 * T], BF16)
    an16 = p.sb("an16", [128, 2 * T], BF16)
    ovf = p.sb("ovf", [128, 2 * T], F32)
    sqf = p.sb("sqf", [128, T], F32)
    rrb = p.sb("rrb", [128, T], F32)
    ssps4 = p.ps("ssps4", [128, 512], F32)
    aps = p.ps("aps", [128, 512], F32)
    aps2 = p.ps("aps2", [128, 512], F32)
    ovA = p.sb("ovA", [128, T], BF16)
    ovB = p.sb("ovB", [128, T], BF16)
    p.op("gpsimd", lambda g: g.collective_compute(
        "AllGather", ALU.bypass, replica_groups=RG_PAIRS,
        ins=[d_oF.opt()], outs=[exo_o.opt()]), wr=("exo_o",))
    for kt in range(12):
        p.op("gpsimd", lambda g, kt=kt: g.dma_start(
            out=wo_s[:, kt * DM:(kt + 1) * DM],
            in_=wo_dr[kt * 128:(kt + 1) * 128, :]), wr=("wo_s",), dma=True)
    p.op("sync", lambda s: s.dma_start(out=nw_s[:, :], in_=nw_dr),
         wr=("nw_s",), dma=True, stream="xin")
    for h in range(H):
        for j in range(2):
            if h < 3:
                sA = d_oF[h * SEQ:h * SEQ + T, j * 128:(j + 1) * 128]
                sB = d_oF[h * SEQ + T:h * SEQ + 2 * T, j * 128:(j + 1) * 128]
                rdk = ()
            else:
                sA = exo_o[3 * SEQ + (h - 3) * SEQ:
                           3 * SEQ + (h - 3) * SEQ + T, j * 128:(j + 1) * 128]
                sB = exo_o[(h - 3) * SEQ + T:(h - 3) * SEQ + 2 * T,
                           j * 128:(j + 1) * 128]
                rdk = ("exo_o",)
            p.op("sync", lambda s, sA=sA: s.dma_start(
                out=ovA[:, :], in_=sA, transpose=True),
                wr=("ovA",), rd=rdk, dma=True, stream="oin")
            p.op("sync", lambda s, sB=sB: s.dma_start(
                out=ovB[:, :], in_=sB, transpose=True),
                wr=("ovB",), rd=rdk, dma=True, stream="oin2")
            p.op("vector", lambda v, j=j: v.tensor_scalar(
                ov16[:, j * T:(j + 1) * T], ovA[:, :], fc0[:, 0:1],
                None, ALU.mult), rd=("ovA", "fc0"), wr=("ov16",))
            p.op("vector", lambda v, j=j: v.scalar_tensor_tensor(
                ov16[:, j * T:(j + 1) * T], ovB[:, :], fc1[:, 0:1],
                ov16[:, j * T:(j + 1) * T], ALU.mult, ALU.add),
                rd=("ovB", "fc1", "ov16"), wr=("ov16",))
            p.op("sync", lambda s, h=h, j=j: s.dma_start(
                out=gs16b[:, j * T:(j + 1) * T],
                in_=d_gs[(2 * h + j) * 128:(2 * h + j + 1) * 128, :]),
                wr=("gs16b",), dma=True, stream="gin")
        for sl in range(4):
            f0 = sl * 512
            for j in range(2):
                p.op("scalar", lambda s, j=j, f0=f0: s.activation(
                    sqf[:, f0:f0 + 512], ov16[:, j * T + f0:j * T + f0 + 512],
                    AF.Square), rd=("ov16",), wr=("sqf",))
                p.mm_group("ssps4", [
                    (lambda t, f0=f0, j=j: t.matmul(
                        ssps4[:, :], ones_s[:, :], sqf[:, f0:f0 + 512],
                        start=(j == 0), stop=(j == 1)), ("sqf", "ones_s"))])
            p.op("scalar", lambda s, f0=f0: s.activation(
                rrb[:, f0:f0 + 512], ssps4[:, :], AF.Ln, bias=epsc,
                scale=1.0 / DV), rd=("ssps4", epsk), wr=("rrb",))
        p.op("scalar", lambda s: s.activation(rrb[:, :], rrb[:, :], AF.Exp,
                                              scale=-0.5),
             rd=("rrb",), wr=("rrb",))
        for j in range(2):
            p.op("vector", lambda v, j=j: v.scalar_tensor_tensor(
                ovf[:, j * T:(j + 1) * T], ov16[:, j * T:(j + 1) * T],
                nw_s[:, j:j + 1], rrb[:, :], ALU.mult, ALU.mult),
                rd=("ov16", "nw_s", "rrb"), wr=("ovf",))
            p.op("vector", lambda v, j=j: v.tensor_mul(
                an16[:, j * T:(j + 1) * T], ovf[:, j * T:(j + 1) * T],
                gs16b[:, j * T:(j + 1) * T]),
                rd=("ovf", "gs16b"), wr=("an16",))
        for sl in range(4):
            f0 = sl * 512
            for d in range(8):
                psk, psum = ("aps", aps) if d % 2 == 0 else ("aps2", aps2)
                p.mm_group(psk, [
                    (lambda t, d=d, f0=f0, h=h, j=j, psum=psum: t.matmul(
                        psum[:, :],
                        wo_s[:, (2 * h + j) * DM + d * 128:
                             (2 * h + j) * DM + d * 128 + 128],
                        an16[:, j * T + f0:j * T + f0 + 512],
                        start=(j == 0), stop=(j == 1)), ("wo_s", "an16"))
                    for j in range(2)])
                p.op("vector", lambda v, d=d, f0=f0, psum=psum, psk=psk:
                     v.tensor_add(
                         res1[:, d * T + f0:d * T + f0 + 512],
                         res1[:, d * T + f0:d * T + f0 + 512], psum[:, :]),
                     rd=(psk, "res1"), wr=("res1",))
    p.barrier()

    CS = 512
    wupP = [p.sb(f"wup_s{i}", [128, 8 * HID], BF16) for i in range(2)]
    wdnP = [p.sb(f"wdn_s{i}", [128, 11 * DM], BF16) for i in range(2)]
    webs = p.sb("webs", [128, CS], BF16)
    xe16 = p.sb("xe16", [128, 8 * CS], BF16)
    hps = p.ps("hps", [128, CS], F32)
    hps2 = p.ps("hps2", [128, CS], F32)
    dps = p.ps("dps", [128, CS], F32)
    dps2 = p.ps("dps2", [128, CS], F32)
    h16m = p.sb("h16m", [128, 11 * CS], BF16)
    hrl = p.sb("hrl", [128, CS], BF16)
    for e in range(9):
        eb = e % 2
        wup_s, wdn_s = wupP[eb], wdnP[eb]
        kup, kdn = f"wup_s{eb}", f"wdn_s{eb}"
        for kt in range(8):
            p.op("gpsimd", lambda g, e=e, kt=kt, wup_s=wup_s: g.dma_start(
                out=wup_s[:, kt * HID:(kt + 1) * HID],
                in_=wup_dr[e * DM + kt * 128:e * DM + kt * 128 + 128, :]),
                wr=(kup,), dma=True, stream=f"wup{eb}{kt % 2}")
        for kt in range(11):
            p.op("gpsimd", lambda g, e=e, kt=kt, wdn_s=wdn_s: g.dma_start(
                out=wdn_s[:, kt * DM:(kt + 1) * DM],
                in_=wdn_dr[e * HID + kt * 128:e * HID + kt * 128 + 128, :]),
                wr=(kdn,), dma=True, stream=f"wdn{eb}{kt % 2}")
        for sl in range(T // CS):
            f0 = sl * CS
            if e < 8:
                p.op("gpsimd", lambda g, e=e, f0=f0: g.dma_start(
                    out=webs[:, 0:CS],
                    in_=d_mw[e:e + 1, f0:f0 + CS].partition_broadcast(128)),
                    wr=("webs",), dma=True, stream="web")
                for d in range(8):
                    p.op("vector", lambda v, d=d, f0=f0: v.tensor_mul(
                        xe16[:, d * CS:(d + 1) * CS],
                        xn1b[:, d * T + f0:d * T + f0 + CS], webs[:, 0:CS]),
                        rd=("xn1b", "webs"), wr=("xe16",))
            for pt in range(11):
                psk, psum = ("hps", hps) if pt % 2 == 0 else ("hps2", hps2)
                if e < 8:
                    p.mm_group(psk, [
                        (lambda t, kt=kt, pt=pt, psum=psum, wup_s=wup_s: t.matmul(
                            psum[:, :],
                            wup_s[:, kt * HID + pt * 128:kt * HID + pt * 128 + 128],
                            xe16[:, kt * CS:(kt + 1) * CS],
                            start=(kt == 0), stop=(kt == 7)),
                         (kup, "xe16")) for kt in range(8)])
                else:
                    p.mm_group(psk, [
                        (lambda t, kt=kt, pt=pt, psum=psum, f0=f0, wup_s=wup_s: t.matmul(
                            psum[:, :],
                            wup_s[:, kt * HID + pt * 128:kt * HID + pt * 128 + 128],
                            xn1b[:, kt * T + f0:kt * T + f0 + CS],
                            start=(kt == 0), stop=(kt == 7)),
                         (kup, "xn1b")) for kt in range(8)])
                p.op("vector", lambda v, psum=psum, psk=psk: v.tensor_scalar_max(
                    hrl[:, :], psum[:, :], 0.0), rd=(psk,), wr=("hrl",))
                p.op("scalar", lambda s, pt=pt: s.activation(
                    h16m[:, pt * CS:(pt + 1) * CS], hrl[:, :], AF.Square),
                    rd=("hrl",), wr=("h16m",))
            for d in range(8):
                psk, psum = ("dps", dps) if d % 2 == 0 else ("dps2", dps2)
                p.mm_group(psk, [
                    (lambda t, kt=kt, d=d, psum=psum, wdn_s=wdn_s: t.matmul(
                        psum[:, :],
                        wdn_s[:, kt * DM + d * 128:kt * DM + d * 128 + 128],
                        h16m[:, kt * CS:(kt + 1) * CS],
                        start=(kt == 0), stop=(kt == 10)),
                     (kdn, "h16m")) for kt in range(11)])
                p.op("vector", lambda v, d=d, f0=f0, psum=psum, psk=psk:
                     v.tensor_add(
                         res1[:, d * T + f0:d * T + f0 + CS],
                         res1[:, d * T + f0:d * T + f0 + CS], psum[:, :]),
                     rd=(psk, "res1"), wr=("res1",))
    for d in range(8):
        p.op("sync", lambda s, d=d: s.dma_start(
            out=o_fin[d * 128:(d + 1) * 128, :], in_=res1[:, d * T:(d + 1) * T]),
            rd=("res1",), dma=True, stream="fin")
    p.barrier()
    era4.close()
    return p


def host_fused(inputs):
    x = np.asarray(inputs["L0_attn_x"], dtype=np.float32)
    wx = np.asarray(inputs["L0_attn_W_x"], np.float32)
    wxa = np.zeros((DMAMBA, 128), np.float32)
    wxa[:, 0:64] = wx[:, 0:64]
    wxa[:, 64:80] = wx[:, 64:80]
    wq = np.asarray(inputs["L1_attn_Wq"], np.float32)
    wk = np.asarray(inputs["L1_attn_Wk"], np.float32)
    wv = np.asarray(inputs["L1_attn_Wv"], np.float32)
    wg = np.asarray(inputs["L1_attn_Wg"], np.float32)
    wb = np.asarray(inputs["L1_attn_Wb"], np.float32)
    wa = np.asarray(inputs["L1_attn_Wa"], np.float32)
    wba = np.zeros((DM, 16), np.float32)
    wba[:, 0:6] = wb
    wba[:, 8:14] = wa
    cq = np.asarray(inputs["L1_attn_conv_q"], np.float32)
    ck = np.asarray(inputs["L1_attn_conv_k"], np.float32)
    cv = np.asarray(inputs["L1_attn_conv_v"], np.float32)
    cwq = np.concatenate([cq.reshape(6, 128, 4), ck.reshape(6, 128, 4),
                          cv.reshape(12, 128, 4)], axis=0)
    wup = np.asarray(inputs["L1_ffn_w_up"], np.float32)
    wdn = np.asarray(inputs["L1_ffn_w_down"], np.float32)
    sup = np.asarray(inputs["L1_ffn_shared_w_up"], np.float32)
    sdn = np.asarray(inputs["L1_ffn_shared_w_down"], np.float32)
    tril_s = np.tril(np.ones((128, 128), np.float32), -1)
    blockm = np.zeros((128, 128), np.float32)
    for r in range(8):
        blockm[r * 16:(r + 1) * 16, r * 16:(r + 1) * 16] = 1
    bdl = tril_s * blockm
    sbl = tril_s - bdl
    shared = {
        "w_in": np.ascontiguousarray(np.asarray(inputs["L0_attn_W_in"], np.float32)),
        "cwT": np.ascontiguousarray(np.asarray(inputs["L0_attn_conv_w"], np.float32)[:, 0, :].T.reshape(16, 128, 4).transpose(1, 0, 2).reshape(128, 64)),
        "cb": np.ascontiguousarray(np.asarray(inputs["L0_attn_conv_b"], np.float32).reshape(16, 128).T),
        "w_xa": wxa,
        "w_xc": np.ascontiguousarray(wx[:, 80:96]),
        "w_dt": np.ascontiguousarray(np.asarray(inputs["L0_attn_W_dt"], np.float32)),
        "b_dt": np.ascontiguousarray(np.asarray(inputs["L0_attn_b_dt"], np.float32).reshape(16, 128).T),
        "w1T": np.ascontiguousarray(np.asarray(inputs["L0_ffn_W1"], np.float32).T),
        "b1": np.ascontiguousarray(np.asarray(inputs["L0_ffn_b1"], np.float32).reshape(32, 128).T),
        "w2T": np.ascontiguousarray(np.asarray(inputs["L0_ffn_W2"], np.float32).T),
        "b2": np.ascontiguousarray(np.asarray(inputs["L0_ffn_b2"], np.float32).reshape(8, 128).T),
        "alog0": np.ascontiguousarray(np.asarray(inputs["L0_attn_A_log"], np.float32).reshape(16, 128, 16).transpose(1, 0, 2).reshape(128, 256)),
        "dvec": np.ascontiguousarray(np.asarray(inputs["L0_attn_D"], np.float32).reshape(16, 128).T),
        "wout": np.ascontiguousarray(np.asarray(inputs["L0_attn_W_out"], np.float32)),
        "wqkvg": np.ascontiguousarray(np.concatenate([wq, wk, wv, wg], axis=1)),
        "wba": np.ascontiguousarray(wba.reshape(8, 128, 16).transpose(1, 0, 2).reshape(128, 128)),
        "gwT": np.ascontiguousarray(np.asarray(inputs["L1_ffn_gate_w"], np.float32).T),
        "ebias": np.asarray(inputs["L1_ffn_expert_bias"], np.float32).reshape(1, 8),
        "cwq": np.ascontiguousarray(cwq.transpose(1, 0, 2).reshape(128, 96)),
        "alog1": np.asarray(inputs["L1_attn_A_log"], np.float32).reshape(6, 1),
        "dtb": np.asarray(inputs["L1_attn_dt_bias"], np.float32).reshape(6, 1),
        "nw": np.ascontiguousarray(np.asarray(inputs["L1_attn_norm_w"], np.float32).reshape(2, 128).T),
        "wo": np.ascontiguousarray(np.asarray(inputs["L1_attn_Wo"], np.float32)),
        "wup": np.ascontiguousarray(np.concatenate(
            [wup.transpose(0, 2, 1).reshape(8 * DM, HID), sup.T], axis=0)),
        "wdn": np.ascontiguousarray(np.concatenate(
            [wdn.transpose(0, 2, 1).reshape(8 * HID, DM), sdn.T], axis=0)),
        "bdl": bdl, "bdlT": np.ascontiguousarray(bdl.T),
        "sbl": sbl, "sblT": np.ascontiguousarray(sbl.T),
        "trui": np.triu(np.ones((128, 128), np.float32), 0),
        "eye": np.eye(128, dtype=np.float32),
        "lrowd": np.concatenate([np.zeros((127, 128), np.float32),
                                 np.ones((1, 128), np.float32)], axis=0),
    }
    mperm = np.r_[8:16, 0:8]

    def pc(a, tile):
        g = a.reshape(a.shape[0], 16, tile)
        return np.ascontiguousarray(g[:, mperm, :].reshape(a.shape[0], -1))

    def pr(a, tile):
        g = a.reshape(16, tile, a.shape[1])
        return np.ascontiguousarray(g[mperm].reshape(-1, a.shape[1]))

    swap = dict(shared)
    w_in_b = shared["w_in"]
    swap["w_in"] = np.ascontiguousarray(np.concatenate(
        [pc(w_in_b[:, :2 * DMAMBA // 2], 128),
         pc(w_in_b[:, 2 * DMAMBA // 2:], 128)], axis=1))
    swap["cwT"] = pc(shared["cwT"], 4)
    swap["cb"] = pc(shared["cb"], 1)
    swap["w_xa"] = pr(shared["w_xa"], 128)
    swap["w_xc"] = pr(shared["w_xc"], 128)
    swap["w_dt"] = pc(shared["w_dt"], 128)
    swap["b_dt"] = pc(shared["b_dt"], 1)
    swap["alog0"] = pc(shared["alog0"], 16)
    swap["dvec"] = pc(shared["dvec"], 1)
    swap["wout"] = pr(shared["wout"], 128)

    hp = [3, 4, 5, 0, 1, 2]
    wq2 = wq.reshape(DM, 6, 128)[:, hp].reshape(DM, 768)
    wk2 = wk.reshape(DM, 6, 128)[:, hp].reshape(DM, 768)
    wv2 = wv.reshape(DM, 6, 256)[:, hp].reshape(DM, 1536)
    wg2 = wg.reshape(DM, 6, 256)[:, hp].reshape(DM, 1536)
    swap["wqkvg"] = np.ascontiguousarray(
        np.concatenate([wq2, wk2, wv2, wg2], axis=1))
    wba2 = np.zeros((DM, 16), np.float32)
    wba2[:, 0:6] = wb[:, hp]
    wba2[:, 8:14] = wa[:, hp]
    swap["wba"] = np.ascontiguousarray(
        wba2.reshape(8, 128, 16).transpose(1, 0, 2).reshape(128, 128))
    cperm = hp + [6 + h for h in hp] + [12 + 2 * h + j for h in hp
                                        for j in range(2)]
    swap["cwq"] = np.ascontiguousarray(
        cwq[cperm].transpose(1, 0, 2).reshape(128, 96))
    swap["alog1"] = np.ascontiguousarray(shared["alog1"][hp])
    swap["dtb"] = np.ascontiguousarray(shared["dtb"][hp])
    swap["wo"] = np.ascontiguousarray(
        shared["wo"].reshape(6, 256, DM)[hp].reshape(VD, DM))

    maps = []
    for c in range(8):
        b, hh = c // 2, c % 2
        s0 = hh * T
        xhm = np.zeros((T + 3, DM), dtype=np.float32)
        lo, hi = s0 - 1, s0 + T + 2
        clo, chi = max(lo, 0), min(hi, SEQ)
        xhm[clo - lo:chi - lo] = x[b, clo:chi]
        m = dict(shared if hh == 0 else swap)
        m["xh"] = np.ascontiguousarray(xhm.T)
        m["fc1"] = np.full((128, 1), float(hh), np.float32)
        m["fc0"] = np.full((128, 1), 1.0 - float(hh), np.float32)
        maps.append(m)
    return maps


def kernel_fused(**inputs):
    inputs = {k: np.asarray(v) for k, v in inputs.items()}
    maps = host_fused(inputs)
    p = build_fused()
    nc = p.realize()
    res = run_bass_kernel_spmd(nc, maps, core_ids=list(range(8)))
    out = np.empty((B, SEQ, DM), np.float32)
    for c in range(8):
        b, hh = c // 2, c % 2
        out[b, hh * T:(hh + 1) * T, :] = res.results[c]["o_fin"].T
    return out


def _run(p, maps):
    nc = p.realize()
    return run_bass_kernel_spmd(nc, maps, core_ids=list(range(8)))


def kernel(**inputs):
    return kernel_fused(**inputs)


def kernel_multi(**inputs):
    inputs = {k: np.asarray(v) for k, v in inputs.items()}

    resA = _run(build_l0pre(), host_l0pre(inputs))

    def fm(res, key, rows):
        out = np.empty((rows, B, SEQ), np.float32)
        for c in range(8):
            b, h = c // 2, c % 2
            out[:, b, h * T:(h + 1) * T] = res.results[c][key]
        return out

    dt_full = fm(resA, "o_dt", DMAMBA)
    xmc_full = fm(resA, "o_xmc", DMAMBA)
    sz_full = fm(resA, "o_sz", DMAMBA)
    bc = fm(resA, "o_bc", 32)
    Bc_full = np.ascontiguousarray(bc[0:16].transpose(1, 2, 0))
    Cc_full = np.ascontiguousarray(bc[16:32].transpose(1, 2, 0))
    part_full = fm(resA, "o_part", DM)

    resB = _run(build_l0scan(),
                host_l0scan(dt_full, xmc_full, sz_full, Bc_full, Cc_full, inputs))
    yg_full = np.concatenate([resB.results[c]["o_yg"] for c in range(8)],
                             axis=0).reshape(DMAMBA, B, SEQ)

    resD = _run(build_mid(), host_mid(inputs, yg_full, part_full))
    q_all = fm(resD, "o_q", KD).transpose(1, 0, 2)
    k_all = fm(resD, "o_k", KD).transpose(1, 0, 2)
    v_all = fm(resD, "o_v", VD).transpose(1, 0, 2)
    beta_all = fm(resD, "o_beta", 6).transpose(1, 0, 2)
    g_all = fm(resD, "o_g", 6).transpose(1, 0, 2)

    resE = _run(build_l1delta(),
                host_l1delta(np.ascontiguousarray(q_all),
                             np.ascontiguousarray(k_all),
                             np.ascontiguousarray(v_all),
                             np.ascontiguousarray(beta_all),
                             np.ascontiguousarray(g_all)))
    o_fm = np.empty((B, VD, SEQ), np.float32)
    for c in range(8):
        for i in range(3):
            gidx = 3 * c + i
            b, h = gidx // H, gidx % H
            o_fm[b, h * DV:(h + 1) * DV, :] = \
                resE.results[c]["o_o"][i * SEQ:(i + 1) * SEQ, :].T

    gs_c = [resD.results[c]["o_gs"] for c in range(8)]
    out1_c = [resD.results[c]["o_out1"] for c in range(8)]
    xn1_c = [resD.results[c]["o_xn1"] for c in range(8)]
    mw_c = [resD.results[c]["o_mw"] for c in range(8)]
    resF = _run(build_l1post(),
                host_l1post(inputs, o_fm, gs_c, out1_c, xn1_c, mw_c))

    out = np.empty((B, SEQ, DM), np.float32)
    for c in range(8):
        b, h = c // 2, c % 2
        out[b, h * T:(h + 1) * T, :] = resF.results[c]["o_fin"].T
    return out


# version 25 (final)
# speedup vs baseline: 1.0320x; 1.0031x over previous
import numpy as np
from contextlib import ExitStack

import concourse.bass as bass
import concourse.mybir as mybir
from concourse.bass_utils import run_bass_kernel_spmd

F32 = mybir.dt.float32
BF16 = mybir.dt.bfloat16
AF = mybir.ActivationFunctionType
ALU = mybir.AluOpType
AX = mybir.AxisListType

B, SEQ, DM = 4, 4096, 1024
T = 2048
DMAMBA, DSTATE, DTRANK = 2048, 16, 64
FF0 = 4096
H, DK, DV = 6, 128, 256
KD, VD = H * DK, H * DV
NE, HID = 8, 1408
MAGIC = 12582912.0

ENGINES = ("sync", "scalar", "vector", "tensor", "gpsimd")


class Prog:

    def __init__(self, nc):
        self.nc = nc
        self.es = ExitStack()
        self.pes = ExitStack()
        self.sems = {}
        self.cnt = {}
        self.cur = []
        self.wr = {}
        self.rd = {}
        self.waited = {e: {} for e in ENGINES}
        self.floor = {}
        self.pend = {}
        self.nops = 0
        self.consts = {}
        self.psum_keys = set()
        self._cpool = self.es.enter_context(nc.sbuf_tensor("constpool", [128, 8], F32))

    def constcol(self, val):
        if val not in self.consts:
            i = len(self.consts)
            assert i < 8
            key = f"__const{i}"
            self.op("vector", lambda v, i=i, val=val: v.memset(
                self._cpool[:, i:i + 1], val), wr=(key,))
            self.consts[val] = (self._cpool[:, i:i + 1], key)
        return self.consts[val]

    def _sem(self, name):
        if name not in self.sems:
            self.sems[name] = self.es.enter_context(self.nc.semaphore(name))
            self.cnt[name] = 0
        return self.sems[name]

    def sb(self, name, shape, dtype, scope="phase"):
        if isinstance(scope, ExitStack):
            es = scope
        else:
            es = self.pes if scope == "phase" else self.es
        return es.enter_context(self.nc.sbuf_tensor(name, list(shape), dtype))

    def ps(self, name, shape, dtype=F32, scope="phase"):
        if isinstance(scope, ExitStack):
            es = scope
        else:
            es = self.pes if scope == "phase" else self.es
        self.psum_keys.add(name)
        return es.enter_context(self.nc.psum_tensor(name, list(shape), dtype))

    def _emit_phase(self):
        if not self.cur:
            return
        per = {}
        for rec in self.cur:
            per.setdefault(rec[0], []).append(rec)
        sems = self.sems
        with self.nc.Block() as block:
            for eng, ops in per.items():
                def mk(ops=ops):
                    def body(engobj):
                        for (_, fn, wl, tok, sem) in ops:
                            for (s, v) in wl:
                                engobj.wait_ge(sems[s], v)
                            if fn is None:
                                continue
                            inst = fn(engobj)
                            if tok is not None:
                                amt = 16 if sem.startswith("dma_") else 1
                                inst.then_inc(sems[sem], amt)
                    return body
                getattr(block, eng)(mk())
        self.cur = []

    def barrier(self, keep=()):
        assert not self.pend, f"open psum groups at barrier: {list(self.pend)}"
        drains = {s: v for s, v in self.cnt.items()
                  if s.startswith("dma_") and v > self.floor.get(s, 0)}
        if drains:
            self._record("sync", None, drains, None, "sync")
        self._emit_phase()
        self.pes.close()
        self.pes = ExitStack()
        for e in ENGINES:
            self.waited[e] = {}
        self.floor = dict(self.cnt)
        saved = {k: self.wr[k] for k in keep if k in self.wr}
        for k, (s, v) in saved.items():
            self.floor[s] = min(self.floor.get(s, 0), v - 1)
        self.wr = dict(saved)
        self.rd = {}

    def _need(self, waits, tok):
        if tok is None:
            return
        s, v = tok
        if v <= self.floor.get(s, 0):
            return
        if waits.get(s, 0) < v:
            waits[s] = v

    def _record(self, eng, fn, waits, tok, sem):
        wl = []
        for s, v in waits.items():
            if self.waited[eng].get(s, 0) < v:
                wl.append((s, v))
                self.waited[eng][s] = v
        self.cur.append((eng, fn, wl, tok, sem))
        self.nops += 1

    def op(self, eng, fn, rd=(), wr=(), dma=False, stream=None):
        assert eng in ENGINES
        sem = ("dma_" + (stream or eng)) if dma else eng
        self._sem(sem)
        waits = {}
        if dma:
            self._need(waits, (sem, self.cnt[sem]))
        for k in rd:
            assert k not in self.pend, f"read of open psum group {k}"
            self._need(waits, self.wr.get(k))
            if k in self.psum_keys:
                for t in self.rd.get(k, ()):
                    self._need(waits, t)
        for k in wr:
            assert k not in self.pend, f"write to open psum group {k}"
            self._need(waits, self.wr.get(k))
            for t in self.rd.get(k, ()):
                self._need(waits, t)
        amt = 16 if dma else 1
        self.cnt[sem] += amt
        tok = (sem, self.cnt[sem])
        self._record(eng, fn, waits, tok, sem)
        for k in wr:
            self.wr[k] = tok
            self.rd[k] = []
        for k in rd:
            self.rd.setdefault(k, []).append(tok)
        return tok

    def mm_group(self, psum_key, members):
        n = len(members)
        assert n >= 1
        all_rd = set()
        first_waits = {}
        self._need(first_waits, self.wr.get(psum_key))
        for t in self.rd.get(psum_key, ()):
            self._need(first_waits, t)
        self._sem("tensor")
        for i, (fn, rdk) in enumerate(members):
            waits = dict(first_waits) if i == 0 else {}
            for k in rdk:
                assert k not in self.pend
                self._need(waits, self.wr.get(k))
                all_rd.add(k)
            last = i == n - 1
            if last:
                self.cnt["tensor"] += 1
                tok = ("tensor", self.cnt["tensor"])
            else:
                tok = None
            self._record("tensor", fn, waits, tok, "tensor")
        self.wr[psum_key] = tok
        self.rd[psum_key] = []
        for k in all_rd:
            self.rd.setdefault(k, []).append(tok)
        return tok

    def mm_acc(self, psum_key, fn, rd=(), start=False, stop=False):
        self._sem("tensor")
        waits = {}
        if start:
            assert psum_key not in self.pend
            self._need(waits, self.wr.get(psum_key))
            for t in self.rd.get(psum_key, ()):
                self._need(waits, t)
            self.pend[psum_key] = set()
        else:
            assert psum_key in self.pend
        for k in rd:
            self._need(waits, self.wr.get(k))
        self.pend[psum_key].update(rd)
        tok = None
        if stop:
            self.cnt["tensor"] += 1
            tok = ("tensor", self.cnt["tensor"])
        self._record("tensor", fn, waits, tok, "tensor")
        if stop:
            for k in self.pend[psum_key]:
                self.rd.setdefault(k, []).append(tok)
            del self.pend[psum_key]
            self.wr[psum_key] = tok
            self.rd[psum_key] = []
        return tok

    def realize(self):
        self.barrier()
        return self.nc


def cdiv(a, b):
    return (a + b - 1) // b


def ternarize(p, src_dram, n_row_tiles, ncols, dst_sb, dst_stride, tag, tmp,
              sgn=None):
    CW = min(ncols, int(tmp.shape[1]) // 2)
    nh = ncols // CW
    acc = p.sb(f"tacc{tag}", [128, 2], F32)
    red = p.sb(f"tred{tag}", [128, 2], F32)
    ones1 = p.sb(f"tones{tag}", [128, 128], F32)
    if sgn is None:
        sgn = p.sb(f"tsgn{tag}", [128, 2 * CW], F32)
    rsc_ps = p.ps(f"trs{tag}", [128, 1], F32)
    rsc = p.sb(f"trsc{tag}", [128, 1], F32)
    p.op("vector", lambda v: v.memset(ones1[:, :], 1.0), wr=(f"ones1{tag}",))
    p.op("vector", lambda v: v.memset(acc[:, :], 0.0), wr=(f"acc{tag}",))
    for idx in range(n_row_tiles * nh):
        i, hh, par = idx // nh, idx % nh, idx % 2
        p.op("gpsimd", lambda g, i=i, hh=hh, par=par: g.dma_start(
            out=tmp[:, par * CW:par * CW + CW],
            in_=src_dram[i * 128:(i + 1) * 128, hh * CW:(hh + 1) * CW]),
            wr=(f"ttmp{par}",), dma=True, stream=f"tz{tag}{par}")
        p.op("vector", lambda v, par=par: v.reduce_sum(
            red[:, par:par + 1], tmp[:, par * CW:par * CW + CW], AX.X,
            apply_absolute_value=True),
            rd=(f"ttmp{par}",), wr=(f"red{tag}{par}",))
        p.op("vector", lambda v, par=par: v.tensor_add(
            acc[:, 0:1], acc[:, 0:1], red[:, par:par + 1]),
            rd=(f"red{tag}{par}", f"acc{tag}"), wr=(f"acc{tag}",))
    p.mm_group(f"rsps{tag}", [
        (lambda t: t.matmul(rsc_ps[0:1, 0:1], ones1[:, 0:1], acc[:, 0:1],
                            start=True, stop=True), (f"acc{tag}", f"ones1{tag}"))])
    p.op("vector", lambda v: v.tensor_scalar(
        red[0:1, 0:1], rsc_ps[0:1, 0:1], 1.0 / (n_row_tiles * 128 * ncols), 1e-8,
        ALU.mult, ALU.max), rd=(f"rsps{tag}",), wr=(f"red{tag}0",))
    p.op("vector", lambda v: v.reciprocal(red[0:1, 0:1], red[0:1, 0:1]),
         rd=(f"red{tag}0",), wr=(f"red{tag}0",))
    p.mm_group(f"rsps{tag}", [
        (lambda t: t.matmul(rsc_ps[:, 0:1], ones1[0:1, :], red[0:1, 0:1],
                            start=True, stop=True), (f"red{tag}0", f"ones1{tag}"))])
    p.op("vector", lambda v: v.tensor_copy(rsc[:, :], rsc_ps[:, :]),
         rd=(f"rsps{tag}",), wr=(f"rsc{tag}",))
    for idx in range(n_row_tiles * nh):
        i, hh, par = idx // nh, idx % nh, idx % 2
        p.op("gpsimd", lambda g, i=i, hh=hh, par=par: g.dma_start(
            out=tmp[:, par * CW:par * CW + CW],
            in_=src_dram[i * 128:(i + 1) * 128, hh * CW:(hh + 1) * CW]),
            wr=(f"ttmp{par}",), dma=True, stream=f"tz{tag}{par}")
        p.op("vector", lambda v, par=par: v.tensor_scalar(
            tmp[:, par * CW:par * CW + CW], tmp[:, par * CW:par * CW + CW],
            rsc[:, 0:1], None, ALU.mult),
            rd=(f"ttmp{par}", f"rsc{tag}"), wr=(f"ttmp{par}",))
        p.op("vector", lambda v, par=par: v.tensor_scalar(
            sgn[:, par * CW:par * CW + CW], tmp[:, par * CW:par * CW + CW],
            0.5, None, ALU.is_ge),
            rd=(f"ttmp{par}",), wr=(f"sgn{par}",)),
        p.op("vector", lambda v, par=par: v.tensor_scalar(
            tmp[:, par * CW:par * CW + CW], tmp[:, par * CW:par * CW + CW],
            -0.5, None, ALU.is_le),
            rd=(f"ttmp{par}",), wr=(f"ttmp{par}",))
        p.op("vector", lambda v, i=i, hh=hh, par=par: v.tensor_sub(
            dst_sb[:, i * dst_stride + hh * CW:i * dst_stride + (hh + 1) * CW],
            sgn[:, par * CW:par * CW + CW], tmp[:, par * CW:par * CW + CW]),
            rd=(f"ttmp{par}", f"sgn{par}"), wr=(f"wt{tag}",))



def build_l0pre(nch=None, phase2=True, stage=99):
    nc = bass.Bass("TRN2", target_bir_lowering=False)
    p = Prog(nc)
    TH = T + 3
    CH = 512
    NCH = nch if nch is not None else (T // CH)

    xh = nc.dram_tensor("xh", [DM, TH], F32, kind="ExternalInput").ap()
    w_in = nc.dram_tensor("w_in", [DM, 2 * DMAMBA], F32, kind="ExternalInput").ap()
    cwT = nc.dram_tensor("cwT", [128, 64], F32, kind="ExternalInput").ap()
    cb = nc.dram_tensor("cb", [128, 16], F32, kind="ExternalInput").ap()
    w_x = nc.dram_tensor("w_x", [DMAMBA, 128], F32, kind="ExternalInput").ap()
    w_dt = nc.dram_tensor("w_dt", [DTRANK, DMAMBA], F32, kind="ExternalInput").ap()
    b_dt = nc.dram_tensor("b_dt", [128, 16], F32, kind="ExternalInput").ap()
    w1T = nc.dram_tensor("w1T", [DM, FF0], F32, kind="ExternalInput").ap()
    b1d = nc.dram_tensor("b1", [128, 32], F32, kind="ExternalInput").ap()
    w2T = nc.dram_tensor("w2T", [FF0, DM], F32, kind="ExternalInput").ap()
    b2d = nc.dram_tensor("b2", [128, 8], F32, kind="ExternalInput").ap()

    o_dt = nc.dram_tensor("o_dt", [DMAMBA, T], F32, kind="ExternalOutput").ap()
    o_xmc = nc.dram_tensor("o_xmc", [DMAMBA, T], F32, kind="ExternalOutput").ap()
    o_sz = nc.dram_tensor("o_sz", [DMAMBA, T], F32, kind="ExternalOutput").ap()
    o_bc = nc.dram_tensor("o_bc", [32, T], F32, kind="ExternalOutput").ap()
    o_part = nc.dram_tensor("o_part", [DM, T], F32, kind="ExternalOutput").ap()

    xn16 = p.sb("xn16", [128, 8 * TH], BF16, scope="prog")

    w_in_s = p.sb("w_in_s", [128, 8 * 2 * DMAMBA], BF16)
    cw_s = p.sb("cw_s", [128, 16 * 4], F32)
    cb_s = p.sb("cb_s", [128, 16], F32)
    wx_s = p.sb("wx_s", [128, 16 * 128], BF16)
    wdt_s = p.sb("wdt_s", [64, DMAMBA], BF16)
    bdt_s = p.sb("bdt_s", [128, 16], F32)
    ones_s = p.sb("ones_s", [128, 128], F32)

    for kt in range(8):
        p.op("gpsimd", lambda g, kt=kt: g.dma_start(
            out=w_in_s[:, kt * 2 * DMAMBA:(kt + 1) * 2 * DMAMBA],
            in_=w_in[kt * 128:(kt + 1) * 128, :]), wr=("w_in_s",), dma=True)
    for mt in range(16):
        p.op("gpsimd", lambda g, mt=mt: g.dma_start(
            out=wx_s[:, mt * 128:(mt + 1) * 128],
            in_=w_x[mt * 128:(mt + 1) * 128, :]), wr=("wx_s",), dma=True)
    p.op("gpsimd", lambda g: g.dma_start(out=wdt_s[:, :], in_=w_dt),
         wr=("wdt_s",), dma=True)
    p.op("sync", lambda s: s.dma_start(out=cw_s[:, :], in_=cwT), wr=("cw_s",), dma=True)
    p.op("sync", lambda s: s.dma_start(out=cb_s[:, :], in_=cb), wr=("cb_s",), dma=True)
    p.op("sync", lambda s: s.dma_start(out=bdt_s[:, :], in_=b_dt), wr=("bdt_s",), dma=True)
    p.op("vector", lambda v: v.memset(ones_s[:, :], 1.0), wr=("ones_s",))

    NT = CH + 3
    xc = p.sb("xc", [128, 8 * NT], F32)
    stat = p.ps("stat", [128, 512], F32)
    stat2 = p.ps("stat2", [128, 512], F32)
    x2t = p.sb("x2t", [128, NT], F32)
    mb = p.sb("mb", [128, NT], F32)
    rb = p.sb("rb", [128, NT], F32)
    rbx = p.sb("rbx", [128, NT], F32)
    xm_ps = p.ps("xm_ps", [128, 512], F32)
    xm_ps2 = p.ps("xm_ps2", [128, 512], F32)
    xmt = p.sb("xmt", [128, NT], F32)
    xmc16 = p.sb("xmc16", [128, 16 * CH], BF16)
    xmcf = p.sb("xmcf", [128, CH], F32)
    szf = p.sb("szf", [128, CH], F32)
    xdbl_ps = p.ps("xdbl_ps", [128, CH], F32)
    xdbl = p.sb("xdbl", [128, CH], BF16)
    xdblf = p.sb("xdblf", [128, CH], F32)
    dt_ps = p.ps("dt_ps", [128, 512], F32)
    dtt = p.sb("dtt", [128, CH], F32)
    dta = p.sb("dta", [128, CH], F32)

    def emit_chunk_phase1(ci):
        xoff = ci * CH
        if stage < 1:
            return
        for d in range(8):
            p.op("sync", lambda s, d=d, xoff=xoff: s.dma_start(
                out=xc[:, d * NT:d * NT + NT],
                in_=xh[d * 128:(d + 1) * 128, xoff:xoff + NT]),
                wr=(f"xc{d}",), dma=True)
        if stage < 2:
            return
        for sl in range(2):
            f0, f1 = sl * 512, min(NT, 512 + sl * 512)
            p.mm_group("stat", [
                (lambda t, d=d, f0=f0, f1=f1: t.matmul(
                    stat[:, f0:f1], ones_s[:, :], xc[:, d * NT + f0:d * NT + f1],
                    start=(d == 0), stop=(d == 7)), (f"xc{d}", "ones_s"))
                for d in range(8)])
        for d in range(8):
            p.op("scalar", lambda s, d=d: s.activation(
                x2t[:, :], xc[:, d * NT:d * NT + NT], AF.Square),
                rd=(f"xc{d}",), wr=("x2t",))
            for sl in range(2):
                f0, f1 = sl * 512, min(NT, 512 + sl * 512)
                p.mm_group("stat2", [
                    (lambda t, d=d, f0=f0, f1=f1: t.matmul(
                        stat2[:, f0:f1], ones_s[:, :], x2t[:, f0:f1],
                        start=(d == 0), stop=(d == 7)), ("x2t", "ones_s"))])
        if stage < 3:
            return
        p.op("scalar", lambda s: s.activation(mb[:, :], stat[:, 0:NT], AF.Copy,
                                              scale=1.0 / DM),
             rd=("stat",), wr=("mb",))
        p.op("scalar", lambda s: s.activation(rb[:, :], mb[:, :], AF.Square),
             rd=("mb",), wr=("rb",))
        p.op("vector", lambda v: v.scalar_tensor_tensor(
            rb[:, :], stat2[:, 0:NT], 1.0 / DM, rb[:, :], ALU.mult, ALU.subtract),
            rd=("stat2", "rb"), wr=("rb",))
        epsc, epsk = p.constcol(1e-5)
        p.op("scalar", lambda s, epsc=epsc: s.activation(
            rb[:, :], rb[:, :], AF.Sqrt, bias=epsc[:, 0:1]),
            rd=("rb", epsk), wr=("rb",))
        p.op("vector", lambda v: v.reciprocal(rb[:, :], rb[:, :]),
             rd=("rb",), wr=("rb",))
        for d in range(8):
            p.op("vector", lambda v, d=d: v.tensor_sub(
                xc[:, d * NT:d * NT + NT], xc[:, d * NT:d * NT + NT], mb[:, :]),
                rd=(f"xc{d}", "mb"), wr=(f"xc{d}",))
            p.op("vector", lambda v, d=d, xoff=xoff: v.tensor_mul(
                xn16[:, d * TH + xoff:d * TH + xoff + NT],
                xc[:, d * NT:d * NT + NT], rb[:, :]),
                rd=(f"xc{d}", "rb"), wr=("xn16",))

        if stage < 4:
            return
        for mt in range(16):
            for sl in range(2):
                f0, f1 = sl * 512, min(NT, 512 + sl * 512)
                psk = "xm_ps" if sl == 0 else "xm_ps2"
                psum = xm_ps if sl == 0 else xm_ps2
                p.mm_group(psk, [
                    (lambda t, kt=kt, f0=f0, f1=f1, mt=mt, psum=psum, xoff=xoff:
                     t.matmul(psum[:, 0:f1 - f0],
                              w_in_s[:, kt * 2 * DMAMBA + mt * 128:
                                     kt * 2 * DMAMBA + mt * 128 + 128],
                              xn16[:, kt * TH + xoff + f0:kt * TH + xoff + f1],
                              start=(kt == 0), stop=(kt == 7)),
                     ("w_in_s", "xn16")) for kt in range(8)])
                p.op("scalar", lambda s, f0=f0, f1=f1, psum=psum, psk=psk: s.copy(
                    xmt[:, f0:f1], psum[:, 0:f1 - f0]),
                    rd=(psk,), wr=("xmt",))
            if stage < 5:
                continue
            p.op("vector", lambda v, mt=mt: v.tensor_scalar(
                xmcf[:, :], xmt[:, 0:CH], cw_s[:, mt * 4:mt * 4 + 1], None, ALU.mult),
                rd=("xmt", "cw_s"), wr=("xmcf",))
            for k in range(1, 4):
                p.op("vector", lambda v, mt=mt, k=k: v.scalar_tensor_tensor(
                    xmcf[:, :], xmt[:, k:k + CH], cw_s[:, mt * 4 + k:mt * 4 + k + 1],
                    xmcf[:, :], ALU.mult, ALU.add),
                    rd=("xmt", "cw_s", "xmcf"), wr=("xmcf",))
            p.op("scalar", lambda s, mt=mt: s.activation(
                xmc16[:, mt * CH:(mt + 1) * CH], xmcf[:, :], AF.Silu,
                bias=cb_s[:, mt:mt + 1]),
                rd=("xmcf", "cb_s"), wr=("xmc16",))
            p.op("scalar", lambda s, mt=mt: s.activation(
                xmcf[:, :], xmcf[:, :], AF.Silu, bias=cb_s[:, mt:mt + 1]),
                rd=("xmcf", "cb_s"), wr=("xmcf",))
            p.op("sync", lambda s, mt=mt, ci=ci: s.dma_start(
                out=o_xmc[mt * 128:(mt + 1) * 128, ci * CH:(ci + 1) * CH],
                in_=xmcf[:, :]), rd=("xmcf",), dma=True)
            if stage < 6:
                continue
            p.mm_group("xm_ps", [
                (lambda t, kt=kt, mt=mt, xoff=xoff: t.matmul(
                    xm_ps[:, 0:CH],
                    w_in_s[:, kt * 2 * DMAMBA + (16 + mt) * 128:
                           kt * 2 * DMAMBA + (16 + mt) * 128 + 128],
                    xn16[:, kt * TH + xoff + 1:kt * TH + xoff + 1 + CH],
                    start=(kt == 0), stop=(kt == 7)),
                 ("w_in_s", "xn16")) for kt in range(8)])
            p.op("scalar", lambda s: s.activation(szf[:, :], xm_ps[:, 0:CH], AF.Silu),
                 rd=("xm_ps",), wr=("szf",))
            p.op("sync", lambda s, mt=mt, ci=ci: s.dma_start(
                out=o_sz[mt * 128:(mt + 1) * 128, ci * CH:(ci + 1) * CH],
                in_=szf[:, :]), rd=("szf",), dma=True)
        if stage < 7:
            return
        p.mm_group("xdbl_ps", [
            (lambda t, mt=mt: t.matmul(
                xdbl_ps[:, :], wx_s[:, mt * 128:(mt + 1) * 128],
                xmc16[:, mt * CH:(mt + 1) * CH],
                start=(mt == 0), stop=(mt == 15)),
             ("wx_s", "xmc16")) for mt in range(16)])
        if stage >= 7.06:
            p.op("vector", lambda v: v.tensor_copy(xdbl[:, :], xdbl_ps[:, :]),
                 rd=("xdbl_ps",), wr=("xdbl",))
        if stage == 7.07:
            p.op("scalar", lambda s: s.copy(xdblf[:, :], xmcf[:, :]),
                 rd=("xmcf",), wr=("xdblf",))
        if stage >= 7.08:
            p.op("scalar", lambda s: s.copy(xdblf[:, :], xdbl_ps[:, :]),
                 rd=("xdbl_ps",), wr=("xdblf",))
        if stage >= 7.2:
            p.op("sync", lambda s, ci=ci: s.dma_start(
                out=o_bc[0:16, ci * CH:(ci + 1) * CH], in_=xdblf[64:80, :]),
                rd=("xdblf",), dma=True)
        if stage >= 7.3:
            p.op("sync", lambda s, ci=ci: s.dma_start(
                out=o_bc[16:32, ci * CH:(ci + 1) * CH], in_=xdblf[96:112, :]),
                rd=("xdblf",), dma=True)
        if stage < 8:
            return
        for pt in range(16):
            p.mm_group("dt_ps", [
                (lambda t, pt=pt: t.matmul(
                    dt_ps[:, 0:CH], wdt_s[:, pt * 128:(pt + 1) * 128],
                    xdbl[0:64, :], start=True, stop=True),
                 ("wdt_s", "xdbl"))])
            p.op("scalar", lambda s, pt=pt: s.activation(
                dta[:, :], dt_ps[:, 0:CH], AF.Abs, bias=bdt_s[:, pt:pt + 1]),
                rd=("dt_ps", "bdt_s"), wr=("dta",))
            p.op("scalar", lambda s: s.activation(dta[:, :], dta[:, :],
                                                  AF.Exp, scale=-1.0),
                 rd=("dta",), wr=("dta",))
            onec, onek = p.constcol(1.0)
            p.op("scalar", lambda s, onec=onec: s.activation(
                dta[:, :], dta[:, :], AF.Ln, bias=onec[:, 0:1]),
                rd=("dta", onek), wr=("dta",))
            p.op("scalar", lambda s, pt=pt: s.activation(
                dtt[:, :], dt_ps[:, 0:CH], AF.Relu, bias=bdt_s[:, pt:pt + 1]),
                rd=("dt_ps", "bdt_s"), wr=("dtt",))
            p.op("vector", lambda v: v.tensor_add(dtt[:, :], dtt[:, :], dta[:, :]),
                 rd=("dta", "dtt"), wr=("dtt",))
            p.op("sync", lambda s, pt=pt, ci=ci: s.dma_start(
                out=o_dt[pt * 128:(pt + 1) * 128, ci * CH:(ci + 1) * CH],
                in_=dtt[:, :]), rd=("dtt",), dma=True)

    for ci in range(NCH):
        emit_chunk_phase1(ci)
    p.barrier()
    if not phase2:
        return p

    w1t = p.sb("w1t", [128, 8 * FF0], BF16)
    w2t = p.sb("w2t", [128, 32 * DM], BF16)
    ttmp = p.sb("ttmp", [128, 2048], F32)
    tsgn = p.sb("tsgn", [128, 2048], F32)
    ternarize(p, w1T, 8, FF0, w1t, FF0, "w1", ttmp, tsgn)
    ternarize(p, w2T, 32, DM, w2t, DM, "w2", ttmp, tsgn)
    b1_s = p.sb("b1_s", [128, 32], F32)
    b2_s = p.sb("b2_s", [128, 8], F32)
    p.op("sync", lambda s: s.dma_start(out=b1_s[:, :], in_=b1d), wr=("b1_s",), dma=True)
    p.op("sync", lambda s: s.dma_start(out=b2_s[:, :], in_=b2d), wr=("b2_s",), dma=True)

    C2 = 256
    h_ps = p.ps("h_ps", [128, C2], F32)
    h_ps2 = p.ps("h_ps2", [128, C2], F32)
    h16 = p.sb("h16", [128, 32 * C2], BF16)
    ou_ps = p.ps("ou_ps", [128, C2], F32)
    ou_ps2 = p.ps("ou_ps2", [128, C2], F32)
    xpc = p.sb("xpc", [128, C2], F32)
    prt = p.sb("prt", [128, C2], F32)

    for ci in range(T // C2):
        xoff = ci * C2 + 1
        for pt in range(32):
            psk, psum = ("h_ps", h_ps) if pt % 2 == 0 else ("h_ps2", h_ps2)
            p.mm_group(psk, [
                (lambda t, kt=kt, pt=pt, psum=psum, xoff=xoff: t.matmul(
                    psum[:, :], w1t[:, kt * FF0 + pt * 128:kt * FF0 + pt * 128 + 128],
                    xn16[:, kt * TH + xoff:kt * TH + xoff + C2],
                    start=(kt == 0), stop=(kt == 7)),
                 ("wtw1", "xn16")) for kt in range(8)])
            p.op("scalar", lambda s, pt=pt, psum=psum, psk=psk: s.activation(
                h16[:, pt * C2:(pt + 1) * C2], psum[:, :], AF.Gelu,
                bias=b1_s[:, pt:pt + 1]),
                rd=(psk, "b1_s"), wr=("h16",))
        for po in range(8):
            p.op("sync", lambda s, po=po, xoff=xoff: s.dma_start(
                out=xpc[:, :], in_=xh[po * 128:(po + 1) * 128, xoff:xoff + C2]),
                wr=("xpc",), dma=True)
            psk, psum = ("ou_ps", ou_ps) if po % 2 == 0 else ("ou_ps2", ou_ps2)
            p.mm_group(psk, [
                (lambda t, ft=ft, po=po, psum=psum: t.matmul(
                    psum[:, :], w2t[:, ft * DM + po * 128:ft * DM + po * 128 + 128],
                    h16[:, ft * C2:(ft + 1) * C2],
                    start=(ft == 0), stop=(ft == 31)),
                 ("wtw2", "h16")) for ft in range(32)])
            p.op("vector", lambda v, po=po, psum=psum, psk=psk: v.scalar_tensor_tensor(
                prt[:, :], psum[:, :], b2_s[:, po:po + 1], xpc[:, :],
                ALU.add, ALU.add), rd=(psk, "b2_s", "xpc"), wr=("prt",))
            p.op("sync", lambda s, po=po, ci=ci: s.dma_start(
                out=o_part[po * 128:(po + 1) * 128, ci * C2:(ci + 1) * C2],
                in_=prt[:, :]), rd=("prt",), dma=True)
    p.barrier()
    return p


def host_l0pre(inputs):
    x = np.asarray(inputs["L0_attn_x"], dtype=np.float32)
    wx = np.asarray(inputs["L0_attn_W_x"], dtype=np.float32)
    wxp = np.zeros((DMAMBA, 128), dtype=np.float32)
    wxp[:, 0:64] = wx[:, 0:64]
    wxp[:, 64:80] = wx[:, 64:80]
    wxp[:, 96:112] = wx[:, 80:96]
    shared = {
        "w_in": np.ascontiguousarray(np.asarray(inputs["L0_attn_W_in"], np.float32)),
        "cwT": np.ascontiguousarray(np.asarray(inputs["L0_attn_conv_w"], np.float32)[:, 0, :].T.reshape(16, 128, 4).transpose(1, 0, 2).reshape(128, 64)),
        "cb": np.ascontiguousarray(np.asarray(inputs["L0_attn_conv_b"], np.float32).reshape(16, 128).T),
        "w_x": wxp,
        "w_dt": np.ascontiguousarray(np.asarray(inputs["L0_attn_W_dt"], np.float32)),
        "b_dt": np.ascontiguousarray(np.asarray(inputs["L0_attn_b_dt"], np.float32).reshape(16, 128).T),
        "w1T": np.ascontiguousarray(np.asarray(inputs["L0_ffn_W1"], np.float32).T),
        "b1": np.ascontiguousarray(np.asarray(inputs["L0_ffn_b1"], np.float32).reshape(32, 128).T),
        "w2T": np.ascontiguousarray(np.asarray(inputs["L0_ffn_W2"], np.float32).T),
        "b2": np.ascontiguousarray(np.asarray(inputs["L0_ffn_b2"], np.float32).reshape(8, 128).T),
    }
    maps = []
    for c in range(8):
        b, h = c // 2, c % 2
        s0 = h * T
        xhm = np.zeros((T + 3, DM), dtype=np.float32)
        lo, hi = s0 - 1, s0 + T + 2
        clo, chi = max(lo, 0), min(hi, SEQ)
        xhm[clo - lo:chi - lo] = x[b, clo:chi]
        m = dict(shared)
        m["xh"] = np.ascontiguousarray(xhm.T)
        maps.append(m)
    return maps





def build_l0scan():
    nc = bass.Bass("TRN2", target_bir_lowering=False)
    p = Prog(nc)
    NTOK = B * SEQ
    L2 = 2048
    NH = SEQ // L2

    dt_d = nc.dram_tensor("dt", [256, NTOK], F32, kind="ExternalInput").ap()
    xmc_d = nc.dram_tensor("xmc", [256, NTOK], F32, kind="ExternalInput").ap()
    sz_d = nc.dram_tensor("sz", [256, NTOK], F32, kind="ExternalInput").ap()
    Bd = nc.dram_tensor("Bmat", [B * DSTATE, SEQ], F32, kind="ExternalInput").ap()
    Cd = nc.dram_tensor("Cmat", [B * DSTATE, SEQ], F32, kind="ExternalInput").ap()
    Alog_d = nc.dram_tensor("Alog", [256, DSTATE], F32, kind="ExternalInput").ap()
    Dd = nc.dram_tensor("Dvec", [128, 2], F32, kind="ExternalInput").ap()
    o_yg = nc.dram_tensor("o_yg", [256, NTOK], F32, kind="ExternalOutput").ap()

    A_s = p.sb("A_s", [128, 2 * DSTATE], F32, scope="prog")
    D_s = p.sb("D_s", [128, 2], F32, scope="prog")
    p.op("sync", lambda s: s.dma_start(
        out=A_s.ap().rearrange("p (t s) -> p t s", t=2),
        in_=Alog_d.rearrange("(t p) s -> p t s", p=128)), wr=("A_s",), dma=True)
    p.op("scalar", lambda s: s.activation(A_s[:, :], A_s[:, :], AF.Exp),
         rd=("A_s",), wr=("A_s",))
    p.op("vector", lambda v: v.tensor_scalar_mul(A_s[:, :], A_s[:, :], -1.0),
         rd=("A_s",), wr=("A_s",))
    p.op("sync", lambda s: s.dma_start(out=D_s[:, :], in_=Dd), wr=("D_s",), dma=True)

    dts = [p.sb(f"dts{t}", [128, L2], F32) for t in range(2)]
    xms = [p.sb(f"xms{t}", [128, L2], F32) for t in range(2)]
    szs = [p.sb(f"szs{t}", [128, L2], F32) for t in range(2)]
    ws = [p.sb(f"ws{t}", [128, L2], F32) for t in range(2)]
    ys = [p.sb(f"ys{t}", [128, L2], F32) for t in range(2)]
    carry = [p.sb(f"carry{t}", [128, DSTATE], F32) for t in range(2)]
    brep = p.sb("brep", [128, L2], F32)
    crep = p.sb("crep", [128, L2], F32)
    da = p.sb("da", [128, L2], F32)
    bs = p.sb("bs", [128, L2], F32)
    st = p.sb("st", [128, L2], F32)
    zt = p.sb("zt", [128, L2], F32)

    for b in range(B):
        for hf in range(NH):
            off = b * SEQ + hf * L2
            for t in range(2):
                p.op("sync", lambda s, t=t, off=off: s.dma_start(
                    out=dts[t][:, :], in_=dt_d[t * 128:(t + 1) * 128, off:off + L2]),
                    wr=(f"dts{t}",), dma=True, stream="in0")
                p.op("sync", lambda s, t=t, off=off: s.dma_start(
                    out=xms[t][:, :], in_=xmc_d[t * 128:(t + 1) * 128, off:off + L2]),
                    wr=(f"xms{t}",), dma=True, stream="in1")
                p.op("sync", lambda s, t=t, off=off: s.dma_start(
                    out=szs[t][:, :], in_=sz_d[t * 128:(t + 1) * 128, off:off + L2]),
                    wr=(f"szs{t}",), dma=True, stream="in2")
                p.op("vector", lambda v, t=t: v.tensor_mul(
                    ws[t][:, :], dts[t][:, :], xms[t][:, :]),
                    rd=(f"dts{t}", f"xms{t}"), wr=(f"ws{t}",))
                if hf == 0:
                    p.op("vector", lambda v, t=t: v.memset(carry[t][:, :], 0.0),
                         wr=(f"carry{t}",))
            for s in range(DSTATE):
                p.op("gpsimd", lambda g, b=b, s=s, hf=hf: g.dma_start(
                    out=brep[:, :],
                    in_=Bd[b * DSTATE + s:b * DSTATE + s + 1,
                           hf * L2:(hf + 1) * L2].partition_broadcast(128)),
                    wr=("brep",), dma=True, stream="br")
                p.op("gpsimd", lambda g, b=b, s=s, hf=hf: g.dma_start(
                    out=crep[:, :],
                    in_=Cd[b * DSTATE + s:b * DSTATE + s + 1,
                           hf * L2:(hf + 1) * L2].partition_broadcast(128)),
                    wr=("crep",), dma=True, stream="cr")
                for t in range(2):
                    p.op("scalar", lambda sc, t=t, s=s: sc.activation(
                        da[:, :], dts[t][:, :], AF.Exp,
                        scale=A_s[:, t * DSTATE + s:t * DSTATE + s + 1]),
                        rd=(f"dts{t}", "A_s"), wr=("da",))
                    p.op("vector", lambda v, t=t: v.tensor_mul(
                        bs[:, :], ws[t][:, :], brep[:, :]),
                        rd=(f"ws{t}", "brep"), wr=("bs",))
                    p.op("vector", lambda v, t=t, s=s: v.tensor_tensor_scan(
                        st[:, :], da[:, :], bs[:, :],
                        carry[t][:, s:s + 1], ALU.mult, ALU.add),
                        rd=("da", "bs", f"carry{t}"), wr=("st",))
                    p.op("vector", lambda v, t=t, s=s: v.tensor_copy(
                        carry[t][:, s:s + 1], st[:, L2 - 1:L2]),
                        rd=("st",), wr=(f"carry{t}",))
                    p.op("vector", lambda v: v.tensor_mul(
                        zt[:, :], st[:, :], crep[:, :]),
                        rd=("st", "crep"), wr=("zt",))
                    if s == 0:
                        p.op("vector", lambda v, t=t: v.tensor_copy(
                            ys[t][:, :], zt[:, :]), rd=("zt",), wr=(f"ys{t}",))
                    else:
                        p.op("vector", lambda v, t=t: v.tensor_add(
                            ys[t][:, :], ys[t][:, :], zt[:, :]),
                            rd=("zt", f"ys{t}"), wr=(f"ys{t}",))
            for t in range(2):
                p.op("vector", lambda v, t=t: v.scalar_tensor_tensor(
                    ys[t][:, :], xms[t][:, :], D_s[:, t:t + 1], ys[t][:, :],
                    ALU.mult, ALU.add),
                    rd=(f"xms{t}", "D_s", f"ys{t}"), wr=(f"ys{t}",))
                p.op("vector", lambda v, t=t: v.tensor_mul(
                    ys[t][:, :], ys[t][:, :], szs[t][:, :]),
                    rd=(f"szs{t}", f"ys{t}"), wr=(f"ys{t}",))
                p.op("sync", lambda s, t=t, off=off: s.dma_start(
                    out=o_yg[t * 128:(t + 1) * 128, off:off + L2], in_=ys[t][:, :]),
                    rd=(f"ys{t}",), dma=True, stream="out")
    p.barrier()
    return p


def host_l0scan(dt_full, xmc_full, sz_full, Bc_full, Cc_full, inputs):
    Alog = np.asarray(inputs["L0_attn_A_log"], np.float32)
    Dv = np.asarray(inputs["L0_attn_D"], np.float32)
    Bm = np.ascontiguousarray(Bc_full.transpose(0, 2, 1).reshape(B * DSTATE, SEQ))
    Cm = np.ascontiguousarray(Cc_full.transpose(0, 2, 1).reshape(B * DSTATE, SEQ))
    maps = []
    for c in range(8):
        ch = slice(c * 256, (c + 1) * 256)
        maps.append({
            "dt": np.ascontiguousarray(dt_full[ch].reshape(256, B * SEQ)),
            "xmc": np.ascontiguousarray(xmc_full[ch].reshape(256, B * SEQ)),
            "sz": np.ascontiguousarray(sz_full[ch].reshape(256, B * SEQ)),
            "Bmat": Bm, "Cmat": Cm,
            "Alog": np.ascontiguousarray(Alog[ch]),
            "Dvec": np.ascontiguousarray(Dv[ch].reshape(2, 128).T),
            })
    return maps



def build_mid():
    nc = bass.Bass("TRN2", target_bir_lowering=False)
    p = Prog(nc)
    TH = T + 3
    SLABS = [(0, 3), (3, 512), (515, 512), (1027, 512), (1539, 512)]
    QSC = float(DK) ** -0.5

    yg_d = nc.dram_tensor("yg", [DMAMBA, TH], F32, kind="ExternalInput").ap()
    part_d = nc.dram_tensor("part", [DM, TH], F32, kind="ExternalInput").ap()
    wout_d = nc.dram_tensor("wout", [DMAMBA, DM], F32, kind="ExternalInput").ap()
    wqkvg_d = nc.dram_tensor("wqkvg", [DM, 4608], F32, kind="ExternalInput").ap()
    wba_d = nc.dram_tensor("wba", [128, 8 * 16], F32, kind="ExternalInput").ap()
    gwT_d = nc.dram_tensor("gwT", [DM, 8], F32, kind="ExternalInput").ap()
    ebias_d = nc.dram_tensor("ebias", [1, 8], F32, kind="ExternalInput").ap()
    cwq_d = nc.dram_tensor("cwq", [128, 4 * 24], F32, kind="ExternalInput").ap()
    alog_d = nc.dram_tensor("alog", [6, 1], F32, kind="ExternalInput").ap()
    dtb_d = nc.dram_tensor("dtb", [6, 1], F32, kind="ExternalInput").ap()

    o_out1 = nc.dram_tensor("o_out1", [DM, T], F32, kind="ExternalOutput").ap()
    o_xn1 = nc.dram_tensor("o_xn1", [DM, T], BF16, kind="ExternalOutput").ap()
    o_q = nc.dram_tensor("o_q", [KD, T], F32, kind="ExternalOutput").ap()
    o_k = nc.dram_tensor("o_k", [KD, T], F32, kind="ExternalOutput").ap()
    o_v = nc.dram_tensor("o_v", [VD, T], F32, kind="ExternalOutput").ap()
    o_gs = nc.dram_tensor("o_gs", [VD, T], F32, kind="ExternalOutput").ap()
    o_beta = nc.dram_tensor("o_beta", [6, T], F32, kind="ExternalOutput").ap()
    o_g = nc.dram_tensor("o_g", [6, T], F32, kind="ExternalOutput").ap()
    o_mw = nc.dram_tensor("o_mw", [8, T], F32, kind="ExternalOutput").ap()

    xn16 = p.sb("xn16", [128, 8 * TH], BF16, scope="prog")
    ones_s = p.sb("ones_s", [128, 128], F32, scope="prog")
    p.op("vector", lambda v: v.memset(ones_s[:, :], 1.0), wr=("ones_s",))
    epsc, epsk = p.constcol(1e-5)
    eps12c, eps12k = p.constcol(1e-12)
    onec, onek = p.constcol(1.0)

    wout_s = p.sb("wout_s", [128, 16 * DM], BF16)
    gw_s = p.sb("gw_s", [128, 8 * 8], F32)
    eb128 = p.sb("eb128", [128, 8], F32)
    ones8 = p.sb("ones8", [128, 8], F32)
    ident = p.sb("ident", [128, 128], F32)
    iota_c = p.sb("iota_c", [128, 1], F32)
    for kt in range(16):
        p.op("gpsimd", lambda g, kt=kt: g.dma_start(
            out=wout_s[:, kt * DM:(kt + 1) * DM],
            in_=wout_d[kt * 128:(kt + 1) * 128, :]), wr=("wout_s",), dma=True)
    for kt in range(8):
        p.op("sync", lambda s, kt=kt: s.dma_start(
            out=gw_s[:, kt * 8:(kt + 1) * 8],
            in_=gwT_d[kt * 128:(kt + 1) * 128, :]), wr=("gw_s",), dma=True, stream="w2")
    p.op("sync", lambda s: s.dma_start(
        out=eb128[:, :], in_=ebias_d[0:1, :].partition_broadcast(128)),
        wr=("eb128",), dma=True, stream="w2")
    p.op("vector", lambda v: v.memset(ones8[:, :], 1.0), wr=("ones8",))
    p.op("gpsimd", lambda g: g.iota(ident[:, :], [[1, 128]], channel_multiplier=0,
                                    allow_small_or_imprecise_dtypes=True),
         wr=("ident",))
    p.op("gpsimd", lambda g: g.iota(iota_c[:, :], [[0, 1]], channel_multiplier=1,
                                    allow_small_or_imprecise_dtypes=True),
         wr=("iota_c",))
    p.op("vector", lambda v: v.tensor_scalar(
        ident[:, :], ident[:, :], iota_c[:, 0:1], None, ALU.is_equal),
        rd=("ident", "iota_c"), wr=("ident",))

    ygs = p.sb("ygs", [128, 16 * 512], BF16)
    prt = p.sb("prt", [128, 512], F32)
    out1 = p.sb("out1", [128, 8 * 512], F32)
    ygps = p.ps("ygps", [128, 512], F32)
    ygps2 = p.ps("ygps2", [128, 512], F32)
    stat = p.ps("stat", [128, 512], F32)
    stat2 = p.ps("stat2", [128, 512], F32)
    x2t = p.sb("x2t", [128, 512], F32)
    mb = p.sb("mb", [128, 512], F32)
    rb = p.sb("rb", [128, 512], F32)
    gps = p.ps("gps", [8, 512], F32)
    scr = p.sb("scr", [8, 512], F32)
    trps = p.ps("trps", [128, 8], F32)
    trps2 = p.ps("trps2", [8, 128], F32)
    tok8 = p.sb("tok8", [128, 8], F32)
    t8a = p.sb("t8a", [128, 8], F32)
    t8b = p.sb("t8b", [128, 8], F32)
    t8c = p.sb("t8c", [128, 8], F32)
    m1 = p.sb("m1", [128, 1], F32)
    mwti = p.sb("mwti", [8, 128], F32)

    for si, (f0, w) in enumerate(SLABS):
        for mt in range(16):
            p.op("gpsimd", lambda g, mt=mt, f0=f0, w=w: g.dma_start(
                out=ygs[:, mt * 512:mt * 512 + w],
                in_=yg_d[mt * 128:(mt + 1) * 128, f0:f0 + w]),
                wr=("ygs",), dma=True, stream="yg")
        for d in range(8):
            p.op("sync", lambda s, d=d, f0=f0, w=w: s.dma_start(
                out=prt[:, 0:w], in_=part_d[d * 128:(d + 1) * 128, f0:f0 + w]),
                wr=("prt",), dma=True, stream="in0")
            psk, psum = ("ygps", ygps) if d % 2 == 0 else ("ygps2", ygps2)
            p.mm_group(psk, [
                (lambda t, mt=mt, w=w, psum=psum: t.matmul(
                    psum[:, 0:w], wout_s[:, mt * DM + d * 128:mt * DM + d * 128 + 128],
                    ygs[:, mt * 512:mt * 512 + w],
                    start=(mt == 0), stop=(mt == 15)),
                 ("wout_s", "ygs")) for mt in range(16)])
            p.op("vector", lambda v, d=d, w=w, psum=psum, psk=psk: v.tensor_add(
                out1[:, d * 512:d * 512 + w], psum[:, 0:w], prt[:, 0:w]),
                rd=(psk, "prt"), wr=("out1",))
            if si > 0:
                p.op("sync", lambda s, d=d, f0=f0, w=w: s.dma_start(
                    out=o_out1[d * 128:(d + 1) * 128, f0 - 3:f0 - 3 + w],
                    in_=out1[:, d * 512:d * 512 + w]),
                    rd=("out1",), dma=True, stream="oo1")
        p.mm_group("stat", [
            (lambda t, d=d, w=w: t.matmul(
                stat[:, 0:w], ones_s[:, :], out1[:, d * 512:d * 512 + w],
                start=(d == 0), stop=(d == 7)), ("out1", "ones_s"))
            for d in range(8)])
        for d in range(8):
            p.op("scalar", lambda s, d=d, w=w: s.activation(
                x2t[:, 0:w], out1[:, d * 512:d * 512 + w], AF.Square),
                rd=("out1",), wr=("x2t",))
            p.mm_group("stat2", [
                (lambda t, d=d, w=w: t.matmul(
                    stat2[:, 0:w], ones_s[:, :], x2t[:, 0:w],
                    start=(d == 0), stop=(d == 7)), ("x2t", "ones_s"))])
        p.op("scalar", lambda s, w=w: s.activation(mb[:, 0:w], stat[:, 0:w],
                                                   AF.Copy, scale=1.0 / DM),
             rd=("stat",), wr=("mb",))
        p.op("scalar", lambda s, w=w: s.activation(rb[:, 0:w], mb[:, 0:w], AF.Square),
             rd=("mb",), wr=("rb",))
        p.op("vector", lambda v, w=w: v.scalar_tensor_tensor(
            rb[:, 0:w], stat2[:, 0:w], 1.0 / DM, rb[:, 0:w], ALU.mult, ALU.subtract),
            rd=("stat2", "rb"), wr=("rb",))
        p.op("scalar", lambda s, w=w: s.activation(
            rb[:, 0:w], rb[:, 0:w], AF.Sqrt, bias=epsc),
            rd=("rb", epsk), wr=("rb",))
        p.op("vector", lambda v, w=w: v.reciprocal(rb[:, 0:w], rb[:, 0:w]),
             rd=("rb",), wr=("rb",))
        for d in range(8):
            p.op("vector", lambda v, d=d, w=w: v.tensor_sub(
                out1[:, d * 512:d * 512 + w], out1[:, d * 512:d * 512 + w], mb[:, 0:w]),
                rd=("out1", "mb"), wr=("out1",))
            p.op("vector", lambda v, d=d, w=w: v.tensor_mul(
                out1[:, d * 512:d * 512 + w], out1[:, d * 512:d * 512 + w], rb[:, 0:w]),
                rd=("out1", "rb"), wr=("out1",))
            p.op("vector", lambda v, d=d, w=w, f0=f0: v.tensor_copy(
                xn16[:, d * TH + f0:d * TH + f0 + w], out1[:, d * 512:d * 512 + w]),
                rd=("out1",), wr=("xn16",))
            if si > 0:
                p.op("sync", lambda s, d=d, f0=f0, w=w: s.dma_start(
                    out=o_xn1[d * 128:(d + 1) * 128, f0 - 3:f0 - 3 + w],
                    in_=xn16[:, d * TH + f0:d * TH + f0 + w]),
                    rd=("xn16",), dma=True, stream="ox1")
        if si == 0:
            continue
        p.mm_group("gps", [
            (lambda t, d=d, w=w: t.matmul(
                gps[:, 0:w], gw_s[:, d * 8:(d + 1) * 8], out1[:, d * 512:d * 512 + w],
                start=(d == 0), stop=(d == 7)), ("out1", "gw_s"))
            for d in range(8)])
        p.op("scalar", lambda s, w=w: s.activation(scr[:, 0:w], gps[:, 0:w],
                                                   AF.Sigmoid),
             rd=("gps",), wr=("scr",))
        for blk in range(4):
            c0 = blk * 128
            p.mm_group("trps", [
                (lambda t, c0=c0: t.transpose(
                    trps[:, :], scr[:, c0:c0 + 128], ident[0:8, 0:8]),
                 ("scr", "ident"))])
            p.op("vector", lambda v: v.tensor_copy(tok8[:, :], trps[:, :]),
                 rd=("trps",), wr=("tok8",))
            p.op("vector", lambda v: v.tensor_add(t8a[:, :], tok8[:, :], eb128[:, :]),
                 rd=("tok8", "eb128"), wr=("t8a",))
            p.op("vector", lambda v: v.reduce_max(m1[:, :], t8a[:, :], AX.X),
                 rd=("t8a",), wr=("m1",))
            p.op("vector", lambda v: v.tensor_scalar(
                t8b[:, :], t8a[:, :], m1[:, 0:1], None, ALU.is_ge),
                rd=("t8a", "m1"), wr=("t8b",))
            p.op("vector", lambda v: v.tensor_tensor_scan(
                t8c[:, :], ones8[:, :], t8b[:, :], 0.0, ALU.mult, ALU.add),
                rd=("ones8", "t8b"), wr=("t8c",))
            p.op("vector", lambda v: v.tensor_scalar(
                t8c[:, :], t8c[:, :], 1.0, None, ALU.is_le),
                rd=("t8c",), wr=("t8c",))
            p.op("vector", lambda v: v.tensor_mul(t8b[:, :], t8b[:, :], t8c[:, :]),
                 rd=("t8b", "t8c"), wr=("t8b",))
            p.op("vector", lambda v: v.scalar_tensor_tensor(
                t8a[:, :], t8b[:, :], -1e30, t8a[:, :], ALU.mult, ALU.add),
                rd=("t8a", "t8b"), wr=("t8a",))
            p.op("vector", lambda v: v.reduce_max(m1[:, :], t8a[:, :], AX.X),
                 rd=("t8a",), wr=("m1",))
            p.op("vector", lambda v: v.tensor_scalar(
                t8a[:, :], t8a[:, :], m1[:, 0:1], None, ALU.is_ge),
                rd=("t8a", "m1"), wr=("t8a",))
            p.op("vector", lambda v: v.tensor_tensor_scan(
                t8c[:, :], ones8[:, :], t8a[:, :], 0.0, ALU.mult, ALU.add),
                rd=("ones8", "t8a"), wr=("t8c",))
            p.op("vector", lambda v: v.tensor_scalar(
                t8c[:, :], t8c[:, :], 1.0, None, ALU.is_le),
                rd=("t8c",), wr=("t8c",))
            p.op("vector", lambda v: v.tensor_mul(t8a[:, :], t8a[:, :], t8c[:, :]),
                 rd=("t8a", "t8c"), wr=("t8a",))
            p.op("vector", lambda v: v.tensor_add(t8b[:, :], t8b[:, :], t8a[:, :]),
                 rd=("t8a", "t8b"), wr=("t8b",))
            p.op("vector", lambda v: v.tensor_mul(t8b[:, :], t8b[:, :], tok8[:, :]),
                 rd=("t8b", "tok8"), wr=("t8b",))
            p.mm_group("trps2", [
                (lambda t: t.transpose(trps2[:, :], t8b[:, :], ident[:, :]),
                 ("t8b", "ident"))])
            p.op("vector", lambda v: v.tensor_copy(mwti[:, :], trps2[:, :]),
                 rd=("trps2",), wr=("mwti",))
            p.op("sync", lambda s, f0=f0, c0=c0: s.dma_start(
                out=o_mw[:, f0 - 3 + c0:f0 - 3 + c0 + 128], in_=mwti[:, :]),
                rd=("mwti",), dma=True, stream="omw")
    p.barrier()

    wq_s = p.sb("wq_s", [128, 8 * 4608], BF16)
    wba_s = p.sb("wba_s", [128, 8 * 16], BF16)
    cwq_s = p.sb("cwq_s", [128, 4 * 24], F32)
    negA = p.sb("negA", [6, 1], F32)
    dtb_s = p.sb("dtb_s", [6, 1], F32)
    for kt in range(8):
        p.op("gpsimd", lambda g, kt=kt: g.dma_start(
            out=wq_s[:, kt * 4608:(kt + 1) * 4608],
            in_=wqkvg_d[kt * 128:(kt + 1) * 128, :]), wr=("wq_s",), dma=True)
    p.op("gpsimd", lambda g: g.dma_start(out=wba_s[:, :], in_=wba_d),
         wr=("wba_s",), dma=True)
    p.op("sync", lambda s: s.dma_start(out=cwq_s[:, :], in_=cwq_d),
         wr=("cwq_s",), dma=True, stream="w2")
    p.op("sync", lambda s: s.dma_start(out=negA[:, :], in_=alog_d),
         wr=("negA",), dma=True, stream="w2")
    p.op("sync", lambda s: s.dma_start(out=dtb_s[:, :], in_=dtb_d),
         wr=("dtb_s",), dma=True, stream="w2")
    p.op("scalar", lambda s: s.activation(negA[:, :], negA[:, :], AF.Exp),
         rd=("negA",), wr=("negA",))
    p.op("vector", lambda v: v.tensor_scalar_mul(negA[:, :], negA[:, :], -1.0),
         rd=("negA",), wr=("negA",))

    xq = p.sb("xq", [128, TH], F32)
    xcs = p.sb("xcs", [128, T], F32)
    sqb = p.sb("sqb", [128, T], F32)
    rr = p.sb("rr", [128, T], F32)
    qp = p.ps("qp", [128, 512], F32)
    qp2 = p.ps("qp2", [128, 512], F32)
    ssps = p.ps("ssps", [128, 512], F32)
    bps = p.ps("bps", [6, 512], F32)
    aps = p.ps("aps", [6, 512], F32)
    bg6 = p.sb("bg6", [6, 512], F32)
    bg6b = p.sb("bg6b", [6, 512], F32)

    for pt in range(36):
        for si, (f0, w) in enumerate(SLABS):
            psk, psum = ("qp", qp) if si % 2 == 0 else ("qp2", qp2)
            p.mm_group(psk, [
                (lambda t, kt=kt, pt=pt, f0=f0, w=w, psum=psum: t.matmul(
                    psum[:, 0:w],
                    wq_s[:, kt * 4608 + pt * 128:kt * 4608 + pt * 128 + 128],
                    xn16[:, kt * TH + f0:kt * TH + f0 + w],
                    start=(kt == 0), stop=(kt == 7)),
                 ("wq_s", "xn16")) for kt in range(8)])
            if pt < 24:
                p.op("scalar", lambda s, f0=f0, w=w, psum=psum, psk=psk: s.copy(
                    xq[:, f0:f0 + w], psum[:, 0:w]), rd=(psk,), wr=("xq",))
            else:
                if si > 0:
                    p.op("scalar", lambda s, f0=f0, w=w, psum=psum, psk=psk, pt=pt:
                         s.activation(xcs[:, f0 - 3:f0 - 3 + w], psum[:, 0:w], AF.Silu),
                         rd=(psk,), wr=("xcs",))
        if pt >= 24:
            p.op("sync", lambda s, pt=pt: s.dma_start(
                out=o_gs[(pt - 24) * 128:(pt - 23) * 128, :], in_=xcs[:, :]),
                rd=("xcs",), dma=True, stream="ogs")
            continue
        p.op("vector", lambda v, pt=pt: v.tensor_scalar(
            xcs[:, :], xq[:, 0:T], cwq_s[:, pt * 4:pt * 4 + 1], None, ALU.mult),
            rd=("xq", "cwq_s"), wr=("xcs",))
        for k in range(1, 4):
            p.op("vector", lambda v, pt=pt, k=k: v.scalar_tensor_tensor(
                xcs[:, :], xq[:, k:k + T], cwq_s[:, pt * 4 + k:pt * 4 + k + 1],
                xcs[:, :], ALU.mult, ALU.add),
                rd=("xq", "cwq_s", "xcs"), wr=("xcs",))
        p.op("scalar", lambda s: s.activation(xcs[:, :], xcs[:, :], AF.Silu),
             rd=("xcs",), wr=("xcs",))
        if pt >= 12:
            p.op("sync", lambda s, pt=pt: s.dma_start(
                out=o_v[(pt - 12) * 128:(pt - 11) * 128, :], in_=xcs[:, :]),
                rd=("xcs",), dma=True, stream="ov")
            continue
        p.op("scalar", lambda s: s.activation(sqb[:, :], xcs[:, :], AF.Square),
             rd=("xcs",), wr=("sqb",))
        for sl in range(4):
            p.mm_group("ssps", [
                (lambda t, sl=sl: t.matmul(
                    ssps[:, :], ones16[:, :], sqb[:, sl * 512:(sl + 1) * 512],
                    start=True, stop=True), ("sqb", "ones16"))])
            p.op("scalar", lambda s, sl=sl: s.activation(
                rr[:, sl * 512:(sl + 1) * 512], ssps[:, :], AF.Sqrt, bias=eps12c),
                rd=("ssps", eps12k), wr=("rr",))
        p.op("vector", lambda v: v.reciprocal(rr[:, :], rr[:, :]),
             rd=("rr",), wr=("rr",))
        sc = QSC if pt < 6 else 1.0
        p.op("vector", lambda v, sc=sc: v.scalar_tensor_tensor(
            xcs[:, :], xcs[:, :], sc, rr[:, :], ALU.mult, ALU.mult),
            rd=("xcs", "rr"), wr=("xcs",))
        if pt < 6:
            p.op("sync", lambda s, pt=pt: s.dma_start(
                out=o_q[pt * 128:(pt + 1) * 128, :], in_=xcs[:, :]),
                rd=("xcs",), dma=True, stream="oq")
        else:
            p.op("sync", lambda s, pt=pt: s.dma_start(
                out=o_k[(pt - 6) * 128:(pt - 5) * 128, :], in_=xcs[:, :]),
                rd=("xcs",), dma=True, stream="ok")
    for si, (f0, w) in enumerate(SLABS):
        if si == 0:
            continue
        p.mm_group("bps", [
            (lambda t, kt=kt, f0=f0, w=w: t.matmul(
                bps[:, 0:w], wba_s[:, kt * 16:kt * 16 + 6],
                xn16[:, kt * TH + f0:kt * TH + f0 + w],
                start=(kt == 0), stop=(kt == 7)),
             ("wba_s", "xn16")) for kt in range(8)])
        p.op("scalar", lambda s, w=w: s.activation(bg6[:, 0:w], bps[0:6, 0:w],
                                                   AF.Sigmoid),
             rd=("bps",), wr=("bg6",))
        p.op("sync", lambda s, f0=f0, w=w: s.dma_start(
            out=o_beta[:, f0 - 3:f0 - 3 + w], in_=bg6[0:6, 0:w]),
            rd=("bg6",), dma=True, stream="obeta")
        p.mm_group("aps", [
            (lambda t, kt=kt, f0=f0, w=w: t.matmul(
                aps[:, 0:w], wba_s[:, kt * 16 + 8:kt * 16 + 14],
                xn16[:, kt * TH + f0:kt * TH + f0 + w],
                start=(kt == 0), stop=(kt == 7)),
             ("wba_s", "xn16")) for kt in range(8)])
        p.op("scalar", lambda s, w=w: s.activation(
            bg6[:, 0:w], aps[0:6, 0:w], AF.Abs, bias=dtb_s[0:6, 0:1]),
            rd=("aps", "dtb_s"), wr=("bg6",))
        p.op("scalar", lambda s, w=w: s.activation(bg6[:, 0:w], bg6[:, 0:w],
                                                   AF.Exp, scale=-1.0),
             rd=("bg6",), wr=("bg6",))
        p.op("scalar", lambda s, w=w: s.activation(
            bg6[:, 0:w], bg6[:, 0:w], AF.Ln, bias=onec[0:6, 0:1]),
            rd=("bg6", onek), wr=("bg6",))
        p.op("scalar", lambda s, w=w: s.activation(
            bg6b[:, 0:w], aps[0:6, 0:w], AF.Relu, bias=dtb_s[0:6, 0:1]),
            rd=("aps", "dtb_s"), wr=("bg6b",))
        p.op("vector", lambda v, w=w: v.tensor_add(bg6[:, 0:w], bg6[:, 0:w],
                                                   bg6b[:, 0:w]),
             rd=("bg6", "bg6b"), wr=("bg6",))
        p.op("vector", lambda v, w=w: v.tensor_scalar(
            bg6[:, 0:w], bg6[:, 0:w], negA[0:6, 0:1], None, ALU.mult),
            rd=("bg6", "negA"), wr=("bg6",))
        p.op("sync", lambda s, f0=f0, w=w: s.dma_start(
            out=o_g[:, f0 - 3:f0 - 3 + w], in_=bg6[0:6, 0:w]),
            rd=("bg6",), dma=True, stream="og")
    p.barrier()
    return p


def host_mid(inputs, yg_full, part_full):
    wq = np.asarray(inputs["L1_attn_Wq"], np.float32)
    wk = np.asarray(inputs["L1_attn_Wk"], np.float32)
    wv = np.asarray(inputs["L1_attn_Wv"], np.float32)
    wg = np.asarray(inputs["L1_attn_Wg"], np.float32)
    wqkvg = np.ascontiguousarray(np.concatenate([wq, wk, wv, wg], axis=1))
    wb = np.asarray(inputs["L1_attn_Wb"], np.float32)
    wa = np.asarray(inputs["L1_attn_Wa"], np.float32)
    wba = np.zeros((DM, 16), np.float32)
    wba[:, 0:6] = wb
    wba[:, 8:14] = wa
    wba = np.ascontiguousarray(wba.reshape(8, 128, 16).transpose(1, 0, 2).reshape(128, 128))
    cq = np.asarray(inputs["L1_attn_conv_q"], np.float32)
    ck = np.asarray(inputs["L1_attn_conv_k"], np.float32)
    cv = np.asarray(inputs["L1_attn_conv_v"], np.float32)
    cwq = np.concatenate([cq.reshape(6, 128, 4), ck.reshape(6, 128, 4),
                          cv.reshape(12, 128, 4)], axis=0)
    cwq = np.ascontiguousarray(cwq.transpose(1, 0, 2).reshape(128, 96))
    shared = {
        "wout": np.ascontiguousarray(np.asarray(inputs["L0_attn_W_out"], np.float32)),
        "wqkvg": wqkvg,
        "wba": wba,
        "gwT": np.ascontiguousarray(np.asarray(inputs["L1_ffn_gate_w"], np.float32).T),
        "ebias": np.asarray(inputs["L1_ffn_expert_bias"], np.float32).reshape(1, 8),
        "cwq": cwq,
        "alog": np.asarray(inputs["L1_attn_A_log"], np.float32).reshape(6, 1),
        "dtb": np.asarray(inputs["L1_attn_dt_bias"], np.float32).reshape(6, 1),
    }
    maps = []
    for c in range(8):
        b, hh = c // 2, c % 2
        s0 = hh * T
        yg = np.zeros((DMAMBA, T + 3), np.float32)
        pt = np.zeros((DM, T + 3), np.float32)
        lo = s0 - 3
        clo = max(lo, 0)
        yg[:, clo - lo:] = yg_full[:, b, clo:s0 + T]
        pt[:, clo - lo:] = part_full[:, b, clo:s0 + T]
        m = dict(shared)
        m["yg"] = yg
        m["part"] = pt
        maps.append(m)
    return maps



def build_l1delta():
    nc = bass.Bass("TRN2", target_bir_lowering=False)
    p = Prog(nc)
    NU, C = 3, 128
    NCH = SEQ // C

    qf_d = nc.dram_tensor("qf", [NU * DK, SEQ], F32, kind="ExternalInput").ap()
    kf_d = nc.dram_tensor("kf", [NU * DK, SEQ], F32, kind="ExternalInput").ap()
    kT_d = nc.dram_tensor("kT", [NU * SEQ, DK], F32, kind="ExternalInput").ap()
    vT_d = nc.dram_tensor("vT", [NU * SEQ, DV], F32, kind="ExternalInput").ap()
    beta_d = nc.dram_tensor("betar", [NU, SEQ], F32, kind="ExternalInput").ap()
    g_d = nc.dram_tensor("gr", [NU, SEQ], F32, kind="ExternalInput").ap()
    bdl_d = nc.dram_tensor("bdl", [128, 128], F32, kind="ExternalInput").ap()
    bdlT_d = nc.dram_tensor("bdlT", [128, 128], F32, kind="ExternalInput").ap()
    sbl_d = nc.dram_tensor("sbl", [128, 128], F32, kind="ExternalInput").ap()
    sblT_d = nc.dram_tensor("sblT", [128, 128], F32, kind="ExternalInput").ap()
    trui_d = nc.dram_tensor("trui", [128, 128], F32, kind="ExternalInput").ap()
    eye_d = nc.dram_tensor("eye", [128, 128], F32, kind="ExternalInput").ap()
    o_d = nc.dram_tensor("o_o", [NU * SEQ, DV], F32, kind="ExternalOutput").ap()

    ones_s = p.sb("ones_s", [128, 128], F32, scope="prog")
    one11 = p.sb("one11", [1, 1], F32, scope="prog")
    masks = {}
    for nm, d in [("bdl", bdl_d), ("bdlT", bdlT_d), ("sbl", sbl_d),
                  ("sblT", sblT_d), ("trui", trui_d), ("eye", eye_d)]:
        masks[nm] = p.sb(nm + "_s", [128, 128], F32, scope="prog")
        p.op("sync", lambda s, nm=nm, d=d: s.dma_start(out=masks[nm][:, :], in_=d),
             wr=(nm,), dma=True, stream="msk")
    p.op("vector", lambda v: v.memset(ones_s[:, :], 1.0), wr=("ones_s",))
    p.op("vector", lambda v: v.memset(one11[:, :], 1.0), wr=("one11",))

    brow = p.sb("brow", [1, SEQ], F32, scope="prog")
    grow = p.sb("grow", [1, SEQ], F32, scope="prog")
    gcum = p.sb("gcum", [1, SEQ], F32, scope="prog")
    rst = p.sb("rst", [1, SEQ], F32, scope="prog")
    p.op("vector", lambda v: v.memset(rst[:, :], 1.0), wr=("rst",))
    p.op("vector", lambda v: v.memset(
        bass.AP(rst, 0, [[SEQ, 1], [C, NCH]]), 0.0), rd=("rst",), wr=("rst",))

    S0 = p.sb("S0", [128, DV], F32, scope="prog")
    S016 = p.sb("S016", [128, DV], BF16, scope="prog")
    qc = p.sb("qc", [128, C], BF16)
    kc = p.sb("kc", [128, C], BF16)
    ktc = p.sb("ktc", [128, DK], F32)
    vtc = p.sb("vtc", [128, DV], F32)
    rc16 = p.sb("rc16", [128, DV], BF16)
    bcol = p.sb("bcol", [128, 1], F32)
    gcol = p.sb("gcol", [128, 1], F32)
    lamc = p.sb("lamc", [128, 1], F32)
    nbl = p.sb("nbl", [128, 1], F32)
    gCc = p.sb("gCc", [128, 1], F32)
    ecol = p.sb("ecol", [128, 1], F32)
    dec = p.sb("dec", [128, 128], F32)
    dec2 = p.sb("dec2", [128, 128], F32)
    dd = p.sb("dd", [128, 128], F32)
    ddt = p.sb("ddt", [128, 128], F32)
    aqt = p.sb("aqt", [128, 128], BF16)
    Dm = p.sb("Dm", [128, 128], BF16)
    DmT = p.sb("DmT", [128, 128], BF16)
    Nm = p.sb("Nm", [128, 128], BF16)
    NmT = p.sb("NmT", [128, 128], BF16)
    Wm = p.sb("Wm", [128, 128], BF16)
    WmT = p.sb("WmT", [128, 128], BF16)
    Zm = p.sb("Zm", [128, 128], BF16)
    ZmT = p.sb("ZmT", [128, 128], BF16)
    Z2 = p.sb("Z2", [128, 128], BF16)
    Z2T = p.sb("Z2T", [128, 128], BF16)
    Z4T = p.sb("Z4T", [128, 128], BF16)
    tmpA = p.sb("tmpA", [128, 128], BF16)
    tmpB = p.sb("tmpB", [128, 128], BF16)
    U = p.sb("U", [128, DV], BF16)
    Uf = p.sb("Uf", [128, DV], F32)
    ktil = p.sb("ktil", [128, DK], BF16)
    qlam = p.sb("qlam", [128, C], BF16)
    ot = p.sb("ot", [128, DV], F32)
    psA = p.ps("psA", [128, DV], F32)
    psB = p.ps("psB", [128, DV], F32)
    psC = p.ps("psC", [128, 128], F32)
    psD = p.ps("psD", [128, 128], F32)
    psE = p.ps("psE", [128, 2], F32)

    def mm1(psk, psum, stat, statk, mov, movk):
        p.mm_group(psk, [
            (lambda t, psum=psum, stat=stat, mov=mov: t.matmul(
                psum[:, :], stat, mov, start=True, stop=True), (statk, movk))])

    def evac16(dst, dstk, psum, psk, eng="vector"):
        if eng == "vector":
            p.op("vector", lambda v: v.tensor_copy(dst[:, :], psum[:, :]),
                 rd=(psk,), wr=(dstk,))
        else:
            p.op("scalar", lambda s: s.copy(dst[:, :], psum[:, :]),
                 rd=(psk,), wr=(dstk,))

    for u in range(NU):
        p.op("sync", lambda s, u=u: s.dma_start(
            out=brow[:, :], in_=beta_d[u:u + 1, :]), wr=("brow",), dma=True,
            stream="bg")
        p.op("sync", lambda s, u=u: s.dma_start(
            out=grow[:, :], in_=g_d[u:u + 1, :]), wr=("grow",), dma=True,
            stream="bg")
        p.op("vector", lambda v: v.tensor_tensor_scan(
            gcum[:, :], rst[:, :], grow[:, :], 0.0, ALU.mult, ALU.add),
            rd=("rst", "grow"), wr=("gcum",))
        p.op("vector", lambda v: v.memset(S0[:, :], 0.0), wr=("S0",))
        p.op("vector", lambda v: v.memset(S016[:, :], 0.0), wr=("S016",))
        for ci in range(NCH):
            c0 = ci * C
            p.op("gpsimd", lambda g, u=u, c0=c0: g.dma_start(
                out=qc[:, :], in_=qf_d[u * DK:(u + 1) * DK, c0:c0 + C]),
                wr=("qc",), dma=True, stream="ld0")
            p.op("gpsimd", lambda g, u=u, c0=c0: g.dma_start(
                out=kc[:, :], in_=kf_d[u * DK:(u + 1) * DK, c0:c0 + C]),
                wr=("kc",), dma=True, stream="ld1")
            p.op("sync", lambda s, u=u, c0=c0: s.dma_start(
                out=ktc[:, :], in_=kT_d[u * SEQ + c0:u * SEQ + c0 + C, :]),
                wr=("ktc",), dma=True, stream="ld2")
            p.op("sync", lambda s, u=u, c0=c0: s.dma_start(
                out=vtc[:, :], in_=vT_d[u * SEQ + c0:u * SEQ + c0 + C, :]),
                wr=("vtc",), dma=True, stream="ld3")
            p.mm_group("psE", [
                (lambda t, c0=c0: t.matmul(psE[:, 0:1], brow[0:1, c0:c0 + C],
                                           one11[:, :], start=True, stop=True),
                 ("brow", "one11")),
                (lambda t, c0=c0: t.matmul(psE[:, 1:2], gcum[0:1, c0:c0 + C],
                                           one11[:, :], start=True, stop=True),
                 ("gcum", "one11"))])
            p.op("vector", lambda v: v.tensor_copy(bcol[:, :], psE[:, 0:1]),
                 rd=("psE",), wr=("bcol",))
            p.op("vector", lambda v: v.tensor_copy(gcol[:, :], psE[:, 1:2]),
                 rd=("psE",), wr=("gcol",))
            p.mm_group("psE", [
                (lambda t, c0=c0: t.matmul(
                    psE[:, 0:1], ones_s[0:1, :],
                    gcum[0:1, c0 + C - 1:c0 + C], start=True, stop=True),
                 ("gcum", "ones_s"))])
            p.op("vector", lambda v: v.tensor_copy(gCc[:, :], psE[:, 0:1]),
                 rd=("psE",), wr=("gCc",))
            p.op("scalar", lambda s: s.activation(lamc[:, :], gcol[:, :], AF.Exp),
                 rd=("gcol",), wr=("lamc",))
            p.op("vector", lambda v: v.tensor_mul(nbl[:, :], bcol[:, :], lamc[:, :]),
                 rd=("bcol", "lamc"), wr=("nbl",))
            p.op("vector", lambda v: v.tensor_scalar_mul(nbl[:, :], nbl[:, :], -1.0),
                 rd=("nbl",), wr=("nbl",))
            p.op("scalar", lambda s: s.activation(
                ecol[:, :], gcol[:, :], AF.Exp, scale=-1.0, bias=gCc[:, 0:1]),
                rd=("gcol", "gCc"), wr=("ecol",))
            mm1("psC", psC, ones_s[0:1, :], "ones_s", gcum[0:1, c0:c0 + C], "gcum")
            p.op("vector", lambda v: v.tensor_scalar(
                dec[:, :], psC[:, :], gcol[:, 0:1], 0.0, ALU.subtract, ALU.min),
                rd=("psC", "gcol"), wr=("dec",))
            p.op("scalar", lambda s: s.activation(dec[:, :], dec[:, :], AF.Exp),
                 rd=("dec",), wr=("dec",))
            p.op("vector", lambda v: v.tensor_scalar(
                dec2[:, :], psC[:, :], gcol[:, 0:1], None, ALU.subtract),
                rd=("psC", "gcol"), wr=("dec2",))
            p.op("vector", lambda v: v.tensor_scalar(
                dec2[:, :], dec2[:, :], -1.0, 0.0, ALU.mult, ALU.min),
                rd=("dec2",), wr=("dec2",))
            p.op("scalar", lambda s: s.activation(dec2[:, :], dec2[:, :], AF.Exp),
                 rd=("dec2",), wr=("dec2",))
            mm1("psD", psD, ones_s[0:1, :], "ones_s", brow[0:1, c0:c0 + C], "brow")
            mm1("psC", psC, kc[:, :], "kc", kc[:, :], "kc")
            p.op("vector", lambda v: v.tensor_mul(dd[:, :], psC[:, :], dec2[:, :]),
                 rd=("psC", "dec2"), wr=("dd",))
            p.op("vector", lambda v: v.tensor_scalar(
                dd[:, :], dd[:, :], bcol[:, 0:1], None, ALU.mult),
                rd=("dd", "bcol"), wr=("dd",))
            p.op("vector", lambda v: v.tensor_mul(ddt[:, :], psC[:, :], dec[:, :]),
                 rd=("psC", "dec"), wr=("ddt",))
            p.op("vector", lambda v: v.tensor_mul(ddt[:, :], ddt[:, :], psD[:, :]),
                 rd=("ddt", "psD"), wr=("ddt",))
            for dst, dstk, srcn, mk in [(Dm, "Dm", "dd", "bdl"), (Nm, "Nm", "dd", "sbl"),
                                        (DmT, "DmT", "ddt", "bdlT"),
                                        (NmT, "NmT", "ddt", "sblT")]:
                srcb = dd if srcn == "dd" else ddt
                p.op("vector", lambda v, dst=dst, srcb=srcb, mk=mk: v.tensor_mul(
                    dst[:, :], srcb[:, :], masks[mk][:, :]),
                    rd=(srcn, mk), wr=(dstk,))
            mm1("psC", psC, kc[:, :], "kc", qc[:, :], "qc")
            p.op("vector", lambda v: v.tensor_mul(aqt[:, :], psC[:, :], dec[:, :]),
                 rd=("psC", "dec"), wr=("aqt",))
            p.op("vector", lambda v: v.tensor_mul(aqt[:, :], aqt[:, :],
                                                  masks["trui"][:, :]),
                 rd=("aqt", "trui"), wr=("aqt",))
            p.op("vector", lambda v: v.tensor_scalar(
                vtc[:, :], vtc[:, :], bcol[:, 0:1], None, ALU.mult),
                rd=("vtc", "bcol"), wr=("vtc",))
            mm1("psA", psA, kc[:, :], "kc", S016[:, :], "S016")
            p.op("vector", lambda v: v.scalar_tensor_tensor(
                Uf[:, :], psA[:, :], nbl[:, 0:1], vtc[:, :], ALU.mult, ALU.add),
                rd=("psA", "nbl", "vtc"), wr=("Uf",))
            p.op("vector", lambda v: v.tensor_copy(rc16[:, :], Uf[:, :]),
                 rd=("Uf",), wr=("rc16",))
            p.op("vector", lambda v: v.tensor_sub(Wm[:, :], masks["eye"][:, :],
                                                  Dm[:, :]),
                 rd=("eye", "Dm"), wr=("Wm",))
            p.op("vector", lambda v: v.tensor_sub(WmT[:, :], masks["eye"][:, :],
                                                  DmT[:, :]),
                 rd=("eye", "DmT"), wr=("WmT",))
            pk, pkT, pkk, pkTk = Dm, DmT, "Dm", "DmT"
            for lev in range(3):
                mm1("psC", psC, pkT[:, :], pkTk, pk[:, :], pkk)
                mm1("psD", psD, pk[:, :], pkk, pkT[:, :], pkTk)
                p.op("vector", lambda v: v.tensor_add(tmpA[:, :], psC[:, :],
                                                      masks["eye"][:, :]),
                     rd=("psC", "eye"), wr=("tmpA",))
                p.op("vector", lambda v: v.tensor_add(tmpB[:, :], psD[:, :],
                                                      masks["eye"][:, :]),
                     rd=("psD", "eye"), wr=("tmpB",))
                mm1("psC", psC, WmT[:, :], "WmT", tmpA[:, :], "tmpA")
                mm1("psD", psD, tmpA[:, :], "tmpA", WmT[:, :], "WmT")
                evac16(Wm, "Wm", psC, "psC")
                evac16(WmT, "WmT", psD, "psD")
                if lev < 2:
                    mm1("psC", psC, pkT[:, :], pkTk, pk[:, :], pkk)
                    mm1("psD", psD, pk[:, :], pkk, pkT[:, :], pkTk)
                    nk, nkT = (tmpA, tmpB) if False else (None, None)
                    if lev == 0:
                        evac16(Z2, "Z2", psC, "psC")
                        evac16(Z2T, "Z2T", psD, "psD")
                        pk, pkT, pkk, pkTk = Z2, Z2T, "Z2", "Z2T"
                    else:
                        evac16(Z4T, "Z4T", psC, "psC")
                        evac16(tmpA, "tmpA", psD, "psD")
                        pk, pkT, pkk, pkTk = Z4T, tmpA, "Z4T", "tmpA"
            mm1("psC", psC, WmT[:, :], "WmT", Nm[:, :], "Nm")
            mm1("psD", psD, Nm[:, :], "Nm", WmT[:, :], "WmT")
            evac16(Zm, "Zm", psC, "psC")
            evac16(ZmT, "ZmT", psD, "psD")
            mm1("psC", psC, ZmT[:, :], "ZmT", Zm[:, :], "Zm")
            mm1("psD", psD, Zm[:, :], "Zm", ZmT[:, :], "ZmT")
            evac16(Z2, "Z2", psC, "psC")
            evac16(Z2T, "Z2T", psD, "psD")
            mm1("psC", psC, Z2[:, :], "Z2", Z2T[:, :], "Z2T")
            evac16(Z4T, "Z4T", psC, "psC")
            mm1("psA", psA, WmT[:, :], "WmT", rc16[:, :], "rc16")
            p.op("vector", lambda v: v.tensor_copy(Uf[:, :], psA[:, :]),
                 rd=("psA",), wr=("Uf",))
            p.op("vector", lambda v: v.tensor_copy(U[:, :], Uf[:, :]),
                 rd=("Uf",), wr=("U",))
            for statb, statk, sign in [(Z4T, "Z4T", 1), (Z2T, "Z2T", 1),
                                       (ZmT, "ZmT", -1)]:
                mm1("psA", psA, statb[:, :], statk, U[:, :], "U")
                if sign > 0:
                    p.op("vector", lambda v: v.tensor_add(Uf[:, :], Uf[:, :],
                                                          psA[:, :]),
                         rd=("psA", "Uf"), wr=("Uf",))
                else:
                    p.op("vector", lambda v: v.tensor_sub(Uf[:, :], Uf[:, :],
                                                          psA[:, :]),
                         rd=("psA", "Uf"), wr=("Uf",))
                p.op("vector", lambda v: v.tensor_copy(U[:, :], Uf[:, :]),
                     rd=("Uf",), wr=("U",))
            mm1("psC", psC, ones_s[0:1, :], "ones_s", gcum[0:1, c0:c0 + C], "gcum")
            p.op("scalar", lambda s: s.activation(dec[:, :], psC[:, :], AF.Exp),
                 rd=("psC",), wr=("dec",))
            p.op("vector", lambda v: v.tensor_mul(qlam[:, :], qc[:, :], dec[:, :]),
                 rd=("qc", "dec"), wr=("qlam",))
            p.mm_group("psB", [
                (lambda t: t.matmul(psB[:, :], aqt[:, :], U[:, :],
                                    start=True, stop=False), ("aqt", "U")),
                (lambda t: t.matmul(psB[:, :], qlam[:, :], S016[:, :],
                                    start=False, stop=True), ("qlam", "S016"))])
            p.op("scalar", lambda s: s.copy(ot[:, :], psB[:, :]),
                 rd=("psB",), wr=("ot",))
            p.op("sync", lambda s, u=u, c0=c0: s.dma_start(
                out=o_d[u * SEQ + c0:u * SEQ + c0 + C, :], in_=ot[:, :]),
                rd=("ot",), dma=True, stream="oo")
            p.op("vector", lambda v: v.tensor_scalar(
                ktil[:, :], ktc[:, :], ecol[:, 0:1], None, ALU.mult),
                rd=("ktc", "ecol"), wr=("ktil",))
            mm1("psA", psA, ktil[:, :], "ktil", U[:, :], "U")
            p.op("scalar", lambda s: s.activation(lamc[:, :], gCc[:, :], AF.Exp),
                 rd=("gCc",), wr=("lamc",))
            p.op("vector", lambda v: v.scalar_tensor_tensor(
                S0[:, :], S0[:, :], lamc[:, 0:1], psA[:, :], ALU.mult, ALU.add),
                rd=("psA", "lamc", "S0"), wr=("S0",))
            p.op("vector", lambda v: v.tensor_copy(S016[:, :], S0[:, :]),
                 rd=("S0",), wr=("S016",))
    p.barrier()
    return p


def host_l1delta(q_all, k_all, v_all, beta_all, g_all):
    tril_s = np.tril(np.ones((128, 128), np.float32), -1)
    blockm = np.zeros((128, 128), np.float32)
    for r in range(8):
        blockm[r * 16:(r + 1) * 16, r * 16:(r + 1) * 16] = 1
    bdl = tril_s * blockm
    sbl = tril_s - bdl
    trui = np.triu(np.ones((128, 128), np.float32), 0)
    eye = np.eye(128, dtype=np.float32)
    maps = []
    for c in range(8):
        units = [(g // H, g % H) for g in range(3 * c, 3 * c + 3)]
        qf = np.concatenate([q_all[b, h * DK:(h + 1) * DK] for b, h in units], 0)
        kf = np.concatenate([k_all[b, h * DK:(h + 1) * DK] for b, h in units], 0)
        kT = np.concatenate([k_all[b, h * DK:(h + 1) * DK].T for b, h in units], 0)
        vT = np.concatenate([v_all[b, h * DV:(h + 1) * DV].T for b, h in units], 0)
        br = np.stack([beta_all[b, h] for b, h in units], 0)
        gr = np.stack([g_all[b, h] for b, h in units], 0)
        maps.append({
            "qf": np.ascontiguousarray(qf), "kf": np.ascontiguousarray(kf),
            "kT": np.ascontiguousarray(kT), "vT": np.ascontiguousarray(vT),
            "betar": np.ascontiguousarray(br), "gr": np.ascontiguousarray(gr),
            "bdl": bdl, "bdlT": np.ascontiguousarray(bdl.T),
            "sbl": sbl, "sblT": np.ascontiguousarray(sbl.T),
            "trui": trui, "eye": eye})
    return maps



def build_l1post():
    nc = bass.Bass("TRN2", target_bir_lowering=False)
    p = Prog(nc)

    o_dr = nc.dram_tensor("o_in", [VD, T], F32, kind="ExternalInput").ap()
    gs_dr = nc.dram_tensor("gs", [VD, T], F32, kind="ExternalInput").ap()
    out1_dr = nc.dram_tensor("out1", [DM, T], F32, kind="ExternalInput").ap()
    xn1_dr = nc.dram_tensor("xn1", [DM, T], BF16, kind="ExternalInput").ap()
    mw_dr = nc.dram_tensor("mw", [8, T], F32, kind="ExternalInput").ap()
    nw_dr = nc.dram_tensor("nw", [128, 2], F32, kind="ExternalInput").ap()
    wo_dr = nc.dram_tensor("wo", [VD, DM], F32, kind="ExternalInput").ap()
    wup_dr = nc.dram_tensor("wup", [9 * DM, HID], F32, kind="ExternalInput").ap()
    wdn_dr = nc.dram_tensor("wdn", [9 * HID, DM], F32, kind="ExternalInput").ap()
    o_fin = nc.dram_tensor("o_fin", [DM, T], F32, kind="ExternalOutput").ap()

    ones_s = p.sb("ones_s", [128, 128], F32, scope="prog")
    p.op("vector", lambda v: v.memset(ones_s[:, :], 1.0), wr=("ones_s",))
    res1 = p.sb("res1", [128, 8 * T], F32, scope="prog")
    xn1b = p.sb("xn1b", [128, 8 * T], BF16, scope="prog")
    for d in range(8):
        p.op("sync", lambda s, d=d: s.dma_start(
            out=xn1b[:, d * T:(d + 1) * T], in_=xn1_dr[d * 128:(d + 1) * 128, :]),
            wr=("xn1b",), dma=True, stream="xin")
        p.op("sync", lambda s, d=d: s.dma_start(
            out=res1[:, d * T:(d + 1) * T], in_=out1_dr[d * 128:(d + 1) * 128, :]),
            wr=("res1",), dma=True, stream="xin2")
    epsc, epsk = p.constcol(1e-5)

    wo_s = p.sb("wo_s", [128, 12 * DM], BF16)
    nw_s = p.sb("nw_s", [128, 2], F32)
    ov = p.sb("ov", [128, 2 * T], F32)
    gs16 = p.sb("gs16", [128, 2 * T], BF16)
    an16 = p.sb("an16", [128, 2 * T], BF16)
    sq = p.sb("sq", [128, T], F32)
    rrb = p.sb("rrb", [128, T], F32)
    ssps = p.ps("ssps", [128, 512], F32)
    aps = p.ps("aps", [128, 512], F32)
    aps2 = p.ps("aps2", [128, 512], F32)
    for kt in range(12):
        p.op("gpsimd", lambda g, kt=kt: g.dma_start(
            out=wo_s[:, kt * DM:(kt + 1) * DM],
            in_=wo_dr[kt * 128:(kt + 1) * 128, :]), wr=("wo_s",), dma=True)
    p.op("sync", lambda s: s.dma_start(out=nw_s[:, :], in_=nw_dr),
         wr=("nw_s",), dma=True, stream="xin")
    for h in range(H):
        for j in range(2):
            p.op("sync", lambda s, h=h, j=j: s.dma_start(
                out=ov[:, j * T:(j + 1) * T],
                in_=o_dr[(2 * h + j) * 128:(2 * h + j + 1) * 128, :]),
                wr=("ov",), dma=True, stream="oin")
            p.op("gpsimd", lambda g, h=h, j=j: g.dma_start(
                out=gs16[:, j * T:(j + 1) * T],
                in_=gs_dr[(2 * h + j) * 128:(2 * h + j + 1) * 128, :]),
                wr=("gs16",), dma=True, stream="gin")
        for sl in range(4):
            f0 = sl * 512
            for j in range(2):
                p.op("scalar", lambda s, j=j, f0=f0: s.activation(
                    sq[:, f0:f0 + 512], ov[:, j * T + f0:j * T + f0 + 512],
                    AF.Square), rd=("ov",), wr=("sq",))
                p.mm_group("ssps", [
                    (lambda t, f0=f0, j=j: t.matmul(
                        ssps[:, :], ones_s[:, :], sq[:, f0:f0 + 512],
                        start=(j == 0), stop=(j == 1)), ("sq", "ones_s"))])
            p.op("scalar", lambda s, f0=f0: s.activation(
                rrb[:, f0:f0 + 512], ssps[:, :], AF.Sqrt, bias=epsc,
                scale=1.0 / DV), rd=("ssps", epsk), wr=("rrb",))
        p.op("vector", lambda v: v.reciprocal(rrb[:, :], rrb[:, :]),
             rd=("rrb",), wr=("rrb",))
        for j in range(2):
            p.op("vector", lambda v, j=j: v.scalar_tensor_tensor(
                ov[:, j * T:(j + 1) * T], ov[:, j * T:(j + 1) * T],
                nw_s[:, j:j + 1], rrb[:, :], ALU.mult, ALU.mult),
                rd=("ov", "nw_s", "rrb"), wr=("ov",))
            p.op("vector", lambda v, j=j: v.tensor_mul(
                an16[:, j * T:(j + 1) * T], ov[:, j * T:(j + 1) * T],
                gs16[:, j * T:(j + 1) * T]),
                rd=("ov", "gs16"), wr=("an16",))
        for sl in range(4):
            f0 = sl * 512
            for d in range(8):
                psk, psum = ("aps", aps) if d % 2 == 0 else ("aps2", aps2)
                p.mm_group(psk, [
                    (lambda t, d=d, f0=f0, h=h, j=j, psum=psum: t.matmul(
                        psum[:, :],
                        wo_s[:, (2 * h + j) * DM + d * 128:(2 * h + j) * DM + d * 128 + 128],
                        an16[:, j * T + f0:j * T + f0 + 512],
                        start=(j == 0), stop=(j == 1)), ("wo_s", "an16"))
                    for j in range(2)])
                p.op("vector", lambda v, d=d, f0=f0, psum=psum, psk=psk: v.tensor_add(
                    res1[:, d * T + f0:d * T + f0 + 512],
                    res1[:, d * T + f0:d * T + f0 + 512], psum[:, :]),
                    rd=(psk, "res1"), wr=("res1",))
    p.barrier()

    CS = 256
    NSL = T // CS
    wup_s = [p.sb(f"wup{i}", [128, 8 * HID], BF16) for i in range(2)]
    wdn_s = [p.sb(f"wdn{i}", [128, 11 * DM], BF16) for i in range(2)]
    webs = p.sb("webs", [128, 256], F32)
    xe16 = p.sb("xe16", [128, 8 * CS], BF16)
    hps = p.ps("hps", [128, CS], F32)
    hps2 = p.ps("hps2", [128, CS], F32)
    dps = p.ps("dps", [128, CS], F32)
    dps2 = p.ps("dps2", [128, CS], F32)
    h16 = p.sb("h16", [128, 11 * CS], BF16)
    hrl = p.sb("hrl", [128, CS], BF16)

    def load_expert(e, buf):
        for kt in range(8):
            p.op("gpsimd", lambda g, e=e, kt=kt, buf=buf: g.dma_start(
                out=wup_s[buf][:, kt * HID:(kt + 1) * HID],
                in_=wup_dr[e * DM + kt * 128:e * DM + kt * 128 + 128, :]),
                wr=(f"wup{buf}",), dma=True, stream="wup")
        for kt in range(11):
            p.op("gpsimd", lambda g, e=e, kt=kt, buf=buf: g.dma_start(
                out=wdn_s[buf][:, kt * DM:(kt + 1) * DM],
                in_=wdn_dr[e * HID + kt * 128:e * HID + kt * 128 + 128, :]),
                wr=(f"wdn{buf}",), dma=True, stream="wdn")

    load_expert(0, 0)
    for e in range(9):
        buf = e % 2
        if e < 8:
            load_expert(e + 1, (e + 1) % 2)
        for sl in range(NSL):
            f0 = sl * CS
            if e < 8:
                p.op("sync", lambda s, e=e, f0=f0: s.dma_start(
                    out=webs[:, 0:CS],
                    in_=mw_dr[e:e + 1, f0:f0 + CS].partition_broadcast(128)),
                    wr=("webs",), dma=True, stream="web")
                for d in range(8):
                    p.op("vector", lambda v, d=d, f0=f0: v.tensor_mul(
                        xe16[:, d * CS:(d + 1) * CS],
                        xn1b[:, d * T + f0:d * T + f0 + CS], webs[:, 0:CS]),
                        rd=("xn1b", "webs"), wr=("xe16",))
            for pt in range(11):
                psk, psum = ("hps", hps) if pt % 2 == 0 else ("hps2", hps2)
                if e < 8:
                    p.mm_group(psk, [
                        (lambda t, kt=kt, pt=pt, psum=psum, buf=buf: t.matmul(
                            psum[:, :],
                            wup_s[buf][:, kt * HID + pt * 128:kt * HID + pt * 128 + 128],
                            xe16[:, kt * CS:(kt + 1) * CS],
                            start=(kt == 0), stop=(kt == 7)),
                         (f"wup{buf}", "xe16")) for kt in range(8)])
                else:
                    p.mm_group(psk, [
                        (lambda t, kt=kt, pt=pt, psum=psum, buf=buf, f0=f0: t.matmul(
                            psum[:, :],
                            wup_s[buf][:, kt * HID + pt * 128:kt * HID + pt * 128 + 128],
                            xn1b[:, kt * T + f0:kt * T + f0 + CS],
                            start=(kt == 0), stop=(kt == 7)),
                         (f"wup{buf}", "xn1b")) for kt in range(8)])
                p.op("vector", lambda v, pt=pt, psum=psum, psk=psk: v.tensor_scalar_max(
                    hrl[:, :], psum[:, :], 0.0), rd=(psk,), wr=("hrl",))
                p.op("scalar", lambda s, pt=pt: s.activation(
                    h16[:, pt * CS:(pt + 1) * CS], hrl[:, :], AF.Square),
                    rd=("hrl",), wr=("h16",))
            for d in range(8):
                psk, psum = ("dps", dps) if d % 2 == 0 else ("dps2", dps2)
                p.mm_group(psk, [
                    (lambda t, kt=kt, d=d, psum=psum, buf=buf: t.matmul(
                        psum[:, :],
                        wdn_s[buf][:, kt * DM + d * 128:kt * DM + d * 128 + 128],
                        h16[:, kt * CS:(kt + 1) * CS],
                        start=(kt == 0), stop=(kt == 10)),
                     (f"wdn{buf}", "h16")) for kt in range(11)])
                p.op("vector", lambda v, d=d, f0=f0, psum=psum, psk=psk: v.tensor_add(
                    res1[:, d * T + f0:d * T + f0 + CS],
                    res1[:, d * T + f0:d * T + f0 + CS], psum[:, :]),
                    rd=(psk, "res1"), wr=("res1",))
    for d in range(8):
        p.op("sync", lambda s, d=d: s.dma_start(
            out=o_fin[d * 128:(d + 1) * 128, :], in_=res1[:, d * T:(d + 1) * T]),
            rd=("res1",), dma=True, stream="fin")
    p.barrier()
    return p


def host_l1post(inputs, o_fm, gs_cores, out1_cores, xn1_cores, mw_cores):
    nw = np.asarray(inputs["L1_attn_norm_w"], np.float32)
    nw_s = np.ascontiguousarray(nw.reshape(2, 128).T)
    wup = np.asarray(inputs["L1_ffn_w_up"], np.float32)
    wdn = np.asarray(inputs["L1_ffn_w_down"], np.float32)
    sup = np.asarray(inputs["L1_ffn_shared_w_up"], np.float32)
    sdn = np.asarray(inputs["L1_ffn_shared_w_down"], np.float32)
    wupT = np.concatenate([wup.transpose(0, 2, 1).reshape(8 * DM, HID),
                           sup.T], axis=0)
    wdnT = np.concatenate([wdn.transpose(0, 2, 1).reshape(8 * HID, DM),
                           sdn.T], axis=0)
    wo = np.ascontiguousarray(np.asarray(inputs["L1_attn_Wo"], np.float32))
    maps = []
    for c in range(8):
        b, hh = c // 2, c % 2
        s0 = hh * T
        maps.append({
            "o_in": np.ascontiguousarray(o_fm[b, :, s0:s0 + T]),
            "gs": gs_cores[c], "out1": out1_cores[c], "xn1": xn1_cores[c],
            "mw": mw_cores[c], "nw": nw_s, "wo": wo,
            "wup": np.ascontiguousarray(wupT), "wdn": np.ascontiguousarray(wdnT)})
    return maps



RG_PAIRS = [[0, 1], [2, 3], [4, 5], [6, 7]]


def build_fused():
    nc = bass.Bass("TRN2", target_bir_lowering=False)
    p = Prog(nc)
    TH = T + 3
    QSC = float(DK) ** -0.5

    xh = nc.dram_tensor("xh", [DM, TH], F32, kind="ExternalInput").ap()
    w_in = nc.dram_tensor("w_in", [DM, 2 * DMAMBA], F32, kind="ExternalInput").ap()
    cwT = nc.dram_tensor("cwT", [128, 64], F32, kind="ExternalInput").ap()
    cb = nc.dram_tensor("cb", [128, 16], F32, kind="ExternalInput").ap()
    w_xa = nc.dram_tensor("w_xa", [DMAMBA, 128], F32, kind="ExternalInput").ap()
    w_xc = nc.dram_tensor("w_xc", [DMAMBA, 16], F32, kind="ExternalInput").ap()
    w_dt = nc.dram_tensor("w_dt", [DTRANK, DMAMBA], F32, kind="ExternalInput").ap()
    b_dt = nc.dram_tensor("b_dt", [128, 16], F32, kind="ExternalInput").ap()
    w1T = nc.dram_tensor("w1T", [DM, FF0], F32, kind="ExternalInput").ap()
    b1d = nc.dram_tensor("b1", [128, 32], F32, kind="ExternalInput").ap()
    w2T = nc.dram_tensor("w2T", [FF0, DM], F32, kind="ExternalInput").ap()
    b2d = nc.dram_tensor("b2", [128, 8], F32, kind="ExternalInput").ap()
    alog0 = nc.dram_tensor("alog0", [128, 256], F32, kind="ExternalInput").ap()
    dvec = nc.dram_tensor("dvec", [128, 16], F32, kind="ExternalInput").ap()
    wout_d = nc.dram_tensor("wout", [DMAMBA, DM], F32, kind="ExternalInput").ap()
    wqkvg_d = nc.dram_tensor("wqkvg", [DM, 4608], F32, kind="ExternalInput").ap()
    wba_d = nc.dram_tensor("wba", [128, 8 * 16], F32, kind="ExternalInput").ap()
    gwT_d = nc.dram_tensor("gwT", [DM, 8], F32, kind="ExternalInput").ap()
    ebias_d = nc.dram_tensor("ebias", [1, 8], F32, kind="ExternalInput").ap()
    cwq_d = nc.dram_tensor("cwq", [128, 4 * 24], F32, kind="ExternalInput").ap()
    alog1 = nc.dram_tensor("alog1", [6, 1], F32, kind="ExternalInput").ap()
    dtb_d = nc.dram_tensor("dtb", [6, 1], F32, kind="ExternalInput").ap()
    nw_dr = nc.dram_tensor("nw", [128, 2], F32, kind="ExternalInput").ap()
    wo_dr = nc.dram_tensor("wo", [VD, DM], F32, kind="ExternalInput").ap()
    wup_dr = nc.dram_tensor("wup", [9 * DM, HID], F32, kind="ExternalInput").ap()
    wdn_dr = nc.dram_tensor("wdn", [9 * HID, DM], F32, kind="ExternalInput").ap()
    bdl_d = nc.dram_tensor("bdl", [128, 128], F32, kind="ExternalInput").ap()
    bdlT_d = nc.dram_tensor("bdlT", [128, 128], F32, kind="ExternalInput").ap()
    sbl_d = nc.dram_tensor("sbl", [128, 128], F32, kind="ExternalInput").ap()
    sblT_d = nc.dram_tensor("sblT", [128, 128], F32, kind="ExternalInput").ap()
    trui_d = nc.dram_tensor("trui", [128, 128], F32, kind="ExternalInput").ap()
    eye_d = nc.dram_tensor("eye", [128, 128], F32, kind="ExternalInput").ap()
    lrow_d = nc.dram_tensor("lrowd", [128, 128], F32, kind="ExternalInput").ap()
    fc1_d = nc.dram_tensor("fc1", [128, 1], F32, kind="ExternalInput").ap()
    fc0_d = nc.dram_tensor("fc0", [128, 1], F32, kind="ExternalInput").ap()
    o_fin = nc.dram_tensor("o_fin", [DM, T], F32, kind="ExternalOutput").ap()

    d_dt = nc.dram_tensor("d_dt", [1024, T], BF16).ap()
    d_xmc = nc.dram_tensor("d_xmc", [1024, T], BF16).ap()
    d_sz = nc.dram_tensor("d_sz", [1024, T], BF16).ap()
    d_dtx = nc.dram_tensor("d_dtx", [1024, T], BF16).ap()
    d_xmcx = nc.dram_tensor("d_xmcx", [1024, T], BF16).ap()
    d_szx = nc.dram_tensor("d_szx", [1024, T], BF16).ap()
    exo_dt = nc.dram_tensor("exo_dt", [2048, T], BF16).ap()
    exo_xmc = nc.dram_tensor("exo_xmc", [2048, T], BF16).ap()
    exo_sz = nc.dram_tensor("exo_sz", [2048, T], BF16).ap()
    d_bcg = nc.dram_tensor("d_bcg", [32, 2 * T], BF16).ap()
    d_ygF = nc.dram_tensor("d_ygF", [1024, 2 * T], BF16).ap()
    d_ygoF = nc.dram_tensor("d_ygoF", [2048, 2 * T], BF16).ap()
    d_part = nc.dram_tensor("d_part", [DM, T], F32).ap()
    d_out1 = nc.dram_tensor("d_out1", [DM, T], F32).ap()
    d_xn1 = nc.dram_tensor("d_xn1", [DM, T], BF16).ap()
    d_gs = nc.dram_tensor("d_gs", [VD, T], BF16).ap()
    d_mw = nc.dram_tensor("d_mw", [8, T], F32).ap()
    d_qloc = nc.dram_tensor("d_qloc", [128, 3 * T], BF16).ap()
    d_kloc = nc.dram_tensor("d_kloc", [128, 3 * T], BF16).ap()
    d_vloc = nc.dram_tensor("d_vloc", [128, 6 * T], BF16).ap()
    d_qx = nc.dram_tensor("d_qx", [128, 3 * T], BF16).ap()
    d_kx = nc.dram_tensor("d_kx", [128, 3 * T], BF16).ap()
    d_vx = nc.dram_tensor("d_vx", [128, 6 * T], BF16).ap()
    d_bgx = nc.dram_tensor("d_bgx", [6, T], F32).ap()
    exo_q = nc.dram_tensor("exo_q", [256, 3 * T], BF16).ap()
    exo_k = nc.dram_tensor("exo_k", [256, 3 * T], BF16).ap()
    exo_v = nc.dram_tensor("exo_v", [256, 6 * T], BF16).ap()
    exo_bg = nc.dram_tensor("exo_bg", [12, T], F32).ap()
    d_oF = nc.dram_tensor("d_oF", [3 * SEQ, DV], BF16).ap()
    exo_o = nc.dram_tensor("exo_o", [6 * SEQ, DV], BF16).ap()
    d_bc = nc.dram_tensor("d_bc", [16, 2 * T], BF16).ap()
    cc1i = nc.dram_tensor("cc1i", [128, 24], F32).ap()
    cc1o = nc.dram_tensor("cc1o", [128, 24], F32).ap()

    epsc, epsk = p.constcol(1e-5)
    eps12c, eps12k = p.constcol(1e-12)
    onec, onek = p.constcol(1.0)
    ones_s = p.sb("ones_s", [128, 128], F32, scope="prog")
    one11 = p.sb("one11", [1, 1], F32, scope="prog")
    fc1 = p.sb("fc1s", [128, 1], F32, scope="prog")
    fc0 = p.sb("fc0s", [128, 1], F32, scope="prog")
    A_s = p.sb("A_s", [128, 256], F32, scope="prog")
    D_s = p.sb("D_s", [128, 16], F32, scope="prog")
    carry = p.sb("carry", [128, 128], F32, scope="prog")
    p.op("vector", lambda v: v.memset(ones_s[:, :], 1.0), wr=("ones_s",))
    p.op("vector", lambda v: v.memset(one11[:, :], 1.0), wr=("one11",))
    p.op("sync", lambda s: s.dma_start(out=fc1[:, :], in_=fc1_d), wr=("fc1",),
         dma=True, stream="c0")
    p.op("sync", lambda s: s.dma_start(out=fc0[:, :], in_=fc0_d), wr=("fc0",),
         dma=True, stream="c0")
    p.op("sync", lambda s: s.dma_start(out=A_s[:, :], in_=alog0), wr=("A_s",),
         dma=True, stream="c0")
    p.op("scalar", lambda s: s.activation(A_s[:, :], A_s[:, :], AF.Exp),
         rd=("A_s",), wr=("A_s",))
    p.op("vector", lambda v: v.tensor_scalar_mul(A_s[:, :], A_s[:, :], -1.0),
         rd=("A_s",), wr=("A_s",))
    p.op("sync", lambda s: s.dma_start(out=D_s[:, :], in_=dvec), wr=("D_s",),
         dma=True, stream="c0")

    eraBC = ExitStack()
    BCb = p.sb("BCb", [16, 2 * T], BF16, scope=eraBC)

    era1 = ExitStack()
    xn16 = p.sb("xn16", [128, 8 * TH], BF16, scope=era1)

    CH = 512
    w_in_s = p.sb("w_in_s", [128, 8 * 2 * DMAMBA], BF16)
    cw_s = p.sb("cw_s", [128, 64], F32)
    cb_s = p.sb("cb_s", [128, 16], F32)
    wx_s = p.sb("wx_s", [128, 16 * 128], BF16)
    wxc_s = p.sb("wxc_s", [128, 16 * 16], BF16)
    wdt_s = p.sb("wdt_s", [64, DMAMBA], BF16)
    bdt_s = p.sb("bdt_s", [128, 16], F32)
    for kt in range(8):
        p.op("gpsimd", lambda g, kt=kt: g.dma_start(
            out=w_in_s[:, kt * 2 * DMAMBA:(kt + 1) * 2 * DMAMBA],
            in_=w_in[kt * 128:(kt + 1) * 128, :]), wr=("w_in_s",), dma=True)
    for mt in range(16):
        p.op("gpsimd", lambda g, mt=mt: g.dma_start(
            out=wx_s[:, mt * 128:(mt + 1) * 128],
            in_=w_xa[mt * 128:(mt + 1) * 128, :]), wr=("wx_s",), dma=True,
            stream="w2")
        p.op("gpsimd", lambda g, mt=mt: g.dma_start(
            out=wxc_s[:, mt * 16:(mt + 1) * 16],
            in_=w_xc[mt * 128:(mt + 1) * 128, :]), wr=("wxc_s",), dma=True,
            stream="w2")
    p.op("gpsimd", lambda g: g.dma_start(out=wdt_s[:, :], in_=w_dt),
         wr=("wdt_s",), dma=True, stream="w2")
    p.op("sync", lambda s: s.dma_start(out=cw_s[:, :], in_=cwT), wr=("cw_s",),
         dma=True, stream="c0")
    p.op("sync", lambda s: s.dma_start(out=cb_s[:, :], in_=cb), wr=("cb_s",),
         dma=True, stream="c0")
    p.op("sync", lambda s: s.dma_start(out=bdt_s[:, :], in_=b_dt), wr=("bdt_s",),
         dma=True, stream="c0")

    NT = CH + 3
    xc = p.sb("xc", [128, 8 * NT], F32)
    stat = p.ps("stat", [128, 512], F32)
    stat2 = p.ps("stat2", [128, 512], F32)
    x2t = p.sb("x2t", [128, NT], F32)
    mb = p.sb("mb", [128, NT], F32)
    rb = p.sb("rb", [128, NT], F32)
    rbx = p.sb("rbx", [128, NT], F32)
    xm_ps = p.ps("xm_ps", [128, 512], F32)
    xm_ps2 = p.ps("xm_ps2", [128, 512], F32)
    xmt = p.sb("xmt", [128, NT], F32)
    xmc16 = p.sb("xmc16", [128, 16 * CH], BF16)
    xmcf = p.sb("xmcf", [128, CH], F32)
    sz16 = p.sb("sz16", [128, CH], BF16)
    xdbl_ps = p.ps("xdbl_ps", [128, CH], F32)
    xdblC_ps = p.ps("xdblC_ps", [16, CH], F32)
    xdbl = p.sb("xdbl", [128, CH], BF16)
    dt_ps = p.ps("dt_ps", [128, 512], F32)
    dtt = p.sb("dtt", [128, CH], F32)
    dt16 = p.sb("dt16", [128, CH], BF16)
    dta = p.sb("dta", [128, CH], F32)

    for ci in range(T // CH):
        xoff = ci * CH
        ck = "xc"
        for d in range(8):
            p.op("sync", lambda s, d=d, xoff=xoff: s.dma_start(
                out=xc[:, d * NT:d * NT + NT],
                in_=xh[d * 128:(d + 1) * 128, xoff:xoff + NT]),
                wr=("xc",), dma=True, stream="xin")
        for sl in range(2):
            f0, f1 = sl * 512, min(NT, 512 + sl * 512)
            p.mm_group("stat", [
                (lambda t, d=d, f0=f0, f1=f1: t.matmul(
                    stat[:, 0:f1 - f0], ones_s[:, :],
                    xc[:, d * NT + f0:d * NT + f1],
                    start=(d == 0), stop=(d == 7)), ("xc", "ones_s"))
                for d in range(8)])
            p.op("scalar", lambda s, f0=f0, f1=f1: s.activation(
                mb[:, f0:f1], stat[:, 0:f1 - f0], AF.Copy, scale=1.0 / DM),
                rd=("stat",), wr=("mb",))
        for sl in range(2):
            f0, f1 = sl * 512, min(NT, 512 + sl * 512)
            for d in range(8):
                p.op("scalar", lambda s, d=d, f0=f0, f1=f1: s.activation(
                    x2t[:, f0:f1], xc[:, d * NT + f0:d * NT + f1], AF.Square),
                    rd=("xc",), wr=("x2t",))
                p.mm_group("stat2", [
                    (lambda t, d=d, f0=f0, f1=f1: t.matmul(
                        stat2[:, 0:f1 - f0], ones_s[:, :], x2t[:, f0:f1],
                        start=(d == 0), stop=(d == 7)), ("x2t", "ones_s"))])
            p.op("scalar", lambda s, f0=f0, f1=f1: s.activation(
                rbx[:, f0:f1], stat2[:, 0:f1 - f0], AF.Copy, scale=1.0 / DM),
                rd=("stat2",), wr=("rbx",))
        p.op("scalar", lambda s: s.activation(rb[:, :], mb[:, :], AF.Square),
             rd=("mb",), wr=("rb",))
        p.op("vector", lambda v: v.tensor_sub(rb[:, :], rbx[:, :], rb[:, :]),
            rd=("rbx", "rb"), wr=("rb",))
        p.op("scalar", lambda s: s.activation(rb[:, :], rb[:, :], AF.Ln,
                                              bias=epsc),
             rd=("rb", epsk), wr=("rb",))
        p.op("scalar", lambda s: s.activation(rb[:, :], rb[:, :], AF.Exp,
                                              scale=-0.5),
             rd=("rb",), wr=("rb",))
        for d in range(8):
            p.op("vector", lambda v, d=d: v.tensor_sub(
                xc[:, d * NT:d * NT + NT], xc[:, d * NT:d * NT + NT], mb[:, :]),
                rd=("xc", "mb"), wr=("xc",))
            p.op("vector", lambda v, d=d, xoff=xoff: v.tensor_mul(
                xn16[:, d * TH + xoff:d * TH + xoff + NT],
                xc[:, d * NT:d * NT + NT], rb[:, :]),
                rd=("xc", "rb"), wr=("xn16",))
        for mt in range(16):
            for sl in range(2):
                f0, f1 = sl * 512, min(NT, 512 + sl * 512)
                psk, psum = ("xm_ps", xm_ps) if sl == 0 else ("xm_ps2", xm_ps2)
                p.mm_group(psk, [
                    (lambda t, kt=kt, f0=f0, f1=f1, mt=mt, psum=psum, xoff=xoff:
                     t.matmul(psum[:, 0:f1 - f0],
                              w_in_s[:, kt * 2 * DMAMBA + mt * 128:
                                     kt * 2 * DMAMBA + mt * 128 + 128],
                              xn16[:, kt * TH + xoff + f0:kt * TH + xoff + f1],
                              start=(kt == 0), stop=(kt == 7)),
                     ("w_in_s", "xn16")) for kt in range(8)])
                p.op("scalar", lambda s, f0=f0, f1=f1, psum=psum, psk=psk: s.copy(
                    xmt[:, f0:f1], psum[:, 0:f1 - f0]),
                    rd=(psk,), wr=("xmt",))
            p.op("vector", lambda v, mt=mt: v.tensor_scalar(
                xmcf[:, :], xmt[:, 0:CH], cw_s[:, mt * 4:mt * 4 + 1], None,
                ALU.mult), rd=("xmt", "cw_s"), wr=("xmcf",))
            for k in range(1, 4):
                p.op("vector", lambda v, mt=mt, k=k: v.scalar_tensor_tensor(
                    xmcf[:, :], xmt[:, k:k + CH], cw_s[:, mt * 4 + k:mt * 4 + k + 1],
                    xmcf[:, :], ALU.mult, ALU.add),
                    rd=("xmt", "cw_s", "xmcf"), wr=("xmcf",))
            p.op("scalar", lambda s, mt=mt: s.activation(
                xmc16[:, mt * CH:(mt + 1) * CH], xmcf[:, :], AF.Silu,
                bias=cb_s[:, mt:mt + 1]),
                rd=("xmcf", "cb_s"), wr=("xmc16",))
            p.op("sync", lambda s, mt=mt, ci=ci: s.dma_start(
                out=(d_xmc if mt < 8 else d_xmcx)[
                    (mt % 8) * 128:(mt % 8) * 128 + 128, ci * CH:(ci + 1) * CH],
                in_=xmc16[:, mt * CH:(mt + 1) * CH]),
                rd=("xmc16",), dma=True, stream="oxmc")
            p.mm_group("xm_ps", [
                (lambda t, kt=kt, mt=mt, xoff=xoff: t.matmul(
                    xm_ps[:, 0:CH],
                    w_in_s[:, kt * 2 * DMAMBA + (16 + mt) * 128:
                           kt * 2 * DMAMBA + (16 + mt) * 128 + 128],
                    xn16[:, kt * TH + xoff + 1:kt * TH + xoff + 1 + CH],
                    start=(kt == 0), stop=(kt == 7)),
                 ("w_in_s", "xn16")) for kt in range(8)])
            p.op("scalar", lambda s: s.activation(sz16[:, :], xm_ps[:, 0:CH],
                                                  AF.Silu),
                 rd=("xm_ps",), wr=("sz16",))
            p.op("sync", lambda s, mt=mt, ci=ci: s.dma_start(
                out=(d_sz if mt < 8 else d_szx)[
                    (mt % 8) * 128:(mt % 8) * 128 + 128, ci * CH:(ci + 1) * CH],
                in_=sz16[:, :]), rd=("sz16",), dma=True, stream="osz")
        p.mm_group("xdbl_ps", [
            (lambda t, mt=mt: t.matmul(
                xdbl_ps[:, :], wx_s[:, mt * 128:(mt + 1) * 128],
                xmc16[:, mt * CH:(mt + 1) * CH],
                start=(mt == 0), stop=(mt == 15)),
             ("wx_s", "xmc16")) for mt in range(16)])
        p.mm_group("xdblC_ps", [
            (lambda t, mt=mt: t.matmul(
                xdblC_ps[:, :], wxc_s[:, mt * 16:(mt + 1) * 16],
                xmc16[:, mt * CH:(mt + 1) * CH],
                start=(mt == 0), stop=(mt == 15)),
             ("wxc_s", "xmc16")) for mt in range(16)])
        p.op("vector", lambda v: v.tensor_copy(xdbl[:, :], xdbl_ps[:, :]),
             rd=("xdbl_ps",), wr=("xdbl",))
        p.op("vector", lambda v, ci=ci: v.tensor_copy(
            BCb[0:16, ci * CH:(ci + 1) * CH], xdbl_ps[64:80, :]),
            rd=("xdbl_ps",), wr=("BCb",))
        p.op("vector", lambda v, ci=ci: v.tensor_copy(
            BCb[0:16, T + ci * CH:T + (ci + 1) * CH], xdblC_ps[0:16, :]),
            rd=("xdblC_ps",), wr=("BCb",))
        for pt in range(16):
            p.mm_group("dt_ps", [
                (lambda t, pt=pt: t.matmul(
                    dt_ps[:, 0:CH], wdt_s[:, pt * 128:(pt + 1) * 128],
                    xdbl[0:64, :], start=True, stop=True),
                 ("wdt_s", "xdbl"))])
            p.op("vector", lambda v, pt=pt: v.tensor_scalar(
                dtt[:, :], dt_ps[:, 0:CH], bdt_s[:, pt:pt + 1], 0.0,
                ALU.add, ALU.max),
                rd=("dt_ps", "bdt_s"), wr=("dtt",))
            p.op("vector", lambda v: v.scalar_tensor_tensor(
                dta[:, :], dtt[:, :], -2.0, dt_ps[:, 0:CH], ALU.mult, ALU.add),
                rd=("dt_ps", "dtt"), wr=("dta",))
            p.op("scalar", lambda s, pt=pt: s.activation(
                dta[:, :], dta[:, :], AF.Exp, bias=bdt_s[:, pt:pt + 1]),
                rd=("dta", "bdt_s"), wr=("dta",))
            p.op("scalar", lambda s: s.activation(dta[:, :], dta[:, :],
                                                  AF.Ln, bias=onec),
                 rd=("dta", onek), wr=("dta",))
            p.op("vector", lambda v: v.tensor_add(dt16[:, :], dtt[:, :], dta[:, :]),
                 rd=("dta", "dtt"), wr=("dt16",))
            p.op("sync", lambda s, pt=pt, ci=ci: s.dma_start(
                out=(d_dt if pt < 8 else d_dtx)[
                    (pt % 8) * 128:(pt % 8) * 128 + 128, ci * CH:(ci + 1) * CH],
                in_=dt16[:, :]), rd=("dt16",), dma=True, stream="odt")
    p.barrier()

    p.op("sync", lambda s: s.dma_start(out=d_bc, in_=BCb[:, :]),
         rd=("BCb",), wr=("d_bc",), dma=True, stream="obc")
    p.op("gpsimd", lambda g: g.collective_compute(
        "AllGather", ALU.bypass, replica_groups=RG_PAIRS,
        ins=[d_bc.opt()], outs=[d_bcg.opt()]),
        rd=("d_bc",), wr=("d_bcg",))
    p.op("gpsimd", lambda g: g.collective_compute(
        "AllGather", ALU.bypass, replica_groups=RG_PAIRS,
        ins=[d_dtx.opt()], outs=[exo_dt.opt()]), wr=("exo_dt",))
    p.op("gpsimd", lambda g: g.collective_compute(
        "AllGather", ALU.bypass, replica_groups=RG_PAIRS,
        ins=[d_xmcx.opt()], outs=[exo_xmc.opt()]), wr=("exo_xmc",))
    p.op("gpsimd", lambda g: g.collective_compute(
        "AllGather", ALU.bypass, replica_groups=RG_PAIRS,
        ins=[d_szx.opt()], outs=[exo_sz.opt()]), wr=("exo_sz",))

    w1t = p.sb("w1t", [128, 8 * FF0], BF16)
    w2t = p.sb("w2t", [128, 32 * DM], BF16)
    ttmp = p.sb("ttmp", [128, 2048], F32)
    tsgn = p.sb("tsgn", [128, 2048], F32)
    ternarize(p, w1T, 8, FF0, w1t, FF0, "w1", ttmp, tsgn)
    ternarize(p, w2T, 32, DM, w2t, DM, "w2", ttmp, tsgn)
    b1_s = p.sb("b1_s", [128, 32], F32)
    b2_s = p.sb("b2_s", [128, 8], F32)
    p.op("sync", lambda s: s.dma_start(out=b1_s[:, :], in_=b1d), wr=("b1_s",),
         dma=True, stream="c0")
    p.op("sync", lambda s: s.dma_start(out=b2_s[:, :], in_=b2d), wr=("b2_s",),
         dma=True, stream="c0")
    C2 = 256
    h_ps = p.ps("h_ps", [128, C2], F32)
    h_ps2 = p.ps("h_ps2", [128, C2], F32)
    h16f = p.sb("h16f", [128, 32 * C2], BF16)
    ou_ps = p.ps("ou_ps", [128, C2], F32)
    ou_ps2 = p.ps("ou_ps2", [128, C2], F32)
    xpc = p.sb("xpc", [128, C2], F32)
    prt = p.sb("prt", [128, C2], F32)
    for ci in range(T // C2):
        xoff = ci * C2 + 1
        for pt in range(32):
            psk, psum = ("h_ps", h_ps) if pt % 2 == 0 else ("h_ps2", h_ps2)
            p.mm_group(psk, [
                (lambda t, kt=kt, pt=pt, psum=psum, xoff=xoff: t.matmul(
                    psum[:, :], w1t[:, kt * FF0 + pt * 128:kt * FF0 + pt * 128 + 128],
                    xn16[:, kt * TH + xoff:kt * TH + xoff + C2],
                    start=(kt == 0), stop=(kt == 7)),
                 ("wtw1", "xn16")) for kt in range(8)])
            p.op("scalar", lambda s, pt=pt, psum=psum, psk=psk: s.activation(
                h16f[:, pt * C2:(pt + 1) * C2], psum[:, :], AF.Gelu,
                bias=b1_s[:, pt:pt + 1]),
                rd=(psk, "b1_s"), wr=("h16f",))
        for po in range(8):
            p.op("sync", lambda s, po=po, xoff=xoff: s.dma_start(
                out=xpc[:, :], in_=xh[po * 128:(po + 1) * 128, xoff:xoff + C2]),
                wr=("xpc",), dma=True, stream="xin")
            psk, psum = ("ou_ps", ou_ps) if po % 2 == 0 else ("ou_ps2", ou_ps2)
            p.mm_group(psk, [
                (lambda t, ft=ft, po=po, psum=psum: t.matmul(
                    psum[:, :], w2t[:, ft * DM + po * 128:ft * DM + po * 128 + 128],
                    h16f[:, ft * C2:(ft + 1) * C2],
                    start=(ft == 0), stop=(ft == 31)),
                 ("wtw2", "h16f")) for ft in range(32)])
            p.op("vector", lambda v, po=po, psum=psum, psk=psk: v.scalar_tensor_tensor(
                prt[:, :], psum[:, :], b2_s[:, po:po + 1], xpc[:, :],
                ALU.add, ALU.add), rd=(psk, "b2_s", "xpc"), wr=("prt",))
            p.op("sync", lambda s, po=po, ci=ci: s.dma_start(
                out=d_part[po * 128:(po + 1) * 128, ci * C2:(ci + 1) * C2],
                in_=prt[:, :]), rd=("prt",), dma=True, stream="opart")
    p.barrier(keep=("d_bcg", "exo_dt", "exo_xmc", "exo_sz"))
    era1.close()

    era2 = ExitStack()
    LH = 2048
    brepP = [p.sb("brepA0", [128, 16 * LH], BF16, scope=era2)]
    crepP = [p.sb("crepA0", [128, 16 * LH], BF16, scope=era2)]
    stg = {}
    for k in ("dt", "xm", "sz"):
        b0 = p.sb(f"tL{k}0", [128, LH], BF16, scope=era2)
        b1 = p.sb(f"tE{k}0", [128, LH], BF16, scope=era2)
        for par in range(2):
            stg[(k, par, 0)] = b0
            stg[(k, par, 1)] = b1
    dtb = p.sb("dtb_e2", [128, LH], BF16, scope=era2)
    xmcb = p.sb("xmcb", [128, LH], BF16, scope=era2)
    szb = p.sb("szb", [128, LH], BF16, scope=era2)
    w16 = p.sb("w16", [128, LH], BF16, scope=era2)
    da_b = p.sb("da_b", [128, LH], BF16, scope=era2)
    bs_b = p.sb("bs_b", [128, LH], BF16, scope=era2)
    st_b = p.sb("st_b", [128, LH], BF16, scope=era2)
    yac = p.sb("yac", [128, LH], BF16, scope=era2)

    srcL = {"dt": d_dt, "xm": d_xmc, "sz": d_sz}
    srcE = {"dt": exo_dt, "xm": exo_xmc, "sz": exo_sz}
    exok = {"dt": "exo_dt", "xm": "exo_xmc", "sz": "exo_sz"}
    dstb = {"dt": (dtb, "dtb"), "xm": (xmcb, "xmcb"), "sz": (szb, "szb")}

    def stage_loads(mtl, lh):
        h, c0h, par = (lh * LH) // T, (lh * LH) % T, mtl % 2
        for k in ("dt", "xm", "sz"):
            bL, bE = stg[(k, par, 0)], stg[(k, par, 1)]
            p.op("sync", lambda s, k=k, mtl=mtl, c0h=c0h, bL=bL: s.dma_start(
                out=bL[:, :],
                in_=srcL[k][mtl * 128:(mtl + 1) * 128, c0h:c0h + LH]),
                wr=(f"tL{k}0",), dma=True, stream=f"iL{k}{par}")
            p.op("sync", lambda s, k=k, mtl=mtl, h=h, c0h=c0h, bE=bE:
                 s.dma_start(
                     out=bE[:, :],
                     in_=srcE[k][h * 1024 + mtl * 128:
                                 h * 1024 + (mtl + 1) * 128, c0h:c0h + LH]),
                 wr=(f"tE{k}0",), rd=(exok[k],), dma=True,
                 stream=f"iE{k}{par}")

    def blend(mtl, lh):
        h, par = (lh * LH) // T, mtl % 2
        cL, cLk = (fc0, "fc0") if h == 0 else (fc1, "fc1")
        cE, cEk = (fc1, "fc1") if h == 0 else (fc0, "fc0")
        for k in ("dt", "xm", "sz"):
            d, dk = dstb[k]
            bL, bE = stg[(k, par, 0)], stg[(k, par, 1)]
            p.op("vector", lambda v, d=d, bL=bL, cL=cL: v.tensor_scalar(
                d[:, :], bL[:, :], cL[:, 0:1], None, ALU.mult),
                rd=(f"tL{k}0", cLk), wr=(dk,))
            p.op("vector", lambda v, d=d, bE=bE, cE=cE: v.scalar_tensor_tensor(
                d[:, :], bE[:, :], cE[:, 0:1], d[:, :], ALU.mult, ALU.add),
                rd=(f"tE{k}0", cEk, dk), wr=(dk,))

    stage_loads(0, 0)
    NLH = SEQ // LH
    for lh in range(NLH):
        h, c0h, lp = (lh * LH) // T, (lh * LH) % T, 0
        brepA, crepA = brepP[lp], crepP[lp]
        bk, ck = f"brepA{lp}", f"crepA{lp}"
        for s in range(DSTATE):
            p.op("gpsimd", lambda g, s=s, h=h, c0h=c0h, brepA=brepA:
                 g.dma_start(
                     out=brepA[:, s * LH:(s + 1) * LH],
                     in_=d_bcg[h * 16 + s:h * 16 + s + 1,
                               c0h:c0h + LH].partition_broadcast(128)),
                 wr=(bk + str(s),), rd=("d_bcg",), dma=True,
                 stream=f"brep{s % 2}")
            p.op("gpsimd", lambda g, s=s, h=h, c0h=c0h, crepA=crepA:
                 g.dma_start(
                     out=crepA[:, s * LH:(s + 1) * LH],
                     in_=d_bcg[h * 16 + s:h * 16 + s + 1,
                               T + c0h:T + c0h + LH].partition_broadcast(128)),
                 wr=(ck + str(s),), rd=("d_bcg",), dma=True,
                 stream=f"crep{s % 2}")
        for mtl in range(8):
            blend(mtl, lh)
            nmtl, nlh = (mtl + 1, lh) if mtl < 7 else (0, lh + 1)
            if nlh < NLH:
                stage_loads(nmtl, nlh)
            p.op("vector", lambda v: v.tensor_mul(w16[:, :], dtb[:, :],
                                                  xmcb[:, :]),
                 rd=("dtb", "xmcb"), wr=("w16",))
            for s in range(DSTATE):
                col = mtl * 16 + s
                p.op("scalar", lambda sc, col=col: sc.activation(
                    da_b[:, :], dtb[:, :], AF.Exp,
                    scale=A_s[:, col:col + 1]),
                    rd=("dtb", "A_s"), wr=("da_b",))
                p.op("vector", lambda v, s=s, brepA=brepA: v.tensor_mul(
                    bs_b[:, :], w16[:, :], brepA[:, s * LH:(s + 1) * LH]),
                    rd=("w16", bk + str(s)), wr=("bs_b",))
                if lh == 0:
                    p.op("vector", lambda v: v.tensor_tensor_scan(
                        st_b[:, :], da_b[:, :], bs_b[:, :], 0.0,
                        ALU.mult, ALU.add),
                        rd=("da_b", "bs_b"), wr=("st_b",))
                else:
                    p.op("vector", lambda v, col=col: v.tensor_tensor_scan(
                        st_b[:, :], da_b[:, :], bs_b[:, :],
                        carry[:, col:col + 1], ALU.mult, ALU.add),
                        rd=("da_b", "bs_b", "carry"), wr=("st_b",))
                if lh < NLH - 1:
                    p.op("scalar", lambda sc, col=col: sc.copy(
                        carry[:, col:col + 1], st_b[:, LH - 1:LH]),
                        rd=("st_b",), wr=("carry",))
                if s == 0:
                    p.op("vector", lambda v, crepA=crepA: v.tensor_mul(
                        yac[:, :], st_b[:, :], crepA[:, 0:LH]),
                        rd=("st_b", ck + "0"), wr=("yac",))
                else:
                    p.op("vector", lambda v, s=s, crepA=crepA: v.tensor_mul(
                        bs_b[:, :], st_b[:, :],
                        crepA[:, s * LH:(s + 1) * LH]),
                        rd=("st_b", ck + str(s)), wr=("bs_b",))
                    p.op("vector", lambda v: v.tensor_add(
                        yac[:, :], yac[:, :], bs_b[:, :]),
                        rd=("bs_b", "yac"), wr=("yac",))
            p.op("vector", lambda v, mtl=mtl: v.scalar_tensor_tensor(
                yac[:, :], xmcb[:, :], D_s[:, mtl:mtl + 1], yac[:, :],
                ALU.mult, ALU.add),
                rd=("xmcb", "D_s", "yac"), wr=("yac",))
            p.op("vector", lambda v: v.tensor_mul(
                st_b[:, :], yac[:, :], szb[:, :]),
                rd=("yac", "szb"), wr=("st_b",))
            p.op("sync", lambda s_, mtl=mtl, lh=lh: s_.dma_start(
                out=d_ygF[mtl * 128:(mtl + 1) * 128, lh * LH:(lh + 1) * LH],
                in_=st_b[:, :]), rd=("st_b",), dma=True,
                stream=f"oyg{mtl % 2}")
    p.barrier()
    era2.close()
    eraBC.close()

    era3 = ExitStack()
    p.op("gpsimd", lambda g: g.collective_compute(
        "AllGather", ALU.bypass, replica_groups=RG_PAIRS,
        ins=[d_ygF.opt()], outs=[d_ygoF.opt()]), wr=("d_ygoF",))
    lrow = p.sb("lrow", [128, 128], F32, scope=era3)
    eyeS = p.sb("eyeS", [128, 128], F32, scope=era3)
    eye16 = p.sb("eye16", [128, 128], BF16, scope=era3)
    ones16 = p.sb("ones16", [128, 128], BF16, scope=era3)
    eraM2 = ExitStack()
    xnE = p.sb("xnE", [128, 8 * TH], BF16, scope=eraM2)
    grow6 = p.sb("grow6", [6, T], F32, scope=eraM2)
    brow6 = p.sb("brow6", [6, T], F32, scope=eraM2)
    gcum6 = p.sb("gcum6", [6, T], F32, scope=eraM2)
    rst6 = p.sb("rst6", [6, T], F32, scope=eraM2)
    p.op("vector", lambda v: v.memset(ones16[:, :], 1.0), wr=("ones16",))
    p.op("sync", lambda s: s.dma_start(out=lrow[:, :], in_=lrow_d),
         wr=("lrow",), dma=True, stream="c0")

    wout_s = p.sb("wout_s", [128, 16 * DM], BF16)
    gw_s = p.sb("gw_s", [128, 8 * 8], F32)
    eb128 = p.sb("eb128", [128, 8], F32)
    ones8 = p.sb("ones8", [128, 8], F32)
    ygs = p.sb("ygs", [128, 16 * 512], BF16)
    prtm = p.sb("prtm", [128, 512], F32)
    out1 = p.sb("out1", [128, 8 * 512], F32)
    ygps = p.ps("ygps", [128, 512], F32)
    ygps2 = p.ps("ygps2", [128, 512], F32)
    statm = p.ps("statm", [128, 512], F32)
    statm2 = p.ps("statm2", [128, 512], F32)
    x2m = p.sb("x2m", [128, 512], F32)
    mbm = p.sb("mbm", [128, 512], F32)
    rbm = p.sb("rbm", [128, 512], F32)
    gpsm = p.ps("gpsm", [8, 512], F32)
    scrm = p.sb("scrm", [8, 512], F32)
    trpsm = p.ps("trpsm", [128, 8], F32)
    trpsm2 = p.ps("trpsm2", [8, 128], F32)
    tok8 = p.sb("tok8", [128, 8], F32)
    t8a = p.sb("t8a", [128, 8], F32)
    t8b = p.sb("t8b", [128, 8], F32)
    t8c = p.sb("t8c", [128, 8], F32)
    m1c = p.sb("m1c", [128, 1], F32)
    mwti = p.sb("mwti", [8, 128], F32)
    hl3 = p.sb("hl3", [128, 24], F32)
    for kt in range(16):
        p.op("gpsimd", lambda g, kt=kt: g.dma_start(
            out=wout_s[:, kt * DM:(kt + 1) * DM],
            in_=wout_d[kt * 128:(kt + 1) * 128, :]), wr=("wout_s",), dma=True)
    for kt in range(8):
        p.op("sync", lambda s, kt=kt: s.dma_start(
            out=gw_s[:, kt * 8:(kt + 1) * 8],
            in_=gwT_d[kt * 128:(kt + 1) * 128, :]), wr=("gw_s",), dma=True,
            stream="c0")
    p.op("sync", lambda s: s.dma_start(
        out=eb128[:, :], in_=ebias_d[0:1, :].partition_broadcast(128)),
        wr=("eb128",), dma=True, stream="c0")
    p.op("sync", lambda s: s.dma_start(out=eyeS[:, :], in_=eye_d),
         wr=("eye",), dma=True, stream="c0")
    p.op("vector", lambda v: v.tensor_copy(eye16[:, :], eyeS[:, :]),
         rd=("eye",), wr=("eye16",))
    p.op("vector", lambda v: v.memset(ones8[:, :], 1.0), wr=("ones8",))

    ygtA = [p.sb(f"ygtA{i}", [128, 512], BF16) for i in range(2)]
    ygtB = [p.sb(f"ygtB{i}", [128, 512], BF16) for i in range(2)]
    for si in range(4):
        f0 = si * 512
        for mt in range(16):
            par = mt % 2
            if mt < 8:
                sA = d_ygF[mt * 128:(mt + 1) * 128, f0:f0 + 512]
                sB = d_ygF[mt * 128:(mt + 1) * 128, T + f0:T + f0 + 512]
                rdk = ()
            else:
                ml = mt - 8
                sA = d_ygoF[1024 + ml * 128:1024 + (ml + 1) * 128, f0:f0 + 512]
                sB = d_ygoF[ml * 128:(ml + 1) * 128, T + f0:T + f0 + 512]
                rdk = ("d_ygoF",)
            p.op("gpsimd", lambda g, sA=sA, par=par: g.dma_start(
                out=ygtA[par][:, :], in_=sA),
                wr=(f"ygtA{par}",), rd=rdk, dma=True, stream=f"yg{par}")
            p.op("gpsimd", lambda g, sB=sB, par=par: g.dma_start(
                out=ygtB[par][:, :], in_=sB),
                wr=(f"ygtB{par}",), rd=rdk, dma=True, stream=f"yh{par}")
            p.op("vector", lambda v, mt=mt, par=par: v.tensor_scalar(
                ygs[:, mt * 512:mt * 512 + 512], ygtA[par][:, :],
                fc0[:, 0:1], None, ALU.mult),
                rd=(f"ygtA{par}", "fc0"), wr=("ygs",))
            p.op("vector", lambda v, mt=mt, par=par: v.scalar_tensor_tensor(
                ygs[:, mt * 512:mt * 512 + 512], ygtB[par][:, :],
                fc1[:, 0:1], ygs[:, mt * 512:mt * 512 + 512],
                ALU.mult, ALU.add),
                rd=(f"ygtB{par}", "fc1", "ygs"), wr=("ygs",))
        for d in range(8):
            p.op("sync", lambda s, d=d, f0=f0: s.dma_start(
                out=prtm[:, :], in_=d_part[d * 128:(d + 1) * 128, f0:f0 + 512]),
                wr=("prtm",), dma=True, stream="ipart")
            psk, psum = ("ygps", ygps) if d % 2 == 0 else ("ygps2", ygps2)
            p.mm_group(psk, [
                (lambda t, mt=mt, d=d, psum=psum: t.matmul(
                    psum[:, :], wout_s[:, mt * DM + d * 128:mt * DM + d * 128 + 128],
                    ygs[:, mt * 512:(mt + 1) * 512],
                    start=(mt == 0), stop=(mt == 15)),
                 ("wout_s", "ygs")) for mt in range(16)])
            p.op("vector", lambda v, d=d, psum=psum, psk=psk: v.tensor_add(
                out1[:, d * 512:d * 512 + 512], psum[:, :], prtm[:, :]),
                rd=(psk, "prtm"), wr=("out1",))
            p.op("sync", lambda s, d=d, f0=f0: s.dma_start(
                out=d_out1[d * 128:(d + 1) * 128, f0:f0 + 512],
                in_=out1[:, d * 512:d * 512 + 512]),
                rd=("out1",), dma=True, stream="oo1")
            if si == 3:
                p.op("vector", lambda v, d=d: v.tensor_copy(
                    hl3[:, d * 3:(d + 1) * 3], out1[:, d * 512 + 509:d * 512 + 512]),
                    rd=("out1",), wr=("hl3",))
        p.mm_group("statm", [
            (lambda t, d=d: t.matmul(
                statm[:, :], ones_s[:, :], out1[:, d * 512:d * 512 + 512],
                start=(d == 0), stop=(d == 7)), ("out1", "ones_s"))
            for d in range(8)])
        for d in range(8):
            p.op("scalar", lambda s, d=d: s.activation(
                x2m[:, :], out1[:, d * 512:d * 512 + 512], AF.Square),
                rd=("out1",), wr=("x2m",))
            p.mm_group("statm2", [
                (lambda t, d=d: t.matmul(
                    statm2[:, :], ones_s[:, :], x2m[:, :],
                    start=(d == 0), stop=(d == 7)), ("x2m", "ones_s"))])
        p.op("scalar", lambda s: s.activation(mbm[:, :], statm[:, :], AF.Copy,
                                              scale=1.0 / DM),
             rd=("statm",), wr=("mbm",))
        p.op("scalar", lambda s: s.activation(rbm[:, :], mbm[:, :], AF.Square),
             rd=("mbm",), wr=("rbm",))
        p.op("vector", lambda v: v.scalar_tensor_tensor(
            rbm[:, :], statm2[:, :], 1.0 / DM, rbm[:, :], ALU.mult, ALU.subtract),
            rd=("statm2", "rbm"), wr=("rbm",))
        p.op("scalar", lambda s: s.activation(rbm[:, :], rbm[:, :], AF.Ln,
                                              bias=epsc),
             rd=("rbm", epsk), wr=("rbm",))
        p.op("scalar", lambda s: s.activation(rbm[:, :], rbm[:, :], AF.Exp,
                                              scale=-0.5),
             rd=("rbm",), wr=("rbm",))
        for d in range(8):
            p.op("vector", lambda v, d=d: v.tensor_sub(
                out1[:, d * 512:d * 512 + 512], out1[:, d * 512:d * 512 + 512],
                mbm[:, :]), rd=("out1", "mbm"), wr=("out1",))
            p.op("vector", lambda v, d=d: v.tensor_mul(
                out1[:, d * 512:d * 512 + 512], out1[:, d * 512:d * 512 + 512],
                rbm[:, :]), rd=("out1", "rbm"), wr=("out1",))
            p.op("vector", lambda v, d=d, f0=f0: v.tensor_copy(
                xnE[:, d * TH + 3 + f0:d * TH + 3 + f0 + 512],
                out1[:, d * 512:d * 512 + 512]),
                rd=("out1",), wr=("xnE",))
            p.op("sync", lambda s, d=d, f0=f0: s.dma_start(
                out=d_xn1[d * 128:(d + 1) * 128, f0:f0 + 512],
                in_=xnE[:, d * TH + 3 + f0:d * TH + 3 + f0 + 512]),
                rd=("xnE",), dma=True, stream="ox1")
        p.mm_group("gpsm", [
            (lambda t, d=d: t.matmul(
                gpsm[:, :], gw_s[:, d * 8:(d + 1) * 8],
                out1[:, d * 512:d * 512 + 512],
                start=(d == 0), stop=(d == 7)), ("out1", "gw_s"))
            for d in range(8)])
        p.op("scalar", lambda s: s.activation(scrm[:, :], gpsm[:, :], AF.Sigmoid),
             rd=("gpsm",), wr=("scrm",))
        for blk in range(4):
            c0 = blk * 128
            p.mm_group("trpsm", [
                (lambda t, c0=c0: t.transpose(
                    trpsm[:, :], scrm[:, c0:c0 + 128], eyeS[0:8, 0:8]),
                 ("scrm", "eye"))])
            p.op("vector", lambda v: v.tensor_copy(tok8[:, :], trpsm[:, :]),
                 rd=("trpsm",), wr=("tok8",))
            p.op("vector", lambda v: v.tensor_add(t8a[:, :], tok8[:, :],
                                                  eb128[:, :]),
                 rd=("tok8", "eb128"), wr=("t8a",))
            p.op("vector", lambda v: v.reduce_max(m1c[:, :], t8a[:, :], AX.X),
                 rd=("t8a",), wr=("m1c",))
            p.op("vector", lambda v: v.tensor_scalar(
                t8b[:, :], t8a[:, :], m1c[:, 0:1], None, ALU.is_ge),
                rd=("t8a", "m1c"), wr=("t8b",))
            p.op("vector", lambda v: v.tensor_tensor_scan(
                t8c[:, :], ones8[:, :], t8b[:, :], 0.0, ALU.mult, ALU.add),
                rd=("ones8", "t8b"), wr=("t8c",))
            p.op("vector", lambda v: v.tensor_scalar(
                t8c[:, :], t8c[:, :], 1.0, None, ALU.is_le),
                rd=("t8c",), wr=("t8c",))
            p.op("vector", lambda v: v.tensor_mul(t8b[:, :], t8b[:, :], t8c[:, :]),
                 rd=("t8b", "t8c"), wr=("t8b",))
            p.op("vector", lambda v: v.scalar_tensor_tensor(
                t8a[:, :], t8b[:, :], -1e30, t8a[:, :], ALU.mult, ALU.add),
                rd=("t8a", "t8b"), wr=("t8a",))
            p.op("vector", lambda v: v.reduce_max(m1c[:, :], t8a[:, :], AX.X),
                 rd=("t8a",), wr=("m1c",))
            p.op("vector", lambda v: v.tensor_scalar(
                t8a[:, :], t8a[:, :], m1c[:, 0:1], None, ALU.is_ge),
                rd=("t8a", "m1c"), wr=("t8a",))
            p.op("vector", lambda v: v.tensor_tensor_scan(
                t8c[:, :], ones8[:, :], t8a[:, :], 0.0, ALU.mult, ALU.add),
                rd=("ones8", "t8a"), wr=("t8c",))
            p.op("vector", lambda v: v.tensor_scalar(
                t8c[:, :], t8c[:, :], 1.0, None, ALU.is_le),
                rd=("t8c",), wr=("t8c",))
            p.op("vector", lambda v: v.tensor_mul(t8a[:, :], t8a[:, :], t8c[:, :]),
                 rd=("t8a", "t8c"), wr=("t8a",))
            p.op("vector", lambda v: v.tensor_add(t8b[:, :], t8b[:, :], t8a[:, :]),
                 rd=("t8a", "t8b"), wr=("t8b",))
            p.op("vector", lambda v: v.tensor_mul(t8b[:, :], t8b[:, :], tok8[:, :]),
                 rd=("t8b", "tok8"), wr=("t8b",))
            p.mm_group("trpsm2", [
                (lambda t: t.transpose(trpsm2[:, :], t8b[:, :], eyeS[:, :]),
                 ("t8b", "eye"))])
            p.op("vector", lambda v: v.tensor_copy(mwti[:, :], trpsm2[:, :]),
                 rd=("trpsm2",), wr=("mwti",))
            p.op("sync", lambda s, f0=f0, c0=c0: s.dma_start(
                out=d_mw[:, f0 + c0:f0 + c0 + 128], in_=mwti[:, :]),
                rd=("mwti",), dma=True, stream="omw")
    p.op("vector", lambda v: v.tensor_scalar(
        hl3[:, :], hl3[:, :], fc0[:, 0:1], None, ALU.mult),
        rd=("hl3", "fc0"), wr=("hl3",))
    p.op("sync", lambda s: s.dma_start(out=cc1i, in_=hl3[:, :]),
         rd=("hl3",), wr=("cc1i",), dma=True, stream="cc")
    p.op("gpsimd", lambda g: g.collective_compute(
        "AllReduce", ALU.add, replica_groups=RG_PAIRS,
        ins=[cc1i.opt()], outs=[cc1o.opt()]),
        rd=("cc1i",), wr=("cc1o",))
    p.op("sync", lambda s: s.dma_start(out=hl3[:, :], in_=cc1o),
         rd=("cc1o",), wr=("hl3",), dma=True, stream="cc")
    p.op("vector", lambda v: v.tensor_scalar(
        hl3[:, :], hl3[:, :], fc1[:, 0:1], None, ALU.mult),
        rd=("hl3", "fc1"), wr=("hl3",))
    p.mm_group("statm", [
        (lambda t, d=d: t.matmul(
            statm[:, 0:3], ones_s[:, :], hl3[:, d * 3:(d + 1) * 3],
            start=(d == 0), stop=(d == 7)), ("hl3", "ones_s"))
        for d in range(8)])
    for d in range(8):
        p.op("scalar", lambda s, d=d: s.activation(
            x2m[:, 0:3], hl3[:, d * 3:(d + 1) * 3], AF.Square),
            rd=("hl3",), wr=("x2m",))
        p.mm_group("statm2", [
            (lambda t, d=d: t.matmul(
                statm2[:, 0:3], ones_s[:, :], x2m[:, 0:3],
                start=(d == 0), stop=(d == 7)), ("x2m", "ones_s"))])
    p.op("scalar", lambda s: s.activation(mbm[:, 0:3], statm[:, 0:3], AF.Copy,
                                          scale=1.0 / DM),
         rd=("statm",), wr=("mbm",))
    p.op("scalar", lambda s: s.activation(rbm[:, 0:3], mbm[:, 0:3], AF.Square),
         rd=("mbm",), wr=("rbm",))
    p.op("vector", lambda v: v.scalar_tensor_tensor(
        rbm[:, 0:3], statm2[:, 0:3], 1.0 / DM, rbm[:, 0:3], ALU.mult,
        ALU.subtract), rd=("statm2", "rbm"), wr=("rbm",))
    p.op("scalar", lambda s: s.activation(rbm[:, 0:3], rbm[:, 0:3], AF.Ln,
                                          bias=epsc),
         rd=("rbm", epsk), wr=("rbm",))
    p.op("scalar", lambda s: s.activation(rbm[:, 0:3], rbm[:, 0:3], AF.Exp,
                                          scale=-0.5),
         rd=("rbm",), wr=("rbm",))
    for d in range(8):
        p.op("vector", lambda v, d=d: v.tensor_sub(
            hl3[:, d * 3:(d + 1) * 3], hl3[:, d * 3:(d + 1) * 3], mbm[:, 0:3]),
            rd=("hl3", "mbm"), wr=("hl3",))
        p.op("vector", lambda v, d=d: v.tensor_mul(
            xnE[:, d * TH:d * TH + 3], hl3[:, d * 3:(d + 1) * 3], rbm[:, 0:3]),
            rd=("hl3", "rbm"), wr=("xnE",))
    p.barrier()

    wqpt = [p.sb(f"wqpt{i}", [128, 8 * 128], BF16) for i in range(2)]
    wba_s16 = p.sb("wba_s16", [128, 8 * 16], BF16)
    p.op("gpsimd", lambda g: g.dma_start(out=wba_s16[:, :], in_=wba_d),
         wr=("wba_s16",), dma=True, stream="wq")
    cwq_s = p.sb("cwq_s", [128, 96], F32)
    negA = p.sb("negA_sb", [6, 1], F32)
    dtb_s = p.sb("dtb_sb", [6, 1], F32)
    xq = p.sb("xq", [128, TH], F32)
    xcs = p.sb("xcs", [128, T], F32)
    sqb = p.sb("sqb", [128, T], BF16)
    rr = p.sb("rr", [128, T], F32)
    qp = p.ps("qp", [128, 512], F32)
    qp2 = p.ps("qp2", [128, 512], F32)
    ssps = p.ps("ssps", [128, 512], F32)
    bps = p.ps("bps", [6, 512], F32)
    bg6 = p.sb("bg6", [6, 512], F32)
    bg6b = p.sb("bg6b", [6, 512], F32)
    SLB = [(0, 3), (3, 512), (515, 512), (1027, 512), (1539, 512)]
    p.op("sync", lambda s: s.dma_start(out=cwq_s[:, :], in_=cwq_d),
         wr=("cwq_s",), dma=True, stream="c0")
    p.op("sync", lambda s: s.dma_start(out=negA[:, :], in_=alog1),
         wr=("negA",), dma=True, stream="c0")
    p.op("sync", lambda s: s.dma_start(out=dtb_s[:, :], in_=dtb_d),
         wr=("dtb_s",), dma=True, stream="c0")
    p.op("scalar", lambda s: s.activation(negA[:, :], negA[:, :], AF.Exp),
         rd=("negA",), wr=("negA",))
    p.op("vector", lambda v: v.tensor_scalar_mul(negA[:, :], negA[:, :], -1.0),
         rd=("negA",), wr=("negA",))

    def load_wqpt(pt, buf):
        p.op("gpsimd", lambda g, pt=pt, buf=buf: g.dma_start(
            out=wqpt[buf].ap().rearrange("p (kt f) -> p kt f", kt=8),
            in_=wqkvg_d[:, pt * 128:(pt + 1) * 128].rearrange(
                "(kt p) f -> p kt f", p=128)),
            wr=(f"wqpt{buf}",), dma=True, stream="wq")

    load_wqpt(0, 0)
    for pt in range(36):
        buf = pt % 2
        if pt < 35:
            load_wqpt(pt + 1, (pt + 1) % 2)
        for si, (f0, w) in enumerate(SLB):
            psk, psum = ("qp", qp) if si % 2 == 0 else ("qp2", qp2)
            p.mm_group(psk, [
                (lambda t, kt=kt, f0=f0, w=w, psum=psum, buf=buf: t.matmul(
                    psum[:, 0:w], wqpt[buf][:, kt * 128:(kt + 1) * 128],
                    xnE[:, kt * TH + f0:kt * TH + f0 + w],
                    start=(kt == 0), stop=(kt == 7)),
                 (f"wqpt{buf}", "xnE")) for kt in range(8)])
            if pt < 24:
                p.op("scalar", lambda s, f0=f0, w=w, psum=psum, psk=psk: s.copy(
                    xq[:, f0:f0 + w], psum[:, 0:w]), rd=(psk,), wr=("xq",))
            else:
                if si > 0:
                    p.op("scalar", lambda s, f0=f0, w=w, psum=psum, psk=psk:
                         s.activation(sqb[:, f0 - 3:f0 - 3 + w], psum[:, 0:w],
                                      AF.Silu),
                         rd=(psk,), wr=("sqb",))
        if pt >= 24:
            p.op("sync", lambda s, pt=pt: s.dma_start(
                out=d_gs[(pt - 24) * 128:(pt - 23) * 128, :], in_=sqb[:, 0:T]),
                rd=("sqb",), dma=True, stream="ogs")
            continue
        p.op("vector", lambda v, pt=pt: v.tensor_scalar(
            xcs[:, :], xq[:, 0:T], cwq_s[:, pt * 4:pt * 4 + 1], None, ALU.mult),
            rd=("xq", "cwq_s"), wr=("xcs",))
        for k in range(1, 4):
            p.op("vector", lambda v, pt=pt, k=k: v.scalar_tensor_tensor(
                xcs[:, :], xq[:, k:k + T], cwq_s[:, pt * 4 + k:pt * 4 + k + 1],
                xcs[:, :], ALU.mult, ALU.add),
                rd=("xq", "cwq_s", "xcs"), wr=("xcs",))
        p.op("scalar", lambda s: s.activation(xcs[:, :], xcs[:, :], AF.Silu),
             rd=("xcs",), wr=("xcs",))
        if pt >= 12:
            p.op("vector", lambda v: v.tensor_copy(sqb[:, 0:T], xcs[:, :]),
                 rd=("xcs",), wr=("sqb",))
            tl = pt - 12
            vdst = d_vloc if tl < 6 else d_vx
            vk = "d_vloc" if tl < 6 else "d_vx"
            p.op("sync", lambda s, vdst=vdst, tl=tl: s.dma_start(
                out=vdst[:, (tl % 6) * T:(tl % 6) * T + T], in_=sqb[:, 0:T]),
                rd=("sqb",), wr=(vk,), dma=True, stream="okt")
            continue
        p.op("scalar", lambda s: s.activation(sqb[:, 0:T], xcs[:, :], AF.Square),
             rd=("xcs",), wr=("sqb",))
        for sl in range(4):
            p.mm_group("ssps", [
                (lambda t, sl=sl: t.matmul(
                    ssps[:, :], ones16[:, :], sqb[:, sl * 512:(sl + 1) * 512],
                    start=True, stop=True), ("sqb", "ones16"))])
            p.op("scalar", lambda s, sl=sl: s.activation(
                rr[:, sl * 512:(sl + 1) * 512], ssps[:, :], AF.Ln,
                bias=eps12c), rd=("ssps", eps12k), wr=("rr",))
        p.op("scalar", lambda s: s.activation(rr[:, :], rr[:, :], AF.Exp,
                                              scale=-0.5),
             rd=("rr",), wr=("rr",))
        sc = QSC if pt < 6 else 1.0
        hd = pt if pt < 6 else pt - 6
        p.op("vector", lambda v, sc=sc: v.scalar_tensor_tensor(
            sqb[:, 0:T], xcs[:, :], sc, rr[:, :],
            ALU.mult, ALU.mult),
            rd=("xcs", "rr"), wr=("sqb",))
        if pt < 6:
            qdst = d_qloc if hd < 3 else d_qx
            qk = "d_qloc" if hd < 3 else "d_qx"
        else:
            qdst = d_kloc if hd < 3 else d_kx
            qk = "d_kloc" if hd < 3 else "d_kx"
        p.op("sync", lambda s, qdst=qdst, hd=hd: s.dma_start(
            out=qdst[:, (hd % 3) * T:(hd % 3) * T + T], in_=sqb[:, 0:T]),
            rd=("sqb",), wr=(qk,), dma=True, stream="okt")
    for si, (f0, w) in enumerate(SLB):
        if si == 0:
            continue
        p.mm_group("bps", [
            (lambda t, kt=kt, f0=f0, w=w: t.matmul(
                bps[:, 0:w], wba_s16[:, kt * 16:kt * 16 + 6],
                xnE[:, kt * TH + f0:kt * TH + f0 + w],
                start=(kt == 0), stop=(kt == 7)),
             ("wba_s16", "xnE")) for kt in range(8)])
        p.op("scalar", lambda s, f0=f0, w=w: s.activation(
            brow6[:, f0 - 3:f0 - 3 + w], bps[0:6, 0:w], AF.Sigmoid),
            rd=("bps",), wr=("brow6",))
        p.mm_group("bps", [
            (lambda t, kt=kt, f0=f0, w=w: t.matmul(
                bps[:, 0:w], wba_s16[:, kt * 16 + 8:kt * 16 + 14],
                xnE[:, kt * TH + f0:kt * TH + f0 + w],
                start=(kt == 0), stop=(kt == 7)),
             ("wba_s16", "xnE")) for kt in range(8)])
        p.op("vector", lambda v, w=w: v.tensor_scalar(
            bg6b[:, 0:w], bps[0:6, 0:w], dtb_s[0:6, 0:1], 0.0,
            ALU.add, ALU.max),
            rd=("bps", "dtb_s"), wr=("bg6b",))
        p.op("vector", lambda v, w=w: v.scalar_tensor_tensor(
            bg6[:, 0:w], bg6b[:, 0:w], -2.0, bps[0:6, 0:w], ALU.mult, ALU.add),
            rd=("bps", "bg6b"), wr=("bg6",))
        p.op("scalar", lambda s, w=w: s.activation(
            bg6[:, 0:w], bg6[:, 0:w], AF.Exp, bias=dtb_s[0:6, 0:1]),
            rd=("bg6", "dtb_s"), wr=("bg6",))
        p.op("scalar", lambda s, w=w: s.activation(
            bg6[:, 0:w], bg6[:, 0:w], AF.Ln, bias=onec[0:6, 0:1]),
            rd=("bg6", onek), wr=("bg6",))
        p.op("vector", lambda v, w=w: v.tensor_add(bg6[:, 0:w], bg6[:, 0:w],
                                                   bg6b[:, 0:w]),
             rd=("bg6", "bg6b"), wr=("bg6",))
        p.op("vector", lambda v, f0=f0, w=w: v.tensor_scalar(
            grow6[:, f0 - 3:f0 - 3 + w], bg6[:, 0:w], negA[0:6, 0:1], None,
            ALU.mult), rd=("bg6", "negA"), wr=("grow6",))
    p.op("vector", lambda v: v.memset(rst6[:, :], 1.0), wr=("rst6",))
    p.op("vector", lambda v: v.memset(
        bass.AP(rst6, 0, [[T, 6], [128, 16]]), 0.0), rd=("rst6",), wr=("rst6",))
    p.op("vector", lambda v: v.tensor_tensor_scan(
        gcum6[:, :], rst6[:, :], grow6[:, :], 0.0, ALU.mult, ALU.add),
        rd=("rst6", "grow6"), wr=("gcum6",))
    p.op("sync", lambda s: s.dma_start(out=d_bgx[0:3, :], in_=gcum6[3:6, :]),
         rd=("gcum6",), wr=("d_bgx",), dma=True, stream="obg")
    p.op("sync", lambda s: s.dma_start(out=d_bgx[3:6, :], in_=brow6[3:6, :]),
         rd=("brow6",), wr=("d_bgx",), dma=True, stream="obg")
    p.op("gpsimd", lambda g: g.collective_compute(
        "AllGather", ALU.bypass, replica_groups=RG_PAIRS,
        ins=[d_bgx.opt()], outs=[exo_bg.opt()]),
        rd=("d_bgx",), wr=("exo_bg",))
    p.op("gpsimd", lambda g: g.collective_compute(
        "AllGather", ALU.bypass, replica_groups=RG_PAIRS,
        ins=[d_qx.opt()], outs=[exo_q.opt()]),
        rd=("d_qx",), wr=("exo_q",))
    p.op("gpsimd", lambda g: g.collective_compute(
        "AllGather", ALU.bypass, replica_groups=RG_PAIRS,
        ins=[d_kx.opt()], outs=[exo_k.opt()]),
        rd=("d_kx",), wr=("exo_k",))
    p.op("gpsimd", lambda g: g.collective_compute(
        "AllGather", ALU.bypass, replica_groups=RG_PAIRS,
        ins=[d_vx.opt()], outs=[exo_v.opt()]),
        rd=("d_vx",), wr=("exo_v",))
    p.barrier(keep=("exo_q", "exo_k", "exo_v", "exo_bg"))

    era_qkv = ExitStack()
    qF = p.sb("qF", [128, 3 * SEQ], BF16, scope=era_qkv)
    kF = p.sb("kF", [128, 3 * SEQ], BF16, scope=era_qkv)
    gcolB = p.sb("gcolB", [128, 96], F32, scope=era_qkv)
    bcolB = p.sb("bcolB", [128, 96], F32, scope=era_qkv)
    gcumF = p.sb("gcumF", [3, SEQ], F32)
    browF = p.sb("browF", [3, SEQ], F32)
    stq = [p.sb(f"stq{i}", [128, T], BF16) for i in range(2)]
    stk = [p.sb(f"stk{i}", [128, T], BF16) for i in range(2)]
    stgg = p.sb("stgg", [3, T], F32)
    stgb = p.sb("stgb", [3, T], F32)
    tp_ps3 = p.ps("tp_ps3", [128, 8], F32)
    for g in range(2):
        cL, cLk = (fc0, "fc0") if g == 0 else (fc1, "fc1")
        cE, cEk = (fc1, "fc1") if g == 0 else (fc0, "fc0")
        p.op("sync", lambda s, g=g: s.dma_start(
            out=stgg[:, :], in_=exo_bg[g * 6:g * 6 + 3, :]),
            wr=("stgg",), rd=("exo_bg",), dma=True, stream="ibg")
        p.op("sync", lambda s, g=g: s.dma_start(
            out=stgb[:, :], in_=exo_bg[g * 6 + 3:g * 6 + 6, :]),
            wr=("stgb",), rd=("exo_bg",), dma=True, stream="ibg")
        p.op("vector", lambda v, g=g, cL=cL: v.tensor_scalar(
            gcumF[0:3, g * T:(g + 1) * T], gcum6[0:3, :], cL[0:3, 0:1],
            None, ALU.mult), rd=("gcum6", cLk), wr=("gcumF",))
        p.op("vector", lambda v, g=g, cE=cE: v.scalar_tensor_tensor(
            gcumF[0:3, g * T:(g + 1) * T], stgg[0:3, :], cE[0:3, 0:1],
            gcumF[0:3, g * T:(g + 1) * T], ALU.mult, ALU.add),
            rd=("stgg", cEk, "gcumF"), wr=("gcumF",))
        p.op("vector", lambda v, g=g, cL=cL: v.tensor_scalar(
            browF[0:3, g * T:(g + 1) * T], brow6[0:3, :], cL[0:3, 0:1],
            None, ALU.mult), rd=("brow6", cLk), wr=("browF",))
        p.op("vector", lambda v, g=g, cE=cE: v.scalar_tensor_tensor(
            browF[0:3, g * T:(g + 1) * T], stgb[0:3, :], cE[0:3, 0:1],
            browF[0:3, g * T:(g + 1) * T], ALU.mult, ALU.add),
            rd=("stgb", cEk, "browF"), wr=("browF",))
        for i in range(3):
            par = i % 2
            p.op("sync", lambda s, g=g, i=i, par=par: s.dma_start(
                out=stq[par][:, :],
                in_=exo_q[g * 128:(g + 1) * 128, i * T:(i + 1) * T]),
                wr=(f"stq{par}",), rd=("exo_q",), dma=True, stream=f"iq{par}")
            p.op("sync", lambda s, g=g, i=i, par=par: s.dma_start(
                out=stk[par][:, :],
                in_=exo_k[g * 128:(g + 1) * 128, i * T:(i + 1) * T]),
                wr=(f"stk{par}",), rd=("exo_k",), dma=True, stream=f"ik{par}")
            p.op("gpsimd", lambda gg, g=g, i=i, par=par: gg.dma_start(
                out=qF[:, i * SEQ + g * T:i * SEQ + (g + 1) * T],
                in_=d_qloc[:, i * T:(i + 1) * T]),
                wr=("qF",), dma=True, stream=f"jq{par}")
            p.op("gpsimd", lambda gg, g=g, i=i, par=par: gg.dma_start(
                out=kF[:, i * SEQ + g * T:i * SEQ + (g + 1) * T],
                in_=d_kloc[:, i * T:(i + 1) * T]),
                wr=("kF",), dma=True, stream=f"jk{par}")
            p.op("vector", lambda v, g=g, i=i, cL=cL: v.tensor_scalar(
                qF[:, i * SEQ + g * T:i * SEQ + (g + 1) * T],
                qF[:, i * SEQ + g * T:i * SEQ + (g + 1) * T], cL[:, 0:1],
                None, ALU.mult), rd=("qF", cLk), wr=("qF",))
            p.op("vector", lambda v, g=g, i=i, par=par, cE=cE:
                 v.scalar_tensor_tensor(
                     qF[:, i * SEQ + g * T:i * SEQ + (g + 1) * T],
                     stq[par][:, :], cE[:, 0:1],
                     qF[:, i * SEQ + g * T:i * SEQ + (g + 1) * T],
                     ALU.mult, ALU.add),
                 rd=(f"stq{par}", cEk, "qF"), wr=("qF",))
            p.op("vector", lambda v, g=g, i=i, cL=cL: v.tensor_scalar(
                kF[:, i * SEQ + g * T:i * SEQ + (g + 1) * T],
                kF[:, i * SEQ + g * T:i * SEQ + (g + 1) * T], cL[:, 0:1],
                None, ALU.mult), rd=("kF", cLk), wr=("kF",))
            p.op("vector", lambda v, g=g, i=i, par=par, cE=cE:
                 v.scalar_tensor_tensor(
                     kF[:, i * SEQ + g * T:i * SEQ + (g + 1) * T],
                     stk[par][:, :], cE[:, 0:1],
                     kF[:, i * SEQ + g * T:i * SEQ + (g + 1) * T],
                     ALU.mult, ALU.add),
                 rd=(f"stk{par}", cEk, "kF"), wr=("kF",))
    for blk in range(32):
        p.mm_group("tp_ps3", [
            (lambda t, blk=blk: t.transpose(
                tp_ps3[:, 0:3], gcumF[0:3, blk * 128:(blk + 1) * 128],
                eyeS[0:3, 0:3]), ("gcumF", "eye"))])
        p.op("vector", lambda v, blk=blk: v.tensor_copy(
            gcolB[:, blk * 3:(blk + 1) * 3], tp_ps3[:, 0:3]),
            rd=("tp_ps3",), wr=("gcolB",))
        p.mm_group("tp_ps3", [
            (lambda t, blk=blk: t.transpose(
                tp_ps3[:, 0:3], browF[0:3, blk * 128:(blk + 1) * 128],
                eyeS[0:3, 0:3]), ("browF", "eye"))])
        p.op("vector", lambda v, blk=blk: v.tensor_copy(
            bcolB[:, blk * 3:(blk + 1) * 3], tp_ps3[:, 0:3]),
            rd=("tp_ps3",), wr=("bcolB",))
    p.barrier(keep=("exo_v",))

    masksF = {}
    for nm, dd_ in [("bdl", bdl_d), ("bdlT", bdlT_d), ("sbl", sbl_d),
                    ("sblT", sblT_d), ("trui", trui_d)]:
        masksF[nm] = p.sb("m3" + nm, [128, 128], F32)
        p.op("sync", lambda s, nm=nm, dd_=dd_: s.dma_start(
            out=masksF[nm][:, :], in_=dd_), wr=(nm,), dma=True, stream="msk")
    masksF["eye"] = eyeS
    vslP = [p.sb(f"vsl{i}", [128, DV], BF16) for i in range(2)]
    vseP = [p.sb(f"vse{i}", [128, DV], BF16) for i in range(2)]
    PSA = [p.ps(f"psA{j}", [128, DV], F32) for j in range(2)]
    PSC = [p.ps(f"psC{j}", [128, 128], F32) for j in range(2)]
    PSD = [p.ps(f"psD{j}", [128, 128], F32) for j in range(2)]
    psER = p.ps("psER", [128, 130], F32)
    psT = p.ps("psT", [128, DV], BF16)

    BS = []
    for _hi in range(3):
        _d = {}
        _d["S0"] = p.sb(f"S0_{_hi}", [128, DV], F32)
        _d["S016"] = p.sb(f"S016_{_hi}", [128, DV], BF16)
        _d["ktc"] = p.sb(f"ktc_{_hi}", [128, DK], BF16)
        _d["vtc"] = p.sb(f"vtc_{_hi}", [128, DV], BF16)
        _d["vfm"] = p.sb(f"vfm_{_hi}", [128, DV], BF16)
        _d["bcol"] = p.sb(f"bcol_{_hi}", [128, 1], F32)
        _d["lamc"] = p.sb(f"lamc_{_hi}", [128, 1], F32)
        _d["lamCc"] = p.sb(f"lamCc_{_hi}", [128, 1], F32)
        _d["nbl"] = p.sb(f"nbl_{_hi}", [128, 1], F32)
        _d["ecol"] = p.sb(f"ecol_{_hi}", [128, 1], F32)
        _d["negel"] = p.sb(f"negel_{_hi}", [128, 1], F32)
        _d["growr"] = p.sb(f"growr_{_hi}", [1, 128], F32)
        _d["browr"] = p.sb(f"browr_{_hi}", [1, 128], F32)
        _d["dec"] = p.sb(f"dec_{_hi}", [128, 128], F32)
        _d["dec2"] = p.sb(f"dec2_{_hi}", [128, 128], F32)
        _d["dd"] = p.sb(f"dd_{_hi}", [128, 128], F32)
        _d["ddt"] = p.sb(f"ddt_{_hi}", [128, 128], F32)
        _d["aqt"] = p.sb(f"aqt_{_hi}", [128, 128], BF16)
        _d["Dm"] = p.sb(f"Dm_{_hi}", [128, 128], BF16)
        _d["DmT"] = p.sb(f"DmT_{_hi}", [128, 128], BF16)
        _d["Nm"] = p.sb(f"Nm_{_hi}", [128, 128], BF16)
        _d["NmT"] = p.sb(f"NmT_{_hi}", [128, 128], BF16)
        _d["Wm"] = p.sb(f"Wm_{_hi}", [128, 128], BF16)
        _d["WmT"] = p.sb(f"WmT_{_hi}", [128, 128], BF16)
        _d["Zm"] = p.sb(f"Zm_{_hi}", [128, 128], BF16)
        _d["ZmT"] = p.sb(f"ZmT_{_hi}", [128, 128], BF16)
        _d["Z2"] = p.sb(f"Z2_{_hi}", [128, 128], BF16)
        _d["Z2T"] = p.sb(f"Z2T_{_hi}", [128, 128], BF16)
        _d["Z4T"] = p.sb(f"Z4T_{_hi}", [128, 128], BF16)
        _d["tmpA"] = p.sb(f"tmpA_{_hi}", [128, 128], BF16)
        _d["tmpB"] = p.sb(f"tmpB_{_hi}", [128, 128], BF16)
        _d["U"] = p.sb(f"U_{_hi}", [128, DV], BF16)
        _d["Uf"] = p.sb(f"Uf_{_hi}", [128, DV], F32)
        _d["rc16"] = p.sb(f"rc16_{_hi}", [128, DV], BF16)
        _d["ktil"] = p.sb(f"ktil_{_hi}", [128, DK], BF16)
        _d["qlam"] = p.sb(f"qlam_{_hi}", [128, 128], BF16)
        _d["ot"] = p.sb(f"ot_{_hi}", [128, DV], BF16)
        BS.append(_d)
    def mm1(psk, psum, stat, statk, mov, movk):
        p.mm_group(psk, [
            (lambda t, psum=psum, stat=stat, mov=mov: t.matmul(
                psum, stat, mov, start=True, stop=True), (statk, movk))])


    def emit_init(i):
        sx = str(i)
        bs = BS[i]
        S0 = bs["S0"]
        S016 = bs["S016"]
        p.op("vector", lambda v: v.memset(S0[:, :], 0.0), wr=(("S0"+sx),))
        p.op("vector", lambda v: v.tensor_copy(S016[:, :], S0[:, :]),
             rd=(("S0"+sx),), wr=(("S016"+sx),))

    def chain_gen(i):
        sx = str(i)
        bs = BS[i]
        j2 = i % 2
        psA = PSA[j2]
        psC = PSC[j2]
        psD = PSD[j2]
        kpsA = "ps" + "A" + str(j2)
        kpsC = "ps" + "C" + str(j2)
        kpsD = "ps" + "D" + str(j2)
        S0 = bs["S0"]
        S016 = bs["S016"]
        ktc = bs["ktc"]
        vtc = bs["vtc"]
        vfm = bs["vfm"]
        bcol = bs["bcol"]
        lamc = bs["lamc"]
        lamCc = bs["lamCc"]
        nbl = bs["nbl"]
        ecol = bs["ecol"]
        negel = bs["negel"]
        growr = bs["growr"]
        browr = bs["browr"]
        dec = bs["dec"]
        dec2 = bs["dec2"]
        dd = bs["dd"]
        ddt = bs["ddt"]
        aqt = bs["aqt"]
        Dm = bs["Dm"]
        DmT = bs["DmT"]
        Nm = bs["Nm"]
        NmT = bs["NmT"]
        Wm = bs["Wm"]
        WmT = bs["WmT"]
        Zm = bs["Zm"]
        ZmT = bs["ZmT"]
        Z2 = bs["Z2"]
        Z2T = bs["Z2T"]
        Z4T = bs["Z4T"]
        tmpA = bs["tmpA"]
        tmpB = bs["tmpB"]
        U = bs["U"]
        Uf = bs["Uf"]
        rc16 = bs["rc16"]
        ktil = bs["ktil"]
        qlam = bs["qlam"]
        ot = bs["ot"]
        for ci in range(SEQ // 128):
            c0 = ci * 128
            cix = ci * 3 + i
            g = ci // 16
            rel = (ci % 16) * 128
            par = (ci * 3 + i) % 2
            cL, cLk = (fc0, "fc0") if g == 0 else (fc1, "fc1")
            cE, cEk = (fc1, "fc1") if g == 0 else (fc0, "fc0")
            vsl, vse = vslP[par], vseP[par]
            for j in range(2):
                p.op("sync", lambda s, i=i, j=j, rel=rel, vsl=vsl: s.dma_start(
                    out=vsl[:, j * 128:(j + 1) * 128],
                    in_=d_vloc[:, (2 * i + j) * T + rel:
                               (2 * i + j) * T + rel + 128]),
                    wr=(f"vsl{par}",), dma=True, stream=f"ld2{par}")
                p.op("sync", lambda s, i=i, j=j, g=g, rel=rel, vse=vse:
                     s.dma_start(
                         out=vse[:, j * 128:(j + 1) * 128],
                         in_=exo_v[g * 128:(g + 1) * 128,
                                   (2 * i + j) * T + rel:
                                   (2 * i + j) * T + rel + 128]),
                     wr=(f"vse{par}",), rd=("exo_v",), dma=True,
                     stream=f"ld3{par}")
            p.op("vector", lambda v, vsl=vsl, cL=cL: v.tensor_scalar(
                vfm[:, :], vsl[:, :], cL[:, 0:1], None, ALU.mult),
                rd=(f"vsl{par}", cLk), wr=(("vfm"+sx),))
            p.op("vector", lambda v, vse=vse, cE=cE: v.scalar_tensor_tensor(
                vfm[:, :], vse[:, :], cE[:, 0:1], vfm[:, :],
                ALU.mult, ALU.add),
                rd=(f"vse{par}", cEk, ("vfm"+sx)), wr=(("vfm"+sx),))
            p.mm_group("psT", [
                (lambda t, j=j: t.transpose(
                    psT[:, j * 128:(j + 1) * 128],
                    vfm[:, j * 128:(j + 1) * 128], eye16[:, :]),
                 (("vfm"+sx), "eye16")) for j in range(2)])
            p.op("scalar", lambda s: s.copy(vtc[:, :], psT[:, :]),
                 rd=("psT",), wr=(("vtc"+sx),))
            yield
            p.mm_group("psT", [
                (lambda t, i=i, c0=c0: t.transpose(
                    psT[:, 0:128], kF[:, i * SEQ + c0:i * SEQ + c0 + 128],
                    eye16[:, :]), ("kF", "eye16"))])
            p.op("scalar", lambda s: s.copy(ktc[:, :], psT[:, 0:128]),
                 rd=("psT",), wr=(("ktc"+sx),))
            yield
            p.op("scalar", lambda _s, cix=cix: _s.copy(
                bcol[:, :], bcolB[:, cix:cix + 1]), rd=("bcolB",), wr=(("bcol"+sx),))
            p.op("scalar", lambda s, cix=cix: s.activation(
                lamc[:, :], gcolB[:, cix:cix + 1], AF.Exp),
                rd=("gcolB",), wr=(("lamc"+sx),))
            mm1("psER", psER[:, 128:129], lrow[:, :], "lrow", lamc[:, :], ("lamc"+sx))
            p.op("scalar", lambda _s: _s.copy(lamCc[:, :], psER[:, 128:129]),
                 rd=("psER",), wr=(("lamCc"+sx),))
            p.op("scalar", lambda s, cix=cix: s.activation(
                negel[:, :], gcolB[:, cix:cix + 1], AF.Exp, scale=-1.0),
                rd=("gcolB",), wr=(("negel"+sx),))
            p.op("vector", lambda v: v.tensor_mul(ecol[:, :], negel[:, :],
                                                  lamCc[:, :]),
                 rd=(("negel"+sx), ("lamCc"+sx)), wr=(("ecol"+sx),))
            p.op("vector", lambda v: v.tensor_mul(nbl[:, :], bcol[:, :],
                                                  lamc[:, :]),
                 rd=(("bcol"+sx), ("lamc"+sx)), wr=(("nbl"+sx),))
            p.op("vector", lambda v: v.tensor_scalar_mul(nbl[:, :], nbl[:, :],
                                                         -1.0),
                 rd=(("nbl"+sx),), wr=(("nbl"+sx),))
            mm1("psER", psER[0:1, 0:128], gcolB[:, cix:cix + 1], "gcolB",
                eyeS[:, :], "eye")
            p.op("scalar", lambda _s: _s.copy(growr[:, :], psER[0:1, 0:128]),
                 rd=("psER",), wr=(("growr"+sx),))
            mm1("psER", psER[0:1, 0:128], bcolB[:, cix:cix + 1], "bcolB",
                eyeS[:, :], "eye")
            p.op("scalar", lambda _s: _s.copy(browr[:, :], psER[0:1, 0:128]),
                 rd=("psER",), wr=(("browr"+sx),))
            mm1(kpsC, psC[:, :], ones_s[0:1, :], "ones_s",
                growr[:, :], ("growr"+sx))
            p.op("vector", lambda v, cix=cix: v.tensor_scalar(
                dec[:, :], psC[:, :], gcolB[:, cix:cix + 1], 0.0,
                ALU.subtract, ALU.min), rd=(kpsC, "gcolB"), wr=(("dec"+sx),))
            p.op("scalar", lambda s: s.activation(dec[:, :], dec[:, :], AF.Exp),
                 rd=(("dec"+sx),), wr=(("dec"+sx),))
            p.op("vector", lambda v, cix=cix: v.tensor_scalar(
                dec2[:, :], psC[:, :], gcolB[:, cix:cix + 1], None,
                ALU.subtract), rd=(kpsC, "gcolB"), wr=(("dec2"+sx),))
            p.op("vector", lambda v: v.tensor_scalar(
                dec2[:, :], dec2[:, :], -1.0, 0.0, ALU.mult, ALU.min),
                rd=(("dec2"+sx),), wr=(("dec2"+sx),))
            p.op("scalar", lambda s: s.activation(dec2[:, :], dec2[:, :],
                                                  AF.Exp),
                 rd=(("dec2"+sx),), wr=(("dec2"+sx),))
            mm1(kpsD, psD[:, :], ones_s[0:1, :], "ones_s",
                browr[:, :], ("browr"+sx))
            yield
            p.mm_group(kpsC, [
                (lambda t, i=i, c0=c0: t.matmul(
                    psC[:, :], kF[:, i * SEQ + c0:i * SEQ + c0 + 128],
                    kF[:, i * SEQ + c0:i * SEQ + c0 + 128],
                    start=True, stop=True), ("kF",))])
            p.op("vector", lambda v: v.tensor_mul(dd[:, :], psC[:, :],
                                                  dec2[:, :]),
                 rd=(kpsC, ("dec2"+sx)), wr=(("dd"+sx),))
            p.op("vector", lambda v: v.tensor_scalar(
                dd[:, :], dd[:, :], bcol[:, 0:1], None, ALU.mult),
                rd=(("dd"+sx), ("bcol"+sx)), wr=(("dd"+sx),))
            p.op("vector", lambda v: v.tensor_mul(ddt[:, :], psC[:, :],
                                                  dec[:, :]),
                 rd=(kpsC, ("dec"+sx)), wr=(("ddt"+sx),))
            p.op("vector", lambda v: v.tensor_mul(ddt[:, :], ddt[:, :],
                                                  psD[:, :]),
                 rd=(("ddt"+sx), kpsD), wr=(("ddt"+sx),))
            for dst, dstk, srcn, mk in [
                    (Dm, ("Dm"+sx), ("dd"+sx), "bdl"), (Nm, ("Nm"+sx), ("dd"+sx), "sbl"),
                    (DmT, ("DmT"+sx), ("ddt"+sx), "bdlT"), (NmT, ("NmT"+sx), ("ddt"+sx), "sblT")]:
                srcb = dd if srcn == ("dd"+sx) else ddt
                p.op("vector", lambda v, dst=dst, srcb=srcb, mk=mk: v.tensor_mul(
                    dst[:, :], srcb[:, :], masksF[mk][:, :]),
                    rd=(srcn, mk), wr=(dstk,))
            p.mm_group(kpsC, [
                (lambda t, i=i, c0=c0: t.matmul(
                    psC[:, :], kF[:, i * SEQ + c0:i * SEQ + c0 + 128],
                    qF[:, i * SEQ + c0:i * SEQ + c0 + 128],
                    start=True, stop=True), ("kF", "qF"))])
            p.op("vector", lambda v: v.tensor_mul(aqt[:, :], psC[:, :],
                                                  dec[:, :]),
                 rd=(kpsC, ("dec"+sx)), wr=(("aqt"+sx),))
            p.op("vector", lambda v: v.tensor_mul(
                aqt[:, :], aqt[:, :], masksF["trui"][:, :]),
                rd=(("aqt"+sx), "trui"), wr=(("aqt"+sx),))
            yield
            p.mm_group(kpsA, [
                (lambda t, i=i, c0=c0: t.matmul(
                    psA[:, :], kF[:, i * SEQ + c0:i * SEQ + c0 + 128],
                    S016[:, :], start=True, stop=True), ("kF", ("S016"+sx)))])
            p.op("vector", lambda v: v.tensor_scalar(
                Uf[:, :], vtc[:, :], bcol[:, 0:1], None, ALU.mult),
                rd=(("vtc"+sx), ("bcol"+sx)), wr=(("Uf"+sx),))
            p.op("vector", lambda v: v.scalar_tensor_tensor(
                Uf[:, :], psA[:, :], nbl[:, 0:1], Uf[:, :],
                ALU.mult, ALU.add),
                rd=(kpsA, ("nbl"+sx), ("Uf"+sx)), wr=(("Uf"+sx),))
            p.op("vector", lambda v: v.tensor_copy(rc16[:, :], Uf[:, :]),
                 rd=(("Uf"+sx),), wr=(("rc16"+sx),))
            yield
            p.op("vector", lambda v: v.tensor_sub(Wm[:, :],
                                                  masksF["eye"][:, :], Dm[:, :]),
                 rd=("eye", ("Dm"+sx)), wr=(("Wm"+sx),))
            p.op("vector", lambda v: v.tensor_sub(WmT[:, :],
                                                  masksF["eye"][:, :], DmT[:, :]),
                 rd=("eye", ("DmT"+sx)), wr=(("WmT"+sx),))
            pk, pkT, pkk, pkTk = Dm, DmT, ("Dm"+sx), ("DmT"+sx)
            for lev in range(3):
                yield
                mm1(kpsC, psC[:, :], pkT[:, :], pkTk, pk[:, :], pkk)
                mm1(kpsD, psD[:, :], pk[:, :], pkk, pkT[:, :], pkTk)
                p.op("vector", lambda v: v.tensor_add(
                    tmpA[:, :], psC[:, :], masksF["eye"][:, :]),
                    rd=(kpsC, "eye"), wr=(("tmpA"+sx),))
                p.op("vector", lambda v: v.tensor_add(
                    tmpB[:, :], psD[:, :], masksF["eye"][:, :]),
                    rd=(kpsD, "eye"), wr=(("tmpB"+sx),))
                if lev < 2:
                    p.op("scalar", lambda _s: _s.copy(Z2[:, :], psC[:, :]),
                         rd=(kpsC,), wr=(("Z2"+sx),)) if lev == 0 else \
                    p.op("scalar", lambda _s: _s.copy(Z4T[:, :], psC[:, :]),
                         rd=(kpsC,), wr=(("Z4T"+sx),))
                    p.op("vector", lambda v: v.tensor_copy(Z2T[:, :], psD[:, :]),
                         rd=(kpsD,), wr=(("Z2T"+sx),)) if lev == 0 else \
                    p.op("vector", lambda v: v.tensor_copy(ZmT[:, :], psD[:, :]),
                         rd=(kpsD,), wr=(("ZmT"+sx),))
                mm1(kpsC, psC[:, :], WmT[:, :], ("WmT"+sx), tmpA[:, :], ("tmpA"+sx))
                mm1(kpsD, psD[:, :], tmpA[:, :], ("tmpA"+sx), WmT[:, :], ("WmT"+sx))
                p.op("scalar", lambda _s: _s.copy(Wm[:, :], psC[:, :]),
                     rd=(kpsC,), wr=(("Wm"+sx),))
                p.op("vector", lambda v: v.tensor_copy(WmT[:, :], psD[:, :]),
                     rd=(kpsD,), wr=(("WmT"+sx),))
                if lev == 0:
                    pk, pkT, pkk, pkTk = Z2, Z2T, ("Z2"+sx), ("Z2T"+sx)
                elif lev == 1:
                    pk, pkT, pkk, pkTk = Z4T, ZmT, ("Z4T"+sx), ("ZmT"+sx)
            yield
            mm1(kpsC, psC[:, :], WmT[:, :], ("WmT"+sx), Nm[:, :], ("Nm"+sx))
            mm1(kpsD, psD[:, :], Nm[:, :], ("Nm"+sx), WmT[:, :], ("WmT"+sx))
            p.op("scalar", lambda _s: _s.copy(Zm[:, :], psC[:, :]),
                 rd=(kpsC,), wr=(("Zm"+sx),))
            p.op("vector", lambda v: v.tensor_copy(ZmT[:, :], psD[:, :]),
                 rd=(kpsD,), wr=(("ZmT"+sx),))
            mm1(kpsC, psC[:, :], ZmT[:, :], ("ZmT"+sx), Zm[:, :], ("Zm"+sx))
            mm1(kpsD, psD[:, :], Zm[:, :], ("Zm"+sx), ZmT[:, :], ("ZmT"+sx))
            p.op("scalar", lambda _s: _s.copy(Z2[:, :], psC[:, :]),
                 rd=(kpsC,), wr=(("Z2"+sx),))
            p.op("vector", lambda v: v.tensor_copy(Z2T[:, :], psD[:, :]),
                 rd=(kpsD,), wr=(("Z2T"+sx),))
            mm1(kpsC, psC[:, :], Z2[:, :], ("Z2"+sx), Z2T[:, :], ("Z2T"+sx))
            p.op("scalar", lambda _s: _s.copy(Z4T[:, :], psC[:, :]),
                 rd=(kpsC,), wr=(("Z4T"+sx),))
            yield
            mm1(kpsA, psA[:, :], WmT[:, :], ("WmT"+sx), rc16[:, :], ("rc16"+sx))
            p.op("scalar", lambda _s: _s.copy(Uf[:, :], psA[:, :]),
                 rd=(kpsA,), wr=(("Uf"+sx),))
            p.op("vector", lambda v: v.tensor_copy(U[:, :], Uf[:, :]),
                 rd=(("Uf"+sx),), wr=(("U"+sx),))
            for statb, statk, sign in [(Z4T, ("Z4T"+sx), 1), (Z2T, ("Z2T"+sx), 1),
                                       (ZmT, ("ZmT"+sx), -1)]:
                mm1(kpsA, psA[:, :], statb[:, :], statk, U[:, :], ("U"+sx))
                if sign > 0:
                    p.op("vector", lambda v: v.tensor_add(
                        Uf[:, :], Uf[:, :], psA[:, :]),
                        rd=(kpsA, ("Uf"+sx)), wr=(("Uf"+sx),))
                else:
                    p.op("vector", lambda v: v.tensor_sub(
                        Uf[:, :], Uf[:, :], psA[:, :]),
                        rd=(kpsA, ("Uf"+sx)), wr=(("Uf"+sx),))
                p.op("vector", lambda v: v.tensor_copy(U[:, :], Uf[:, :]),
                     rd=(("Uf"+sx),), wr=(("U"+sx),))
            yield
            mm1(kpsC, psC[:, :], ones_s[0:1, :], "ones_s",
                growr[:, :], ("growr"+sx))
            p.op("scalar", lambda s: s.activation(dec[:, :], psC[:, :],
                                                  AF.Exp),
                 rd=(kpsC,), wr=(("dec"+sx),))
            p.op("vector", lambda v, i=i, c0=c0: v.tensor_mul(
                qlam[:, :], qF[:, i * SEQ + c0:i * SEQ + c0 + 128], dec[:, :]),
                rd=("qF", ("dec"+sx)), wr=(("qlam"+sx),))
            p.mm_group(kpsA, [
                (lambda t: t.matmul(psA[:, :], aqt[:, :], U[:, :],
                                    start=True, stop=False),
                 (("aqt"+sx), ("U"+sx))),
                (lambda t: t.matmul(psA[:, :], qlam[:, :], S016[:, :],
                                    start=False, stop=True),
                 (("qlam"+sx), ("S016"+sx)))])
            p.op("scalar", lambda s: s.copy(ot[:, :], psA[:, :]),
                 rd=(kpsA,), wr=(("ot"+sx),))
            p.op("sync", lambda s, i=i, c0=c0: s.dma_start(
                out=d_oF[i * SEQ + c0:i * SEQ + c0 + 128, :], in_=ot[:, :]),
                rd=(("ot"+sx),), dma=True, stream="oo")
            yield
            p.op("vector", lambda v: v.tensor_scalar(
                ktil[:, :], ktc[:, :], ecol[:, 0:1], None, ALU.mult),
                rd=(("ktc"+sx), ("ecol"+sx)), wr=(("ktil"+sx),))
            mm1(kpsA, psA[:, :], ktil[:, :], ("ktil"+sx), U[:, :], ("U"+sx))
            p.op("vector", lambda v: v.scalar_tensor_tensor(
                S0[:, :], S0[:, :], lamCc[:, 0:1], psA[:, :],
                ALU.mult, ALU.add),
                rd=(kpsA, ("lamCc"+sx), ("S0"+sx)), wr=(("S0"+sx),))
            p.op("scalar", lambda _s: _s.copy(S016[:, :], S0[:, :]),
                 rd=(("S0"+sx),), wr=(("S016"+sx),))

    for _hi in range(3):
        emit_init(_hi)
    _gens = [chain_gen(_hi) for _hi in range(3)]
    _alive = list(_gens)
    while _alive:
        for _g in list(_alive):
            try:
                next(_g)
            except StopIteration:
                _alive.remove(_g)
    p.barrier()
    era_qkv.close()
    eraM2.close()
    era3.close()

    era4 = ExitStack()
    res1 = p.sb("res1", [128, 8 * T], F32, scope=era4)
    xn1b = p.sb("xn1b", [128, 8 * T], BF16, scope=era4)
    for d in range(8):
        p.op("sync", lambda s, d=d: s.dma_start(
            out=res1[:, d * T:(d + 1) * T], in_=d_out1[d * 128:(d + 1) * 128, :]),
            wr=("res1",), dma=True, stream="xin2")
        p.op("sync", lambda s, d=d: s.dma_start(
            out=xn1b[:, d * T:(d + 1) * T], in_=d_xn1[d * 128:(d + 1) * 128, :]),
            wr=("xn1b",), dma=True, stream="xin")
    wo_s = p.sb("wo_s", [128, 12 * DM], BF16)
    nw_s = p.sb("nw_s", [128, 2], F32)
    ov16 = p.sb("ov16", [128, 2 * T], BF16)
    gs16b = p.sb("gs16b", [128, 2 * T], BF16)
    an16 = p.sb("an16", [128, 2 * T], BF16)
    ovf = p.sb("ovf", [128, 2 * T], F32)
    sqf = p.sb("sqf", [128, T], F32)
    rrb = p.sb("rrb", [128, T], F32)
    ssps4 = p.ps("ssps4", [128, 512], F32)
    aps = p.ps("aps", [128, 512], F32)
    aps2 = p.ps("aps2", [128, 512], F32)
    ovA = p.sb("ovA", [128, T], BF16)
    ovB = p.sb("ovB", [128, T], BF16)
    p.op("gpsimd", lambda g: g.collective_compute(
        "AllGather", ALU.bypass, replica_groups=RG_PAIRS,
        ins=[d_oF.opt()], outs=[exo_o.opt()]), wr=("exo_o",))
    for kt in range(12):
        p.op("gpsimd", lambda g, kt=kt: g.dma_start(
            out=wo_s[:, kt * DM:(kt + 1) * DM],
            in_=wo_dr[kt * 128:(kt + 1) * 128, :]), wr=("wo_s",), dma=True)
    p.op("sync", lambda s: s.dma_start(out=nw_s[:, :], in_=nw_dr),
         wr=("nw_s",), dma=True, stream="xin")
    for h in range(H):
        for j in range(2):
            if h < 3:
                sA = d_oF[h * SEQ:h * SEQ + T, j * 128:(j + 1) * 128]
                sB = d_oF[h * SEQ + T:h * SEQ + 2 * T, j * 128:(j + 1) * 128]
                rdk = ()
            else:
                sA = exo_o[3 * SEQ + (h - 3) * SEQ:
                           3 * SEQ + (h - 3) * SEQ + T, j * 128:(j + 1) * 128]
                sB = exo_o[(h - 3) * SEQ + T:(h - 3) * SEQ + 2 * T,
                           j * 128:(j + 1) * 128]
                rdk = ("exo_o",)
            p.op("sync", lambda s, sA=sA: s.dma_start(
                out=ovA[:, :], in_=sA, transpose=True),
                wr=("ovA",), rd=rdk, dma=True, stream="oin")
            p.op("sync", lambda s, sB=sB: s.dma_start(
                out=ovB[:, :], in_=sB, transpose=True),
                wr=("ovB",), rd=rdk, dma=True, stream="oin2")
            p.op("vector", lambda v, j=j: v.tensor_scalar(
                ov16[:, j * T:(j + 1) * T], ovA[:, :], fc0[:, 0:1],
                None, ALU.mult), rd=("ovA", "fc0"), wr=("ov16",))
            p.op("vector", lambda v, j=j: v.scalar_tensor_tensor(
                ov16[:, j * T:(j + 1) * T], ovB[:, :], fc1[:, 0:1],
                ov16[:, j * T:(j + 1) * T], ALU.mult, ALU.add),
                rd=("ovB", "fc1", "ov16"), wr=("ov16",))
            p.op("sync", lambda s, h=h, j=j: s.dma_start(
                out=gs16b[:, j * T:(j + 1) * T],
                in_=d_gs[(2 * h + j) * 128:(2 * h + j + 1) * 128, :]),
                wr=("gs16b",), dma=True, stream="gin")
        for sl in range(4):
            f0 = sl * 512
            for j in range(2):
                p.op("scalar", lambda s, j=j, f0=f0: s.activation(
                    sqf[:, f0:f0 + 512], ov16[:, j * T + f0:j * T + f0 + 512],
                    AF.Square), rd=("ov16",), wr=("sqf",))
                p.mm_group("ssps4", [
                    (lambda t, f0=f0, j=j: t.matmul(
                        ssps4[:, :], ones_s[:, :], sqf[:, f0:f0 + 512],
                        start=(j == 0), stop=(j == 1)), ("sqf", "ones_s"))])
            p.op("scalar", lambda s, f0=f0: s.activation(
                rrb[:, f0:f0 + 512], ssps4[:, :], AF.Ln, bias=epsc,
                scale=1.0 / DV), rd=("ssps4", epsk), wr=("rrb",))
        p.op("scalar", lambda s: s.activation(rrb[:, :], rrb[:, :], AF.Exp,
                                              scale=-0.5),
             rd=("rrb",), wr=("rrb",))
        for j in range(2):
            p.op("vector", lambda v, j=j: v.scalar_tensor_tensor(
                ovf[:, j * T:(j + 1) * T], ov16[:, j * T:(j + 1) * T],
                nw_s[:, j:j + 1], rrb[:, :], ALU.mult, ALU.mult),
                rd=("ov16", "nw_s", "rrb"), wr=("ovf",))
            p.op("vector", lambda v, j=j: v.tensor_mul(
                an16[:, j * T:(j + 1) * T], ovf[:, j * T:(j + 1) * T],
                gs16b[:, j * T:(j + 1) * T]),
                rd=("ovf", "gs16b"), wr=("an16",))
        for sl in range(4):
            f0 = sl * 512
            for d in range(8):
                psk, psum = ("aps", aps) if d % 2 == 0 else ("aps2", aps2)
                p.mm_group(psk, [
                    (lambda t, d=d, f0=f0, h=h, j=j, psum=psum: t.matmul(
                        psum[:, :],
                        wo_s[:, (2 * h + j) * DM + d * 128:
                             (2 * h + j) * DM + d * 128 + 128],
                        an16[:, j * T + f0:j * T + f0 + 512],
                        start=(j == 0), stop=(j == 1)), ("wo_s", "an16"))
                    for j in range(2)])
                p.op("vector", lambda v, d=d, f0=f0, psum=psum, psk=psk:
                     v.tensor_add(
                         res1[:, d * T + f0:d * T + f0 + 512],
                         res1[:, d * T + f0:d * T + f0 + 512], psum[:, :]),
                     rd=(psk, "res1"), wr=("res1",))
    p.barrier()

    CS = 512
    wupP = [p.sb(f"wup_s{i}", [128, 8 * HID], BF16) for i in range(2)]
    wdnP = [p.sb(f"wdn_s{i}", [128, 11 * DM], BF16) for i in range(2)]
    webs = p.sb("webs", [128, CS], BF16)
    xe16 = p.sb("xe16", [128, 8 * CS], BF16)
    hps = p.ps("hps", [128, CS], F32)
    hps2 = p.ps("hps2", [128, CS], F32)
    dps = p.ps("dps", [128, CS], F32)
    dps2 = p.ps("dps2", [128, CS], F32)
    h16m = p.sb("h16m", [128, 11 * CS], BF16)
    hrl = p.sb("hrl", [128, CS], BF16)
    for e in range(9):
        eb = e % 2
        wup_s, wdn_s = wupP[eb], wdnP[eb]
        kup, kdn = f"wup_s{eb}", f"wdn_s{eb}"
        for kt in range(8):
            p.op("gpsimd", lambda g, e=e, kt=kt, wup_s=wup_s: g.dma_start(
                out=wup_s[:, kt * HID:(kt + 1) * HID],
                in_=wup_dr[e * DM + kt * 128:e * DM + kt * 128 + 128, :]),
                wr=(kup,), dma=True, stream=f"wup{eb}{kt % 2}")
        for kt in range(11):
            p.op("gpsimd", lambda g, e=e, kt=kt, wdn_s=wdn_s: g.dma_start(
                out=wdn_s[:, kt * DM:(kt + 1) * DM],
                in_=wdn_dr[e * HID + kt * 128:e * HID + kt * 128 + 128, :]),
                wr=(kdn,), dma=True, stream=f"wdn{eb}{kt % 2}")
        for sl in range(T // CS):
            f0 = sl * CS
            if e < 8:
                p.op("gpsimd", lambda g, e=e, f0=f0: g.dma_start(
                    out=webs[:, 0:CS],
                    in_=d_mw[e:e + 1, f0:f0 + CS].partition_broadcast(128)),
                    wr=("webs",), dma=True, stream="web")
                for d in range(8):
                    p.op("vector", lambda v, d=d, f0=f0: v.tensor_mul(
                        xe16[:, d * CS:(d + 1) * CS],
                        xn1b[:, d * T + f0:d * T + f0 + CS], webs[:, 0:CS]),
                        rd=("xn1b", "webs"), wr=("xe16",))
            for pt in range(11):
                psk, psum = ("hps", hps) if pt % 2 == 0 else ("hps2", hps2)
                if e < 8:
                    p.mm_group(psk, [
                        (lambda t, kt=kt, pt=pt, psum=psum, wup_s=wup_s: t.matmul(
                            psum[:, :],
                            wup_s[:, kt * HID + pt * 128:kt * HID + pt * 128 + 128],
                            xe16[:, kt * CS:(kt + 1) * CS],
                            start=(kt == 0), stop=(kt == 7)),
                         (kup, "xe16")) for kt in range(8)])
                else:
                    p.mm_group(psk, [
                        (lambda t, kt=kt, pt=pt, psum=psum, f0=f0, wup_s=wup_s: t.matmul(
                            psum[:, :],
                            wup_s[:, kt * HID + pt * 128:kt * HID + pt * 128 + 128],
                            xn1b[:, kt * T + f0:kt * T + f0 + CS],
                            start=(kt == 0), stop=(kt == 7)),
                         (kup, "xn1b")) for kt in range(8)])
                p.op("vector", lambda v, psum=psum, psk=psk: v.tensor_scalar_max(
                    hrl[:, :], psum[:, :], 0.0), rd=(psk,), wr=("hrl",))
                p.op("scalar", lambda s, pt=pt: s.activation(
                    h16m[:, pt * CS:(pt + 1) * CS], hrl[:, :], AF.Square),
                    rd=("hrl",), wr=("h16m",))
            for d in range(8):
                psk, psum = ("dps", dps) if d % 2 == 0 else ("dps2", dps2)
                p.mm_group(psk, [
                    (lambda t, kt=kt, d=d, psum=psum, wdn_s=wdn_s: t.matmul(
                        psum[:, :],
                        wdn_s[:, kt * DM + d * 128:kt * DM + d * 128 + 128],
                        h16m[:, kt * CS:(kt + 1) * CS],
                        start=(kt == 0), stop=(kt == 10)),
                     (kdn, "h16m")) for kt in range(11)])
                p.op("vector", lambda v, d=d, f0=f0, psum=psum, psk=psk:
                     v.tensor_add(
                         res1[:, d * T + f0:d * T + f0 + CS],
                         res1[:, d * T + f0:d * T + f0 + CS], psum[:, :]),
                     rd=(psk, "res1"), wr=("res1",))
    for d in range(8):
        p.op("sync", lambda s, d=d: s.dma_start(
            out=o_fin[d * 128:(d + 1) * 128, :], in_=res1[:, d * T:(d + 1) * T]),
            rd=("res1",), dma=True, stream="fin")
    p.barrier()
    era4.close()
    return p


def host_fused(inputs):
    x = np.asarray(inputs["L0_attn_x"], dtype=np.float32)
    wx = np.asarray(inputs["L0_attn_W_x"], np.float32)
    wxa = np.zeros((DMAMBA, 128), np.float32)
    wxa[:, 0:64] = wx[:, 0:64]
    wxa[:, 64:80] = wx[:, 64:80]
    wq = np.asarray(inputs["L1_attn_Wq"], np.float32)
    wk = np.asarray(inputs["L1_attn_Wk"], np.float32)
    wv = np.asarray(inputs["L1_attn_Wv"], np.float32)
    wg = np.asarray(inputs["L1_attn_Wg"], np.float32)
    wb = np.asarray(inputs["L1_attn_Wb"], np.float32)
    wa = np.asarray(inputs["L1_attn_Wa"], np.float32)
    wba = np.zeros((DM, 16), np.float32)
    wba[:, 0:6] = wb
    wba[:, 8:14] = wa
    cq = np.asarray(inputs["L1_attn_conv_q"], np.float32)
    ck = np.asarray(inputs["L1_attn_conv_k"], np.float32)
    cv = np.asarray(inputs["L1_attn_conv_v"], np.float32)
    cwq = np.concatenate([cq.reshape(6, 128, 4), ck.reshape(6, 128, 4),
                          cv.reshape(12, 128, 4)], axis=0)
    wup = np.asarray(inputs["L1_ffn_w_up"], np.float32)
    wdn = np.asarray(inputs["L1_ffn_w_down"], np.float32)
    sup = np.asarray(inputs["L1_ffn_shared_w_up"], np.float32)
    sdn = np.asarray(inputs["L1_ffn_shared_w_down"], np.float32)
    tril_s = np.tril(np.ones((128, 128), np.float32), -1)
    blockm = np.zeros((128, 128), np.float32)
    for r in range(8):
        blockm[r * 16:(r + 1) * 16, r * 16:(r + 1) * 16] = 1
    bdl = tril_s * blockm
    sbl = tril_s - bdl
    shared = {
        "w_in": np.ascontiguousarray(np.asarray(inputs["L0_attn_W_in"], np.float32)),
        "cwT": np.ascontiguousarray(np.asarray(inputs["L0_attn_conv_w"], np.float32)[:, 0, :].T.reshape(16, 128, 4).transpose(1, 0, 2).reshape(128, 64)),
        "cb": np.ascontiguousarray(np.asarray(inputs["L0_attn_conv_b"], np.float32).reshape(16, 128).T),
        "w_xa": wxa,
        "w_xc": np.ascontiguousarray(wx[:, 80:96]),
        "w_dt": np.ascontiguousarray(np.asarray(inputs["L0_attn_W_dt"], np.float32)),
        "b_dt": np.ascontiguousarray(np.asarray(inputs["L0_attn_b_dt"], np.float32).reshape(16, 128).T),
        "w1T": np.ascontiguousarray(np.asarray(inputs["L0_ffn_W1"], np.float32).T),
        "b1": np.ascontiguousarray(np.asarray(inputs["L0_ffn_b1"], np.float32).reshape(32, 128).T),
        "w2T": np.ascontiguousarray(np.asarray(inputs["L0_ffn_W2"], np.float32).T),
        "b2": np.ascontiguousarray(np.asarray(inputs["L0_ffn_b2"], np.float32).reshape(8, 128).T),
        "alog0": np.ascontiguousarray(np.asarray(inputs["L0_attn_A_log"], np.float32).reshape(16, 128, 16).transpose(1, 0, 2).reshape(128, 256)),
        "dvec": np.ascontiguousarray(np.asarray(inputs["L0_attn_D"], np.float32).reshape(16, 128).T),
        "wout": np.ascontiguousarray(np.asarray(inputs["L0_attn_W_out"], np.float32)),
        "wqkvg": np.ascontiguousarray(np.concatenate([wq, wk, wv, wg], axis=1)),
        "wba": np.ascontiguousarray(wba.reshape(8, 128, 16).transpose(1, 0, 2).reshape(128, 128)),
        "gwT": np.ascontiguousarray(np.asarray(inputs["L1_ffn_gate_w"], np.float32).T),
        "ebias": np.asarray(inputs["L1_ffn_expert_bias"], np.float32).reshape(1, 8),
        "cwq": np.ascontiguousarray(cwq.transpose(1, 0, 2).reshape(128, 96)),
        "alog1": np.asarray(inputs["L1_attn_A_log"], np.float32).reshape(6, 1),
        "dtb": np.asarray(inputs["L1_attn_dt_bias"], np.float32).reshape(6, 1),
        "nw": np.ascontiguousarray(np.asarray(inputs["L1_attn_norm_w"], np.float32).reshape(2, 128).T),
        "wo": np.ascontiguousarray(np.asarray(inputs["L1_attn_Wo"], np.float32)),
        "wup": np.ascontiguousarray(np.concatenate(
            [wup.transpose(0, 2, 1).reshape(8 * DM, HID), sup.T], axis=0)),
        "wdn": np.ascontiguousarray(np.concatenate(
            [wdn.transpose(0, 2, 1).reshape(8 * HID, DM), sdn.T], axis=0)),
        "bdl": bdl, "bdlT": np.ascontiguousarray(bdl.T),
        "sbl": sbl, "sblT": np.ascontiguousarray(sbl.T),
        "trui": np.triu(np.ones((128, 128), np.float32), 0),
        "eye": np.eye(128, dtype=np.float32),
        "lrowd": np.concatenate([np.zeros((127, 128), np.float32),
                                 np.ones((1, 128), np.float32)], axis=0),
    }
    mperm = np.r_[8:16, 0:8]

    def pc(a, tile):
        g = a.reshape(a.shape[0], 16, tile)
        return np.ascontiguousarray(g[:, mperm, :].reshape(a.shape[0], -1))

    def pr(a, tile):
        g = a.reshape(16, tile, a.shape[1])
        return np.ascontiguousarray(g[mperm].reshape(-1, a.shape[1]))

    swap = dict(shared)
    w_in_b = shared["w_in"]
    swap["w_in"] = np.ascontiguousarray(np.concatenate(
        [pc(w_in_b[:, :2 * DMAMBA // 2], 128),
         pc(w_in_b[:, 2 * DMAMBA // 2:], 128)], axis=1))
    swap["cwT"] = pc(shared["cwT"], 4)
    swap["cb"] = pc(shared["cb"], 1)
    swap["w_xa"] = pr(shared["w_xa"], 128)
    swap["w_xc"] = pr(shared["w_xc"], 128)
    swap["w_dt"] = pc(shared["w_dt"], 128)
    swap["b_dt"] = pc(shared["b_dt"], 1)
    swap["alog0"] = pc(shared["alog0"], 16)
    swap["dvec"] = pc(shared["dvec"], 1)
    swap["wout"] = pr(shared["wout"], 128)

    hp = [3, 4, 5, 0, 1, 2]
    wq2 = wq.reshape(DM, 6, 128)[:, hp].reshape(DM, 768)
    wk2 = wk.reshape(DM, 6, 128)[:, hp].reshape(DM, 768)
    wv2 = wv.reshape(DM, 6, 256)[:, hp].reshape(DM, 1536)
    wg2 = wg.reshape(DM, 6, 256)[:, hp].reshape(DM, 1536)
    swap["wqkvg"] = np.ascontiguousarray(
        np.concatenate([wq2, wk2, wv2, wg2], axis=1))
    wba2 = np.zeros((DM, 16), np.float32)
    wba2[:, 0:6] = wb[:, hp]
    wba2[:, 8:14] = wa[:, hp]
    swap["wba"] = np.ascontiguousarray(
        wba2.reshape(8, 128, 16).transpose(1, 0, 2).reshape(128, 128))
    cperm = hp + [6 + h for h in hp] + [12 + 2 * h + j for h in hp
                                        for j in range(2)]
    swap["cwq"] = np.ascontiguousarray(
        cwq[cperm].transpose(1, 0, 2).reshape(128, 96))
    swap["alog1"] = np.ascontiguousarray(shared["alog1"][hp])
    swap["dtb"] = np.ascontiguousarray(shared["dtb"][hp])
    swap["wo"] = np.ascontiguousarray(
        shared["wo"].reshape(6, 256, DM)[hp].reshape(VD, DM))

    maps = []
    for c in range(8):
        b, hh = c // 2, c % 2
        s0 = hh * T
        xhm = np.zeros((T + 3, DM), dtype=np.float32)
        lo, hi = s0 - 1, s0 + T + 2
        clo, chi = max(lo, 0), min(hi, SEQ)
        xhm[clo - lo:chi - lo] = x[b, clo:chi]
        m = dict(shared if hh == 0 else swap)
        m["xh"] = np.ascontiguousarray(xhm.T)
        m["fc1"] = np.full((128, 1), float(hh), np.float32)
        m["fc0"] = np.full((128, 1), 1.0 - float(hh), np.float32)
        maps.append(m)
    return maps


def kernel_fused(**inputs):
    inputs = {k: np.asarray(v) for k, v in inputs.items()}
    maps = host_fused(inputs)
    p = build_fused()
    nc = p.realize()
    res = run_bass_kernel_spmd(nc, maps, core_ids=list(range(8)))
    out = np.empty((B, SEQ, DM), np.float32)
    for c in range(8):
        b, hh = c // 2, c % 2
        out[b, hh * T:(hh + 1) * T, :] = res.results[c]["o_fin"].T
    return out


def _run(p, maps):
    nc = p.realize()
    return run_bass_kernel_spmd(nc, maps, core_ids=list(range(8)))


def kernel(**inputs):
    return kernel_fused(**inputs)


def kernel_multi(**inputs):
    inputs = {k: np.asarray(v) for k, v in inputs.items()}

    resA = _run(build_l0pre(), host_l0pre(inputs))

    def fm(res, key, rows):
        out = np.empty((rows, B, SEQ), np.float32)
        for c in range(8):
            b, h = c // 2, c % 2
            out[:, b, h * T:(h + 1) * T] = res.results[c][key]
        return out

    dt_full = fm(resA, "o_dt", DMAMBA)
    xmc_full = fm(resA, "o_xmc", DMAMBA)
    sz_full = fm(resA, "o_sz", DMAMBA)
    bc = fm(resA, "o_bc", 32)
    Bc_full = np.ascontiguousarray(bc[0:16].transpose(1, 2, 0))
    Cc_full = np.ascontiguousarray(bc[16:32].transpose(1, 2, 0))
    part_full = fm(resA, "o_part", DM)

    resB = _run(build_l0scan(),
                host_l0scan(dt_full, xmc_full, sz_full, Bc_full, Cc_full, inputs))
    yg_full = np.concatenate([resB.results[c]["o_yg"] for c in range(8)],
                             axis=0).reshape(DMAMBA, B, SEQ)

    resD = _run(build_mid(), host_mid(inputs, yg_full, part_full))
    q_all = fm(resD, "o_q", KD).transpose(1, 0, 2)
    k_all = fm(resD, "o_k", KD).transpose(1, 0, 2)
    v_all = fm(resD, "o_v", VD).transpose(1, 0, 2)
    beta_all = fm(resD, "o_beta", 6).transpose(1, 0, 2)
    g_all = fm(resD, "o_g", 6).transpose(1, 0, 2)

    resE = _run(build_l1delta(),
                host_l1delta(np.ascontiguousarray(q_all),
                             np.ascontiguousarray(k_all),
                             np.ascontiguousarray(v_all),
                             np.ascontiguousarray(beta_all),
                             np.ascontiguousarray(g_all)))
    o_fm = np.empty((B, VD, SEQ), np.float32)
    for c in range(8):
        for i in range(3):
            gidx = 3 * c + i
            b, h = gidx // H, gidx % H
            o_fm[b, h * DV:(h + 1) * DV, :] = \
                resE.results[c]["o_o"][i * SEQ:(i + 1) * SEQ, :].T

    gs_c = [resD.results[c]["o_gs"] for c in range(8)]
    out1_c = [resD.results[c]["o_out1"] for c in range(8)]
    xn1_c = [resD.results[c]["o_xn1"] for c in range(8)]
    mw_c = [resD.results[c]["o_mw"] for c in range(8)]
    resF = _run(build_l1post(),
                host_l1post(inputs, o_fm, gs_c, out1_c, xn1_c, mw_c))

    out = np.empty((B, SEQ, DM), np.float32)
    for c in range(8):
        b, h = c // 2, c % 2
        out[b, h * T:(h + 1) * T, :] = resF.results[c]["o_fin"].T
    return out
